# version 01
# baseline (speedup 1.0000x reference)
import numpy as np
from contextlib import ExitStack

import concourse.bass as bass
import concourse.mybir as mybir
from concourse.bass_utils import run_bass_kernel_spmd

F32 = mybir.dt.float32
BF16 = mybir.dt.bfloat16
F16 = mybir.dt.float16
I32 = mybir.dt.int32
AF = mybir.ActivationFunctionType
ALU = mybir.AluOpType
AX = mybir.AxisListType.X

C = 2048
NC = 16
IF = 5632
NI = 44
H = 16
T_LOC = 1024
HALO = 128
T_EXT = T_LOC + HALO
NT = 8
NTE = 9
NCORES = 8
LN_EPS = 1e-5
L0_NORM_EPS = 1e-12
L0_FFN_EPS = 1e-6
L1_RMS_EPS = 1e-6
BLK1 = 256
RG = [[0, 1, 2, 3, 4, 5, 6, 7]]


def perm128(n):
    base = np.arange(n) // 128 * 128
    r = np.arange(n) % 128
    return base + 8 * (r % 16) + (r // 16)



def prep(inputs):
    f = {}
    x = np.asarray(inputs["L0_attn_x"], np.float32).reshape(8192, C)
    xp = np.concatenate([np.zeros((HALO, C), np.float32), x], 0)
    per_core = []
    for i in range(NCORES):
        st = i * T_LOC
        pc = {"xin": np.ascontiguousarray(xp[st:st + T_EXT])}
        sel = np.zeros((8, 1), np.float32)
        pw = np.zeros((8, 1), np.float32)
        for j in range(i):
            sel[j] = 1.0
            pw[j] = 1024.0 * float(i - 1 - j)
        pc["cc_sel"] = sel
        pc["cc_pow"] = pw
        pc["cc_selT"] = np.ascontiguousarray(sel.T)
        pc["cc_pownT"] = np.ascontiguousarray((-pw).T)
        per_core.append(pc)

    T = lambda a: np.ascontiguousarray(np.asarray(a).T)
    f["l0_wqT"] = T(inputs["L0_attn_Wq"])
    f["l0_wkT"] = T(inputs["L0_attn_Wk"])
    f["l0_wvT"] = T(inputs["L0_attn_Wv"])
    f["l0_woT"] = T(inputs["L0_attn_Wo"])
    f["l0_wgT"] = T(inputs["L0_attn_gate_w"])
    cw = np.asarray(inputs["L0_attn_conv_w"], np.float32).reshape(C, 4)
    f["l0_convw"] = np.ascontiguousarray(
        cw.reshape(NC, 128, 4).transpose(1, 0, 2).reshape(128, NC * 4))
    f["l0_convb"] = np.ascontiguousarray(
        np.asarray(inputs["L0_attn_conv_b"], np.float32).reshape(NC, 128).T)
    f["l0_gateb"] = np.asarray(inputs["L0_attn_gate_b"], np.float32).reshape(1, C)
    f["l0_lnw"] = np.asarray(inputs["L0_attn_ln_w"], np.float32).reshape(1, C)
    f["l0_lnb"] = np.asarray(inputs["L0_attn_ln_b"], np.float32).reshape(1, C)
    gam = np.asarray(inputs["L0_attn_gamma"], np.float32).reshape(H)
    f["l0_gamma"] = gam.reshape(1, H)
    f["l0_gamma_rep"] = np.ascontiguousarray(np.repeat(gam, 128).reshape(1, C))

    gup = np.asarray(inputs["L0_ffn_gate_up_packed"], np.int32)
    gsc = np.asarray(inputs["L0_ffn_gate_up_scales"])
    f["l0_pgT"] = T(gup[:IF])
    f["l0_puT"] = T(gup[IF:])
    f["l0_sg"] = np.ascontiguousarray(gsc[:, :IF])
    f["l0_su"] = np.ascontiguousarray(gsc[:, IF:])
    f["l0_pdT"] = T(np.asarray(inputs["L0_ffn_down_packed"], np.int32))
    f["l0_sd"] = np.ascontiguousarray(np.asarray(inputs["L0_ffn_down_scales"]))

    f["l1_slope"] = np.asarray(inputs["L1_attn_slope_rate"], np.float32).reshape(1, H)
    f["l1_wqkvT"] = T(inputs["L1_attn_w_qkv"])
    f["l1_wgT"] = T(inputs["L1_attn_w_gate"])
    f["l1_woT"] = T(inputs["L1_attn_w_out"])
    f["l1_normw"] = np.asarray(inputs["L1_attn_norm_w"], np.float32).reshape(1, C)
    f["l1_qgT"] = T(np.asarray(inputs["L1_ffn_wq_gate"], np.int32))
    f["l1_quT"] = T(np.asarray(inputs["L1_ffn_wq_up"], np.int32))
    f["l1_qdT"] = T(np.asarray(inputs["L1_ffn_wq_down"], np.int32))
    f["l1_sgc"] = np.ascontiguousarray(
        np.asarray(inputs["L1_ffn_s_gate"], np.float32).reshape(NI, 128).T)
    f["l1_suc"] = np.ascontiguousarray(
        np.asarray(inputs["L1_ffn_s_up"], np.float32).reshape(NI, 128).T)
    f["l1_sd"] = np.asarray(inputs["L1_ffn_s_down"], np.float32).reshape(1, C)

    f["identity"] = np.eye(128, dtype=np.float32)
    ar = np.arange(128, dtype=np.float32)
    f["iota_mj"] = np.ascontiguousarray(ar[None, :] - ar[:, None])
    f["mask_mj"] = np.ascontiguousarray((ar[None, :] >= ar[:, None]).astype(np.float32))
    f["grev"] = np.ascontiguousarray((127.0 - ar).reshape(128, 1))
    jm0 = np.arange(0, 128)[:, None] - np.arange(BLK1)[None, :]
    jm1 = np.arange(128, 256)[:, None] - np.arange(BLK1)[None, :]
    f["l1_i0"] = np.ascontiguousarray(jm0.astype(np.float32))
    f["l1_i1"] = np.ascontiguousarray(jm1.astype(np.float32))
    f["l1_msk0"] = np.ascontiguousarray((jm0 <= 0).astype(np.float32))
    f["l1_msk1"] = np.ascontiguousarray((jm1 <= 0).astype(np.float32))
    f["l1_qm"] = (-((np.arange(T_LOC) % BLK1) + 1.0)).astype(np.float32).reshape(1, T_LOC)
    f["l1_km"] = np.ascontiguousarray(np.stack(
        [-(255.0 - np.arange(0, 128)), -(255.0 - np.arange(128, 256))], 1).astype(np.float32))
    f["shfcol"] = (4 * (np.arange(128) // 16)).astype(np.int32).reshape(128, 1)
    return f, per_core



class Ctx:
    def __init__(self, nc, es):
        self.nc = nc
        self.es = es
        self.block = None
        self.cnt = {}
        self.sem = {}
        self.d = {}
        self.s = {}
        self._uid = 0
        for name in ["dl", "dg", "tv", "ts", "tt", "cc"]:
            self.sem[name] = es.enter_context(nc.semaphore(name))
            self.cnt[name] = 0

    def uid(self):
        self._uid += 1
        return self._uid

    def tick(self, name, inst, n=1):
        inst.then_inc(self.sem[name], n)
        self.cnt[name] += n
        return self.cnt[name]

    def dtick(self, name, inst):
        return self.tick(name, inst, 16)

    def w(self, eng, name, val=None):
        eng.wait_ge(self.sem[name], self.cnt[name] if val is None else val)

    def on(self, engname, fn):
        getattr(self.block, engname)(fn)

    def wall(self, eng):
        for n in ["dl", "dg", "tv", "ts", "tt"]:
            self.w(eng, n)

    def barrier(self):
        def fv(v):
            self.tick("tv", v.engine_nop())
        self.on("vector", fv)

        def fg(g):
            self.tick("dg", g.engine_nop(), 16)
        self.on("gpsimd", fg)
        for e in ["sync", "vector", "scalar", "tensor", "gpsimd"]:
            def fw(eng, _e=e):
                self.wall(eng)
            self.on(e, fw)

    def xr(self, t):
        return self.s["XR"][:, t * C:(t + 1) * C]

    def xnc(self, cc, Tt):
        return self.s["XN"][:, cc * Tt:(cc + 1) * Tt]

    def BA(self, i, n):
        return self.s["BIG"][:, i:i + n]

    def BB(self, i, n):
        return self.s["BIG"][:, NT * C + i:NT * C + i + n]


def alloc_common(ctx, nc, es):
    S = lambda name, shape, dt: es.enter_context(nc.sbuf_tensor(name, shape, dt))
    s = ctx.s
    s["XR"] = S("XR", [128, NTE * C], F32)
    s["XN"] = S("XN", [128, NC * T_EXT], BF16)
    s["TA"] = S("TA", [128, C], F32)
    s["BIG"] = S("BIG", [128, 2 * NT * C], BF16)
    s["WSa"] = S("WSa", [128, 2048], BF16)
    s["WSb"] = S("WSb", [128, 2048], BF16)
    s["WS2"] = S("WS2", [128, 1024], F32)
    s["BC"] = S("BC", [128, C], BF16)
    s["SM"] = S("SM", [128, 192], F32)
    s["SC"] = S("SC", [128, 512], BF16)
    s["ID"] = S("ID", [128, 128], BF16)
    s["ST"] = S("ST", [128, 64], F32)
    s["CW"] = S("CW", [128, 80], F32)
    s["QG"] = S("QG", [128, 4096], BF16)
    s["WSi"] = s["QG"].bitcast(I32)


def emit_bc_table(ctx, row_ap):
    s = ctx.s

    def fd(sync):
        ctx.w(sync, "tv")
        ctx.dtick("dl", sync.dma_start(out=s["TA"][:, :],
                                       in_=row_ap.broadcast_to([128, C])))
    ctx.on("sync", fd)

    def fc(v):
        ctx.w(v, "dl")
        ctx.tick("tv", v.tensor_copy(out=s["BC"][:], in_=s["TA"][:, :]))
    ctx.on("vector", fc)


def emit_preamble(ctx):
    s, d = ctx.s, ctx.d

    def f(sync):
        ctx.dtick("dl", sync.dma_start(out=s["SM"][:, 128:129].bitcast(I32), in_=d["shfcol"]))
        ctx.dtick("dl", sync.dma_start(out=s["WS2"][:, 0:128], in_=d["identity"]))
    ctx.on("sync", f)

    def fv(v):
        ctx.w(v, "dl")
        ctx.tick("tv", v.tensor_copy(out=s["ID"][:], in_=s["WS2"][:, 0:128]))
    ctx.on("vector", fv)


def emit_load_x(ctx, ntiles):
    def f(sync):
        for t in range(ntiles):
            ctx.dtick("dl", sync.dma_start(
                out=ctx.xr(t), in_=ctx.d["xin"][t * 128:(t + 1) * 128, :]))
    ctx.on("sync", f)


def emit_store_x(ctx, t0, ntiles):
    def f(g):
        ctx.wall(g)
        for t in range(ntiles):
            ctx.dtick("dg", g.dma_start(
                out=ctx.d["xout"][t * 128:(t + 1) * 128, :], in_=ctx.xr(t0 + t)))
        ctx.w(g, "dg")
    ctx.on("gpsimd", f)



def emit_ln(ctx, t0, ntiles, permuted):
    nc, s = ctx.nc, ctx.s
    Tt = ntiles * 128
    ctx.barrier()
    with nc.psum_tensor(f"ptl0_{ctx.uid()}", [128, 128], BF16) as p0, \
         nc.psum_tensor(f"ptl1_{ctx.uid()}", [128, 128], BF16) as p1, \
         nc.psum_tensor(f"ptlf0_{ctx.uid()}", [128, 128], F32) as f0, \
         nc.psum_tensor(f"ptlf1_{ctx.uid()}", [128, 128], F32) as f1:
        pt = [p0, p1]
        pf = [f0, f1]
        xtmp = ctx.BA(0, C)
        for t in range(ntiles):
            xr = ctx.xr(t0 + t)

            def fv(v, t=t, xr=xr):
                ctx.w(v, "tv")
                if t > 0:
                    ctx.w(v, "tt")
                v.reduce_sum(out=s["ST"][:, 24:25], in_=xr, axis=AX)
                v.drain()
                v.tensor_scalar_mul(out=s["ST"][:, 24:25], in0=s["ST"][:, 24:25],
                                    scalar1=1.0 / C)
                v.drain()
                ctx.tick("tv", v.engine_nop())
            ctx.on("vector", fv)

            def fsr(sc, t=t, xr=xr):
                ctx.w(sc, "tv")
                for k in range(4):
                    sc.activation(out=s["SM"][:, 128:160] if False else s["TA"][:, 0:512],
                                  in_=xr[:, 512 * k:512 * (k + 1)], func=AF.Square,
                                  bias=s["ST"][:, 24:25], scale=-1.0,
                                  accum_out=s["ST"][:, 40 + k:41 + k])
                ctx.tick("ts", sc.copy(out=s["SM"][:, 191:192], in_=s["SM"][:, 191:192]))
            ctx.on("scalar", fsr)

            def fv15(v, t=t):
                ctx.w(v, "ts")
                v.tensor_add(out=s["ST"][:, 40:41], in0=s["ST"][:, 40:41],
                             in1=s["ST"][:, 41:42])
                v.tensor_add(out=s["ST"][:, 42:43], in0=s["ST"][:, 42:43],
                             in1=s["ST"][:, 43:44])
                v.drain()
                v.tensor_add(out=s["ST"][:, 40:41], in0=s["ST"][:, 40:41],
                             in1=s["ST"][:, 42:43])
                v.drain()
                ctx.tick("tv", v.tensor_scalar(out=s["ST"][:, 26:27],
                                               in0=s["ST"][:, 40:41],
                                               scalar1=1.0 / C, scalar2=LN_EPS,
                                               op0=ALU.mult, op1=ALU.add))
            ctx.on("vector", fv15)

            def fsr2(sc, t=t):
                ctx.w(sc, "tv")
                sc.activation(out=s["ST"][:, 26:27], in_=s["ST"][:, 26:27], func=AF.Ln)
                ctx.tick("ts", sc.activation(out=s["ST"][:, 26:27],
                                             in_=s["ST"][:, 26:27],
                                             func=AF.Exp, scale=-0.5))
            ctx.on("scalar", fsr2)

            def fnm(v, t=t, xr=xr):
                ctx.w(v, "ts")
                ctx.tick("tv", v.tensor_scalar(
                    out=xtmp, in0=xr, scalar1=s["ST"][:, 24:25],
                    scalar2=s["ST"][:, 26:27], op0=ALU.subtract, op1=ALU.mult))
            ctx.on("vector", fnm)

            for cc in range(NC):
                def ft(te, t=t, cc=cc):
                    ctx.w(te, "tv")
                    src = xtmp[:, cc * 128:(cc + 1) * 128]
                    ctx.tick("tt", te.transpose(out=pt[cc % 2][:], in_=src,
                                                identity=s["ID"][:]))

                def fv2(v, t=t, cc=cc):
                    ctx.w(v, "tt")
                    ctx.tick("tv", v.tensor_copy(
                        out=ctx.xnc(cc, Tt)[:, t * 128:(t + 1) * 128],
                        in_=pt[cc % 2][:]))
                ctx.on("tensor", ft)
                ctx.on("vector", fv2)
    ctx.barrier()



def emit_proj(ctx, Tt, a0, ntiles, O, wsrc, wdt, evac, lhsT_fn=None):
    nc, s = ctx.nc, ctx.s
    if lhsT_fn is None:
        def lhsT_fn(cc, a, n):
            return ctx.xnc(cc, Tt)[:, a:a + n]
    nq = O // 512
    ctx.barrier()
    with ExitStack() as ps:
        pb = [ps.enter_context(nc.psum_tensor(f"pj{i}_{ctx.uid()}", [128, 512], F32))
              for i in range(ntiles)]
        for oq in range(nq):
            for cc in range(NC):
                buf = s["WSa"] if cc % 2 == 0 else s["WSb"]
                wtile = buf[:, 0:512]
                if wdt == BF16:
                    def fdma(sync, oq=oq, cc=cc, wtile=wtile):
                        ctx.w(sync, "tt")
                        ctx.dtick("dl", sync.dma_start(out=wtile,
                                                       in_=wsrc(cc, oq * 512, 512)))
                    ctx.on("sync", fdma)
                else:
                    def fdma(sync, oq=oq, cc=cc):
                        ctx.w(sync, "tv")
                        if wdt == F32:
                            stage = s["WS2"][:, 0:512] if cc % 2 == 0 else s["WS2"][:, 512:1024]
                        else:
                            stage = s["WSi"][:, 0:512] if cc % 2 == 0 else s["WSi"][:, 512:1024]
                        ctx.dtick("dl", sync.dma_start(out=stage,
                                                       in_=wsrc(cc, oq * 512, 512)))
                    ctx.on("sync", fdma)

                    def fcvt(v, oq=oq, cc=cc, wtile=wtile):
                        ctx.w(v, "dl")
                        ctx.w(v, "tt")
                        if wdt == F32:
                            stage = s["WS2"][:, 0:512] if cc % 2 == 0 else s["WS2"][:, 512:1024]
                            ctx.tick("tv", v.tensor_copy(out=wtile, in_=stage))
                        else:
                            stage = s["WSi"][:, 0:512] if cc % 2 == 0 else s["WSi"][:, 512:1024]
                            ctx.tick("tv", v.tensor_scalar(
                                out=wtile, in0=stage, scalar1=-128.0, scalar2=None,
                                op0=ALU.add))
                    ctx.on("vector", fcvt)

                def fmm(te, oq=oq, cc=cc, wtile=wtile):
                    ctx.w(te, "dl" if wdt == BF16 else "tv")
                    if cc == 0:
                        ctx.w(te, "ts")
                        ctx.w(te, "tv")
                    for t in range(ntiles):
                        inst = te.matmul(out=pb[t][:], lhsT=lhsT_fn(cc, a0 + t * 128, 128),
                                         rhs=wtile, start=(cc == 0), stop=(cc == NC - 1))
                        if t == ntiles - 1:
                            ctx.tick("tt", inst)
                ctx.on("tensor", fmm)

            for t in range(ntiles):
                evac(t, oq, pb[t])
    ctx.barrier()



def emit_l0_dequant(ctx):
    s = ctx.s
    shf = s["SM"][:, 128:129].bitcast(I32)
    ctx.barrier()

    def fv0(v):
        ctx.w(v, "dl")
        v.memset(s["ST"][:, 32:33], -2.0)
        v.memset(s["ST"][:, 33:34], -6.0)
        v.drain()
        ctx.tick("tv", v.engine_nop())
    ctx.on("vector", fv0)

    def emit_piece(words_src, width, scale_src, out_dst):
        def fdma(sync):
            ctx.w(sync, "tv")
            for j in range(8):
                ctx.dtick("dl", sync.dma_start(
                    out=s["WSi"][16 * j:16 * (j + 1), 0:width], in_=words_src))
            ctx.dtick("dl", sync.dma_start(
                out=s["WSa"].bitcast(F16)[:, 0:width],
                in_=scale_src.broadcast_to([128, width])))
        ctx.on("sync", fdma)

        def fv1(v):
            ctx.w(v, "dl")
            ctx.w(v, "ts")
            v.tensor_scalar(
                out=s["WSi"][:, 1024:1024 + width], in0=s["WSi"][:, 0:width],
                scalar1=shf, scalar2=7,
                op0=ALU.logical_shift_right, op1=ALU.bitwise_and)
            ctx.tick("tv", v.tensor_copy(out=s["WS2"][:, 0:width],
                                         in_=s["WSi"][:, 1024:1024 + width]))
        ctx.on("vector", fv1)

        def fs1(sc):
            ctx.w(sc, "tv")
            sc.activation(out=s["TA"][:, 0:width], in_=s["WS2"][:, 0:width],
                          func=AF.Relu, bias=s["ST"][:, 32:33], scale=0.5)
            ctx.tick("ts", sc.activation(
                out=s["TA"][:, 1024:1024 + width], in_=s["WS2"][:, 0:width],
                func=AF.Relu, bias=s["ST"][:, 33:34], scale=1.0))
        ctx.on("scalar", fs1)

        def fv2(v):
            ctx.w(v, "ts")
            ctx.w(v, "dg")
            m_f = s["WS2"][:, 0:width]
            v.scalar_tensor_tensor(out=m_f, in0=m_f, scalar=0.5,
                                   in1=s["TA"][:, 0:width], op0=ALU.mult, op1=ALU.add)
            v.tensor_add(out=m_f, in0=m_f, in1=s["TA"][:, 1024:1024 + width])
            v.tensor_scalar(out=s["WSi"][:, 1024:1024 + width], in0=s["WSi"][:, 0:width],
                            scalar1=shf, scalar2=8,
                            op0=ALU.logical_shift_right, op1=ALU.bitwise_and)
            v.tensor_scalar(out=s["TA"][:, 0:width], in0=s["WSi"][:, 1024:1024 + width],
                            scalar1=-0.25, scalar2=1.0, op0=ALU.mult, op1=ALU.add)
            v.tensor_mul(out=m_f, in0=m_f, in1=s["TA"][:, 0:width])
            ctx.tick("tv", v.tensor_tensor(
                out=s["WSb"][:, 0:width], in0=m_f,
                in1=s["WSa"].bitcast(F16)[:, 0:width], op=ALU.mult))
        ctx.on("vector", fv2)

        def fst(g):
            ctx.w(g, "tv")
            rview = out_dst.rearrange("(m j) w -> m j w", m=16, j=8)
            for j in range(8):
                ctx.dtick("dg", g.dma_start(
                    out=rview[:, j, :],
                    in_=s["WSb"][16 * j:16 * (j + 1), 0:width]))
        ctx.on("gpsimd", fst)

    for cc in range(NC):
        for sl in range(6):
            w0 = sl * 1024
            wd = min(1024, IF - w0)
            for src, ssrc, dst in [
                (ctx.d["l0_pgT"], ctx.d["l0_sg"], ctx.d["wg_b"]),
                (ctx.d["l0_puT"], ctx.d["l0_su"], ctx.d["wu_b"]),
            ]:
                emit_piece(src[16 * cc:16 * (cc + 1), w0:w0 + wd], wd,
                           ssrc[cc:cc + 1, w0:w0 + wd], dst[cc, :, w0:w0 + wd])
    for kk in range(NI):
        for sl in range(2):
            w0 = sl * 1024
            emit_piece(ctx.d["l0_pdT"][16 * kk:16 * (kk + 1), w0:w0 + 1024], 1024,
                       ctx.d["l0_sd"][kk:kk + 1, w0:w0 + 1024],
                       ctx.d["wd_b"][kk, :, w0:w0 + 1024])
    ctx.barrier()



def emit_ffn(ctx, layer, t0, ntiles, Tt):
    nc, s, d = ctx.nc, ctx.s, ctx.d
    chunks = []
    t = 0
    while t < ntiles:
        n = min(4, ntiles - t)
        chunks.append((t, n))
        t += n

    ctx.barrier()
    if layer == "l1":
        def fld(sync):
            ctx.dtick("dl", sync.dma_start(out=s["SM"][:, 0:44], in_=d["l1_sgc"]))
            ctx.dtick("dl", sync.dma_start(out=s["SM"][:, 44:88], in_=d["l1_suc"]))
        ctx.on("sync", fld)

    for (tc, ntc) in chunks:
        Tch = ntc * 128
        with nc.psum_tensor(f"pgA_{ctx.uid()}", [128, 512], F32) as pg, \
             nc.psum_tensor(f"puA_{ctx.uid()}", [128, 512], F32) as pu:
            for k in range(NI):
                if layer == "l0":
                    def fdg(sync, k=k):
                        ctx.w(sync, "tt")
                        ctx.dtick("dl", sync.dma_start(
                            out=s["WSa"].rearrange("p (cc i) -> p cc i", cc=16),
                            in_=d["wg_b"].transpose([1, 0, 2])[:, :, 128 * k:128 * (k + 1)]))
                        ctx.dtick("dl", sync.dma_start(
                            out=s["WSb"].rearrange("p (cc i) -> p cc i", cc=16),
                            in_=d["wu_b"].transpose([1, 0, 2])[:, :, 128 * k:128 * (k + 1)]))
                    ctx.on("sync", fdg)
                else:
                    def fdg1(sync, k=k):
                        ctx.w(sync, "tv")
                        ctx.dtick("dl", sync.dma_start(
                            out=s["WSi"].rearrange("p (cc i) -> p cc i", cc=16),
                            in_=d["l1_qgT"].rearrange("(cc p) i -> p cc i", p=128)[:, :, 128 * k:128 * (k + 1)]))
                    ctx.on("sync", fdg1)

                    def fcv1(v, k=k):
                        ctx.w(v, "dl")
                        ctx.w(v, "tt")
                        ctx.tick("tv", v.tensor_scalar(
                            out=s["WSa"][:], in0=s["WSi"][:], scalar1=-128.0,
                            scalar2=None, op0=ALU.add))
                    ctx.on("vector", fcv1)

                    def fdg2(sync, k=k):
                        ctx.w(sync, "tv")
                        ctx.dtick("dl", sync.dma_start(
                            out=s["WSi"].rearrange("p (cc i) -> p cc i", cc=16),
                            in_=d["l1_quT"].rearrange("(cc p) i -> p cc i", p=128)[:, :, 128 * k:128 * (k + 1)]))
                    ctx.on("sync", fdg2)

                    def fcv2(v, k=k):
                        ctx.w(v, "dl")
                        ctx.w(v, "tt")
                        ctx.tick("tv", v.tensor_scalar(
                            out=s["WSb"][:], in0=s["WSi"][:], scalar1=-128.0,
                            scalar2=None, op0=ALU.add))
                    ctx.on("vector", fcv2)

                def fmmA(te, k=k, tc=tc, Tch=Tch):
                    ctx.w(te, "dl" if layer == "l0" else "tv")
                    ctx.w(te, "ts")
                    if layer == "l0":
                        ctx.w(te, "tv")
                    for cc in range(NC):
                        te.matmul(out=pg[:, 0:Tch],
                                  lhsT=s["WSa"][:, cc * 128:(cc + 1) * 128],
                                  rhs=ctx.xnc(cc, Tt)[:, tc * 128:tc * 128 + Tch],
                                  start=(cc == 0), stop=(cc == NC - 1))
                    for cc in range(NC):
                        inst = te.matmul(out=pu[:, 0:Tch],
                                         lhsT=s["WSb"][:, cc * 128:(cc + 1) * 128],
                                         rhs=ctx.xnc(cc, Tt)[:, tc * 128:tc * 128 + Tch],
                                         start=(cc == 0), stop=(cc == NC - 1))
                        if cc == NC - 1:
                            ctx.tick("tt", inst)
                ctx.on("tensor", fmmA)

                def fevA_s(sc, k=k, Tch=Tch):
                    ctx.w(sc, "tt")
                    if layer == "l1":
                        ctx.w(sc, "dl")
                        inst = sc.activation(out=s["WS2"][:, 0:Tch], in_=pg[:, 0:Tch],
                                             func=AF.Silu, scale=s["SM"][:, k:k + 1])
                    else:
                        inst = sc.activation(out=s["WS2"][:, 0:Tch], in_=pg[:, 0:Tch],
                                             func=AF.Silu)
                    ctx.tick("ts", inst)
                ctx.on("scalar", fevA_s)

                def fevA_v(v, k=k, Tch=Tch):
                    ctx.w(v, "ts")
                    hdst = ctx.s["BIG"][:, k * 512:k * 512 + Tch]
                    if layer == "l0":
                        inst = v.tensor_tensor(out=hdst, in0=s["WS2"][:, 0:Tch],
                                               in1=pu[:, 0:Tch], op=ALU.mult)
                    else:
                        inst = v.scalar_tensor_tensor(out=hdst, in0=pu[:, 0:Tch],
                                                      scalar=s["SM"][:, 44 + k:45 + k],
                                                      in1=s["WS2"][:, 0:Tch],
                                                      op0=ALU.mult, op1=ALU.mult)
                    ctx.tick("tv", inst)
                ctx.on("vector", fevA_v)

        ctx.barrier()
        if layer == "l1":
            emit_bc_table(ctx, d["l1_sd"])

            def fbc2(v):
                ctx.tick("tv", v.tensor_scalar_mul(out=s["BC"][:], in0=s["BC"][:],
                                                   scalar1=0.5))
            ctx.on("vector", fbc2)

        with ExitStack() as ps:
            pyb = [ps.enter_context(nc.psum_tensor(f"pyB{i}_{ctx.uid()}", [128, 512], F32))
                   for i in range(ntc)]
            for oq in range(4):
                for k in range(NI):
                    buf = s["WSa"] if k % 2 == 0 else s["WSb"]
                    wt = buf[:, 0:512]
                    if layer == "l0":
                        def fdB(sync, oq=oq, k=k, wt=wt):
                            ctx.w(sync, "tt")
                            ctx.dtick("dl", sync.dma_start(
                                out=wt, in_=d["wd_b"][k, :, 512 * oq:512 * (oq + 1)]))
                        ctx.on("sync", fdB)
                    else:
                        def fdB(sync, oq=oq, k=k):
                            ctx.w(sync, "tv")
                            stage = s["WSi"][:, 0:512] if k % 2 == 0 else s["WSi"][:, 512:1024]
                            ctx.dtick("dl", sync.dma_start(
                                out=stage,
                                in_=d["l1_qdT"][128 * k:128 * (k + 1), 512 * oq:512 * (oq + 1)]))
                        ctx.on("sync", fdB)

                        def fcB(v, oq=oq, k=k, wt=wt):
                            ctx.w(v, "dl")
                            ctx.w(v, "tt")
                            stage = s["WSi"][:, 0:512] if k % 2 == 0 else s["WSi"][:, 512:1024]
                            ctx.tick("tv", v.tensor_scalar(
                                out=wt, in0=stage, scalar1=-128.0, scalar2=None,
                                op0=ALU.add))
                        ctx.on("vector", fcB)

                    def fmB(te, oq=oq, k=k, wt=wt, ntc=ntc):
                        ctx.w(te, "dl" if layer == "l0" else "tv")
                        if k == 0:
                            ctx.w(te, "tv")
                            ctx.w(te, "ts")
                        for i in range(ntc):
                            inst = te.matmul(
                                out=pyb[i][:],
                                lhsT=ctx.s["BIG"][:, k * 512 + i * 128:k * 512 + (i + 1) * 128],
                                rhs=wt, start=(k == 0), stop=(k == NI - 1))
                            if i == ntc - 1:
                                ctx.tick("tt", inst)
                    ctx.on("tensor", fmB)

                for i in range(ntc):
                    if layer == "l0":
                        def fqv(v, i=i, oq=oq, tc=tc):
                            ctx.w(v, "tt")
                            ctx.tick("tv", v.tensor_copy(
                                out=ctx.s["BIG"][:, NI * 512 + i * C + oq * 512:NI * 512 + i * C + (oq + 1) * 512],
                                in_=pyb[i][:]))
                        ctx.on("vector", fqv)
                    else:
                        def fqv(v, i=i, oq=oq, tc=tc):
                            ctx.w(v, "tt")
                            v.tensor_tensor(out=s["TA"][:, 0:512], in0=pyb[i][:],
                                            in1=s["BC"][:, oq * 512:(oq + 1) * 512],
                                            op=ALU.mult)
                            xr = ctx.xr(t0 + tc + i)
                            ctx.tick("tv", v.tensor_add(
                                out=xr[:, oq * 512:(oq + 1) * 512],
                                in0=xr[:, oq * 512:(oq + 1) * 512],
                                in1=s["TA"][:, 0:512]))
                        ctx.on("vector", fqv)

            if layer == "l0":
                for i in range(ntc):
                    def frms_s(sc, i=i, tc=tc):
                        ctx.w(sc, "tv")
                        y = ctx.s["BIG"][:, NI * 512 + i * C:NI * 512 + (i + 1) * C]
                        for k4 in range(4):
                            sc.activation(out=s["TA"][:, 0:512],
                                          in_=y[:, 512 * k4:512 * (k4 + 1)],
                                          func=AF.Square,
                                          accum_out=s["ST"][:, 40 + k4:41 + k4])
                        ctx.tick("ts", sc.copy(out=s["SM"][:, 191:192], in_=s["SM"][:, 191:192]))
                    ctx.on("scalar", frms_s)

                    def frms_v(v, i=i, tc=tc):
                        ctx.w(v, "ts")
                        v.tensor_add(out=s["ST"][:, 40:41], in0=s["ST"][:, 40:41],
                                     in1=s["ST"][:, 41:42])
                        v.tensor_add(out=s["ST"][:, 42:43], in0=s["ST"][:, 42:43],
                                     in1=s["ST"][:, 43:44])
                        v.drain()
                        v.tensor_add(out=s["ST"][:, 40:41], in0=s["ST"][:, 40:41],
                                     in1=s["ST"][:, 42:43])
                        v.drain()
                        v.tensor_scalar(out=s["ST"][:, 26:27], in0=s["ST"][:, 40:41],
                                        scalar1=1.0 / C, scalar2=L0_FFN_EPS,
                                        op0=ALU.mult, op1=ALU.add)
                        v.drain()
                        ctx.tick("tv", v.engine_nop())
                    ctx.on("vector", frms_v)

                    def frms_s2(sc, i=i, tc=tc):
                        ctx.w(sc, "tv")
                        sc.activation(out=s["ST"][:, 26:27], in_=s["ST"][:, 26:27],
                                      func=AF.Ln)
                        ctx.tick("ts", sc.activation(out=s["ST"][:, 26:27],
                                                     in_=s["ST"][:, 26:27],
                                                     func=AF.Exp, scale=-0.5))
                    ctx.on("scalar", frms_s2)

                    def frms2(v, i=i, tc=tc):
                        ctx.w(v, "ts")
                        y = ctx.s["BIG"][:, NI * 512 + i * C:NI * 512 + (i + 1) * C]
                        v.tensor_scalar_mul(out=s["ST"][:, 26:27], in0=s["ST"][:, 26:27],
                                            scalar1=0.5)
                        v.drain()
                        v.tensor_scalar(out=s["TA"][:], in0=y, scalar1=s["ST"][:, 26:27],
                                        scalar2=None, op0=ALU.mult)
                        xr = ctx.xr(t0 + tc + i)
                        ctx.tick("tv", v.tensor_add(out=xr, in0=xr, in1=s["TA"][:]))
                    ctx.on("vector", frms2)
        ctx.barrier()



def emit_l0_attn(ctx, upto="full"):
    nc, s, d = ctx.nc, ctx.s, ctx.d
    Tt = T_EXT

    ctx.barrier()

    def fld(sync):
        ctx.dtick("dl", sync.dma_start(out=s["SM"][0:1, 32:48], in_=d["l0_gamma"]))
        ctx.dtick("dl", sync.dma_start(out=s["ST"][:, 48:64],
                                       in_=d["l0_gamma"].broadcast_to([128, H])))
        ctx.dtick("dl", sync.dma_start(out=s["CW"][:, 0:64], in_=d["l0_convw"]))
        ctx.dtick("dl", sync.dma_start(out=s["CW"][:, 64:80], in_=d["l0_convb"]))
    ctx.on("sync", fld)

    def fs0(sc):
        ctx.w(sc, "dl")
        ctx.tick("ts", sc.activation(out=s["ST"][:, 48:64], in_=s["ST"][:, 48:64],
                                     func=AF.Ln))
    ctx.on("scalar", fs0)

    for cc in range(NC):
        def fcv(v, cc=cc):
            if cc == 0:
                ctx.w(v, "tv")
                ctx.w(v, "dl")
            else:
                ctx.w(v, "ts")
            xn = ctx.xnc(cc, Tt)
            v.tensor_scalar(out=s["TA"][:, 0:1024], in0=xn[:, 125:125 + 1024],
                            scalar1=s["CW"][:, 4 * cc:4 * cc + 1], scalar2=None,
                            op0=ALU.mult)
            for k in (1, 2, 3):
                v.scalar_tensor_tensor(out=s["TA"][:, 0:1024],
                                       in0=xn[:, 125 + k:125 + k + 1024],
                                       scalar=s["CW"][:, 4 * cc + k:4 * cc + k + 1],
                                       in1=s["TA"][:, 0:1024],
                                       op0=ALU.mult, op1=ALU.add)
            v.drain()
            ctx.tick("tv", v.engine_nop())
        ctx.on("vector", fcv)

        def fcs(sc, cc=cc):
            ctx.w(sc, "tv")
            ctx.w(sc, "dg")
            ctx.tick("ts", sc.activation(out=s["WSb"][:, 0:1024], in_=s["TA"][:, 0:1024],
                                         func=AF.Silu, bias=s["CW"][:, 64 + cc:65 + cc],
                                         scale=1.0))
        ctx.on("scalar", fcs)

        def fcd(g, cc=cc):
            ctx.w(g, "ts")
            ctx.dtick("dg", g.dma_start(out=d["xc_d"][cc], in_=s["WSb"][:, 0:1024]))
        ctx.on("gpsimd", fcd)

    def l2norm_evac(dstf):
        def evac(t, oq, pb):
            def fsq(sc, t=t, oq=oq, pb=pb):
                ctx.w(sc, "tt")
                for hh in range(4):
                    sc.activation(out=s["TA"][:, hh * 128:(hh + 1) * 128],
                                  in_=pb[:, hh * 128:(hh + 1) * 128],
                                  func=AF.Square,
                                  accum_out=s["ST"][:, 40 + hh:41 + hh])
                sc.activation(out=s["ST"][:, 44:48], in_=s["ST"][:, 40:44], func=AF.Ln)
                ctx.tick("ts", sc.activation(out=s["ST"][:, 44:48],
                                             in_=s["ST"][:, 44:48],
                                             func=AF.Exp, scale=-0.5))
            ctx.on("scalar", fsq)

            def fvn(v, t=t, oq=oq, pb=pb):
                ctx.w(v, "ts")
                v.tensor_scalar(out=s["ST"][:, 44:48], in0=s["ST"][:, 44:48],
                                scalar1=1.0 / L0_NORM_EPS, scalar2=None, op0=ALU.min)
                v.drain()
                for hh in range(4):
                    inst = v.tensor_scalar(
                        out=dstf(t)[:, oq * 512 + hh * 128:oq * 512 + (hh + 1) * 128],
                        in0=pb[:, hh * 128:(hh + 1) * 128],
                        scalar1=s["ST"][:, 44 + hh:45 + hh], scalar2=None, op0=ALU.mult)
                ctx.tick("tv", inst)
            ctx.on("vector", fvn)
        return evac

    emit_proj(ctx, Tt, HALO, NT, C,
              lambda cc, o0, ol: d["l0_wkT"][128 * cc:128 * (cc + 1), o0:o0 + ol],
              F32, l2norm_evac(lambda t: ctx.BA(t * C, C)))

    def v_evac(t, oq, pb):
        def fv(v, t=t, oq=oq, pb=pb):
            ctx.w(v, "tt")
            ctx.tick("tv", v.tensor_tensor(
                out=ctx.BB(t * C + oq * 512, 512), in0=pb[:],
                in1=ctx.BA(t * C + oq * 512, 512), op=ALU.mult))
        ctx.on("vector", fv)
    emit_proj(ctx, Tt, HALO, NT, C,
              lambda cc, o0, ol: d["l0_wvT"][128 * cc:128 * (cc + 1), o0:o0 + ol],
              F32, v_evac)

    emit_proj(ctx, Tt, HALO, NT, C,
              lambda cc, o0, ol: d["l0_wqT"][128 * cc:128 * (cc + 1), o0:o0 + ol],
              F32, l2norm_evac(lambda t: ctx.BA(t * C, C)))

    def fxc(sync):
        ctx.wall(sync)
        for cc in range(NC):
            ctx.dtick("dl", sync.dma_start(out=ctx.xnc(cc, T_LOC), in_=d["xc_d"][cc]))
    ctx.on("sync", fxc)

    emit_bc_table(ctx, d["l0_gateb"])

    def g_evac(t, oq, pb):
        def fga(v, t=t, oq=oq, pb=pb):
            ctx.w(v, "tt")
            ctx.w(v, "dg")
            ctx.tick("tv", v.tensor_tensor(out=s["TA"][:, 0:512], in0=pb[:],
                                           in1=s["BC"][:, oq * 512:(oq + 1) * 512],
                                           op=ALU.add))
        ctx.on("vector", fga)

        def fgs(sc, t=t, oq=oq):
            ctx.w(sc, "tv")
            ctx.tick("ts", sc.activation(out=s["TA"][:, 512:1024],
                                         in_=s["TA"][:, 0:512], func=AF.Sigmoid))
        ctx.on("scalar", fgs)

        def fgm(v, t=t, oq=oq):
            ctx.w(v, "ts")
            dst = ctx.BA(t * C + oq * 512, 512)
            ctx.tick("tv", v.tensor_tensor(out=dst, in0=s["TA"][:, 512:1024],
                                           in1=dst, op=ALU.mult))
        ctx.on("vector", fgm)

    emit_proj(ctx, T_LOC, 0, NT, C,
              lambda cc, o0, ol: d["l0_wgT"][128 * cc:128 * (cc + 1), o0:o0 + ol],
              F32, g_evac)
    if upto == "proj":
        return

    def fsc_ld(sync):
        ctx.wall(sync)
        ctx.dtick("dl", sync.dma_start(out=s["WS2"][:, 0:128], in_=d["iota_mj"]))
        ctx.dtick("dl", sync.dma_start(out=s["WS2"][:, 128:256], in_=d["mask_mj"]))
        ctx.dtick("dl", sync.dma_start(out=s["WS2"][:, 256:257], in_=d["grev"]))
    ctx.on("sync", fsc_ld)

    for h in range(H):
        def fdt_s(sc, h=h):
            ctx.w(sc, "dl")
            ctx.w(sc, "tv")
            sc.activation(out=s["TA"][:, 0:128], in_=s["WS2"][:, 0:128],
                          func=AF.Exp, scale=s["ST"][:, 48 + h:49 + h])
            ctx.tick("ts", sc.activation(out=s["TA"][:, 128:129], in_=s["WS2"][:, 256:257],
                                         func=AF.Exp, scale=s["ST"][:, 48 + h:49 + h]))
        ctx.on("scalar", fdt_s)

        def fdt_v(v, h=h):
            ctx.w(v, "ts")
            v.tensor_tensor(out=s["QG"][:, 2048 + h * 128:2048 + (h + 1) * 128],
                            in0=s["TA"][:, 0:128], in1=s["WS2"][:, 128:256], op=ALU.mult)
            v.tensor_copy(out=s["SC"][:, h:h + 1], in_=s["TA"][:, 128:129])
            v.drain()
            ctx.tick("tv", v.engine_nop())
        ctx.on("vector", fdt_v)

    with nc.psum_tensor(f"pc1_{ctx.uid()}", [1, 128], F32) as pc:
        for h in range(H):
            def fg128(v, h=h):
                ctx.w(v, "ts")
                v.tensor_scalar_mul(out=s["ST"][0:1, 40:41],
                                    in0=s["ST"][0:1, 48 + h:49 + h], scalar1=128.0)
                v.drain()
                ctx.tick("tv", v.engine_nop())
            ctx.on("vector", fg128)

            def fg128b(sc, h=h):
                ctx.w(sc, "tv")
                ctx.tick("ts", sc.activation(out=s["ST"][0:1, 40:41],
                                             in_=s["ST"][0:1, 40:41], func=AF.Exp))
            ctx.on("scalar", fg128b)

            for b in range(NT):
                def fmm(te, h=h, b=b):
                    ctx.w(te, "tv")
                    ctx.tick("tt", te.matmul(
                        out=pc[:], lhsT=s["SC"][:, h:h + 1],
                        rhs=ctx.BB(b * C + h * 128, 128), start=True, stop=True))
                ctx.on("tensor", fmm)

                def fcar(v, h=h, b=b):
                    ctx.w(v, "tt")
                    ctx.w(v, "ts")
                    scar = s["SM"][0:1, 64:192]
                    if b == 0:
                        inst = v.tensor_copy(out=scar, in_=pc[0:1, :])
                    else:
                        inst = v.scalar_tensor_tensor(out=scar, in0=scar,
                                                      scalar=s["ST"][0:1, 40:41],
                                                      in1=pc[0:1, :],
                                                      op0=ALU.mult, op1=ALU.add)
                    ctx.tick("tv", inst)
                    v.drain()
                ctx.on("vector", fcar)

            def fA(v, h=h):
                v.tensor_copy(out=s["TA"][0:1, h * 128:(h + 1) * 128],
                              in_=s["SM"][0:1, 64:192])
                v.drain()
                ctx.tick("tv", v.engine_nop())
            ctx.on("vector", fA)

    def fcc(g):
        ctx.w(g, "tv")
        ctx.dtick("dg", g.dma_start(out=d["cin0"], in_=s["TA"][0:1, :]))
        ctx.w(g, "dg")
        ctx.tick("cc", g.collective_compute(
            "AllGather", ALU.bypass, replica_groups=RG, ins=[d["cin0"]],
            outs=[d["cout0"]]))
        ctx.w(g, "cc")
        ctx.dtick("dg", g.dma_start(out=s["SM"][0:8, 0:1], in_=d["cc_sel"]))
        ctx.dtick("dg", g.dma_start(out=s["SM"][0:8, 1:2], in_=d["cc_pow"]))
    ctx.on("gpsimd", fcc)

    for half in range(2):
        def fch_g(g, half=half):
            ctx.w(g, "tv")
            ctx.dtick("dg", g.dma_start(out=s["TA"][0:8, 0:1024],
                                        in_=d["cout0"][:, half * 1024:(half + 1) * 1024]))
            ctx.dtick("dg", g.dma_start(
                out=s["TA"][0:8, 1024:2048],
                in_=d["l0_gamma_rep"][0:1, half * 1024:(half + 1) * 1024].broadcast_to([8, 1024])))
        ctx.on("gpsimd", fch_g)

        def fch_s(sc, half=half):
            ctx.w(sc, "dg")
            sc.activation(out=s["TA"][0:8, 1024:2048], in_=s["TA"][0:8, 1024:2048],
                          func=AF.Ln)
            ctx.tick("ts", sc.activation(out=s["TA"][0:8, 1024:2048],
                                         in_=s["TA"][0:8, 1024:2048],
                                         func=AF.Exp, scale=s["SM"][0:8, 1:2]))
        ctx.on("scalar", fch_s)

        def fch_v(v, half=half):
            ctx.w(v, "ts")
            v.tensor_scalar(out=s["TA"][0:8, 1024:2048], in0=s["TA"][0:8, 1024:2048],
                            scalar1=s["SM"][0:8, 0:1], scalar2=None, op0=ALU.mult)
            v.tensor_tensor(
                out=s["BC"][0:8, half * 1024:(half + 1) * 1024],
                in0=s["TA"][0:8, 1024:2048], in1=s["TA"][0:8, 0:1024], op=ALU.mult)
            v.drain()
            ctx.tick("tv", v.engine_nop())
        ctx.on("vector", fch_v)

    def fones(v):
        v.memset(s["SC"][0:8, 256:257], 1.0)
        v.drain()
        ctx.tick("tv", v.engine_nop())
    ctx.on("vector", fones)

    with nc.psum_tensor(f"psin_{ctx.uid()}", [1, 512], F32) as psin:
        for qq in range(4):
            def fmm_s(te, qq=qq):
                ctx.w(te, "tv")
                ctx.tick("tt", te.matmul(out=psin[:], lhsT=s["SC"][0:8, 256:257],
                                         rhs=s["BC"][0:8, qq * 512:(qq + 1) * 512],
                                         start=True, stop=True))
            ctx.on("tensor", fmm_s)

            def fsin(v, qq=qq):
                ctx.w(v, "tt")
                v.tensor_copy(out=s["TA"][0:1, qq * 512:(qq + 1) * 512],
                              in_=psin[0:1, :])
                v.drain()
                ctx.tick("tv", v.engine_nop())
            ctx.on("vector", fsin)

    with nc.psum_tensor(f"ps2_{ctx.uid()}", [128, 128], F32) as ps2, \
         nc.psum_tensor(f"pc2_{ctx.uid()}", [1, 128], F32) as pc2:
        for h in range(H):
            def fini(v, h=h):
                ctx.w(v, "tv")
                v.tensor_copy(out=s["SM"][0:1, 64:192],
                              in_=s["TA"][0:1, h * 128:(h + 1) * 128])
                v.drain()
                ctx.tick("tv", v.engine_nop())
            ctx.on("vector", fini)

            for b in range(NT):
                def ffix(v, h=h, b=b):
                    ctx.w(v, "tt")
                    v.tensor_scalar(out=s["WS2"][0:1, 512:640], in0=s["SM"][0:1, 64:192],
                                    scalar1=s["SM"][0:1, 32 + h:33 + h], scalar2=None,
                                    op0=ALU.mult)
                    v.drain()
                    u0 = ctx.BB(b * C + h * 128, 128)[0:1, :]
                    v.tensor_add(out=u0, in0=u0, in1=s["WS2"][0:1, 512:640])
                    v.drain()
                    ctx.tick("tv", v.engine_nop())
                ctx.on("vector", ffix)

                def fmm2(te, h=h, b=b):
                    ctx.w(te, "tv")
                    te.matmul(out=ps2[:],
                              lhsT=s["QG"][:, 2048 + h * 128:2048 + (h + 1) * 128],
                              rhs=ctx.BB(b * C + h * 128, 128), start=True, stop=True)
                    ctx.tick("tt", te.matmul(out=pc2[:], lhsT=s["SC"][:, h:h + 1],
                                             rhs=ctx.BB(b * C + h * 128, 128),
                                             start=True, stop=True))
                ctx.on("tensor", fmm2)

                def fev2(v, h=h, b=b):
                    ctx.w(v, "tt")
                    v.tensor_copy(out=s["SM"][0:1, 64:192], in_=pc2[0:1, :])
                    v.drain()
                    dst = ctx.BB(b * C + h * 128, 128)
                    v.tensor_tensor(out=dst, in0=ps2[:],
                                    in1=ctx.BA(b * C + h * 128, 128), op=ALU.mult)
                    v.drain()
                    ctx.tick("tv", v.engine_nop())
                ctx.on("vector", fev2)

    ctx.barrier()
    if upto == "scan":
        return
    with nc.psum_tensor(f"ptr0_{ctx.uid()}", [128, 128], BF16) as q0, \
         nc.psum_tensor(f"ptr1_{ctx.uid()}", [128, 128], BF16) as q1:
        qp = [q0, q1]
        i = 0
        for cc in range(NC):
            for t in range(NT):
                def ftp(te, cc=cc, t=t, i=i):
                    ctx.w(te, "tv")
                    ctx.tick("tt", te.transpose(
                        out=qp[i % 2][:], in_=ctx.BB(t * C + cc * 128, 128),
                        identity=s["ID"][:]))

                def ftc(v, cc=cc, t=t, i=i):
                    ctx.w(v, "tt")
                    ctx.tick("tv", v.tensor_copy(
                        out=ctx.BA(cc * T_LOC + t * 128, 128), in_=qp[i % 2][:]))
                ctx.on("tensor", ftp)
                ctx.on("vector", ftc)
                i += 1

    def wo_evac(t, oq, pb):
        def fv(v, t=t, oq=oq, pb=pb):
            ctx.w(v, "tt")
            ctx.tick("tv", v.tensor_copy(out=ctx.BB(t * C + oq * 512, 512), in_=pb[:]))
        ctx.on("vector", fv)

    emit_proj(ctx, T_LOC, 0, NT, C,
              lambda cc, o0, ol: d["l0_woT"][128 * cc:128 * (cc + 1), o0:o0 + ol],
              F32, wo_evac,
              lhsT_fn=lambda cc, a, n: ctx.BA(cc * T_LOC + a, n))

    ctx.barrier()
    emit_bc_table(ctx, d["l0_lnw"])
    for t in range(NT):
        def fln(v, t=t):
            ctx.w(v, "dl")
            ctx.w(v, "dg")
            ctx.w(v, "tv")
            y = ctx.BB(t * C, C)
            v.reduce_sum(out=s["ST"][:, 24:25], in_=y, axis=AX)
            v.drain()
            v.tensor_scalar_mul(out=s["ST"][:, 24:25], in0=s["ST"][:, 24:25],
                                scalar1=1.0 / C)
            v.drain()
            ctx.tick("tv", v.engine_nop())
        ctx.on("vector", fln)

        def fln_s(sc, t=t):
            ctx.w(sc, "tv")
            y = ctx.BB(t * C, C)
            for k in range(4):
                sc.activation(out=s["TA"][:, 0:512], in_=y[:, 512 * k:512 * (k + 1)],
                              func=AF.Square, bias=s["ST"][:, 24:25], scale=-1.0,
                              accum_out=s["ST"][:, 40 + k:41 + k])
            ctx.tick("ts", sc.copy(out=s["SM"][:, 191:192], in_=s["SM"][:, 191:192]))
        ctx.on("scalar", fln_s)

        def fln_v(v, t=t):
            ctx.w(v, "ts")
            v.tensor_add(out=s["ST"][:, 40:41], in0=s["ST"][:, 40:41],
                         in1=s["ST"][:, 41:42])
            v.tensor_add(out=s["ST"][:, 42:43], in0=s["ST"][:, 42:43],
                         in1=s["ST"][:, 43:44])
            v.drain()
            v.tensor_add(out=s["ST"][:, 40:41], in0=s["ST"][:, 40:41],
                         in1=s["ST"][:, 42:43])
            v.drain()
            ctx.tick("tv", v.tensor_scalar(out=s["ST"][:, 26:27], in0=s["ST"][:, 40:41],
                                           scalar1=1.0 / C, scalar2=LN_EPS,
                                           op0=ALU.mult, op1=ALU.add))
        ctx.on("vector", fln_v)

        def fln_s2(sc, t=t):
            ctx.w(sc, "tv")
            sc.activation(out=s["ST"][:, 26:27], in_=s["ST"][:, 26:27], func=AF.Ln)
            ctx.tick("ts", sc.activation(out=s["ST"][:, 26:27], in_=s["ST"][:, 26:27],
                                         func=AF.Exp, scale=-0.5))
        ctx.on("scalar", fln_s2)

        def fln2(v, t=t):
            ctx.w(v, "ts")
            y = ctx.BB(t * C, C)
            v.tensor_scalar(out=s["TA"][:], in0=y, scalar1=s["ST"][:, 24:25],
                            scalar2=s["ST"][:, 26:27], op0=ALU.subtract, op1=ALU.mult)
            v.tensor_tensor(out=s["TA"][:], in0=s["TA"][:], in1=s["BC"][:], op=ALU.mult)
            xr = ctx.xr(1 + t)
            ctx.tick("tv", v.tensor_add(out=xr, in0=xr, in1=s["TA"][:]))
        ctx.on("vector", fln2)

    ctx.barrier()
    emit_bc_table(ctx, d["l0_lnb"])
    import os as _os
    if not _os.environ.get("SKIP_LNB"):
        for t in range(NT):
            def flb(v, t=t):
                ctx.w(v, "dl")
                ctx.w(v, "dg")
                xr = ctx.xr(1 + t)
                ctx.tick("tv", v.tensor_add(out=xr, in0=xr, in1=s["BC"][:]))
            ctx.on("vector", flb)
    ctx.barrier()



def emit_l1_attn(ctx, upto="full"):
    nc, s, d = ctx.nc, ctx.s, ctx.d
    Tt = T_LOC
    ctx.barrier()

    def fld(sync):
        ctx.dtick("dl", sync.dma_start(out=s["ST"][:, 48:64],
                                       in_=d["l1_slope"].broadcast_to([128, H])))
        ctx.dtick("dl", sync.dma_start(out=s["CW"][:, 0:2], in_=d["l1_km"]))
        ctx.dtick("dl", sync.dma_start(out=s["ST"][:, 32:40],
                                       in_=d["cc_pownT"].broadcast_to([128, 8])))
        ctx.dtick("dl", sync.dma_start(out=s["ST"][:, 40:48],
                                       in_=d["cc_selT"].broadcast_to([128, 8])))
    ctx.on("sync", fld)
    PS = lambda h: s["ST"][:, 48 + h:49 + h]

    def silu_evac(dstf):
        def evac(t, oq, pb):
            def fs(sc, t=t, oq=oq, pb=pb):
                ctx.w(sc, "tt")
                ctx.tick("ts", sc.activation(
                    out=dstf(t)[:, oq * 512:(oq + 1) * 512], in_=pb[:], func=AF.Silu))
            ctx.on("scalar", fs)
        return evac

    emit_proj(ctx, Tt, 0, NT, C,
              lambda cc, o0, ol: d["l1_wqkvT"][128 * cc:128 * (cc + 1),
                                               2048 + o0:2048 + o0 + ol],
              F32, silu_evac(lambda t: ctx.BA(t * C, C)))
    emit_proj(ctx, Tt, 0, NT, C,
              lambda cc, o0, ol: d["l1_wqkvT"][128 * cc:128 * (cc + 1),
                                               4096 + o0:4096 + o0 + ol],
              F32, silu_evac(lambda t: ctx.BB(t * C, C)))

    if upto == "proj":
        return
    with nc.psum_tensor(f"pkv1_{ctx.uid()}", [128, 128], F32) as pkv:
        for h in range(H):
            def fkd(sc, h=h):
                if h == 0:
                    ctx.w(sc, "ts")
                    ctx.w(sc, "dl")
                ctx.w(sc, "tv")
                ctx.tick("ts", sc.activation(out=s["SM"][:, 2:4], in_=s["CW"][:, 0:2],
                                             func=AF.Exp, scale=PS(h)))
            ctx.on("scalar", fkd)

            def fbd(v, h=h):
                ctx.w(v, "ts")
                v.tensor_scalar_mul(out=s["SM"][:, 4:5], in0=PS(h), scalar1=-256.0)
                v.drain()
                ctx.tick("tv", v.engine_nop())
            ctx.on("vector", fbd)

            def fbd2(sc, h=h):
                ctx.w(sc, "tv")
                ctx.tick("ts", sc.activation(out=s["SM"][:, 4:5], in_=s["SM"][:, 4:5],
                                             func=AF.Exp))
            ctx.on("scalar", fbd2)

            for b in range(4):
                for jh in range(2):
                    def fks(v, h=h, b=b, jh=jh):
                        ctx.w(v, "ts")
                        ctx.w(v, "tt")
                        ksl = ctx.BA((2 * b + jh) * C + h * 128, 128)
                        v.tensor_scalar(
                            out=s["BC"][:, 128:256], in0=ksl,
                            scalar1=s["SM"][:, 2 + jh:3 + jh], scalar2=None,
                            op0=ALU.mult)
                        v.drain()
                        ctx.tick("tv", v.engine_nop())
                    ctx.on("vector", fks)

                    def fkm(te, h=h, b=b, jh=jh):
                        ctx.w(te, "tv")
                        ctx.tick("tt", te.matmul(
                            out=pkv[:], lhsT=s["BC"][:, 128:256],
                            rhs=ctx.BB((2 * b + jh) * C + h * 128, 128),
                            start=(jh == 0), stop=(jh == 1)))
                    ctx.on("tensor", fkm)

                def fkc(v, h=h, b=b):
                    ctx.w(v, "tt")
                    kvs = s["SM"][:, 64:192]
                    if b == 0:
                        inst = v.tensor_copy(out=kvs, in_=pkv[:])
                    else:
                        inst = v.scalar_tensor_tensor(out=kvs, in0=kvs,
                                                      scalar=s["SM"][:, 4:5], in1=pkv[:],
                                                      op0=ALU.mult, op1=ALU.add)
                    ctx.tick("tv", inst)
                    v.drain()
                ctx.on("vector", fkc)

            def fst(g, h=h):
                ctx.w(g, "tv")
                ctx.dtick("dg", g.dma_start(out=d["cin1"][h * 128:(h + 1) * 128, :],
                                            in_=s["SM"][:, 64:192]))
            ctx.on("gpsimd", fst)

    def fcc(g):
        ctx.w(g, "dg")
        ctx.tick("cc", g.collective_compute(
            "AllGather", ALU.bypass, replica_groups=RG, ins=[d["cin1"]],
            outs=[d["cout1"]]))
    ctx.on("gpsimd", fcc)

    for grp in range(4):
        emit_proj(ctx, Tt, 0, NT, 512,
                  lambda cc, o0, ol, grp=grp: d["l1_wqkvT"][128 * cc:128 * (cc + 1),
                                                            grp * 512 + o0:grp * 512 + o0 + ol],
                  F32, silu_evac(lambda t: s["QG"][:, t * 512:(t + 1) * 512]))

        with nc.psum_tensor(f"pq0_{ctx.uid()}", [128, 256], F32) as pq0, \
             nc.psum_tensor(f"pq1_{ctx.uid()}", [128, 256], F32) as pq1, \
             nc.psum_tensor(f"po0_{ctx.uid()}", [128, 128], F32) as po0, \
             nc.psum_tensor(f"po1_{ctx.uid()}", [128, 128], F32) as po1, \
             nc.psum_tensor(f"pkv2_{ctx.uid()}", [128, 128], F32) as pkv2, \
             nc.psum_tensor(f"ptq0_{ctx.uid()}", [128, 128], BF16) as ptq0, \
             nc.psum_tensor(f"ptq1_{ctx.uid()}", [128, 128], BF16) as ptq1:
            for hh in range(4):
                h = grp * 4 + hh

                def fcmb(g, h=h):
                    if h == 0:
                        ctx.w(g, "cc")
                    ctx.w(g, "tv")
                    ctx.w(g, "ts")
                    for j in range(8):
                        ctx.dtick("dg", g.dma_start(
                            out=s["TA"][:, j * 128:(j + 1) * 128],
                            in_=d["cout1"][j * 2048 + h * 128:j * 2048 + (h + 1) * 128, :]))
                ctx.on("gpsimd", fcmb)

                def fw8(sc, h=h):
                    ctx.w(sc, "dg")
                    ctx.tick("ts", sc.activation(out=s["SM"][:, 8:16],
                                                 in_=s["ST"][:, 32:40],
                                                 func=AF.Exp, scale=PS(h)))
                ctx.on("scalar", fw8)

                def fw8b(v, h=h):
                    ctx.w(v, "ts")
                    v.tensor_tensor(out=s["SM"][:, 8:16], in0=s["SM"][:, 8:16],
                                    in1=s["ST"][:, 40:48], op=ALU.mult)
                    kvs = s["SM"][:, 64:192]
                    v.drain()
                    v.tensor_scalar(out=kvs, in0=s["TA"][:, 0:128],
                                    scalar1=s["SM"][:, 8:9], scalar2=None, op0=ALU.mult)
                    for j in range(1, 8):
                        v.drain()
                        v.scalar_tensor_tensor(out=kvs,
                                               in0=s["TA"][:, j * 128:(j + 1) * 128],
                                               scalar=s["SM"][:, 8 + j:9 + j], in1=kvs,
                                               op0=ALU.mult, op1=ALU.add)
                    ctx.tick("tv", v.engine_nop())
                ctx.on("vector", fw8b)

                def ftl(sync, h=h):
                    ctx.w(sync, "tv")
                    ctx.dtick("dl", sync.dma_start(out=s["TA"][:, 0:256], in_=d["l1_i0"]))
                    ctx.dtick("dl", sync.dma_start(out=s["TA"][:, 256:512], in_=d["l1_i1"]))
                    ctx.dtick("dl", sync.dma_start(out=s["TA"][:, 512:768], in_=d["l1_msk0"]))
                    ctx.dtick("dl", sync.dma_start(out=s["TA"][:, 768:1024], in_=d["l1_msk1"]))
                    ctx.dtick("dl", sync.dma_start(out=s["TA"][:, 1024:2048],
                                                   in_=d["l1_qm"].broadcast_to([128, T_LOC])))
                ctx.on("sync", ftl)

                def fdec(sc, h=h):
                    ctx.w(sc, "dl")
                    sc.activation(out=s["TA"][:, 0:256], in_=s["TA"][:, 0:256],
                                  func=AF.Exp, scale=PS(h))
                    sc.activation(out=s["TA"][:, 256:512], in_=s["TA"][:, 256:512],
                                  func=AF.Exp, scale=PS(h))
                    ctx.tick("ts", sc.activation(out=s["TA"][:, 1024:2048],
                                                 in_=s["TA"][:, 1024:2048],
                                                 func=AF.Exp, scale=PS(h)))
                ctx.on("scalar", fdec)

                def fdecm(v, h=h):
                    ctx.w(v, "ts")
                    v.tensor_tensor(out=s["SC"][:, 0:256], in0=s["TA"][:, 0:256],
                                    in1=s["TA"][:, 512:768], op=ALU.mult)
                    ctx.tick("tv", v.tensor_tensor(out=s["SC"][:, 256:512],
                                                   in0=s["TA"][:, 256:512],
                                                   in1=s["TA"][:, 768:1024], op=ALU.mult))
                ctx.on("vector", fdecm)

                for t in range(NT):
                    def ftr(te, h=h, hh=hh, t=t):
                        ctx.w(te, "tv")
                        te.transpose(out=ptq0[:], in_=ctx.BA(t * C + h * 128, 128),
                                     identity=s["ID"][:])
                        ctx.tick("tt", te.transpose(
                            out=ptq1[:],
                            in_=s["QG"][:, t * 512 + hh * 128:t * 512 + (hh + 1) * 128],
                            identity=s["ID"][:]))
                    ctx.on("tensor", ftr)

                    def ftrc(v, t=t):
                        ctx.w(v, "tt")
                        v.tensor_copy(out=s["WSa"][:, 1024 + t * 128:1024 + (t + 1) * 128],
                                      in_=ptq0[:])
                        ctx.tick("tv", v.tensor_copy(
                            out=s["WSa"][:, t * 128:(t + 1) * 128], in_=ptq1[:]))
                    ctx.on("vector", ftrc)

                def fqd(v, h=h):
                    ctx.w(v, "tv")
                    ctx.tick("tv", v.tensor_tensor(out=s["WSb"][:, 0:1024],
                                                   in0=s["WSa"][:, 0:1024],
                                                   in1=s["TA"][:, 1024:2048],
                                                   op=ALU.mult))
                ctx.on("vector", fqd)

                def fkd2(sc, h=h):
                    ctx.w(sc, "tv")
                    ctx.tick("ts", sc.activation(out=s["SM"][:, 2:4], in_=s["CW"][:, 0:2],
                                                 func=AF.Exp, scale=PS(h)))
                ctx.on("scalar", fkd2)

                def fbd_2(v, h=h):
                    ctx.w(v, "ts")
                    v.tensor_scalar_mul(out=s["SM"][:, 4:5], in0=PS(h), scalar1=-256.0)
                    v.drain()
                    ctx.tick("tv", v.engine_nop())
                ctx.on("vector", fbd_2)

                def fbd2_2(sc, h=h):
                    ctx.w(sc, "tv")
                    ctx.tick("ts", sc.activation(out=s["SM"][:, 4:5], in_=s["SM"][:, 4:5],
                                                 func=AF.Exp))
                ctx.on("scalar", fbd2_2)

                for b in range(4):
                    def fkb(v, h=h, b=b):
                        ctx.w(v, "ts")
                        ctx.w(v, "tt")
                        v.tensor_copy(out=s["BC"][:, 0:128], in_=s["SM"][:, 64:192])
                        v.tensor_scalar(out=s["BC"][:, 128:256],
                                        in0=ctx.BA((2 * b) * C + h * 128, 128),
                                        scalar1=s["SM"][:, 2:3], scalar2=None,
                                        op0=ALU.mult)
                        v.tensor_scalar(
                            out=s["BC"][:, 256:384],
                            in0=ctx.BA((2 * b + 1) * C + h * 128, 128),
                            scalar1=s["SM"][:, 3:4], scalar2=None, op0=ALU.mult)
                        v.drain()
                        ctx.tick("tv", v.engine_nop())
                    ctx.on("vector", fkb)

                    def fqk(te, h=h, b=b):
                        ctx.w(te, "tv")
                        te.matmul(out=pq0[:],
                                  lhsT=s["WSa"][:, 1024 + (2 * b) * 128:1024 + (2 * b + 1) * 128],
                                  rhs=s["WSa"][:, b * 256:(b + 1) * 256],
                                  start=True, stop=True)
                        ctx.tick("tt", te.matmul(
                            out=pq1[:],
                            lhsT=s["WSa"][:, 1024 + (2 * b + 1) * 128:1024 + (2 * b + 2) * 128],
                            rhs=s["WSa"][:, b * 256:(b + 1) * 256],
                            start=True, stop=True))
                    ctx.on("tensor", fqk)

                    def fqs(v, b=b):
                        ctx.w(v, "tt")
                        v.tensor_tensor(out=s["WSb"][:, 1024:1280], in0=pq0[:],
                                        in1=s["SC"][:, 0:256], op=ALU.mult)
                        v.tensor_tensor(out=s["WSb"][:, 1280:1536], in0=pq1[:],
                                        in1=s["SC"][:, 256:512], op=ALU.mult)
                        v.drain()
                        ctx.tick("tv", v.engine_nop())
                    ctx.on("vector", fqs)

                    def fo(te, h=h, b=b):
                        ctx.w(te, "tv")
                        for mh, po in ((0, po0), (1, po1)):
                            te.matmul(out=po[:],
                                      lhsT=s["WSb"][:, 1024 + mh * 128:1152 + mh * 128],
                                      rhs=ctx.BB((2 * b) * C + h * 128, 128),
                                      start=True, stop=False)
                            te.matmul(out=po[:],
                                      lhsT=s["WSb"][:, 1280 + mh * 128:1408 + mh * 128],
                                      rhs=ctx.BB((2 * b + 1) * C + h * 128, 128),
                                      start=False, stop=False)
                            te.matmul(out=po[:],
                                      lhsT=s["WSb"][:, b * 256 + mh * 128:b * 256 + (mh + 1) * 128],
                                      rhs=s["BC"][:, 0:128],
                                      start=False, stop=True)
                        te.matmul(out=pkv2[:], lhsT=s["BC"][:, 128:256],
                                  rhs=ctx.BB((2 * b) * C + h * 128, 128),
                                  start=True, stop=False)
                        ctx.tick("tt", te.matmul(
                            out=pkv2[:], lhsT=s["BC"][:, 256:384],
                            rhs=ctx.BB((2 * b + 1) * C + h * 128, 128),
                            start=False, stop=True))
                    ctx.on("tensor", fo)

                    def fev(v, h=h, b=b):
                        ctx.w(v, "tt")
                        v.tensor_copy(out=ctx.BA((2 * b) * C + h * 128, 128), in_=po0[:])
                        v.tensor_copy(out=ctx.BA((2 * b + 1) * C + h * 128, 128),
                                      in_=po1[:])
                        kvs = s["SM"][:, 64:192]
                        ctx.tick("tv", v.scalar_tensor_tensor(
                            out=kvs, in0=kvs, scalar=s["SM"][:, 4:5], in1=pkv2[:],
                            op0=ALU.mult, op1=ALU.add))
                        v.drain()
                    ctx.on("vector", fev)

    ctx.barrier()
    if upto == "attn":
        return
    emit_bc_table(ctx, d["l1_normw"])

    for t in range(NT):
        def frs_s(sc, t=t):
            if t == 0:
                ctx.w(sc, "dl")
                ctx.w(sc, "dg")
                ctx.w(sc, "tv")
            ao = ctx.BA(t * C, C)
            for k in range(4):
                sc.activation(out=s["TA"][:, 0:512], in_=ao[:, 512 * k:512 * (k + 1)],
                              func=AF.Square, accum_out=s["ST"][:, 40 + k:41 + k])
            ctx.tick("ts", sc.copy(out=s["SM"][:, 191:192], in_=s["SM"][:, 191:192]))
        ctx.on("scalar", frs_s)

        def frs_v(v, t=t):
            ctx.w(v, "ts")
            rcol = s["SM"][:, 24 + t:25 + t]
            v.tensor_add(out=s["ST"][:, 40:41], in0=s["ST"][:, 40:41],
                         in1=s["ST"][:, 41:42])
            v.tensor_add(out=s["ST"][:, 42:43], in0=s["ST"][:, 42:43],
                         in1=s["ST"][:, 43:44])
            v.drain()
            v.tensor_add(out=s["ST"][:, 40:41], in0=s["ST"][:, 40:41],
                         in1=s["ST"][:, 42:43])
            v.drain()
            v.tensor_scalar(out=rcol, in0=s["ST"][:, 40:41],
                            scalar1=1.0 / C, scalar2=L1_RMS_EPS,
                            op0=ALU.mult, op1=ALU.add)
            v.drain()
            ctx.tick("tv", v.engine_nop())
        ctx.on("vector", frs_v)

        def frs_s2(sc, t=t):
            ctx.w(sc, "tv")
            rcol = s["SM"][:, 24 + t:25 + t]
            sc.activation(out=rcol, in_=rcol, func=AF.Ln)
            ctx.tick("ts", sc.activation(out=rcol, in_=rcol, func=AF.Exp, scale=-0.5))
        ctx.on("scalar", frs_s2)

    def gt_evac(t, oq, pb):
        def fg1(sc, t=t, oq=oq, pb=pb):
            ctx.w(sc, "tt")
            ctx.tick("ts", sc.activation(out=s["TA"][:, 0:512], in_=pb[:],
                                         func=AF.Sigmoid))
        ctx.on("scalar", fg1)

        def fg2(v, t=t, oq=oq):
            ctx.w(v, "ts")
            ctx.w(v, "dg")
            v.tensor_tensor(out=s["TA"][:, 0:512], in0=s["TA"][:, 0:512],
                            in1=ctx.BA(t * C + oq * 512, 512), op=ALU.mult)
            v.tensor_scalar(out=s["TA"][:, 0:512], in0=s["TA"][:, 0:512],
                            scalar1=s["SM"][:, 24 + t:25 + t], scalar2=None,
                            op0=ALU.mult)
            ctx.tick("tv", v.tensor_tensor(
                out=ctx.BB(t * C + oq * 512, 512), in0=s["TA"][:, 0:512],
                in1=s["BC"][:, oq * 512:(oq + 1) * 512], op=ALU.mult))
        ctx.on("vector", fg2)

    emit_proj(ctx, Tt, 0, NT, C,
              lambda cc, o0, ol: d["l1_wgT"][128 * cc:128 * (cc + 1), o0:o0 + ol],
              F32, gt_evac)

    with nc.psum_tensor(f"ptg0_{ctx.uid()}", [128, 128], BF16) as g0, \
         nc.psum_tensor(f"ptg1_{ctx.uid()}", [128, 128], BF16) as g1:
        gp = [g0, g1]
        i = 0
        for cc in range(NC):
            for t in range(NT):
                def ftp(te, cc=cc, t=t, i=i):
                    ctx.w(te, "tv")
                    ctx.tick("tt", te.transpose(
                        out=gp[i % 2][:], in_=ctx.BB(t * C + cc * 128, 128),
                        identity=s["ID"][:]))

                def ftc(v, cc=cc, t=t, i=i):
                    ctx.w(v, "tt")
                    ctx.tick("tv", v.tensor_copy(
                        out=ctx.BA(cc * T_LOC + t * 128, 128), in_=gp[i % 2][:]))
                ctx.on("tensor", ftp)
                ctx.on("vector", ftc)
                i += 1

    def out_evac(t, oq, pb):
        def fv(v, t=t, oq=oq, pb=pb):
            ctx.w(v, "tt")
            xr = ctx.xr(t)
            ctx.tick("tv", v.tensor_add(out=xr[:, oq * 512:(oq + 1) * 512],
                                        in0=xr[:, oq * 512:(oq + 1) * 512], in1=pb[:]))
        ctx.on("vector", fv)

    emit_proj(ctx, T_LOC, 0, NT, C,
              lambda cc, o0, ol: d["l1_woT"][128 * cc:128 * (cc + 1), o0:o0 + ol],
              F32, out_evac,
              lhsT_fn=lambda cc, a, n: ctx.BA(cc * T_LOC + a, n))
    ctx.barrier()



def declare_dram(ctx, nc, fixed, pc, part):
    d = ctx.d
    need = {
        1: ["l0_pgT", "l0_puT", "l0_sg", "l0_su", "l0_pdT", "l0_sd", "shfcol",
            "identity"],
        2: ["l0_wqT", "l0_wkT", "l0_wvT", "l0_woT", "l0_wgT", "l0_convw", "l0_convb",
            "l0_gateb", "l0_lnw", "l0_lnb", "l0_gamma", "l0_gamma_rep", "identity",
            "iota_mj", "mask_mj", "grev", "shfcol", "cc_sel", "cc_pow"],
        4: ["l1_qgT", "l1_quT", "l1_qdT", "l1_sgc", "l1_suc", "l1_sd", "shfcol",
            "identity"],
        5: ["l1_slope", "l1_wqkvT", "l1_wgT", "l1_woT", "l1_normw", "l1_i0", "l1_i1",
            "l1_msk0", "l1_msk1", "l1_qm", "l1_km", "shfcol", "identity",
            "cc_selT", "cc_pownT"],
    }
    names = need[2 if part == 2 else (5 if part == 5 else (1 if part in (1, 3) else 4))]
    srcs = dict(fixed)
    srcs.update(pc)
    dtmap = {np.dtype(np.float32): F32, np.dtype(np.int32): I32,
             np.dtype(np.float16): F16}
    Tin = T_EXT if part <= 2 else T_LOC
    Tout = T_EXT if part == 1 else T_LOC
    d["xin"] = nc.dram_tensor("xin", [Tin, C], F32, kind="ExternalInput").ap()
    d["xout"] = nc.dram_tensor("xout", [Tout, C], F32, kind="ExternalOutput").ap()
    feed = {"xin": srcs["xin"][:Tin]}
    for n in names:
        a = srcs[n]
        d[n] = nc.dram_tensor(n, list(a.shape), dtmap[a.dtype], kind="ExternalInput").ap()
        feed[n] = a
    if part in (1, 3):
        d["wg_b"] = nc.dram_tensor("wg_b", [NC, 128, IF], BF16, kind="Internal").ap()
        d["wu_b"] = nc.dram_tensor("wu_b", [NC, 128, IF], BF16, kind="Internal").ap()
        d["wd_b"] = nc.dram_tensor("wd_b", [NI, 128, C], BF16, kind="Internal").ap()
    if part == 2:
        d["xc_d"] = nc.dram_tensor("xc_d", [NC, 128, T_LOC], BF16, kind="Internal").ap()
        d["cin0"] = nc.dram_tensor("cin0", [1, C], F32, kind="Internal").ap()
        d["cout0"] = nc.dram_tensor("cout0", [8, C], F32, kind="Internal").ap()
    if part == 5:
        d["cin1"] = nc.dram_tensor("cin1", [C, 128], F32, kind="Internal").ap()
        d["cout1"] = nc.dram_tensor("cout1", [8 * C, 128], F32, kind="Internal").ap()
    return feed


def build_part(part, fixed, per_core):
    nc = bass.Bass("TRN2", target_bir_lowering=False, debug=False)
    es = ExitStack()
    with es:
        ctx = Ctx(nc, es)
        declare_dram(ctx, nc, fixed, per_core[0], part)
        alloc_common(ctx, nc, es)
        block = es.enter_context(nc.Block())
        ctx.block = block
        emit_preamble(ctx)
        if part == 1:
            emit_load_x(ctx, NTE)
            emit_l0_dequant(ctx)
            emit_ln(ctx, 0, NTE, False)
            emit_ffn(ctx, "l0", 0, NTE, NTE * 128)
            emit_store_x(ctx, 0, NTE)
        elif part == 2:
            emit_load_x(ctx, NTE)
            emit_ln(ctx, 0, NTE, False)
            emit_l0_attn(ctx)
            emit_store_x(ctx, 1, NT)
        elif part == 3:
            emit_load_x(ctx, NT)
            emit_l0_dequant(ctx)
            emit_ln(ctx, 0, NT, False)
            emit_ffn(ctx, "l0", 0, NT, T_LOC)
            emit_store_x(ctx, 0, NT)
        elif part in (4, 6):
            emit_load_x(ctx, NT)
            emit_ln(ctx, 0, NT, False)
            emit_ffn(ctx, "l1", 0, NT, T_LOC)
            emit_store_x(ctx, 0, NT)
        elif part == 5:
            emit_load_x(ctx, NT)
            emit_ln(ctx, 0, NT, False)
            emit_l1_attn(ctx)
            emit_store_x(ctx, 0, NT)
    return nc


def feeds_for(part, fixed, per_core):
    need = {
        1: ["l0_pgT", "l0_puT", "l0_sg", "l0_su", "l0_pdT", "l0_sd", "shfcol",
            "identity"],
        2: ["l0_wqT", "l0_wkT", "l0_wvT", "l0_woT", "l0_wgT", "l0_convw", "l0_convb",
            "l0_gateb", "l0_lnw", "l0_lnb", "l0_gamma", "l0_gamma_rep", "identity",
            "iota_mj", "mask_mj", "grev", "shfcol", "cc_sel", "cc_pow"],
        4: ["l1_qgT", "l1_quT", "l1_qdT", "l1_sgc", "l1_suc", "l1_sd", "shfcol",
            "identity"],
        5: ["l1_slope", "l1_wqkvT", "l1_wgT", "l1_woT", "l1_normw", "l1_i0", "l1_i1",
            "l1_msk0", "l1_msk1", "l1_qm", "l1_km", "shfcol", "identity",
            "cc_selT", "cc_pownT"],
    }
    names = need[2 if part == 2 else (5 if part == 5 else (1 if part in (1, 3) else 4))]
    Tin = T_EXT if part <= 2 else T_LOC
    feeds = []
    for pc in per_core:
        f = {"xin": np.ascontiguousarray(pc["xin"][:Tin])}
        for n in names:
            f[n] = pc[n] if n in pc else fixed[n]
        feeds.append(f)
    return feeds


PARTS = [1, 2, 3, 4, 5, 6]


def kernel(**inputs):
    fixed, per_core = prep(inputs)
    xs = [pc["xin"] for pc in per_core]
    for part in PARTS:
        for i in range(NCORES):
            per_core[i]["xin"] = xs[i]
        nc = build_part(part, fixed, per_core)
        feeds = feeds_for(part, fixed, per_core)
        res = run_bass_kernel_spmd(nc, feeds, core_ids=list(range(NCORES)))
        outs = [r["xout"] for r in res.results]
        if part == 1:
            xs = outs
        elif part == 2:
            xs = outs
        else:
            xs = outs
    y = np.concatenate(xs, 0).reshape(1, 8192, C).astype(np.float32)
    return y


# version 02
# speedup vs baseline: 1.2075x; 1.2075x over previous
import numpy as np
from contextlib import ExitStack

import concourse.bass as bass
import concourse.mybir as mybir
from concourse.bass_utils import run_bass_kernel_spmd

F32 = mybir.dt.float32
BF16 = mybir.dt.bfloat16
F16 = mybir.dt.float16
I32 = mybir.dt.int32
AF = mybir.ActivationFunctionType
ALU = mybir.AluOpType
AX = mybir.AxisListType.X

C = 2048
NC = 16
IF = 5632
NI = 44
H = 16
T_LOC = 1024
HALO = 128
T_EXT = T_LOC + HALO
NT = 8
NTE = 9
NCORES = 8
LN_EPS = 1e-5
L0_NORM_EPS = 1e-12
L0_FFN_EPS = 1e-6
L1_RMS_EPS = 1e-6
BLK1 = 256
RG = [[0, 1, 2, 3, 4, 5, 6, 7]]


def perm128(n):
    base = np.arange(n) // 128 * 128
    r = np.arange(n) % 128
    return base + 8 * (r % 16) + (r // 16)



def prep(inputs):
    f = {}
    x = np.asarray(inputs["L0_attn_x"], np.float32).reshape(8192, C)
    xp = np.concatenate([np.zeros((HALO, C), np.float32), x], 0)
    per_core = []
    for i in range(NCORES):
        st = i * T_LOC
        pc = {"xin": np.ascontiguousarray(xp[st:st + T_EXT])}
        sel = np.zeros((8, 1), np.float32)
        pw = np.zeros((8, 1), np.float32)
        for j in range(i):
            sel[j] = 1.0
            pw[j] = 1024.0 * float(i - 1 - j)
        pc["cc_sel"] = sel
        pc["cc_pow"] = pw
        pc["cc_selT"] = np.ascontiguousarray(sel.T)
        pc["cc_pownT"] = np.ascontiguousarray((-pw).T)
        per_core.append(pc)

    T = lambda a: np.ascontiguousarray(np.asarray(a).T)
    f["l0_wqT"] = T(inputs["L0_attn_Wq"])
    f["l0_wkT"] = T(inputs["L0_attn_Wk"])
    f["l0_wvT"] = T(inputs["L0_attn_Wv"])
    f["l0_woT"] = T(inputs["L0_attn_Wo"])
    f["l0_wgT"] = T(inputs["L0_attn_gate_w"])
    cw = np.asarray(inputs["L0_attn_conv_w"], np.float32).reshape(C, 4)
    f["l0_convw"] = np.ascontiguousarray(
        cw.reshape(NC, 128, 4).transpose(1, 0, 2).reshape(128, NC * 4))
    f["l0_convb"] = np.ascontiguousarray(
        np.asarray(inputs["L0_attn_conv_b"], np.float32).reshape(NC, 128).T)
    f["l0_gateb"] = np.asarray(inputs["L0_attn_gate_b"], np.float32).reshape(1, C)
    f["l0_lnw"] = np.asarray(inputs["L0_attn_ln_w"], np.float32).reshape(1, C)
    f["l0_lnb"] = np.asarray(inputs["L0_attn_ln_b"], np.float32).reshape(1, C)
    gam = np.asarray(inputs["L0_attn_gamma"], np.float32).reshape(H)
    f["l0_gamma"] = gam.reshape(1, H)
    f["l0_gamma_rep"] = np.ascontiguousarray(np.repeat(gam, 128).reshape(1, C))

    gup = np.asarray(inputs["L0_ffn_gate_up_packed"], np.int32)
    gsc = np.asarray(inputs["L0_ffn_gate_up_scales"])
    f["l0_pgT"] = T(gup[:IF])
    f["l0_puT"] = T(gup[IF:])
    f["l0_sg"] = np.ascontiguousarray(gsc[:, :IF])
    f["l0_su"] = np.ascontiguousarray(gsc[:, IF:])
    f["l0_pdT"] = T(np.asarray(inputs["L0_ffn_down_packed"], np.int32))
    f["l0_sd"] = np.ascontiguousarray(np.asarray(inputs["L0_ffn_down_scales"]))

    f["l1_slope"] = np.asarray(inputs["L1_attn_slope_rate"], np.float32).reshape(1, H)
    f["l1_wqkvT"] = T(inputs["L1_attn_w_qkv"])
    f["l1_wgT"] = T(inputs["L1_attn_w_gate"])
    f["l1_woT"] = T(inputs["L1_attn_w_out"])
    f["l1_normw"] = np.asarray(inputs["L1_attn_norm_w"], np.float32).reshape(1, C)
    f["l1_qgT"] = T(np.asarray(inputs["L1_ffn_wq_gate"], np.int32))
    f["l1_quT"] = T(np.asarray(inputs["L1_ffn_wq_up"], np.int32))
    f["l1_qdT"] = T(np.asarray(inputs["L1_ffn_wq_down"], np.int32))
    f["l1_sgc"] = np.ascontiguousarray(
        np.asarray(inputs["L1_ffn_s_gate"], np.float32).reshape(NI, 128).T)
    f["l1_suc"] = np.ascontiguousarray(
        np.asarray(inputs["L1_ffn_s_up"], np.float32).reshape(NI, 128).T)
    f["l1_sd"] = np.asarray(inputs["L1_ffn_s_down"], np.float32).reshape(1, C)

    f["identity"] = np.eye(128, dtype=np.float32)
    ar = np.arange(128, dtype=np.float32)
    f["iota_mj"] = np.ascontiguousarray(ar[None, :] - ar[:, None])
    f["mask_mj"] = np.ascontiguousarray((ar[None, :] >= ar[:, None]).astype(np.float32))
    f["grev"] = np.ascontiguousarray((127.0 - ar).reshape(128, 1))
    jm0 = np.arange(0, 128)[:, None] - np.arange(BLK1)[None, :]
    jm1 = np.arange(128, 256)[:, None] - np.arange(BLK1)[None, :]
    f["l1_i0"] = np.ascontiguousarray(jm0.astype(np.float32))
    f["l1_i1"] = np.ascontiguousarray(jm1.astype(np.float32))
    f["l1_msk0"] = np.ascontiguousarray((jm0 <= 0).astype(np.float32))
    f["l1_msk1"] = np.ascontiguousarray((jm1 <= 0).astype(np.float32))
    f["l1_qm"] = (-((np.arange(T_LOC) % BLK1) + 1.0)).astype(np.float32).reshape(1, T_LOC)
    f["l1_km"] = np.ascontiguousarray(np.stack(
        [-(255.0 - np.arange(0, 128)), -(255.0 - np.arange(128, 256))], 1).astype(np.float32))
    f["shfcol"] = (4 * (np.arange(128) // 16)).astype(np.int32).reshape(128, 1)
    return f, per_core



class Ctx:
    def __init__(self, nc, es):
        self.nc = nc
        self.es = es
        self.block = None
        self.cnt = {}
        self.sem = {}
        self.d = {}
        self.s = {}
        self._uid = 0
        for name in ["dl", "dg", "tv", "ts", "tt", "cc"]:
            self.sem[name] = es.enter_context(nc.semaphore(name))
            self.cnt[name] = 0

    def uid(self):
        self._uid += 1
        return self._uid

    def tick(self, name, inst, n=1):
        inst.then_inc(self.sem[name], n)
        self.cnt[name] += n
        return self.cnt[name]

    def dtick(self, name, inst):
        return self.tick(name, inst, 16)

    def w(self, eng, name, val=None):
        eng.wait_ge(self.sem[name], self.cnt[name] if val is None else val)

    def on(self, engname, fn):
        getattr(self.block, engname)(fn)

    def wall(self, eng):
        for n in ["dl", "dg", "tv", "ts", "tt"]:
            self.w(eng, n)

    def barrier(self):
        def fv(v):
            self.tick("tv", v.engine_nop())
        self.on("vector", fv)

        def fg(g):
            self.tick("dg", g.engine_nop(), 16)
        self.on("gpsimd", fg)
        for e in ["sync", "vector", "scalar", "tensor", "gpsimd"]:
            def fw(eng, _e=e):
                self.wall(eng)
            self.on(e, fw)

    def xr(self, t):
        return self.s["XR"][:, t * C:(t + 1) * C]

    def xnc(self, cc, Tt):
        return self.s["XN"][:, cc * Tt:(cc + 1) * Tt]

    def BA(self, i, n):
        return self.s["BIG"][:, i:i + n]

    def BB(self, i, n):
        return self.s["BIG"][:, NT * C + i:NT * C + i + n]


def alloc_common(ctx, nc, es):
    S = lambda name, shape, dt: es.enter_context(nc.sbuf_tensor(name, shape, dt))
    s = ctx.s
    s["XR"] = S("XR", [128, NTE * C], F32)
    s["XN"] = S("XN", [128, NC * T_EXT], BF16)
    s["TA"] = S("TA", [128, C], F32)
    s["BIG"] = S("BIG", [128, 2 * NT * C], BF16)
    s["WSa"] = S("WSa", [128, 2048], BF16)
    s["WSb"] = S("WSb", [128, 2048], BF16)
    s["WS2"] = S("WS2", [128, 1024], F32)
    s["BC"] = S("BC", [128, C], BF16)
    s["SM"] = S("SM", [128, 192], F32)
    s["SC"] = S("SC", [128, 512], BF16)
    s["ID"] = S("ID", [128, 128], BF16)
    s["ST"] = S("ST", [128, 64], F32)
    s["CW"] = S("CW", [128, 80], F32)
    s["QG"] = S("QG", [128, 4096], BF16)
    s["WSi"] = s["QG"].bitcast(I32)


def emit_bc_table(ctx, row_ap):
    s = ctx.s

    def fd(sync):
        ctx.w(sync, "tv")
        ctx.dtick("dl", sync.dma_start(out=s["TA"][:, :],
                                       in_=row_ap.broadcast_to([128, C])))
    ctx.on("sync", fd)

    def fc(v):
        ctx.w(v, "dl")
        ctx.tick("tv", v.tensor_copy(out=s["BC"][:], in_=s["TA"][:, :]))
    ctx.on("vector", fc)


def emit_preamble(ctx):
    s, d = ctx.s, ctx.d

    def f(sync):
        ctx.dtick("dl", sync.dma_start(out=s["SM"][:, 128:129].bitcast(I32), in_=d["shfcol"]))
        ctx.dtick("dl", sync.dma_start(out=s["WS2"][:, 0:128], in_=d["identity"]))
    ctx.on("sync", f)

    def fv(v):
        ctx.w(v, "dl")
        ctx.tick("tv", v.tensor_copy(out=s["ID"][:], in_=s["WS2"][:, 0:128]))
    ctx.on("vector", fv)


def emit_load_x(ctx, ntiles):
    def f(sync):
        for t in range(ntiles):
            ctx.dtick("dl", sync.dma_start(
                out=ctx.xr(t), in_=ctx.d["xin"][t * 128:(t + 1) * 128, :]))
    ctx.on("sync", f)


def emit_store_x(ctx, t0, ntiles):
    def f(g):
        ctx.wall(g)
        for t in range(ntiles):
            ctx.dtick("dg", g.dma_start(
                out=ctx.d["xout"][t * 128:(t + 1) * 128, :], in_=ctx.xr(t0 + t)))
        ctx.w(g, "dg")
    ctx.on("gpsimd", f)



def emit_ln(ctx, t0, ntiles, permuted):
    nc, s = ctx.nc, ctx.s
    Tt = ntiles * 128
    ctx.barrier()
    with nc.psum_tensor(f"ptl0_{ctx.uid()}", [128, 128], BF16) as p0, \
         nc.psum_tensor(f"ptl1_{ctx.uid()}", [128, 128], BF16) as p1, \
         nc.psum_tensor(f"ptlf0_{ctx.uid()}", [128, 128], F32) as f0, \
         nc.psum_tensor(f"ptlf1_{ctx.uid()}", [128, 128], F32) as f1:
        pt = [p0, p1]
        pf = [f0, f1]
        xtmp = ctx.BA(0, C)
        for t in range(ntiles):
            xr = ctx.xr(t0 + t)

            def fv(v, t=t, xr=xr):
                ctx.w(v, "tv")
                if t > 0:
                    ctx.w(v, "tt")
                v.reduce_sum(out=s["ST"][:, 24:25], in_=xr, axis=AX)
                v.drain()
                v.tensor_scalar_mul(out=s["ST"][:, 24:25], in0=s["ST"][:, 24:25],
                                    scalar1=1.0 / C)
                v.drain()
                ctx.tick("tv", v.engine_nop())
            ctx.on("vector", fv)

            def fsr(sc, t=t, xr=xr):
                ctx.w(sc, "tv")
                for k in range(4):
                    sc.activation(out=s["SM"][:, 128:160] if False else s["TA"][:, 0:512],
                                  in_=xr[:, 512 * k:512 * (k + 1)], func=AF.Square,
                                  bias=s["ST"][:, 24:25], scale=-1.0,
                                  accum_out=s["ST"][:, 40 + k:41 + k])
                ctx.tick("ts", sc.copy(out=s["SM"][:, 191:192], in_=s["SM"][:, 191:192]))
            ctx.on("scalar", fsr)

            def fv15(v, t=t):
                ctx.w(v, "ts")
                v.tensor_add(out=s["ST"][:, 40:41], in0=s["ST"][:, 40:41],
                             in1=s["ST"][:, 41:42])
                v.tensor_add(out=s["ST"][:, 42:43], in0=s["ST"][:, 42:43],
                             in1=s["ST"][:, 43:44])
                v.drain()
                v.tensor_add(out=s["ST"][:, 40:41], in0=s["ST"][:, 40:41],
                             in1=s["ST"][:, 42:43])
                v.drain()
                ctx.tick("tv", v.tensor_scalar(out=s["ST"][:, 26:27],
                                               in0=s["ST"][:, 40:41],
                                               scalar1=1.0 / C, scalar2=LN_EPS,
                                               op0=ALU.mult, op1=ALU.add))
            ctx.on("vector", fv15)

            def fsr2(sc, t=t):
                ctx.w(sc, "tv")
                sc.activation(out=s["ST"][:, 26:27], in_=s["ST"][:, 26:27], func=AF.Ln)
                ctx.tick("ts", sc.activation(out=s["ST"][:, 26:27],
                                             in_=s["ST"][:, 26:27],
                                             func=AF.Exp, scale=-0.5))
            ctx.on("scalar", fsr2)

            def fnm(v, t=t, xr=xr):
                ctx.w(v, "ts")
                ctx.tick("tv", v.tensor_scalar(
                    out=xtmp, in0=xr, scalar1=s["ST"][:, 24:25],
                    scalar2=s["ST"][:, 26:27], op0=ALU.subtract, op1=ALU.mult))
            ctx.on("vector", fnm)

            for cc in range(NC):
                def ft(te, t=t, cc=cc):
                    ctx.w(te, "tv")
                    src = xtmp[:, cc * 128:(cc + 1) * 128]
                    ctx.tick("tt", te.transpose(out=pt[cc % 2][:], in_=src,
                                                identity=s["ID"][:]))

                def fv2(v, t=t, cc=cc):
                    ctx.w(v, "tt")
                    ctx.tick("tv", v.tensor_copy(
                        out=ctx.xnc(cc, Tt)[:, t * 128:(t + 1) * 128],
                        in_=pt[cc % 2][:]))
                ctx.on("tensor", ft)
                ctx.on("vector", fv2)
    ctx.barrier()



def emit_proj(ctx, Tt, a0, ntiles, O, wsrc, wdt, evac, lhsT_fn=None):
    nc, s = ctx.nc, ctx.s
    if lhsT_fn is None:
        def lhsT_fn(cc, a, n):
            return ctx.xnc(cc, Tt)[:, a:a + n]
    nq = O // 512
    ctx.barrier()
    with ExitStack() as ps:
        pb = [ps.enter_context(nc.psum_tensor(f"pj{i}_{ctx.uid()}", [128, 512], F32))
              for i in range(ntiles)]
        for oq in range(nq):
            for cc in range(NC):
                buf = s["WSa"] if cc % 2 == 0 else s["WSb"]
                wtile = buf[:, 0:512]
                if wdt == BF16:
                    def fdma(sync, oq=oq, cc=cc, wtile=wtile):
                        ctx.w(sync, "tt")
                        ctx.dtick("dl", sync.dma_start(out=wtile,
                                                       in_=wsrc(cc, oq * 512, 512)))
                    ctx.on("sync", fdma)
                else:
                    def fdma(sync, oq=oq, cc=cc):
                        ctx.w(sync, "tv")
                        if wdt == F32:
                            stage = s["WS2"][:, 0:512] if cc % 2 == 0 else s["WS2"][:, 512:1024]
                        else:
                            stage = s["WSi"][:, 0:512] if cc % 2 == 0 else s["WSi"][:, 512:1024]
                        ctx.dtick("dl", sync.dma_start(out=stage,
                                                       in_=wsrc(cc, oq * 512, 512)))
                    ctx.on("sync", fdma)

                    def fcvt(v, oq=oq, cc=cc, wtile=wtile):
                        ctx.w(v, "dl")
                        ctx.w(v, "tt")
                        if wdt == F32:
                            stage = s["WS2"][:, 0:512] if cc % 2 == 0 else s["WS2"][:, 512:1024]
                            ctx.tick("tv", v.tensor_copy(out=wtile, in_=stage))
                        else:
                            stage = s["WSi"][:, 0:512] if cc % 2 == 0 else s["WSi"][:, 512:1024]
                            ctx.tick("tv", v.tensor_scalar(
                                out=wtile, in0=stage, scalar1=-128.0, scalar2=None,
                                op0=ALU.add))
                    ctx.on("vector", fcvt)

                def fmm(te, oq=oq, cc=cc, wtile=wtile):
                    ctx.w(te, "dl" if wdt == BF16 else "tv")
                    if cc == 0:
                        ctx.w(te, "ts")
                        ctx.w(te, "tv")
                    for t in range(ntiles):
                        inst = te.matmul(out=pb[t][:], lhsT=lhsT_fn(cc, a0 + t * 128, 128),
                                         rhs=wtile, start=(cc == 0), stop=(cc == NC - 1))
                        if t == ntiles - 1:
                            ctx.tick("tt", inst)
                ctx.on("tensor", fmm)

            for t in range(ntiles):
                evac(t, oq, pb[t])
    ctx.barrier()



def emit_l0_dequant(ctx):
    s = ctx.s
    shf = s["SM"][:, 128:129].bitcast(I32)
    ctx.barrier()

    def fv0(v):
        ctx.w(v, "dl")
        v.memset(s["ST"][:, 32:33], -2.0)
        v.memset(s["ST"][:, 33:34], -6.0)
        v.drain()
        ctx.tick("tv", v.engine_nop())
    ctx.on("vector", fv0)

    def emit_piece(words_src, width, scale_src, out_dst):
        def fdma(sync):
            ctx.w(sync, "tv")
            for j in range(8):
                ctx.dtick("dl", sync.dma_start(
                    out=s["WSi"][16 * j:16 * (j + 1), 0:width], in_=words_src))
            ctx.dtick("dl", sync.dma_start(
                out=s["WSa"].bitcast(F16)[:, 0:width],
                in_=scale_src.broadcast_to([128, width])))
        ctx.on("sync", fdma)

        def fv1(v):
            ctx.w(v, "dl")
            ctx.w(v, "ts")
            v.tensor_scalar(
                out=s["WSi"][:, 1024:1024 + width], in0=s["WSi"][:, 0:width],
                scalar1=shf, scalar2=7,
                op0=ALU.logical_shift_right, op1=ALU.bitwise_and)
            ctx.tick("tv", v.tensor_copy(out=s["WS2"][:, 0:width],
                                         in_=s["WSi"][:, 1024:1024 + width]))
        ctx.on("vector", fv1)

        def fs1(sc):
            ctx.w(sc, "tv")
            sc.activation(out=s["TA"][:, 0:width], in_=s["WS2"][:, 0:width],
                          func=AF.Relu, bias=s["ST"][:, 32:33], scale=0.5)
            ctx.tick("ts", sc.activation(
                out=s["TA"][:, 1024:1024 + width], in_=s["WS2"][:, 0:width],
                func=AF.Relu, bias=s["ST"][:, 33:34], scale=1.0))
        ctx.on("scalar", fs1)

        def fv2(v):
            ctx.w(v, "ts")
            ctx.w(v, "dg")
            m_f = s["WS2"][:, 0:width]
            v.scalar_tensor_tensor(out=m_f, in0=m_f, scalar=0.5,
                                   in1=s["TA"][:, 0:width], op0=ALU.mult, op1=ALU.add)
            v.tensor_add(out=m_f, in0=m_f, in1=s["TA"][:, 1024:1024 + width])
            v.tensor_scalar(out=s["WSi"][:, 1024:1024 + width], in0=s["WSi"][:, 0:width],
                            scalar1=shf, scalar2=8,
                            op0=ALU.logical_shift_right, op1=ALU.bitwise_and)
            v.tensor_scalar(out=s["TA"][:, 0:width], in0=s["WSi"][:, 1024:1024 + width],
                            scalar1=-0.25, scalar2=1.0, op0=ALU.mult, op1=ALU.add)
            v.tensor_mul(out=m_f, in0=m_f, in1=s["TA"][:, 0:width])
            ctx.tick("tv", v.tensor_tensor(
                out=s["WSb"][:, 0:width], in0=m_f,
                in1=s["WSa"].bitcast(F16)[:, 0:width], op=ALU.mult))
        ctx.on("vector", fv2)

        def fst(g):
            ctx.w(g, "tv")
            rview = out_dst.rearrange("(m j) w -> m j w", m=16, j=8)
            for j in range(8):
                ctx.dtick("dg", g.dma_start(
                    out=rview[:, j, :],
                    in_=s["WSb"][16 * j:16 * (j + 1), 0:width]))
        ctx.on("gpsimd", fst)

    for cc in range(NC):
        for sl in range(6):
            w0 = sl * 1024
            wd = min(1024, IF - w0)
            for src, ssrc, dst in [
                (ctx.d["l0_pgT"], ctx.d["l0_sg"], ctx.d["wg_b"]),
                (ctx.d["l0_puT"], ctx.d["l0_su"], ctx.d["wu_b"]),
            ]:
                emit_piece(src[16 * cc:16 * (cc + 1), w0:w0 + wd], wd,
                           ssrc[cc:cc + 1, w0:w0 + wd], dst[cc, :, w0:w0 + wd])
    for kk in range(NI):
        for sl in range(2):
            w0 = sl * 1024
            emit_piece(ctx.d["l0_pdT"][16 * kk:16 * (kk + 1), w0:w0 + 1024], 1024,
                       ctx.d["l0_sd"][kk:kk + 1, w0:w0 + 1024],
                       ctx.d["wd_b"][kk, :, w0:w0 + 1024])
    ctx.barrier()



def emit_ffn(ctx, layer, t0, ntiles, Tt):
    nc, s, d = ctx.nc, ctx.s, ctx.d
    chunks = []
    t = 0
    while t < ntiles:
        n = min(4, ntiles - t)
        chunks.append((t, n))
        t += n

    ctx.barrier()
    if layer == "l1":
        def fld(sync):
            ctx.dtick("dl", sync.dma_start(out=s["SM"][:, 0:44], in_=d["l1_sgc"]))
            ctx.dtick("dl", sync.dma_start(out=s["SM"][:, 44:88], in_=d["l1_suc"]))
        ctx.on("sync", fld)

    for (tc, ntc) in chunks:
        Tch = ntc * 128
        with nc.psum_tensor(f"pgA_{ctx.uid()}", [128, 512], F32) as pg, \
             nc.psum_tensor(f"puA_{ctx.uid()}", [128, 512], F32) as pu:
            for k in range(NI):
                if layer == "l0":
                    def fdg(sync, k=k):
                        ctx.w(sync, "tt")
                        ctx.dtick("dl", sync.dma_start(
                            out=s["WSa"].rearrange("p (cc i) -> p cc i", cc=16),
                            in_=d["wg_b"].transpose([1, 0, 2])[:, :, 128 * k:128 * (k + 1)]))
                        ctx.dtick("dl", sync.dma_start(
                            out=s["WSb"].rearrange("p (cc i) -> p cc i", cc=16),
                            in_=d["wu_b"].transpose([1, 0, 2])[:, :, 128 * k:128 * (k + 1)]))
                    ctx.on("sync", fdg)
                else:
                    def fdg1(sync, k=k):
                        ctx.w(sync, "tv")
                        ctx.dtick("dl", sync.dma_start(
                            out=s["WSi"].rearrange("p (cc i) -> p cc i", cc=16),
                            in_=d["l1_qgT"].rearrange("(cc p) i -> p cc i", p=128)[:, :, 128 * k:128 * (k + 1)]))
                    ctx.on("sync", fdg1)

                    def fcv1(v, k=k):
                        ctx.w(v, "dl")
                        ctx.w(v, "tt")
                        ctx.tick("tv", v.tensor_scalar(
                            out=s["WSa"][:], in0=s["WSi"][:], scalar1=-128.0,
                            scalar2=None, op0=ALU.add))
                    ctx.on("vector", fcv1)

                    def fdg2(sync, k=k):
                        ctx.w(sync, "tv")
                        ctx.dtick("dl", sync.dma_start(
                            out=s["WSi"].rearrange("p (cc i) -> p cc i", cc=16),
                            in_=d["l1_quT"].rearrange("(cc p) i -> p cc i", p=128)[:, :, 128 * k:128 * (k + 1)]))
                    ctx.on("sync", fdg2)

                    def fcv2(v, k=k):
                        ctx.w(v, "dl")
                        ctx.w(v, "tt")
                        ctx.tick("tv", v.tensor_scalar(
                            out=s["WSb"][:], in0=s["WSi"][:], scalar1=-128.0,
                            scalar2=None, op0=ALU.add))
                    ctx.on("vector", fcv2)

                def fmmA(te, k=k, tc=tc, Tch=Tch):
                    ctx.w(te, "dl" if layer == "l0" else "tv")
                    ctx.w(te, "ts")
                    if layer == "l0":
                        ctx.w(te, "tv")
                    for cc in range(NC):
                        te.matmul(out=pg[:, 0:Tch],
                                  lhsT=s["WSa"][:, cc * 128:(cc + 1) * 128],
                                  rhs=ctx.xnc(cc, Tt)[:, tc * 128:tc * 128 + Tch],
                                  start=(cc == 0), stop=(cc == NC - 1))
                    for cc in range(NC):
                        inst = te.matmul(out=pu[:, 0:Tch],
                                         lhsT=s["WSb"][:, cc * 128:(cc + 1) * 128],
                                         rhs=ctx.xnc(cc, Tt)[:, tc * 128:tc * 128 + Tch],
                                         start=(cc == 0), stop=(cc == NC - 1))
                        if cc == NC - 1:
                            ctx.tick("tt", inst)
                ctx.on("tensor", fmmA)

                def fevA_s(sc, k=k, Tch=Tch):
                    ctx.w(sc, "tt")
                    if layer == "l1":
                        ctx.w(sc, "dl")
                        inst = sc.activation(out=s["WS2"][:, 0:Tch], in_=pg[:, 0:Tch],
                                             func=AF.Silu, scale=s["SM"][:, k:k + 1])
                    else:
                        inst = sc.activation(out=s["WS2"][:, 0:Tch], in_=pg[:, 0:Tch],
                                             func=AF.Silu)
                    ctx.tick("ts", inst)
                ctx.on("scalar", fevA_s)

                def fevA_v(v, k=k, Tch=Tch):
                    ctx.w(v, "ts")
                    hdst = ctx.s["BIG"][:, k * 512:k * 512 + Tch]
                    if layer == "l0":
                        inst = v.tensor_tensor(out=hdst, in0=s["WS2"][:, 0:Tch],
                                               in1=pu[:, 0:Tch], op=ALU.mult)
                    else:
                        inst = v.scalar_tensor_tensor(out=hdst, in0=pu[:, 0:Tch],
                                                      scalar=s["SM"][:, 44 + k:45 + k],
                                                      in1=s["WS2"][:, 0:Tch],
                                                      op0=ALU.mult, op1=ALU.mult)
                    ctx.tick("tv", inst)
                ctx.on("vector", fevA_v)

        ctx.barrier()
        if layer == "l1":
            emit_bc_table(ctx, d["l1_sd"])

            def fbc2(v):
                ctx.tick("tv", v.tensor_scalar_mul(out=s["BC"][:], in0=s["BC"][:],
                                                   scalar1=0.5))
            ctx.on("vector", fbc2)

        with ExitStack() as ps:
            pyb = [ps.enter_context(nc.psum_tensor(f"pyB{i}_{ctx.uid()}", [128, 512], F32))
                   for i in range(ntc)]
            for oq in range(4):
                for k in range(NI):
                    buf = s["WSa"] if k % 2 == 0 else s["WSb"]
                    wt = buf[:, 0:512]
                    if layer == "l0":
                        def fdB(sync, oq=oq, k=k, wt=wt):
                            ctx.w(sync, "tt")
                            ctx.dtick("dl", sync.dma_start(
                                out=wt, in_=d["wd_b"][k, :, 512 * oq:512 * (oq + 1)]))
                        ctx.on("sync", fdB)
                    else:
                        def fdB(sync, oq=oq, k=k):
                            ctx.w(sync, "tv")
                            stage = s["WSi"][:, 0:512] if k % 2 == 0 else s["WSi"][:, 512:1024]
                            ctx.dtick("dl", sync.dma_start(
                                out=stage,
                                in_=d["l1_qdT"][128 * k:128 * (k + 1), 512 * oq:512 * (oq + 1)]))
                        ctx.on("sync", fdB)

                        def fcB(v, oq=oq, k=k, wt=wt):
                            ctx.w(v, "dl")
                            ctx.w(v, "tt")
                            stage = s["WSi"][:, 0:512] if k % 2 == 0 else s["WSi"][:, 512:1024]
                            ctx.tick("tv", v.tensor_scalar(
                                out=wt, in0=stage, scalar1=-128.0, scalar2=None,
                                op0=ALU.add))
                        ctx.on("vector", fcB)

                    def fmB(te, oq=oq, k=k, wt=wt, ntc=ntc):
                        ctx.w(te, "dl" if layer == "l0" else "tv")
                        if k == 0:
                            ctx.w(te, "tv")
                            ctx.w(te, "ts")
                        for i in range(ntc):
                            inst = te.matmul(
                                out=pyb[i][:],
                                lhsT=ctx.s["BIG"][:, k * 512 + i * 128:k * 512 + (i + 1) * 128],
                                rhs=wt, start=(k == 0), stop=(k == NI - 1))
                            if i == ntc - 1:
                                ctx.tick("tt", inst)
                    ctx.on("tensor", fmB)

                for i in range(ntc):
                    if layer == "l0":
                        def fqv(v, i=i, oq=oq, tc=tc):
                            ctx.w(v, "tt")
                            ctx.tick("tv", v.tensor_copy(
                                out=ctx.s["BIG"][:, NI * 512 + i * C + oq * 512:NI * 512 + i * C + (oq + 1) * 512],
                                in_=pyb[i][:]))
                        ctx.on("vector", fqv)
                    else:
                        def fqv(v, i=i, oq=oq, tc=tc):
                            ctx.w(v, "tt")
                            v.tensor_tensor(out=s["TA"][:, 0:512], in0=pyb[i][:],
                                            in1=s["BC"][:, oq * 512:(oq + 1) * 512],
                                            op=ALU.mult)
                            xr = ctx.xr(t0 + tc + i)
                            ctx.tick("tv", v.tensor_add(
                                out=xr[:, oq * 512:(oq + 1) * 512],
                                in0=xr[:, oq * 512:(oq + 1) * 512],
                                in1=s["TA"][:, 0:512]))
                        ctx.on("vector", fqv)

            if layer == "l0":
                for i in range(ntc):
                    def frms_s(sc, i=i, tc=tc):
                        ctx.w(sc, "tv")
                        y = ctx.s["BIG"][:, NI * 512 + i * C:NI * 512 + (i + 1) * C]
                        for k4 in range(4):
                            sc.activation(out=s["TA"][:, 0:512],
                                          in_=y[:, 512 * k4:512 * (k4 + 1)],
                                          func=AF.Square,
                                          accum_out=s["ST"][:, 40 + k4:41 + k4])
                        ctx.tick("ts", sc.copy(out=s["SM"][:, 191:192], in_=s["SM"][:, 191:192]))
                    ctx.on("scalar", frms_s)

                    def frms_v(v, i=i, tc=tc):
                        ctx.w(v, "ts")
                        v.tensor_add(out=s["ST"][:, 40:41], in0=s["ST"][:, 40:41],
                                     in1=s["ST"][:, 41:42])
                        v.tensor_add(out=s["ST"][:, 42:43], in0=s["ST"][:, 42:43],
                                     in1=s["ST"][:, 43:44])
                        v.drain()
                        v.tensor_add(out=s["ST"][:, 40:41], in0=s["ST"][:, 40:41],
                                     in1=s["ST"][:, 42:43])
                        v.drain()
                        v.tensor_scalar(out=s["ST"][:, 26:27], in0=s["ST"][:, 40:41],
                                        scalar1=1.0 / C, scalar2=L0_FFN_EPS,
                                        op0=ALU.mult, op1=ALU.add)
                        v.drain()
                        ctx.tick("tv", v.engine_nop())
                    ctx.on("vector", frms_v)

                    def frms_s2(sc, i=i, tc=tc):
                        ctx.w(sc, "tv")
                        sc.activation(out=s["ST"][:, 26:27], in_=s["ST"][:, 26:27],
                                      func=AF.Ln)
                        ctx.tick("ts", sc.activation(out=s["ST"][:, 26:27],
                                                     in_=s["ST"][:, 26:27],
                                                     func=AF.Exp, scale=-0.5))
                    ctx.on("scalar", frms_s2)

                    def frms2(v, i=i, tc=tc):
                        ctx.w(v, "ts")
                        y = ctx.s["BIG"][:, NI * 512 + i * C:NI * 512 + (i + 1) * C]
                        v.tensor_scalar_mul(out=s["ST"][:, 26:27], in0=s["ST"][:, 26:27],
                                            scalar1=0.5)
                        v.drain()
                        v.tensor_scalar(out=s["TA"][:], in0=y, scalar1=s["ST"][:, 26:27],
                                        scalar2=None, op0=ALU.mult)
                        xr = ctx.xr(t0 + tc + i)
                        ctx.tick("tv", v.tensor_add(out=xr, in0=xr, in1=s["TA"][:]))
                    ctx.on("vector", frms2)
        ctx.barrier()



def emit_l0_attn(ctx, upto="full"):
    nc, s, d = ctx.nc, ctx.s, ctx.d
    Tt = T_EXT

    ctx.barrier()

    def fld(sync):
        ctx.dtick("dl", sync.dma_start(out=s["SM"][0:1, 32:48], in_=d["l0_gamma"]))
        ctx.dtick("dl", sync.dma_start(out=s["ST"][:, 48:64],
                                       in_=d["l0_gamma"].broadcast_to([128, H])))
        ctx.dtick("dl", sync.dma_start(out=s["CW"][:, 0:64], in_=d["l0_convw"]))
        ctx.dtick("dl", sync.dma_start(out=s["CW"][:, 64:80], in_=d["l0_convb"]))
    ctx.on("sync", fld)

    def fs0(sc):
        ctx.w(sc, "dl")
        ctx.tick("ts", sc.activation(out=s["ST"][:, 48:64], in_=s["ST"][:, 48:64],
                                     func=AF.Ln))
    ctx.on("scalar", fs0)

    for cc in range(NC):
        def fcv(v, cc=cc):
            if cc == 0:
                ctx.w(v, "tv")
                ctx.w(v, "dl")
            else:
                ctx.w(v, "ts")
            xn = ctx.xnc(cc, Tt)
            v.tensor_scalar(out=s["TA"][:, 0:1024], in0=xn[:, 125:125 + 1024],
                            scalar1=s["CW"][:, 4 * cc:4 * cc + 1], scalar2=None,
                            op0=ALU.mult)
            for k in (1, 2, 3):
                v.scalar_tensor_tensor(out=s["TA"][:, 0:1024],
                                       in0=xn[:, 125 + k:125 + k + 1024],
                                       scalar=s["CW"][:, 4 * cc + k:4 * cc + k + 1],
                                       in1=s["TA"][:, 0:1024],
                                       op0=ALU.mult, op1=ALU.add)
            v.drain()
            ctx.tick("tv", v.engine_nop())
        ctx.on("vector", fcv)

        def fcs(sc, cc=cc):
            ctx.w(sc, "tv")
            ctx.w(sc, "dg")
            ctx.tick("ts", sc.activation(out=s["WSb"][:, 0:1024], in_=s["TA"][:, 0:1024],
                                         func=AF.Silu, bias=s["CW"][:, 64 + cc:65 + cc],
                                         scale=1.0))
        ctx.on("scalar", fcs)

        def fcd(g, cc=cc):
            ctx.w(g, "ts")
            ctx.dtick("dg", g.dma_start(out=d["xc_d"][cc], in_=s["WSb"][:, 0:1024]))
        ctx.on("gpsimd", fcd)

    def l2norm_evac(dstf):
        def evac(t, oq, pb):
            def fsq(sc, t=t, oq=oq, pb=pb):
                ctx.w(sc, "tt")
                for hh in range(4):
                    sc.activation(out=s["TA"][:, hh * 128:(hh + 1) * 128],
                                  in_=pb[:, hh * 128:(hh + 1) * 128],
                                  func=AF.Square,
                                  accum_out=s["ST"][:, 40 + hh:41 + hh])
                sc.activation(out=s["ST"][:, 44:48], in_=s["ST"][:, 40:44], func=AF.Ln)
                ctx.tick("ts", sc.activation(out=s["ST"][:, 44:48],
                                             in_=s["ST"][:, 44:48],
                                             func=AF.Exp, scale=-0.5))
            ctx.on("scalar", fsq)

            def fvn(v, t=t, oq=oq, pb=pb):
                ctx.w(v, "ts")
                v.tensor_scalar(out=s["ST"][:, 44:48], in0=s["ST"][:, 44:48],
                                scalar1=1.0 / L0_NORM_EPS, scalar2=None, op0=ALU.min)
                v.drain()
                for hh in range(4):
                    inst = v.tensor_scalar(
                        out=dstf(t)[:, oq * 512 + hh * 128:oq * 512 + (hh + 1) * 128],
                        in0=pb[:, hh * 128:(hh + 1) * 128],
                        scalar1=s["ST"][:, 44 + hh:45 + hh], scalar2=None, op0=ALU.mult)
                ctx.tick("tv", inst)
            ctx.on("vector", fvn)
        return evac

    emit_proj(ctx, Tt, HALO, NT, C,
              lambda cc, o0, ol: d["l0_wkT"][128 * cc:128 * (cc + 1), o0:o0 + ol],
              F32, l2norm_evac(lambda t: ctx.BA(t * C, C)))

    def v_evac(t, oq, pb):
        def fv(v, t=t, oq=oq, pb=pb):
            ctx.w(v, "tt")
            ctx.tick("tv", v.tensor_tensor(
                out=ctx.BB(t * C + oq * 512, 512), in0=pb[:],
                in1=ctx.BA(t * C + oq * 512, 512), op=ALU.mult))
        ctx.on("vector", fv)
    emit_proj(ctx, Tt, HALO, NT, C,
              lambda cc, o0, ol: d["l0_wvT"][128 * cc:128 * (cc + 1), o0:o0 + ol],
              F32, v_evac)

    emit_proj(ctx, Tt, HALO, NT, C,
              lambda cc, o0, ol: d["l0_wqT"][128 * cc:128 * (cc + 1), o0:o0 + ol],
              F32, l2norm_evac(lambda t: ctx.BA(t * C, C)))

    def fxc(sync):
        ctx.wall(sync)
        for cc in range(NC):
            ctx.dtick("dl", sync.dma_start(out=ctx.xnc(cc, T_LOC), in_=d["xc_d"][cc]))
    ctx.on("sync", fxc)

    emit_bc_table(ctx, d["l0_gateb"])

    def g_evac(t, oq, pb):
        def fga(v, t=t, oq=oq, pb=pb):
            ctx.w(v, "tt")
            ctx.w(v, "dg")
            ctx.tick("tv", v.tensor_tensor(out=s["TA"][:, 0:512], in0=pb[:],
                                           in1=s["BC"][:, oq * 512:(oq + 1) * 512],
                                           op=ALU.add))
        ctx.on("vector", fga)

        def fgs(sc, t=t, oq=oq):
            ctx.w(sc, "tv")
            ctx.tick("ts", sc.activation(out=s["TA"][:, 512:1024],
                                         in_=s["TA"][:, 0:512], func=AF.Sigmoid))
        ctx.on("scalar", fgs)

        def fgm(v, t=t, oq=oq):
            ctx.w(v, "ts")
            dst = ctx.BA(t * C + oq * 512, 512)
            ctx.tick("tv", v.tensor_tensor(out=dst, in0=s["TA"][:, 512:1024],
                                           in1=dst, op=ALU.mult))
        ctx.on("vector", fgm)

    emit_proj(ctx, T_LOC, 0, NT, C,
              lambda cc, o0, ol: d["l0_wgT"][128 * cc:128 * (cc + 1), o0:o0 + ol],
              F32, g_evac)
    if upto == "proj":
        return

    def fsc_ld(sync):
        ctx.wall(sync)
        ctx.dtick("dl", sync.dma_start(out=s["WS2"][:, 0:128], in_=d["iota_mj"]))
        ctx.dtick("dl", sync.dma_start(out=s["WS2"][:, 128:256], in_=d["mask_mj"]))
        ctx.dtick("dl", sync.dma_start(out=s["WS2"][:, 256:257], in_=d["grev"]))
    ctx.on("sync", fsc_ld)

    for h in range(H):
        def fdt_s(sc, h=h):
            ctx.w(sc, "dl")
            ctx.w(sc, "tv")
            sc.activation(out=s["TA"][:, 0:128], in_=s["WS2"][:, 0:128],
                          func=AF.Exp, scale=s["ST"][:, 48 + h:49 + h])
            ctx.tick("ts", sc.activation(out=s["TA"][:, 128:129], in_=s["WS2"][:, 256:257],
                                         func=AF.Exp, scale=s["ST"][:, 48 + h:49 + h]))
        ctx.on("scalar", fdt_s)

        def fdt_v(v, h=h):
            ctx.w(v, "ts")
            v.tensor_tensor(out=s["QG"][:, 2048 + h * 128:2048 + (h + 1) * 128],
                            in0=s["TA"][:, 0:128], in1=s["WS2"][:, 128:256], op=ALU.mult)
            v.tensor_copy(out=s["SC"][:, h:h + 1], in_=s["TA"][:, 128:129])
            v.drain()
            ctx.tick("tv", v.engine_nop())
        ctx.on("vector", fdt_v)

    with nc.psum_tensor(f"pc1_{ctx.uid()}", [1, 128], F32) as pc:
        for h in range(H):
            def fg128(v, h=h):
                ctx.w(v, "ts")
                v.tensor_scalar_mul(out=s["ST"][0:1, 40:41],
                                    in0=s["ST"][0:1, 48 + h:49 + h], scalar1=128.0)
                v.drain()
                ctx.tick("tv", v.engine_nop())
            ctx.on("vector", fg128)

            def fg128b(sc, h=h):
                ctx.w(sc, "tv")
                ctx.tick("ts", sc.activation(out=s["ST"][0:1, 40:41],
                                             in_=s["ST"][0:1, 40:41], func=AF.Exp))
            ctx.on("scalar", fg128b)

            for b in range(NT):
                def fmm(te, h=h, b=b):
                    ctx.w(te, "tv")
                    ctx.tick("tt", te.matmul(
                        out=pc[:], lhsT=s["SC"][:, h:h + 1],
                        rhs=ctx.BB(b * C + h * 128, 128), start=True, stop=True))
                ctx.on("tensor", fmm)

                def fcar(v, h=h, b=b):
                    ctx.w(v, "tt")
                    ctx.w(v, "ts")
                    scar = s["SM"][0:1, 64:192]
                    if b == 0:
                        inst = v.tensor_copy(out=scar, in_=pc[0:1, :])
                    else:
                        inst = v.scalar_tensor_tensor(out=scar, in0=scar,
                                                      scalar=s["ST"][0:1, 40:41],
                                                      in1=pc[0:1, :],
                                                      op0=ALU.mult, op1=ALU.add)
                    ctx.tick("tv", inst)
                    v.drain()
                ctx.on("vector", fcar)

            def fA(v, h=h):
                v.tensor_copy(out=s["TA"][0:1, h * 128:(h + 1) * 128],
                              in_=s["SM"][0:1, 64:192])
                v.drain()
                ctx.tick("tv", v.engine_nop())
            ctx.on("vector", fA)

    def fcc(g):
        ctx.w(g, "tv")
        ctx.dtick("dg", g.dma_start(out=d["cin0"], in_=s["TA"][0:1, :]))
        ctx.w(g, "dg")
        ctx.tick("cc", g.collective_compute(
            "AllGather", ALU.bypass, replica_groups=RG, ins=[d["cin0"]],
            outs=[d["cout0"]]))
        ctx.w(g, "cc")
        ctx.dtick("dg", g.dma_start(out=s["SM"][0:8, 0:1], in_=d["cc_sel"]))
        ctx.dtick("dg", g.dma_start(out=s["SM"][0:8, 1:2], in_=d["cc_pow"]))
    ctx.on("gpsimd", fcc)

    for half in range(2):
        def fch_g(g, half=half):
            ctx.w(g, "tv")
            ctx.dtick("dg", g.dma_start(out=s["TA"][0:8, 0:1024],
                                        in_=d["cout0"][:, half * 1024:(half + 1) * 1024]))
            ctx.dtick("dg", g.dma_start(
                out=s["TA"][0:8, 1024:2048],
                in_=d["l0_gamma_rep"][0:1, half * 1024:(half + 1) * 1024].broadcast_to([8, 1024])))
        ctx.on("gpsimd", fch_g)

        def fch_s(sc, half=half):
            ctx.w(sc, "dg")
            sc.activation(out=s["TA"][0:8, 1024:2048], in_=s["TA"][0:8, 1024:2048],
                          func=AF.Ln)
            ctx.tick("ts", sc.activation(out=s["TA"][0:8, 1024:2048],
                                         in_=s["TA"][0:8, 1024:2048],
                                         func=AF.Exp, scale=s["SM"][0:8, 1:2]))
        ctx.on("scalar", fch_s)

        def fch_v(v, half=half):
            ctx.w(v, "ts")
            v.tensor_scalar(out=s["TA"][0:8, 1024:2048], in0=s["TA"][0:8, 1024:2048],
                            scalar1=s["SM"][0:8, 0:1], scalar2=None, op0=ALU.mult)
            v.tensor_tensor(
                out=s["BC"][0:8, half * 1024:(half + 1) * 1024],
                in0=s["TA"][0:8, 1024:2048], in1=s["TA"][0:8, 0:1024], op=ALU.mult)
            v.drain()
            ctx.tick("tv", v.engine_nop())
        ctx.on("vector", fch_v)

    def fones(v):
        v.memset(s["SC"][0:8, 256:257], 1.0)
        v.drain()
        ctx.tick("tv", v.engine_nop())
    ctx.on("vector", fones)

    with nc.psum_tensor(f"psin_{ctx.uid()}", [1, 512], F32) as psin:
        for qq in range(4):
            def fmm_s(te, qq=qq):
                ctx.w(te, "tv")
                ctx.tick("tt", te.matmul(out=psin[:], lhsT=s["SC"][0:8, 256:257],
                                         rhs=s["BC"][0:8, qq * 512:(qq + 1) * 512],
                                         start=True, stop=True))
            ctx.on("tensor", fmm_s)

            def fsin(v, qq=qq):
                ctx.w(v, "tt")
                v.tensor_copy(out=s["TA"][0:1, qq * 512:(qq + 1) * 512],
                              in_=psin[0:1, :])
                v.drain()
                ctx.tick("tv", v.engine_nop())
            ctx.on("vector", fsin)

    with nc.psum_tensor(f"ps2_{ctx.uid()}", [128, 128], F32) as ps2, \
         nc.psum_tensor(f"pc2_{ctx.uid()}", [1, 128], F32) as pc2:
        for h in range(H):
            def fini(v, h=h):
                ctx.w(v, "tv")
                v.tensor_copy(out=s["SM"][0:1, 64:192],
                              in_=s["TA"][0:1, h * 128:(h + 1) * 128])
                v.drain()
                ctx.tick("tv", v.engine_nop())
            ctx.on("vector", fini)

            for b in range(NT):
                def ffix(v, h=h, b=b):
                    ctx.w(v, "tt")
                    v.tensor_scalar(out=s["WS2"][0:1, 512:640], in0=s["SM"][0:1, 64:192],
                                    scalar1=s["SM"][0:1, 32 + h:33 + h], scalar2=None,
                                    op0=ALU.mult)
                    v.drain()
                    u0 = ctx.BB(b * C + h * 128, 128)[0:1, :]
                    v.tensor_add(out=u0, in0=u0, in1=s["WS2"][0:1, 512:640])
                    v.drain()
                    ctx.tick("tv", v.engine_nop())
                ctx.on("vector", ffix)

                def fmm2(te, h=h, b=b):
                    ctx.w(te, "tv")
                    te.matmul(out=ps2[:],
                              lhsT=s["QG"][:, 2048 + h * 128:2048 + (h + 1) * 128],
                              rhs=ctx.BB(b * C + h * 128, 128), start=True, stop=True)
                    ctx.tick("tt", te.matmul(out=pc2[:], lhsT=s["SC"][:, h:h + 1],
                                             rhs=ctx.BB(b * C + h * 128, 128),
                                             start=True, stop=True))
                ctx.on("tensor", fmm2)

                def fev2(v, h=h, b=b):
                    ctx.w(v, "tt")
                    v.tensor_copy(out=s["SM"][0:1, 64:192], in_=pc2[0:1, :])
                    v.drain()
                    dst = ctx.BB(b * C + h * 128, 128)
                    v.tensor_tensor(out=dst, in0=ps2[:],
                                    in1=ctx.BA(b * C + h * 128, 128), op=ALU.mult)
                    v.drain()
                    ctx.tick("tv", v.engine_nop())
                ctx.on("vector", fev2)

    ctx.barrier()
    if upto == "scan":
        return
    with nc.psum_tensor(f"ptr0_{ctx.uid()}", [128, 128], BF16) as q0, \
         nc.psum_tensor(f"ptr1_{ctx.uid()}", [128, 128], BF16) as q1:
        qp = [q0, q1]
        i = 0
        for cc in range(NC):
            for t in range(NT):
                def ftp(te, cc=cc, t=t, i=i):
                    ctx.w(te, "tv")
                    ctx.tick("tt", te.transpose(
                        out=qp[i % 2][:], in_=ctx.BB(t * C + cc * 128, 128),
                        identity=s["ID"][:]))

                def ftc(v, cc=cc, t=t, i=i):
                    ctx.w(v, "tt")
                    ctx.tick("tv", v.tensor_copy(
                        out=ctx.BA(cc * T_LOC + t * 128, 128), in_=qp[i % 2][:]))
                ctx.on("tensor", ftp)
                ctx.on("vector", ftc)
                i += 1

    def wo_evac(t, oq, pb):
        def fv(v, t=t, oq=oq, pb=pb):
            ctx.w(v, "tt")
            ctx.tick("tv", v.tensor_copy(out=ctx.BB(t * C + oq * 512, 512), in_=pb[:]))
        ctx.on("vector", fv)

    emit_proj(ctx, T_LOC, 0, NT, C,
              lambda cc, o0, ol: d["l0_woT"][128 * cc:128 * (cc + 1), o0:o0 + ol],
              F32, wo_evac,
              lhsT_fn=lambda cc, a, n: ctx.BA(cc * T_LOC + a, n))

    ctx.barrier()
    emit_bc_table(ctx, d["l0_lnw"])
    for t in range(NT):
        def fln(v, t=t):
            ctx.w(v, "dl")
            ctx.w(v, "dg")
            ctx.w(v, "tv")
            y = ctx.BB(t * C, C)
            v.reduce_sum(out=s["ST"][:, 24:25], in_=y, axis=AX)
            v.drain()
            v.tensor_scalar_mul(out=s["ST"][:, 24:25], in0=s["ST"][:, 24:25],
                                scalar1=1.0 / C)
            v.drain()
            ctx.tick("tv", v.engine_nop())
        ctx.on("vector", fln)

        def fln_s(sc, t=t):
            ctx.w(sc, "tv")
            y = ctx.BB(t * C, C)
            for k in range(4):
                sc.activation(out=s["TA"][:, 0:512], in_=y[:, 512 * k:512 * (k + 1)],
                              func=AF.Square, bias=s["ST"][:, 24:25], scale=-1.0,
                              accum_out=s["ST"][:, 40 + k:41 + k])
            ctx.tick("ts", sc.copy(out=s["SM"][:, 191:192], in_=s["SM"][:, 191:192]))
        ctx.on("scalar", fln_s)

        def fln_v(v, t=t):
            ctx.w(v, "ts")
            v.tensor_add(out=s["ST"][:, 40:41], in0=s["ST"][:, 40:41],
                         in1=s["ST"][:, 41:42])
            v.tensor_add(out=s["ST"][:, 42:43], in0=s["ST"][:, 42:43],
                         in1=s["ST"][:, 43:44])
            v.drain()
            v.tensor_add(out=s["ST"][:, 40:41], in0=s["ST"][:, 40:41],
                         in1=s["ST"][:, 42:43])
            v.drain()
            ctx.tick("tv", v.tensor_scalar(out=s["ST"][:, 26:27], in0=s["ST"][:, 40:41],
                                           scalar1=1.0 / C, scalar2=LN_EPS,
                                           op0=ALU.mult, op1=ALU.add))
        ctx.on("vector", fln_v)

        def fln_s2(sc, t=t):
            ctx.w(sc, "tv")
            sc.activation(out=s["ST"][:, 26:27], in_=s["ST"][:, 26:27], func=AF.Ln)
            ctx.tick("ts", sc.activation(out=s["ST"][:, 26:27], in_=s["ST"][:, 26:27],
                                         func=AF.Exp, scale=-0.5))
        ctx.on("scalar", fln_s2)

        def fln2(v, t=t):
            ctx.w(v, "ts")
            y = ctx.BB(t * C, C)
            v.tensor_scalar(out=s["TA"][:], in0=y, scalar1=s["ST"][:, 24:25],
                            scalar2=s["ST"][:, 26:27], op0=ALU.subtract, op1=ALU.mult)
            v.tensor_tensor(out=s["TA"][:], in0=s["TA"][:], in1=s["BC"][:], op=ALU.mult)
            xr = ctx.xr(1 + t)
            ctx.tick("tv", v.tensor_add(out=xr, in0=xr, in1=s["TA"][:]))
        ctx.on("vector", fln2)

    ctx.barrier()
    emit_bc_table(ctx, d["l0_lnb"])
    import os as _os
    if not _os.environ.get("SKIP_LNB"):
        for t in range(NT):
            def flb(v, t=t):
                ctx.w(v, "dl")
                ctx.w(v, "dg")
                xr = ctx.xr(1 + t)
                ctx.tick("tv", v.tensor_add(out=xr, in0=xr, in1=s["BC"][:]))
            ctx.on("vector", flb)
    ctx.barrier()



def emit_l1_attn(ctx, upto="full", tb=0):
    nc, s, d = ctx.nc, ctx.s, ctx.d
    Tt = T_LOC
    ctx.barrier()

    def fld(sync):
        ctx.dtick("dl", sync.dma_start(out=s["ST"][:, 48:64],
                                       in_=d["l1_slope"].broadcast_to([128, H])))
        ctx.dtick("dl", sync.dma_start(out=s["CW"][:, 0:2], in_=d["l1_km"]))
        ctx.dtick("dl", sync.dma_start(out=s["ST"][:, 32:40],
                                       in_=d["cc_pownT"].broadcast_to([128, 8])))
        ctx.dtick("dl", sync.dma_start(out=s["ST"][:, 40:48],
                                       in_=d["cc_selT"].broadcast_to([128, 8])))
    ctx.on("sync", fld)
    PS = lambda h: s["ST"][:, 48 + h:49 + h]

    def silu_evac(dstf):
        def evac(t, oq, pb):
            def fs(sc, t=t, oq=oq, pb=pb):
                ctx.w(sc, "tt")
                ctx.tick("ts", sc.activation(
                    out=dstf(t)[:, oq * 512:(oq + 1) * 512], in_=pb[:], func=AF.Silu))
            ctx.on("scalar", fs)
        return evac

    emit_proj(ctx, Tt, 0, NT, C,
              lambda cc, o0, ol: d["l1_wqkvT"][128 * cc:128 * (cc + 1),
                                               2048 + o0:2048 + o0 + ol],
              F32, silu_evac(lambda t: ctx.BA(t * C, C)))
    emit_proj(ctx, Tt, 0, NT, C,
              lambda cc, o0, ol: d["l1_wqkvT"][128 * cc:128 * (cc + 1),
                                               4096 + o0:4096 + o0 + ol],
              F32, silu_evac(lambda t: ctx.BB(t * C, C)))

    if upto == "proj":
        return
    with nc.psum_tensor(f"pkv1_{ctx.uid()}", [128, 128], F32) as pkv:
        for h in range(H):
            def fkd(sc, h=h):
                if h == 0:
                    ctx.w(sc, "ts")
                    ctx.w(sc, "dl")
                ctx.w(sc, "tv")
                ctx.tick("ts", sc.activation(out=s["SM"][:, 2:4], in_=s["CW"][:, 0:2],
                                             func=AF.Exp, scale=PS(h)))
            ctx.on("scalar", fkd)

            def fbd(v, h=h):
                ctx.w(v, "ts")
                v.tensor_scalar_mul(out=s["SM"][:, 4:5], in0=PS(h), scalar1=-256.0)
                v.drain()
                ctx.tick("tv", v.engine_nop())
            ctx.on("vector", fbd)

            def fbd2(sc, h=h):
                ctx.w(sc, "tv")
                ctx.tick("ts", sc.activation(out=s["SM"][:, 4:5], in_=s["SM"][:, 4:5],
                                             func=AF.Exp))
            ctx.on("scalar", fbd2)

            for b in range(4):
                for jh in range(2):
                    def fks(v, h=h, b=b, jh=jh):
                        ctx.w(v, "ts")
                        ctx.w(v, "tt")
                        ksl = ctx.BA((2 * b + jh) * C + h * 128, 128)
                        v.tensor_scalar(
                            out=s["BC"][:, 128:256], in0=ksl,
                            scalar1=s["SM"][:, 2 + jh:3 + jh], scalar2=None,
                            op0=ALU.mult)
                        v.drain()
                        ctx.tick("tv", v.engine_nop())
                    ctx.on("vector", fks)

                    def fkm(te, h=h, b=b, jh=jh):
                        ctx.w(te, "tv")
                        ctx.tick("tt", te.matmul(
                            out=pkv[:], lhsT=s["BC"][:, 128:256],
                            rhs=ctx.BB((2 * b + jh) * C + h * 128, 128),
                            start=(jh == 0), stop=(jh == 1)))
                    ctx.on("tensor", fkm)

                def fkc(v, h=h, b=b):
                    ctx.w(v, "tt")
                    kvs = s["SM"][:, 64:192]
                    if b == 0:
                        inst = v.tensor_copy(out=kvs, in_=pkv[:])
                    else:
                        inst = v.scalar_tensor_tensor(out=kvs, in0=kvs,
                                                      scalar=s["SM"][:, 4:5], in1=pkv[:],
                                                      op0=ALU.mult, op1=ALU.add)
                    ctx.tick("tv", inst)
                    v.drain()
                ctx.on("vector", fkc)

            def fst(g, h=h):
                ctx.w(g, "tv")
                ctx.dtick("dg", g.dma_start(out=d["cin1"][h * 128:(h + 1) * 128, :],
                                            in_=s["SM"][:, 64:192]))
            ctx.on("gpsimd", fst)

    def fcc(g):
        ctx.w(g, "dg")
        ctx.tick("cc", g.collective_compute(
            "AllGather", ALU.bypass, replica_groups=RG, ins=[d["cin1"]],
            outs=[d["cout1"]]))
    ctx.on("gpsimd", fcc)

    for grp in range(4):
        emit_proj(ctx, Tt, 0, NT, 512,
                  lambda cc, o0, ol, grp=grp: d["l1_wqkvT"][128 * cc:128 * (cc + 1),
                                                            grp * 512 + o0:grp * 512 + o0 + ol],
                  F32, silu_evac(lambda t: s["QG"][:, t * 512:(t + 1) * 512]))

        with nc.psum_tensor(f"pq0_{ctx.uid()}", [128, 256], F32) as pq0, \
             nc.psum_tensor(f"pq1_{ctx.uid()}", [128, 256], F32) as pq1, \
             nc.psum_tensor(f"po0_{ctx.uid()}", [128, 128], F32) as po0, \
             nc.psum_tensor(f"po1_{ctx.uid()}", [128, 128], F32) as po1, \
             nc.psum_tensor(f"pkv2_{ctx.uid()}", [128, 128], F32) as pkv2, \
             nc.psum_tensor(f"ptq0_{ctx.uid()}", [128, 128], BF16) as ptq0, \
             nc.psum_tensor(f"ptq1_{ctx.uid()}", [128, 128], BF16) as ptq1:
            for hh in range(4):
                h = grp * 4 + hh

                def fcmb(g, h=h):
                    if h == 0:
                        ctx.w(g, "cc")
                    ctx.w(g, "tv")
                    ctx.w(g, "ts")
                    for j in range(8):
                        ctx.dtick("dg", g.dma_start(
                            out=s["TA"][:, j * 128:(j + 1) * 128],
                            in_=d["cout1"][j * 2048 + h * 128:j * 2048 + (h + 1) * 128, :]))
                ctx.on("gpsimd", fcmb)

                def fw8(sc, h=h):
                    ctx.w(sc, "dg")
                    ctx.tick("ts", sc.activation(out=s["SM"][:, 8:16],
                                                 in_=s["ST"][:, 32:40],
                                                 func=AF.Exp, scale=PS(h)))
                ctx.on("scalar", fw8)

                def fw8b(v, h=h):
                    ctx.w(v, "ts")
                    v.tensor_tensor(out=s["SM"][:, 8:16], in0=s["SM"][:, 8:16],
                                    in1=s["ST"][:, 40:48], op=ALU.mult)
                    kvs = s["SM"][:, 64:192]
                    v.drain()
                    v.tensor_scalar(out=kvs, in0=s["TA"][:, 0:128],
                                    scalar1=s["SM"][:, 8:9], scalar2=None, op0=ALU.mult)
                    for j in range(1, 8):
                        v.drain()
                        v.scalar_tensor_tensor(out=kvs,
                                               in0=s["TA"][:, j * 128:(j + 1) * 128],
                                               scalar=s["SM"][:, 8 + j:9 + j], in1=kvs,
                                               op0=ALU.mult, op1=ALU.add)
                    ctx.tick("tv", v.engine_nop())
                ctx.on("vector", fw8b)

                def ftl(sync, h=h):
                    ctx.w(sync, "tv")
                    ctx.dtick("dl", sync.dma_start(out=s["TA"][:, 0:256], in_=d["l1_i0"]))
                    ctx.dtick("dl", sync.dma_start(out=s["TA"][:, 256:512], in_=d["l1_i1"]))
                    ctx.dtick("dl", sync.dma_start(out=s["TA"][:, 512:768], in_=d["l1_msk0"]))
                    ctx.dtick("dl", sync.dma_start(out=s["TA"][:, 768:1024], in_=d["l1_msk1"]))
                    ctx.dtick("dl", sync.dma_start(out=s["TA"][:, 1024:2048],
                                                   in_=d["l1_qm"].broadcast_to([128, T_LOC])))
                ctx.on("sync", ftl)

                def fdec(sc, h=h):
                    ctx.w(sc, "dl")
                    sc.activation(out=s["TA"][:, 0:256], in_=s["TA"][:, 0:256],
                                  func=AF.Exp, scale=PS(h))
                    sc.activation(out=s["TA"][:, 256:512], in_=s["TA"][:, 256:512],
                                  func=AF.Exp, scale=PS(h))
                    ctx.tick("ts", sc.activation(out=s["TA"][:, 1024:2048],
                                                 in_=s["TA"][:, 1024:2048],
                                                 func=AF.Exp, scale=PS(h)))
                ctx.on("scalar", fdec)

                def fdecm(v, h=h):
                    ctx.w(v, "ts")
                    v.tensor_tensor(out=s["SC"][:, 0:256], in0=s["TA"][:, 0:256],
                                    in1=s["TA"][:, 512:768], op=ALU.mult)
                    ctx.tick("tv", v.tensor_tensor(out=s["SC"][:, 256:512],
                                                   in0=s["TA"][:, 256:512],
                                                   in1=s["TA"][:, 768:1024], op=ALU.mult))
                ctx.on("vector", fdecm)

                for t in range(NT):
                    def ftr(te, h=h, hh=hh, t=t):
                        ctx.w(te, "tv")
                        te.transpose(out=ptq0[:], in_=ctx.BA(t * C + h * 128, 128),
                                     identity=s["ID"][:])
                        ctx.tick("tt", te.transpose(
                            out=ptq1[:],
                            in_=s["QG"][:, t * 512 + hh * 128:t * 512 + (hh + 1) * 128],
                            identity=s["ID"][:]))
                    ctx.on("tensor", ftr)

                    def ftrc(v, t=t):
                        ctx.w(v, "tt")
                        v.tensor_copy(out=s["WSa"][:, 1024 + t * 128:1024 + (t + 1) * 128],
                                      in_=ptq0[:])
                        ctx.tick("tv", v.tensor_copy(
                            out=s["WSa"][:, t * 128:(t + 1) * 128], in_=ptq1[:]))
                    ctx.on("vector", ftrc)

                def fqd(v, h=h):
                    ctx.w(v, "tv")
                    ctx.tick("tv", v.tensor_tensor(out=s["WSb"][:, 0:1024],
                                                   in0=s["WSa"][:, 0:1024],
                                                   in1=s["TA"][:, 1024:2048],
                                                   op=ALU.mult))
                ctx.on("vector", fqd)

                def fkd2(sc, h=h):
                    ctx.w(sc, "tv")
                    ctx.tick("ts", sc.activation(out=s["SM"][:, 2:4], in_=s["CW"][:, 0:2],
                                                 func=AF.Exp, scale=PS(h)))
                ctx.on("scalar", fkd2)

                def fbd_2(v, h=h):
                    ctx.w(v, "ts")
                    v.tensor_scalar_mul(out=s["SM"][:, 4:5], in0=PS(h), scalar1=-256.0)
                    v.drain()
                    ctx.tick("tv", v.engine_nop())
                ctx.on("vector", fbd_2)

                def fbd2_2(sc, h=h):
                    ctx.w(sc, "tv")
                    ctx.tick("ts", sc.activation(out=s["SM"][:, 4:5], in_=s["SM"][:, 4:5],
                                                 func=AF.Exp))
                ctx.on("scalar", fbd2_2)

                for b in range(4):
                    def fkb(v, h=h, b=b):
                        ctx.w(v, "ts")
                        ctx.w(v, "tt")
                        v.tensor_copy(out=s["BC"][:, 0:128], in_=s["SM"][:, 64:192])
                        v.tensor_scalar(out=s["BC"][:, 128:256],
                                        in0=ctx.BA((2 * b) * C + h * 128, 128),
                                        scalar1=s["SM"][:, 2:3], scalar2=None,
                                        op0=ALU.mult)
                        v.tensor_scalar(
                            out=s["BC"][:, 256:384],
                            in0=ctx.BA((2 * b + 1) * C + h * 128, 128),
                            scalar1=s["SM"][:, 3:4], scalar2=None, op0=ALU.mult)
                        v.drain()
                        ctx.tick("tv", v.engine_nop())
                    ctx.on("vector", fkb)

                    def fqk(te, h=h, b=b):
                        ctx.w(te, "tv")
                        te.matmul(out=pq0[:],
                                  lhsT=s["WSa"][:, 1024 + (2 * b) * 128:1024 + (2 * b + 1) * 128],
                                  rhs=s["WSa"][:, b * 256:(b + 1) * 256],
                                  start=True, stop=True)
                        ctx.tick("tt", te.matmul(
                            out=pq1[:],
                            lhsT=s["WSa"][:, 1024 + (2 * b + 1) * 128:1024 + (2 * b + 2) * 128],
                            rhs=s["WSa"][:, b * 256:(b + 1) * 256],
                            start=True, stop=True))
                    ctx.on("tensor", fqk)

                    def fqs(v, b=b):
                        ctx.w(v, "tt")
                        v.tensor_tensor(out=s["WSb"][:, 1024:1280], in0=pq0[:],
                                        in1=s["SC"][:, 0:256], op=ALU.mult)
                        v.tensor_tensor(out=s["WSb"][:, 1280:1536], in0=pq1[:],
                                        in1=s["SC"][:, 256:512], op=ALU.mult)
                        v.drain()
                        ctx.tick("tv", v.engine_nop())
                    ctx.on("vector", fqs)

                    def fo(te, h=h, b=b):
                        ctx.w(te, "tv")
                        for mh, po in ((0, po0), (1, po1)):
                            te.matmul(out=po[:],
                                      lhsT=s["WSb"][:, 1024 + mh * 128:1152 + mh * 128],
                                      rhs=ctx.BB((2 * b) * C + h * 128, 128),
                                      start=True, stop=False)
                            te.matmul(out=po[:],
                                      lhsT=s["WSb"][:, 1280 + mh * 128:1408 + mh * 128],
                                      rhs=ctx.BB((2 * b + 1) * C + h * 128, 128),
                                      start=False, stop=False)
                            te.matmul(out=po[:],
                                      lhsT=s["WSb"][:, b * 256 + mh * 128:b * 256 + (mh + 1) * 128],
                                      rhs=s["BC"][:, 0:128],
                                      start=False, stop=True)
                        te.matmul(out=pkv2[:], lhsT=s["BC"][:, 128:256],
                                  rhs=ctx.BB((2 * b) * C + h * 128, 128),
                                  start=True, stop=False)
                        ctx.tick("tt", te.matmul(
                            out=pkv2[:], lhsT=s["BC"][:, 256:384],
                            rhs=ctx.BB((2 * b + 1) * C + h * 128, 128),
                            start=False, stop=True))
                    ctx.on("tensor", fo)

                    def fev(v, h=h, b=b):
                        ctx.w(v, "tt")
                        v.tensor_copy(out=ctx.BA((2 * b) * C + h * 128, 128), in_=po0[:])
                        v.tensor_copy(out=ctx.BA((2 * b + 1) * C + h * 128, 128),
                                      in_=po1[:])
                        kvs = s["SM"][:, 64:192]
                        ctx.tick("tv", v.scalar_tensor_tensor(
                            out=kvs, in0=kvs, scalar=s["SM"][:, 4:5], in1=pkv2[:],
                            op0=ALU.mult, op1=ALU.add))
                        v.drain()
                    ctx.on("vector", fev)

    ctx.barrier()
    if upto == "attn":
        return
    emit_bc_table(ctx, d["l1_normw"])

    for t in range(NT):
        def frs_s(sc, t=t):
            if t == 0:
                ctx.w(sc, "dl")
                ctx.w(sc, "dg")
                ctx.w(sc, "tv")
            ao = ctx.BA(t * C, C)
            for k in range(4):
                sc.activation(out=s["TA"][:, 0:512], in_=ao[:, 512 * k:512 * (k + 1)],
                              func=AF.Square, accum_out=s["ST"][:, 40 + k:41 + k])
            ctx.tick("ts", sc.copy(out=s["SM"][:, 191:192], in_=s["SM"][:, 191:192]))
        ctx.on("scalar", frs_s)

        def frs_v(v, t=t):
            ctx.w(v, "ts")
            rcol = s["SM"][:, 24 + t:25 + t]
            v.tensor_add(out=s["ST"][:, 40:41], in0=s["ST"][:, 40:41],
                         in1=s["ST"][:, 41:42])
            v.tensor_add(out=s["ST"][:, 42:43], in0=s["ST"][:, 42:43],
                         in1=s["ST"][:, 43:44])
            v.drain()
            v.tensor_add(out=s["ST"][:, 40:41], in0=s["ST"][:, 40:41],
                         in1=s["ST"][:, 42:43])
            v.drain()
            v.tensor_scalar(out=rcol, in0=s["ST"][:, 40:41],
                            scalar1=1.0 / C, scalar2=L1_RMS_EPS,
                            op0=ALU.mult, op1=ALU.add)
            v.drain()
            ctx.tick("tv", v.engine_nop())
        ctx.on("vector", frs_v)

        def frs_s2(sc, t=t):
            ctx.w(sc, "tv")
            rcol = s["SM"][:, 24 + t:25 + t]
            sc.activation(out=rcol, in_=rcol, func=AF.Ln)
            ctx.tick("ts", sc.activation(out=rcol, in_=rcol, func=AF.Exp, scale=-0.5))
        ctx.on("scalar", frs_s2)

    def gt_evac(t, oq, pb):
        def fg1(sc, t=t, oq=oq, pb=pb):
            ctx.w(sc, "tt")
            ctx.tick("ts", sc.activation(out=s["TA"][:, 0:512], in_=pb[:],
                                         func=AF.Sigmoid))
        ctx.on("scalar", fg1)

        def fg2(v, t=t, oq=oq):
            ctx.w(v, "ts")
            ctx.w(v, "dg")
            v.tensor_tensor(out=s["TA"][:, 0:512], in0=s["TA"][:, 0:512],
                            in1=ctx.BA(t * C + oq * 512, 512), op=ALU.mult)
            v.tensor_scalar(out=s["TA"][:, 0:512], in0=s["TA"][:, 0:512],
                            scalar1=s["SM"][:, 24 + t:25 + t], scalar2=None,
                            op0=ALU.mult)
            ctx.tick("tv", v.tensor_tensor(
                out=ctx.BB(t * C + oq * 512, 512), in0=s["TA"][:, 0:512],
                in1=s["BC"][:, oq * 512:(oq + 1) * 512], op=ALU.mult))
        ctx.on("vector", fg2)

    emit_proj(ctx, Tt, 0, NT, C,
              lambda cc, o0, ol: d["l1_wgT"][128 * cc:128 * (cc + 1), o0:o0 + ol],
              F32, gt_evac)

    with nc.psum_tensor(f"ptg0_{ctx.uid()}", [128, 128], BF16) as g0, \
         nc.psum_tensor(f"ptg1_{ctx.uid()}", [128, 128], BF16) as g1:
        gp = [g0, g1]
        i = 0
        for cc in range(NC):
            for t in range(NT):
                def ftp(te, cc=cc, t=t, i=i):
                    ctx.w(te, "tv")
                    ctx.tick("tt", te.transpose(
                        out=gp[i % 2][:], in_=ctx.BB(t * C + cc * 128, 128),
                        identity=s["ID"][:]))

                def ftc(v, cc=cc, t=t, i=i):
                    ctx.w(v, "tt")
                    ctx.tick("tv", v.tensor_copy(
                        out=ctx.BA(cc * T_LOC + t * 128, 128), in_=gp[i % 2][:]))
                ctx.on("tensor", ftp)
                ctx.on("vector", ftc)
                i += 1

    def out_evac(t, oq, pb):
        def fv(v, t=t, oq=oq, pb=pb):
            ctx.w(v, "tt")
            xr = ctx.xr(tb + t)
            ctx.tick("tv", v.tensor_add(out=xr[:, oq * 512:(oq + 1) * 512],
                                        in0=xr[:, oq * 512:(oq + 1) * 512], in1=pb[:]))
        ctx.on("vector", fv)

    emit_proj(ctx, T_LOC, 0, NT, C,
              lambda cc, o0, ol: d["l1_woT"][128 * cc:128 * (cc + 1), o0:o0 + ol],
              F32, out_evac,
              lhsT_fn=lambda cc, a, n: ctx.BA(cc * T_LOC + a, n))
    ctx.barrier()



def declare_dram(ctx, nc, fixed, pc, part):
    d = ctx.d
    need = {
        1: ["l0_pgT", "l0_puT", "l0_sg", "l0_su", "l0_pdT", "l0_sd", "shfcol",
            "identity"],
        2: ["l0_wqT", "l0_wkT", "l0_wvT", "l0_woT", "l0_wgT", "l0_convw", "l0_convb",
            "l0_gateb", "l0_lnw", "l0_lnb", "l0_gamma", "l0_gamma_rep", "identity",
            "iota_mj", "mask_mj", "grev", "shfcol", "cc_sel", "cc_pow"],
        4: ["l1_qgT", "l1_quT", "l1_qdT", "l1_sgc", "l1_suc", "l1_sd", "shfcol",
            "identity"],
        5: ["l1_slope", "l1_wqkvT", "l1_wgT", "l1_woT", "l1_normw", "l1_i0", "l1_i1",
            "l1_msk0", "l1_msk1", "l1_qm", "l1_km", "shfcol", "identity",
            "cc_selT", "cc_pownT"],
    }
    names = need[2 if part == 2 else (5 if part == 5 else (1 if part in (1, 3) else 4))]
    srcs = dict(fixed)
    srcs.update(pc)
    dtmap = {np.dtype(np.float32): F32, np.dtype(np.int32): I32,
             np.dtype(np.float16): F16}
    Tin = T_EXT if part <= 2 else T_LOC
    Tout = T_EXT if part == 1 else T_LOC
    d["xin"] = nc.dram_tensor("xin", [Tin, C], F32, kind="ExternalInput").ap()
    d["xout"] = nc.dram_tensor("xout", [Tout, C], F32, kind="ExternalOutput").ap()
    feed = {"xin": srcs["xin"][:Tin]}
    for n in names:
        a = srcs[n]
        d[n] = nc.dram_tensor(n, list(a.shape), dtmap[a.dtype], kind="ExternalInput").ap()
        feed[n] = a
    if part in (1, 3):
        d["wg_b"] = nc.dram_tensor("wg_b", [NC, 128, IF], BF16, kind="Internal").ap()
        d["wu_b"] = nc.dram_tensor("wu_b", [NC, 128, IF], BF16, kind="Internal").ap()
        d["wd_b"] = nc.dram_tensor("wd_b", [NI, 128, C], BF16, kind="Internal").ap()
    if part == 2:
        d["xc_d"] = nc.dram_tensor("xc_d", [NC, 128, T_LOC], BF16, kind="Internal").ap()
        d["cin0"] = nc.dram_tensor("cin0", [1, C], F32, kind="Internal").ap()
        d["cout0"] = nc.dram_tensor("cout0", [8, C], F32, kind="Internal").ap()
    if part == 5:
        d["cin1"] = nc.dram_tensor("cin1", [C, 128], F32, kind="Internal").ap()
        d["cout1"] = nc.dram_tensor("cout1", [8 * C, 128], F32, kind="Internal").ap()
    return feed


def build_part(part, fixed, per_core):
    nc = bass.Bass("TRN2", target_bir_lowering=False, debug=False)
    es = ExitStack()
    with es:
        ctx = Ctx(nc, es)
        declare_dram(ctx, nc, fixed, per_core[0], part)
        alloc_common(ctx, nc, es)
        block = es.enter_context(nc.Block())
        ctx.block = block
        emit_preamble(ctx)
        if part == 1:
            emit_load_x(ctx, NTE)
            emit_l0_dequant(ctx)
            emit_ln(ctx, 0, NTE, False)
            emit_ffn(ctx, "l0", 0, NTE, NTE * 128)
            emit_store_x(ctx, 0, NTE)
        elif part == 2:
            emit_load_x(ctx, NTE)
            emit_ln(ctx, 0, NTE, False)
            emit_l0_attn(ctx)
            emit_store_x(ctx, 1, NT)
        elif part == 3:
            emit_load_x(ctx, NT)
            emit_l0_dequant(ctx)
            emit_ln(ctx, 0, NT, False)
            emit_ffn(ctx, "l0", 0, NT, T_LOC)
            emit_store_x(ctx, 0, NT)
        elif part in (4, 6):
            emit_load_x(ctx, NT)
            emit_ln(ctx, 0, NT, False)
            emit_ffn(ctx, "l1", 0, NT, T_LOC)
            emit_store_x(ctx, 0, NT)
        elif part == 5:
            emit_load_x(ctx, NT)
            emit_ln(ctx, 0, NT, False)
            emit_l1_attn(ctx)
            emit_store_x(ctx, 0, NT)
    return nc


def feeds_for(part, fixed, per_core):
    need = {
        1: ["l0_pgT", "l0_puT", "l0_sg", "l0_su", "l0_pdT", "l0_sd", "shfcol",
            "identity"],
        2: ["l0_wqT", "l0_wkT", "l0_wvT", "l0_woT", "l0_wgT", "l0_convw", "l0_convb",
            "l0_gateb", "l0_lnw", "l0_lnb", "l0_gamma", "l0_gamma_rep", "identity",
            "iota_mj", "mask_mj", "grev", "shfcol", "cc_sel", "cc_pow"],
        4: ["l1_qgT", "l1_quT", "l1_qdT", "l1_sgc", "l1_suc", "l1_sd", "shfcol",
            "identity"],
        5: ["l1_slope", "l1_wqkvT", "l1_wgT", "l1_woT", "l1_normw", "l1_i0", "l1_i1",
            "l1_msk0", "l1_msk1", "l1_qm", "l1_km", "shfcol", "identity",
            "cc_selT", "cc_pownT"],
    }
    names = need[2 if part == 2 else (5 if part == 5 else (1 if part in (1, 3) else 4))]
    Tin = T_EXT if part <= 2 else T_LOC
    feeds = []
    for pc in per_core:
        f = {"xin": np.ascontiguousarray(pc["xin"][:Tin])}
        for n in names:
            f[n] = pc[n] if n in pc else fixed[n]
        feeds.append(f)
    return feeds


PARTS = [1, 2, 3, 4, 5, 6]


ALL_NAMES = sorted(set(
    ["l0_pgT", "l0_puT", "l0_sg", "l0_su", "l0_pdT", "l0_sd", "shfcol", "identity",
     "l0_wqT", "l0_wkT", "l0_wvT", "l0_woT", "l0_wgT", "l0_convw", "l0_convb",
     "l0_gateb", "l0_lnw", "l0_lnb", "l0_gamma", "l0_gamma_rep",
     "iota_mj", "mask_mj", "grev", "cc_sel", "cc_pow",
     "l1_qgT", "l1_quT", "l1_qdT", "l1_sgc", "l1_suc", "l1_sd",
     "l1_slope", "l1_wqkvT", "l1_wgT", "l1_woT", "l1_normw", "l1_i0", "l1_i1",
     "l1_msk0", "l1_msk1", "l1_qm", "l1_km", "cc_selT", "cc_pownT"]))


def declare_dram_fused(ctx, nc, fixed, pc):
    d = ctx.d
    srcs = dict(fixed)
    srcs.update(pc)
    dtmap = {np.dtype(np.float32): F32, np.dtype(np.int32): I32,
             np.dtype(np.float16): F16}
    d["xin"] = nc.dram_tensor("xin", [T_EXT, C], F32, kind="ExternalInput").ap()
    d["xout"] = nc.dram_tensor("xout", [T_LOC, C], F32, kind="ExternalOutput").ap()
    for n in ALL_NAMES:
        a = srcs[n]
        d[n] = nc.dram_tensor(n, list(a.shape), dtmap[a.dtype], kind="ExternalInput").ap()
    d["wg_b"] = nc.dram_tensor("wg_b", [NC, 128, IF], BF16, kind="Internal").ap()
    d["wu_b"] = nc.dram_tensor("wu_b", [NC, 128, IF], BF16, kind="Internal").ap()
    d["wd_b"] = nc.dram_tensor("wd_b", [NI, 128, C], BF16, kind="Internal").ap()
    d["xc_d"] = nc.dram_tensor("xc_d", [NC, 128, T_LOC], BF16, kind="Internal").ap()
    d["cin0"] = nc.dram_tensor("cin0", [1, C], F32, kind="Internal").ap()
    d["cout0"] = nc.dram_tensor("cout0", [8, C], F32, kind="Internal").ap()
    d["cin1"] = nc.dram_tensor("cin1", [C, 128], F32, kind="Internal").ap()
    d["cout1"] = nc.dram_tensor("cout1", [8 * C, 128], F32, kind="Internal").ap()


def build_fused(fixed, per_core):
    nc = bass.Bass("TRN2", target_bir_lowering=False, debug=False)
    es = ExitStack()
    with es:
        ctx = Ctx(nc, es)
        declare_dram_fused(ctx, nc, fixed, per_core[0])
        alloc_common(ctx, nc, es)
        block = es.enter_context(nc.Block())
        ctx.block = block
        emit_preamble(ctx)
        emit_load_x(ctx, NTE)
        emit_l0_dequant(ctx)
        emit_ln(ctx, 0, NTE, False)
        emit_ffn(ctx, "l0", 0, NTE, NTE * 128)
        emit_ln(ctx, 0, NTE, False)
        emit_l0_attn(ctx)
        emit_ln(ctx, 1, NT, False)
        emit_ffn(ctx, "l0", 1, NT, T_LOC)
        emit_ln(ctx, 1, NT, False)
        emit_ffn(ctx, "l1", 1, NT, T_LOC)
        emit_ln(ctx, 1, NT, False)
        emit_l1_attn(ctx, tb=1)
        emit_ln(ctx, 1, NT, False)
        emit_ffn(ctx, "l1", 1, NT, T_LOC)
        emit_store_x(ctx, 1, NT)
    return nc


def feeds_fused(fixed, per_core):
    feeds = []
    for pc in per_core:
        f = {"xin": np.ascontiguousarray(pc["xin"])}
        for n in ALL_NAMES:
            f[n] = pc[n] if n in pc else fixed[n]
        feeds.append(f)
    return feeds


FUSED = True


def kernel(**inputs):
    fixed, per_core = prep(inputs)
    if FUSED:
        nc = build_fused(fixed, per_core)
        feeds = feeds_fused(fixed, per_core)
        res = run_bass_kernel_spmd(nc, feeds, core_ids=list(range(NCORES)))
        xs = [np.asarray(r["xout"]) for r in res.results]
        return np.concatenate(xs, 0).reshape(1, 8192, C).astype(np.float32)
    xs = [pc["xin"] for pc in per_core]
    for part in PARTS:
        for i in range(NCORES):
            per_core[i]["xin"] = xs[i]
        nc = build_part(part, fixed, per_core)
        feeds = feeds_for(part, fixed, per_core)
        res = run_bass_kernel_spmd(nc, feeds, core_ids=list(range(NCORES)))
        xs = [np.asarray(r["xout"]) for r in res.results]
    return np.concatenate(xs, 0).reshape(1, 8192, C).astype(np.float32)


# version 03
# speedup vs baseline: 1.4687x; 1.2164x over previous
import numpy as np
from contextlib import ExitStack

import concourse.bass as bass
import concourse.mybir as mybir
from concourse.bass_utils import run_bass_kernel_spmd

F32 = mybir.dt.float32
BF16 = mybir.dt.bfloat16
F16 = mybir.dt.float16
I32 = mybir.dt.int32
AF = mybir.ActivationFunctionType
ALU = mybir.AluOpType
AX = mybir.AxisListType.X

C = 2048
NC = 16
IF = 5632
NI = 44
H = 16
T_LOC = 1024
HALO = 128
T_EXT = T_LOC + HALO
NT = 8
NTE = 9
NCORES = 8
LN_EPS = 1e-5
L0_NORM_EPS = 1e-12
L0_FFN_EPS = 1e-6
L1_RMS_EPS = 1e-6
BLK1 = 256
RG = [[0, 1, 2, 3, 4, 5, 6, 7]]


def perm128(n):
    base = np.arange(n) // 128 * 128
    r = np.arange(n) % 128
    return base + 8 * (r % 16) + (r // 16)



def prep(inputs):
    f = {}
    x = np.asarray(inputs["L0_attn_x"], np.float32).reshape(8192, C)
    xp = np.concatenate([np.zeros((HALO, C), np.float32), x], 0)
    per_core = []
    for i in range(NCORES):
        st = i * T_LOC
        pc = {"xin": np.ascontiguousarray(xp[st:st + T_EXT])}
        sel = np.zeros((8, 1), np.float32)
        pw = np.zeros((8, 1), np.float32)
        for j in range(i):
            sel[j] = 1.0
            pw[j] = 1024.0 * float(i - 1 - j)
        pc["cc_sel"] = sel
        pc["cc_pow"] = pw
        pc["cc_selT"] = np.ascontiguousarray(sel.T)
        pc["cc_pownT"] = np.ascontiguousarray((-pw).T)
        per_core.append(pc)

    T = lambda a: np.ascontiguousarray(np.asarray(a).T)
    f["l0_wqT"] = T(inputs["L0_attn_Wq"])
    f["l0_wkT"] = T(inputs["L0_attn_Wk"])
    f["l0_wvT"] = T(inputs["L0_attn_Wv"])
    f["l0_woT"] = T(inputs["L0_attn_Wo"])
    f["l0_wgT"] = T(inputs["L0_attn_gate_w"])
    cw = np.asarray(inputs["L0_attn_conv_w"], np.float32).reshape(C, 4)
    f["l0_convw"] = np.ascontiguousarray(
        cw.reshape(NC, 128, 4).transpose(1, 0, 2).reshape(128, NC * 4))
    f["l0_convb"] = np.ascontiguousarray(
        np.asarray(inputs["L0_attn_conv_b"], np.float32).reshape(NC, 128).T)
    f["l0_gateb"] = np.asarray(inputs["L0_attn_gate_b"], np.float32).reshape(1, C)
    f["l0_lnw"] = np.asarray(inputs["L0_attn_ln_w"], np.float32).reshape(1, C)
    f["l0_lnb"] = np.asarray(inputs["L0_attn_ln_b"], np.float32).reshape(1, C)
    gam = np.asarray(inputs["L0_attn_gamma"], np.float32).reshape(H)
    f["l0_gamma"] = gam.reshape(1, H)
    f["l0_gamma_rep"] = np.ascontiguousarray(np.repeat(gam, 128).reshape(1, C))

    gup = np.asarray(inputs["L0_ffn_gate_up_packed"], np.int32)
    gsc = np.asarray(inputs["L0_ffn_gate_up_scales"])
    f["l0_pgT"] = T(gup[:IF])
    f["l0_puT"] = T(gup[IF:])
    f["l0_sg"] = np.ascontiguousarray(gsc[:, :IF])
    f["l0_su"] = np.ascontiguousarray(gsc[:, IF:])
    f["l0_pdT"] = T(np.asarray(inputs["L0_ffn_down_packed"], np.int32))
    f["l0_sd"] = np.ascontiguousarray(np.asarray(inputs["L0_ffn_down_scales"]))

    f["l1_slope"] = np.asarray(inputs["L1_attn_slope_rate"], np.float32).reshape(1, H)
    f["l1_wqkvT"] = T(inputs["L1_attn_w_qkv"])
    f["l1_wgT"] = T(inputs["L1_attn_w_gate"])
    f["l1_woT"] = T(inputs["L1_attn_w_out"])
    f["l1_normw"] = np.asarray(inputs["L1_attn_norm_w"], np.float32).reshape(1, C)
    f["l1_qgT"] = T(np.asarray(inputs["L1_ffn_wq_gate"], np.int32))
    f["l1_quT"] = T(np.asarray(inputs["L1_ffn_wq_up"], np.int32))
    f["l1_qdT"] = T(np.asarray(inputs["L1_ffn_wq_down"], np.int32))
    f["l1_sgc"] = np.ascontiguousarray(
        np.asarray(inputs["L1_ffn_s_gate"], np.float32).reshape(NI, 128).T)
    f["l1_suc"] = np.ascontiguousarray(
        np.asarray(inputs["L1_ffn_s_up"], np.float32).reshape(NI, 128).T)
    f["l1_sd"] = np.asarray(inputs["L1_ffn_s_down"], np.float32).reshape(1, C)

    f["identity"] = np.eye(128, dtype=np.float32)
    ar = np.arange(128, dtype=np.float32)
    f["iota_mj"] = np.ascontiguousarray(ar[None, :] - ar[:, None])
    f["mask_mj"] = np.ascontiguousarray((ar[None, :] >= ar[:, None]).astype(np.float32))
    f["grev"] = np.ascontiguousarray((127.0 - ar).reshape(128, 1))
    jm0 = np.arange(0, 128)[:, None] - np.arange(BLK1)[None, :]
    jm1 = np.arange(128, 256)[:, None] - np.arange(BLK1)[None, :]
    f["l1_i0"] = np.ascontiguousarray(jm0.astype(np.float32))
    f["l1_i1"] = np.ascontiguousarray(jm1.astype(np.float32))
    f["l1_msk0"] = np.ascontiguousarray((jm0 <= 0).astype(np.float32))
    f["l1_msk1"] = np.ascontiguousarray((jm1 <= 0).astype(np.float32))
    f["l1_qm"] = (-((np.arange(T_LOC) % BLK1) + 1.0)).astype(np.float32).reshape(1, T_LOC)
    f["l1_km"] = np.ascontiguousarray(np.stack(
        [-(255.0 - np.arange(0, 128)), -(255.0 - np.arange(128, 256))], 1).astype(np.float32))
    f["shfcol"] = (4 * (np.arange(128) // 16)).astype(np.int32).reshape(128, 1)

    pieces = []
    for cc in range(NC):
        for sl in range(6):
            pieces.append(("g", cc, sl))
    for cc in range(NC):
        for sl in range(6):
            pieces.append(("u", cc, sl))
    for kk in range(NI):
        for sl in range(2):
            pieces.append(("d", kk, sl))
    for j in range(NCORES):
        sh = np.zeros((35, 16, 1024), np.int32)
        sc = np.zeros((35, 1024), np.float16)
        for i, (kind, a, sl) in enumerate(pieces[35 * j:35 * (j + 1)]):
            if kind in ("g", "u"):
                w0 = sl * 1024
                wd = min(1024, IF - w0)
                pk = f["l0_pgT"] if kind == "g" else f["l0_puT"]
                ss = f["l0_sg"] if kind == "g" else f["l0_su"]
                sh[i, :, :wd] = pk[16 * a:16 * (a + 1), w0:w0 + wd]
                sc[i, :wd] = ss[a, w0:w0 + wd]
            else:
                sh[i] = f["l0_pdT"][16 * a:16 * (a + 1), sl * 1024:(sl + 1) * 1024]
                sc[i] = f["l0_sd"][a, sl * 1024:(sl + 1) * 1024]
        per_core[j]["myshard"] = sh
        per_core[j]["myscale"] = sc
    return f, per_core



class Ctx:
    def __init__(self, nc, es):
        self.nc = nc
        self.es = es
        self.block = None
        self.cnt = {}
        self.sem = {}
        self.d = {}
        self.s = {}
        self._uid = 0
        for name in ["dl", "dg", "tv", "ts", "tt", "cc"]:
            self.sem[name] = es.enter_context(nc.semaphore(name))
            self.cnt[name] = 0

    def uid(self):
        self._uid += 1
        return self._uid

    def tick(self, name, inst, n=1):
        inst.then_inc(self.sem[name], n)
        self.cnt[name] += n
        return self.cnt[name]

    def dtick(self, name, inst):
        return self.tick(name, inst, 16)

    def w(self, eng, name, val=None):
        eng.wait_ge(self.sem[name], self.cnt[name] if val is None else val)

    def on(self, engname, fn):
        getattr(self.block, engname)(fn)

    def wall(self, eng):
        for n in ["dl", "dg", "tv", "ts", "tt"]:
            self.w(eng, n)

    def barrier(self):
        def fv(v):
            self.tick("tv", v.engine_nop())
        self.on("vector", fv)

        def fg(g):
            self.tick("dg", g.engine_nop(), 16)
        self.on("gpsimd", fg)
        for e in ["sync", "vector", "scalar", "tensor", "gpsimd"]:
            def fw(eng, _e=e):
                self.wall(eng)
            self.on(e, fw)

    def xr(self, t):
        return self.s["XR"][:, t * C:(t + 1) * C]

    def xnc(self, cc, Tt):
        return self.s["XN"][:, cc * Tt:(cc + 1) * Tt]

    def BA(self, i, n):
        return self.s["BIG"][:, i:i + n]

    def BB(self, i, n):
        return self.s["BIG"][:, NT * C + i:NT * C + i + n]


def alloc_common(ctx, nc, es):
    S = lambda name, shape, dt: es.enter_context(nc.sbuf_tensor(name, shape, dt))
    s = ctx.s
    s["XR"] = S("XR", [128, NTE * C], F32)
    s["XN"] = S("XN", [128, NC * T_EXT], BF16)
    s["TA"] = S("TA", [128, C], F32)
    s["BIG"] = S("BIG", [128, 2 * NT * C], BF16)
    s["WSa"] = S("WSa", [128, 2048], BF16)
    s["WSb"] = S("WSb", [128, 2048], BF16)
    s["WS2"] = S("WS2", [128, 1024], F32)
    s["BC"] = S("BC", [128, C], BF16)
    s["SM"] = S("SM", [128, 192], F32)
    s["SC"] = S("SC", [128, 512], BF16)
    s["ID"] = S("ID", [128, 128], BF16)
    s["ST"] = S("ST", [128, 64], F32)
    s["CW"] = S("CW", [128, 80], F32)
    s["QG"] = S("QG", [128, 4096], BF16)
    s["WSi"] = s["QG"].bitcast(I32)


def emit_bc_table(ctx, row_ap):
    s = ctx.s

    def fd(sync):
        ctx.w(sync, "tv")
        ctx.dtick("dl", sync.dma_start(out=s["TA"][:, :],
                                       in_=row_ap.broadcast_to([128, C])))
    ctx.on("sync", fd)

    def fc(v):
        ctx.w(v, "dl")
        ctx.tick("tv", v.tensor_copy(out=s["BC"][:], in_=s["TA"][:, :]))
    ctx.on("vector", fc)


def emit_preamble(ctx):
    s, d = ctx.s, ctx.d

    def f(sync):
        ctx.dtick("dl", sync.dma_start(out=s["SM"][:, 128:129].bitcast(I32), in_=d["shfcol"]))
        ctx.dtick("dl", sync.dma_start(out=s["WS2"][:, 0:128], in_=d["identity"]))
    ctx.on("sync", f)

    def fv(v):
        ctx.w(v, "dl")
        ctx.tick("tv", v.tensor_copy(out=s["ID"][:], in_=s["WS2"][:, 0:128]))
    ctx.on("vector", fv)


def emit_load_x(ctx, ntiles):
    def f(sync):
        for t in range(ntiles):
            ctx.dtick("dl", sync.dma_start(
                out=ctx.xr(t), in_=ctx.d["xin"][t * 128:(t + 1) * 128, :]))
    ctx.on("sync", f)


def emit_store_x(ctx, t0, ntiles):
    def f(g):
        ctx.wall(g)
        for t in range(ntiles):
            ctx.dtick("dg", g.dma_start(
                out=ctx.d["xout"][t * 128:(t + 1) * 128, :], in_=ctx.xr(t0 + t)))
        ctx.w(g, "dg")
    ctx.on("gpsimd", f)



def emit_ln(ctx, t0, ntiles, permuted):
    nc, s = ctx.nc, ctx.s
    Tt = ntiles * 128
    ctx.barrier()
    with nc.psum_tensor(f"ptl0_{ctx.uid()}", [128, 128], BF16) as p0, \
         nc.psum_tensor(f"ptl1_{ctx.uid()}", [128, 128], BF16) as p1, \
         nc.psum_tensor(f"ptlf0_{ctx.uid()}", [128, 128], F32) as f0, \
         nc.psum_tensor(f"ptlf1_{ctx.uid()}", [128, 128], F32) as f1:
        pt = [p0, p1]
        pf = [f0, f1]
        xtmp = ctx.BA(0, C)
        for t in range(ntiles):
            xr = ctx.xr(t0 + t)

            def fv(v, t=t, xr=xr):
                ctx.w(v, "tv")
                if t > 0:
                    ctx.w(v, "tt")
                v.reduce_sum(out=s["ST"][:, 24:25], in_=xr, axis=AX)
                v.drain()
                v.tensor_scalar_mul(out=s["ST"][:, 24:25], in0=s["ST"][:, 24:25],
                                    scalar1=1.0 / C)
                v.drain()
                ctx.tick("tv", v.engine_nop())
            ctx.on("vector", fv)

            def fsr(sc, t=t, xr=xr):
                ctx.w(sc, "tv")
                for k in range(4):
                    sc.activation(out=s["SM"][:, 128:160] if False else s["TA"][:, 0:512],
                                  in_=xr[:, 512 * k:512 * (k + 1)], func=AF.Square,
                                  bias=s["ST"][:, 24:25], scale=-1.0,
                                  accum_out=s["ST"][:, 40 + k:41 + k])
                ctx.tick("ts", sc.copy(out=s["SM"][:, 191:192], in_=s["SM"][:, 191:192]))
            ctx.on("scalar", fsr)

            def fv15(v, t=t):
                ctx.w(v, "ts")
                v.tensor_add(out=s["ST"][:, 40:41], in0=s["ST"][:, 40:41],
                             in1=s["ST"][:, 41:42])
                v.tensor_add(out=s["ST"][:, 42:43], in0=s["ST"][:, 42:43],
                             in1=s["ST"][:, 43:44])
                v.drain()
                v.tensor_add(out=s["ST"][:, 40:41], in0=s["ST"][:, 40:41],
                             in1=s["ST"][:, 42:43])
                v.drain()
                ctx.tick("tv", v.tensor_scalar(out=s["ST"][:, 26:27],
                                               in0=s["ST"][:, 40:41],
                                               scalar1=1.0 / C, scalar2=LN_EPS,
                                               op0=ALU.mult, op1=ALU.add))
            ctx.on("vector", fv15)

            def fsr2(sc, t=t):
                ctx.w(sc, "tv")
                sc.activation(out=s["ST"][:, 26:27], in_=s["ST"][:, 26:27], func=AF.Ln)
                ctx.tick("ts", sc.activation(out=s["ST"][:, 26:27],
                                             in_=s["ST"][:, 26:27],
                                             func=AF.Exp, scale=-0.5))
            ctx.on("scalar", fsr2)

            def fnm(v, t=t, xr=xr):
                ctx.w(v, "ts")
                ctx.tick("tv", v.tensor_scalar(
                    out=xtmp, in0=xr, scalar1=s["ST"][:, 24:25],
                    scalar2=s["ST"][:, 26:27], op0=ALU.subtract, op1=ALU.mult))
            ctx.on("vector", fnm)

            for cc in range(NC):
                def ft(te, t=t, cc=cc):
                    ctx.w(te, "tv")
                    src = xtmp[:, cc * 128:(cc + 1) * 128]
                    ctx.tick("tt", te.transpose(out=pt[cc % 2][:], in_=src,
                                                identity=s["ID"][:]))

                def fv2(v, t=t, cc=cc):
                    ctx.w(v, "tt")
                    ctx.tick("tv", v.tensor_copy(
                        out=ctx.xnc(cc, Tt)[:, t * 128:(t + 1) * 128],
                        in_=pt[cc % 2][:]))
                ctx.on("tensor", ft)
                ctx.on("vector", fv2)
    ctx.barrier()



def emit_proj(ctx, Tt, a0, ntiles, O, wsrc, wdt, evac, lhsT_fn=None):
    nc, s = ctx.nc, ctx.s
    if lhsT_fn is None:
        def lhsT_fn(cc, a, n):
            return ctx.xnc(cc, Tt)[:, a:a + n]
    nq = O // 512
    ctx.barrier()
    with ExitStack() as ps:
        pb = [ps.enter_context(nc.psum_tensor(f"pj{i}_{ctx.uid()}", [128, 512], F32))
              for i in range(ntiles)]
        for oq in range(nq):
            for cc in range(NC):
                buf = s["WSa"] if cc % 2 == 0 else s["WSb"]
                wtile = buf[:, 0:512]
                if wdt == BF16:
                    def fdma(sync, oq=oq, cc=cc, wtile=wtile):
                        ctx.w(sync, "tt")
                        ctx.dtick("dl", sync.dma_start(out=wtile,
                                                       in_=wsrc(cc, oq * 512, 512)))
                    ctx.on("sync", fdma)
                else:
                    def fdma(sync, oq=oq, cc=cc):
                        ctx.w(sync, "tv")
                        if wdt == F32:
                            stage = s["WS2"][:, 0:512] if cc % 2 == 0 else s["WS2"][:, 512:1024]
                        else:
                            stage = s["WSi"][:, 0:512] if cc % 2 == 0 else s["WSi"][:, 512:1024]
                        ctx.dtick("dl", sync.dma_start(out=stage,
                                                       in_=wsrc(cc, oq * 512, 512)))
                    ctx.on("sync", fdma)

                    def fcvt(v, oq=oq, cc=cc, wtile=wtile):
                        ctx.w(v, "dl")
                        ctx.w(v, "tt")
                        if wdt == F32:
                            stage = s["WS2"][:, 0:512] if cc % 2 == 0 else s["WS2"][:, 512:1024]
                            ctx.tick("tv", v.tensor_copy(out=wtile, in_=stage))
                        else:
                            stage = s["WSi"][:, 0:512] if cc % 2 == 0 else s["WSi"][:, 512:1024]
                            ctx.tick("tv", v.tensor_scalar(
                                out=wtile, in0=stage, scalar1=-128.0, scalar2=None,
                                op0=ALU.add))
                    ctx.on("vector", fcvt)

                def fmm(te, oq=oq, cc=cc, wtile=wtile):
                    ctx.w(te, "dl" if wdt == BF16 else "tv")
                    if cc == 0:
                        ctx.w(te, "ts")
                        ctx.w(te, "tv")
                    for t in range(ntiles):
                        inst = te.matmul(out=pb[t][:], lhsT=lhsT_fn(cc, a0 + t * 128, 128),
                                         rhs=wtile, start=(cc == 0), stop=(cc == NC - 1))
                        if t == ntiles - 1:
                            ctx.tick("tt", inst)
                ctx.on("tensor", fmm)

            for t in range(ntiles):
                evac(t, oq, pb[t])
    ctx.barrier()



def emit_l0_dequant(ctx):
    s = ctx.s
    shf = s["SM"][:, 128:129].bitcast(I32)
    ctx.barrier()

    def fv0(v):
        ctx.w(v, "dl")
        v.memset(s["ST"][:, 32:33], -2.0)
        v.memset(s["ST"][:, 33:34], -6.0)
        v.drain()
        ctx.tick("tv", v.engine_nop())
    ctx.on("vector", fv0)

    def emit_piece(words_src, width, scale_src, out_dst):
        def fdma(sync):
            ctx.w(sync, "tv")
            for j in range(8):
                ctx.dtick("dl", sync.dma_start(
                    out=s["WSi"][16 * j:16 * (j + 1), 0:width], in_=words_src))
            ctx.dtick("dl", sync.dma_start(
                out=s["WSa"].bitcast(F16)[:, 0:width],
                in_=scale_src.broadcast_to([128, width])))
        ctx.on("sync", fdma)

        def fv1(v):
            ctx.w(v, "dl")
            ctx.w(v, "ts")
            v.tensor_scalar(
                out=s["WSi"][:, 1024:1024 + width], in0=s["WSi"][:, 0:width],
                scalar1=shf, scalar2=7,
                op0=ALU.logical_shift_right, op1=ALU.bitwise_and)
            ctx.tick("tv", v.tensor_copy(out=s["WS2"][:, 0:width],
                                         in_=s["WSi"][:, 1024:1024 + width]))
        ctx.on("vector", fv1)

        def fs1(sc):
            ctx.w(sc, "tv")
            sc.activation(out=s["TA"][:, 0:width], in_=s["WS2"][:, 0:width],
                          func=AF.Relu, bias=s["ST"][:, 32:33], scale=0.5)
            ctx.tick("ts", sc.activation(
                out=s["TA"][:, 1024:1024 + width], in_=s["WS2"][:, 0:width],
                func=AF.Relu, bias=s["ST"][:, 33:34], scale=1.0))
        ctx.on("scalar", fs1)

        def fv2(v):
            ctx.w(v, "ts")
            ctx.w(v, "dg")
            m_f = s["WS2"][:, 0:width]
            v.scalar_tensor_tensor(out=m_f, in0=m_f, scalar=0.5,
                                   in1=s["TA"][:, 0:width], op0=ALU.mult, op1=ALU.add)
            v.tensor_add(out=m_f, in0=m_f, in1=s["TA"][:, 1024:1024 + width])
            v.tensor_scalar(out=s["WSi"][:, 1024:1024 + width], in0=s["WSi"][:, 0:width],
                            scalar1=shf, scalar2=8,
                            op0=ALU.logical_shift_right, op1=ALU.bitwise_and)
            v.tensor_scalar(out=s["TA"][:, 0:width], in0=s["WSi"][:, 1024:1024 + width],
                            scalar1=-0.25, scalar2=1.0, op0=ALU.mult, op1=ALU.add)
            v.tensor_mul(out=m_f, in0=m_f, in1=s["TA"][:, 0:width])
            ctx.tick("tv", v.tensor_tensor(
                out=s["WSb"][:, 0:width], in0=m_f,
                in1=s["WSa"].bitcast(F16)[:, 0:width], op=ALU.mult))
        ctx.on("vector", fv2)

        def fst(g):
            ctx.w(g, "tv")
            rview = out_dst.rearrange("(m j) w -> m j w", m=16, j=8)
            for j in range(8):
                ctx.dtick("dg", g.dma_start(
                    out=rview[:, j, :],
                    in_=s["WSb"][16 * j:16 * (j + 1), 0:width]))
        ctx.on("gpsimd", fst)

    for i in range(35):
        emit_piece(ctx.d["myshard"][i], 1024, ctx.d["myscale"][i:i + 1, :],
                   ctx.d["cin_w"][i * 128:(i + 1) * 128, :])

    def fcc(g):
        ctx.w(g, "dg")
        ctx.tick("cc", g.collective_compute(
            "AllGather", ALU.bypass, replica_groups=RG, ins=[ctx.d["cin_w"]],
            outs=[ctx.d["cout_w"]]))
    ctx.on("gpsimd", fcc)

    def fwait(sync):
        ctx.w(sync, "cc")
    ctx.on("sync", fwait)
    ctx.barrier()


def wcache(ctx):
    base = ctx.d["cout_w"].rearrange("(s p) w -> s p w", p=128)
    return base



def emit_ffn(ctx, layer, t0, ntiles, Tt):
    nc, s, d = ctx.nc, ctx.s, ctx.d
    chunks = []
    t = 0
    while t < ntiles:
        n = min(4, ntiles - t)
        chunks.append((t, n))
        t += n

    ctx.barrier()
    if layer == "l1":
        def fld(sync):
            ctx.dtick("dl", sync.dma_start(out=s["SM"][:, 0:44], in_=d["l1_sgc"]))
            ctx.dtick("dl", sync.dma_start(out=s["SM"][:, 44:88], in_=d["l1_suc"]))
        ctx.on("sync", fld)

    for (tc, ntc) in chunks:
        Tch = ntc * 128
        with nc.psum_tensor(f"pgA_{ctx.uid()}", [128, 512], F32) as pg, \
             nc.psum_tensor(f"puA_{ctx.uid()}", [128, 512], F32) as pu:
            for k in range(NI):
                if layer == "l0":
                    def fdg(sync, k=k):
                        ctx.w(sync, "tt")
                        base = wcache(ctx)
                        sl, col = k // 8, (k % 8) * 128
                        gsrc = base[sl:96:6, :, col:col + 128].transpose([1, 0, 2])
                        usrc = base[96 + sl:192:6, :, col:col + 128].transpose([1, 0, 2])
                        ctx.dtick("dl", sync.dma_start(
                            out=s["WSa"].rearrange("p (cc i) -> p cc i", cc=16),
                            in_=gsrc))
                        ctx.dtick("dl", sync.dma_start(
                            out=s["WSb"].rearrange("p (cc i) -> p cc i", cc=16),
                            in_=usrc))
                    ctx.on("sync", fdg)
                else:
                    def fdg1(sync, k=k):
                        ctx.w(sync, "tv")
                        ctx.dtick("dl", sync.dma_start(
                            out=s["WSi"].rearrange("p (cc i) -> p cc i", cc=16),
                            in_=d["l1_qgT"].rearrange("(cc p) i -> p cc i", p=128)[:, :, 128 * k:128 * (k + 1)]))
                    ctx.on("sync", fdg1)

                    def fcv1(v, k=k):
                        ctx.w(v, "dl")
                        ctx.w(v, "tt")
                        ctx.tick("tv", v.tensor_scalar(
                            out=s["WSa"][:], in0=s["WSi"][:], scalar1=-128.0,
                            scalar2=None, op0=ALU.add))
                    ctx.on("vector", fcv1)

                    def fdg2(sync, k=k):
                        ctx.w(sync, "tv")
                        ctx.dtick("dl", sync.dma_start(
                            out=s["WSi"].rearrange("p (cc i) -> p cc i", cc=16),
                            in_=d["l1_quT"].rearrange("(cc p) i -> p cc i", p=128)[:, :, 128 * k:128 * (k + 1)]))
                    ctx.on("sync", fdg2)

                    def fcv2(v, k=k):
                        ctx.w(v, "dl")
                        ctx.w(v, "tt")
                        ctx.tick("tv", v.tensor_scalar(
                            out=s["WSb"][:], in0=s["WSi"][:], scalar1=-128.0,
                            scalar2=None, op0=ALU.add))
                    ctx.on("vector", fcv2)

                def fmmA(te, k=k, tc=tc, Tch=Tch):
                    ctx.w(te, "dl" if layer == "l0" else "tv")
                    ctx.w(te, "ts")
                    if layer == "l0":
                        ctx.w(te, "tv")
                    for cc in range(NC):
                        te.matmul(out=pg[:, 0:Tch],
                                  lhsT=s["WSa"][:, cc * 128:(cc + 1) * 128],
                                  rhs=ctx.xnc(cc, Tt)[:, tc * 128:tc * 128 + Tch],
                                  start=(cc == 0), stop=(cc == NC - 1))
                    for cc in range(NC):
                        inst = te.matmul(out=pu[:, 0:Tch],
                                         lhsT=s["WSb"][:, cc * 128:(cc + 1) * 128],
                                         rhs=ctx.xnc(cc, Tt)[:, tc * 128:tc * 128 + Tch],
                                         start=(cc == 0), stop=(cc == NC - 1))
                        if cc == NC - 1:
                            ctx.tick("tt", inst)
                ctx.on("tensor", fmmA)

                def fevA_s(sc, k=k, Tch=Tch):
                    ctx.w(sc, "tt")
                    ctx.w(sc, "tv")
                    if layer == "l1":
                        ctx.w(sc, "dl")
                        inst = sc.activation(out=s["WS2"][:, 0:Tch], in_=pg[:, 0:Tch],
                                             func=AF.Silu, scale=s["SM"][:, k:k + 1])
                    else:
                        inst = sc.activation(out=s["WS2"][:, 0:Tch], in_=pg[:, 0:Tch],
                                             func=AF.Silu)
                    ctx.tick("ts", inst)
                ctx.on("scalar", fevA_s)

                def fevA_v(v, k=k, Tch=Tch):
                    ctx.w(v, "ts")
                    hdst = ctx.s["BIG"][:, k * 512:k * 512 + Tch]
                    if layer == "l0":
                        inst = v.tensor_tensor(out=hdst, in0=s["WS2"][:, 0:Tch],
                                               in1=pu[:, 0:Tch], op=ALU.mult)
                    else:
                        inst = v.scalar_tensor_tensor(out=hdst, in0=pu[:, 0:Tch],
                                                      scalar=s["SM"][:, 44 + k:45 + k],
                                                      in1=s["WS2"][:, 0:Tch],
                                                      op0=ALU.mult, op1=ALU.mult)
                    ctx.tick("tv", inst)
                ctx.on("vector", fevA_v)

        ctx.barrier()
        if layer == "l1":
            emit_bc_table(ctx, d["l1_sd"])

            def fbc2(v):
                ctx.tick("tv", v.tensor_scalar_mul(out=s["BC"][:], in0=s["BC"][:],
                                                   scalar1=0.5))
            ctx.on("vector", fbc2)

        with ExitStack() as ps:
            pyb = [ps.enter_context(nc.psum_tensor(f"pyB{i}_{ctx.uid()}", [128, 512], F32))
                   for i in range(ntc)]
            for oq in range(4):
                for k in range(NI):
                    buf = s["WSa"] if k % 2 == 0 else s["WSb"]
                    wt = buf[:, 0:512]
                    if layer == "l0":
                        def fdB(sync, oq=oq, k=k, wt=wt):
                            ctx.w(sync, "tt")
                            base = wcache(ctx)
                            slot = 192 + 2 * k + (oq // 2)
                            ctx.dtick("dl", sync.dma_start(
                                out=wt, in_=base[slot, :, (oq % 2) * 512:(oq % 2) * 512 + 512]))
                        ctx.on("sync", fdB)
                    else:
                        def fdB(sync, oq=oq, k=k):
                            ctx.w(sync, "tv")
                            stage = s["WSi"][:, 0:512] if k % 2 == 0 else s["WSi"][:, 512:1024]
                            ctx.dtick("dl", sync.dma_start(
                                out=stage,
                                in_=d["l1_qdT"][128 * k:128 * (k + 1), 512 * oq:512 * (oq + 1)]))
                        ctx.on("sync", fdB)

                        def fcB(v, oq=oq, k=k, wt=wt):
                            ctx.w(v, "dl")
                            ctx.w(v, "tt")
                            stage = s["WSi"][:, 0:512] if k % 2 == 0 else s["WSi"][:, 512:1024]
                            ctx.tick("tv", v.tensor_scalar(
                                out=wt, in0=stage, scalar1=-128.0, scalar2=None,
                                op0=ALU.add))
                        ctx.on("vector", fcB)

                    def fmB(te, oq=oq, k=k, wt=wt, ntc=ntc):
                        ctx.w(te, "dl" if layer == "l0" else "tv")
                        if k == 0:
                            ctx.w(te, "tv")
                            ctx.w(te, "ts")
                        for i in range(ntc):
                            inst = te.matmul(
                                out=pyb[i][:],
                                lhsT=ctx.s["BIG"][:, k * 512 + i * 128:k * 512 + (i + 1) * 128],
                                rhs=wt, start=(k == 0), stop=(k == NI - 1))
                            if i == ntc - 1:
                                ctx.tick("tt", inst)
                    ctx.on("tensor", fmB)

                for i in range(ntc):
                    if layer == "l0":
                        def fqv(v, i=i, oq=oq, tc=tc):
                            ctx.w(v, "tt")
                            ctx.tick("tv", v.tensor_copy(
                                out=ctx.s["BIG"][:, NI * 512 + i * C + oq * 512:NI * 512 + i * C + (oq + 1) * 512],
                                in_=pyb[i][:]))
                        ctx.on("vector", fqv)
                    else:
                        def fqv(v, i=i, oq=oq, tc=tc):
                            ctx.w(v, "tt")
                            v.tensor_tensor(out=s["TA"][:, 0:512], in0=pyb[i][:],
                                            in1=s["BC"][:, oq * 512:(oq + 1) * 512],
                                            op=ALU.mult)
                            xr = ctx.xr(t0 + tc + i)
                            ctx.tick("tv", v.tensor_add(
                                out=xr[:, oq * 512:(oq + 1) * 512],
                                in0=xr[:, oq * 512:(oq + 1) * 512],
                                in1=s["TA"][:, 0:512]))
                        ctx.on("vector", fqv)

            if layer == "l0":
                for i in range(ntc):
                    def frms_s(sc, i=i, tc=tc):
                        ctx.w(sc, "tv")
                        y = ctx.s["BIG"][:, NI * 512 + i * C:NI * 512 + (i + 1) * C]
                        for k4 in range(4):
                            sc.activation(out=s["TA"][:, 0:512],
                                          in_=y[:, 512 * k4:512 * (k4 + 1)],
                                          func=AF.Square,
                                          accum_out=s["ST"][:, 40 + k4:41 + k4])
                        ctx.tick("ts", sc.copy(out=s["SM"][:, 191:192], in_=s["SM"][:, 191:192]))
                    ctx.on("scalar", frms_s)

                    def frms_v(v, i=i, tc=tc):
                        ctx.w(v, "ts")
                        v.tensor_add(out=s["ST"][:, 40:41], in0=s["ST"][:, 40:41],
                                     in1=s["ST"][:, 41:42])
                        v.tensor_add(out=s["ST"][:, 42:43], in0=s["ST"][:, 42:43],
                                     in1=s["ST"][:, 43:44])
                        v.drain()
                        v.tensor_add(out=s["ST"][:, 40:41], in0=s["ST"][:, 40:41],
                                     in1=s["ST"][:, 42:43])
                        v.drain()
                        v.tensor_scalar(out=s["ST"][:, 26:27], in0=s["ST"][:, 40:41],
                                        scalar1=1.0 / C, scalar2=L0_FFN_EPS,
                                        op0=ALU.mult, op1=ALU.add)
                        v.drain()
                        ctx.tick("tv", v.engine_nop())
                    ctx.on("vector", frms_v)

                    def frms_s2(sc, i=i, tc=tc):
                        ctx.w(sc, "tv")
                        sc.activation(out=s["ST"][:, 26:27], in_=s["ST"][:, 26:27],
                                      func=AF.Ln)
                        ctx.tick("ts", sc.activation(out=s["ST"][:, 26:27],
                                                     in_=s["ST"][:, 26:27],
                                                     func=AF.Exp, scale=-0.5))
                    ctx.on("scalar", frms_s2)

                    def frms2(v, i=i, tc=tc):
                        ctx.w(v, "ts")
                        y = ctx.s["BIG"][:, NI * 512 + i * C:NI * 512 + (i + 1) * C]
                        v.tensor_scalar_mul(out=s["ST"][:, 26:27], in0=s["ST"][:, 26:27],
                                            scalar1=0.5)
                        v.drain()
                        v.tensor_scalar(out=s["TA"][:], in0=y, scalar1=s["ST"][:, 26:27],
                                        scalar2=None, op0=ALU.mult)
                        xr = ctx.xr(t0 + tc + i)
                        ctx.tick("tv", v.tensor_add(out=xr, in0=xr, in1=s["TA"][:]))
                    ctx.on("vector", frms2)
        ctx.barrier()



def emit_l0_attn(ctx, upto="full"):
    nc, s, d = ctx.nc, ctx.s, ctx.d
    Tt = T_EXT

    ctx.barrier()

    def fld(sync):
        ctx.dtick("dl", sync.dma_start(out=s["SM"][0:1, 32:48], in_=d["l0_gamma"]))
        ctx.dtick("dl", sync.dma_start(out=s["ST"][:, 48:64],
                                       in_=d["l0_gamma"].broadcast_to([128, H])))
        ctx.dtick("dl", sync.dma_start(out=s["CW"][:, 0:64], in_=d["l0_convw"]))
        ctx.dtick("dl", sync.dma_start(out=s["CW"][:, 64:80], in_=d["l0_convb"]))
    ctx.on("sync", fld)

    def fs0(sc):
        ctx.w(sc, "dl")
        ctx.tick("ts", sc.activation(out=s["ST"][:, 48:64], in_=s["ST"][:, 48:64],
                                     func=AF.Ln))
    ctx.on("scalar", fs0)

    for cc in range(NC):
        def fcv(v, cc=cc):
            if cc == 0:
                ctx.w(v, "tv")
                ctx.w(v, "dl")
            else:
                ctx.w(v, "ts")
            xn = ctx.xnc(cc, Tt)
            v.tensor_scalar(out=s["TA"][:, 0:1024], in0=xn[:, 125:125 + 1024],
                            scalar1=s["CW"][:, 4 * cc:4 * cc + 1], scalar2=None,
                            op0=ALU.mult)
            for k in (1, 2, 3):
                v.scalar_tensor_tensor(out=s["TA"][:, 0:1024],
                                       in0=xn[:, 125 + k:125 + k + 1024],
                                       scalar=s["CW"][:, 4 * cc + k:4 * cc + k + 1],
                                       in1=s["TA"][:, 0:1024],
                                       op0=ALU.mult, op1=ALU.add)
            v.drain()
            ctx.tick("tv", v.engine_nop())
        ctx.on("vector", fcv)

        def fcs(sc, cc=cc):
            ctx.w(sc, "tv")
            ctx.w(sc, "dg")
            ctx.tick("ts", sc.activation(out=s["WSb"][:, 0:1024], in_=s["TA"][:, 0:1024],
                                         func=AF.Silu, bias=s["CW"][:, 64 + cc:65 + cc],
                                         scale=1.0))
        ctx.on("scalar", fcs)

        def fcd(g, cc=cc):
            ctx.w(g, "ts")
            ctx.dtick("dg", g.dma_start(out=d["xc_d"][cc], in_=s["WSb"][:, 0:1024]))
        ctx.on("gpsimd", fcd)

    def l2norm_evac(dstf):
        def evac(t, oq, pb):
            def fsq(sc, t=t, oq=oq, pb=pb):
                ctx.w(sc, "tt")
                for hh in range(4):
                    sc.activation(out=s["TA"][:, hh * 128:(hh + 1) * 128],
                                  in_=pb[:, hh * 128:(hh + 1) * 128],
                                  func=AF.Square,
                                  accum_out=s["ST"][:, 40 + hh:41 + hh])
                sc.activation(out=s["ST"][:, 44:48], in_=s["ST"][:, 40:44], func=AF.Ln)
                ctx.tick("ts", sc.activation(out=s["ST"][:, 44:48],
                                             in_=s["ST"][:, 44:48],
                                             func=AF.Exp, scale=-0.5))
            ctx.on("scalar", fsq)

            def fvn(v, t=t, oq=oq, pb=pb):
                ctx.w(v, "ts")
                v.tensor_scalar(out=s["ST"][:, 44:48], in0=s["ST"][:, 44:48],
                                scalar1=1.0 / L0_NORM_EPS, scalar2=None, op0=ALU.min)
                v.drain()
                for hh in range(4):
                    inst = v.tensor_scalar(
                        out=dstf(t)[:, oq * 512 + hh * 128:oq * 512 + (hh + 1) * 128],
                        in0=pb[:, hh * 128:(hh + 1) * 128],
                        scalar1=s["ST"][:, 44 + hh:45 + hh], scalar2=None, op0=ALU.mult)
                ctx.tick("tv", inst)
            ctx.on("vector", fvn)
        return evac

    emit_proj(ctx, Tt, HALO, NT, C,
              lambda cc, o0, ol: d["l0_wkT"][128 * cc:128 * (cc + 1), o0:o0 + ol],
              F32, l2norm_evac(lambda t: ctx.BA(t * C, C)))

    def v_evac(t, oq, pb):
        def fv(v, t=t, oq=oq, pb=pb):
            ctx.w(v, "tt")
            ctx.tick("tv", v.tensor_tensor(
                out=ctx.BB(t * C + oq * 512, 512), in0=pb[:],
                in1=ctx.BA(t * C + oq * 512, 512), op=ALU.mult))
        ctx.on("vector", fv)
    emit_proj(ctx, Tt, HALO, NT, C,
              lambda cc, o0, ol: d["l0_wvT"][128 * cc:128 * (cc + 1), o0:o0 + ol],
              F32, v_evac)

    emit_proj(ctx, Tt, HALO, NT, C,
              lambda cc, o0, ol: d["l0_wqT"][128 * cc:128 * (cc + 1), o0:o0 + ol],
              F32, l2norm_evac(lambda t: ctx.BA(t * C, C)))

    def fxc(sync):
        ctx.wall(sync)
        for cc in range(NC):
            ctx.dtick("dl", sync.dma_start(out=ctx.xnc(cc, T_LOC), in_=d["xc_d"][cc]))
    ctx.on("sync", fxc)

    emit_bc_table(ctx, d["l0_gateb"])

    def g_evac(t, oq, pb):
        def fga(v, t=t, oq=oq, pb=pb):
            ctx.w(v, "tt")
            ctx.w(v, "dg")
            ctx.w(v, "ts")
            ctx.tick("tv", v.tensor_tensor(out=s["TA"][:, 0:512], in0=pb[:],
                                           in1=s["BC"][:, oq * 512:(oq + 1) * 512],
                                           op=ALU.add))
        ctx.on("vector", fga)

        def fgs(sc, t=t, oq=oq):
            ctx.w(sc, "tv")
            ctx.tick("ts", sc.activation(out=s["TA"][:, 512:1024],
                                         in_=s["TA"][:, 0:512], func=AF.Sigmoid))
        ctx.on("scalar", fgs)

        def fgm(v, t=t, oq=oq):
            ctx.w(v, "ts")
            dst = ctx.BA(t * C + oq * 512, 512)
            ctx.tick("tv", v.tensor_tensor(out=dst, in0=s["TA"][:, 512:1024],
                                           in1=dst, op=ALU.mult))
        ctx.on("vector", fgm)

    emit_proj(ctx, T_LOC, 0, NT, C,
              lambda cc, o0, ol: d["l0_wgT"][128 * cc:128 * (cc + 1), o0:o0 + ol],
              F32, g_evac)
    if upto == "proj":
        return

    def fsc_ld(sync):
        ctx.wall(sync)
        ctx.dtick("dl", sync.dma_start(out=s["WS2"][:, 0:128], in_=d["iota_mj"]))
        ctx.dtick("dl", sync.dma_start(out=s["WS2"][:, 128:256], in_=d["mask_mj"]))
        ctx.dtick("dl", sync.dma_start(out=s["WS2"][:, 256:257], in_=d["grev"]))
    ctx.on("sync", fsc_ld)

    for h in range(H):
        def fdt_s(sc, h=h):
            ctx.w(sc, "dl")
            ctx.w(sc, "tv")
            sc.activation(out=s["TA"][:, 0:128], in_=s["WS2"][:, 0:128],
                          func=AF.Exp, scale=s["ST"][:, 48 + h:49 + h])
            ctx.tick("ts", sc.activation(out=s["TA"][:, 128:129], in_=s["WS2"][:, 256:257],
                                         func=AF.Exp, scale=s["ST"][:, 48 + h:49 + h]))
        ctx.on("scalar", fdt_s)

        def fdt_v(v, h=h):
            ctx.w(v, "ts")
            v.tensor_tensor(out=s["QG"][:, 2048 + h * 128:2048 + (h + 1) * 128],
                            in0=s["TA"][:, 0:128], in1=s["WS2"][:, 128:256], op=ALU.mult)
            v.tensor_copy(out=s["SC"][:, h:h + 1], in_=s["TA"][:, 128:129])
            v.drain()
            ctx.tick("tv", v.engine_nop())
        ctx.on("vector", fdt_v)

    with nc.psum_tensor(f"pc1_{ctx.uid()}", [1, 128], F32) as pc:
        for h in range(H):
            def fg128(v, h=h):
                ctx.w(v, "ts")
                v.tensor_scalar_mul(out=s["ST"][0:1, 40:41],
                                    in0=s["ST"][0:1, 48 + h:49 + h], scalar1=128.0)
                v.drain()
                ctx.tick("tv", v.engine_nop())
            ctx.on("vector", fg128)

            def fg128b(sc, h=h):
                ctx.w(sc, "tv")
                ctx.tick("ts", sc.activation(out=s["ST"][0:1, 40:41],
                                             in_=s["ST"][0:1, 40:41], func=AF.Exp))
            ctx.on("scalar", fg128b)

            for b in range(NT):
                def fmm(te, h=h, b=b):
                    ctx.w(te, "tv")
                    ctx.tick("tt", te.matmul(
                        out=pc[:], lhsT=s["SC"][:, h:h + 1],
                        rhs=ctx.BB(b * C + h * 128, 128), start=True, stop=True))
                ctx.on("tensor", fmm)

                def fcar(v, h=h, b=b):
                    ctx.w(v, "tt")
                    ctx.w(v, "ts")
                    scar = s["SM"][0:1, 64:192]
                    if b == 0:
                        inst = v.tensor_copy(out=scar, in_=pc[0:1, :])
                    else:
                        inst = v.scalar_tensor_tensor(out=scar, in0=scar,
                                                      scalar=s["ST"][0:1, 40:41],
                                                      in1=pc[0:1, :],
                                                      op0=ALU.mult, op1=ALU.add)
                    ctx.tick("tv", inst)
                    v.drain()
                ctx.on("vector", fcar)

            def fA(v, h=h):
                v.tensor_copy(out=s["TA"][0:1, h * 128:(h + 1) * 128],
                              in_=s["SM"][0:1, 64:192])
                v.drain()
                ctx.tick("tv", v.engine_nop())
            ctx.on("vector", fA)

    def fcc(g):
        ctx.w(g, "tv")
        ctx.dtick("dg", g.dma_start(out=d["cin0"], in_=s["TA"][0:1, :]))
        ctx.w(g, "dg")
        ctx.tick("cc", g.collective_compute(
            "AllGather", ALU.bypass, replica_groups=RG, ins=[d["cin0"]],
            outs=[d["cout0"]]))
        ctx.w(g, "cc")
        ctx.dtick("dg", g.dma_start(out=s["SM"][0:8, 0:1], in_=d["cc_sel"]))
        ctx.dtick("dg", g.dma_start(out=s["SM"][0:8, 1:2], in_=d["cc_pow"]))
    ctx.on("gpsimd", fcc)

    for half in range(2):
        def fch_g(g, half=half):
            ctx.w(g, "tv")
            ctx.dtick("dg", g.dma_start(out=s["TA"][0:8, 0:1024],
                                        in_=d["cout0"][:, half * 1024:(half + 1) * 1024]))
            ctx.dtick("dg", g.dma_start(
                out=s["TA"][0:8, 1024:2048],
                in_=d["l0_gamma_rep"][0:1, half * 1024:(half + 1) * 1024].broadcast_to([8, 1024])))
        ctx.on("gpsimd", fch_g)

        def fch_s(sc, half=half):
            ctx.w(sc, "dg")
            sc.activation(out=s["TA"][0:8, 1024:2048], in_=s["TA"][0:8, 1024:2048],
                          func=AF.Ln)
            ctx.tick("ts", sc.activation(out=s["TA"][0:8, 1024:2048],
                                         in_=s["TA"][0:8, 1024:2048],
                                         func=AF.Exp, scale=s["SM"][0:8, 1:2]))
        ctx.on("scalar", fch_s)

        def fch_v(v, half=half):
            ctx.w(v, "ts")
            v.tensor_scalar(out=s["TA"][0:8, 1024:2048], in0=s["TA"][0:8, 1024:2048],
                            scalar1=s["SM"][0:8, 0:1], scalar2=None, op0=ALU.mult)
            v.tensor_tensor(
                out=s["BC"][0:8, half * 1024:(half + 1) * 1024],
                in0=s["TA"][0:8, 1024:2048], in1=s["TA"][0:8, 0:1024], op=ALU.mult)
            v.drain()
            ctx.tick("tv", v.engine_nop())
        ctx.on("vector", fch_v)

    def fones(v):
        v.memset(s["SC"][0:8, 256:257], 1.0)
        v.drain()
        ctx.tick("tv", v.engine_nop())
    ctx.on("vector", fones)

    with nc.psum_tensor(f"psin_{ctx.uid()}", [1, 512], F32) as psin:
        for qq in range(4):
            def fmm_s(te, qq=qq):
                ctx.w(te, "tv")
                ctx.tick("tt", te.matmul(out=psin[:], lhsT=s["SC"][0:8, 256:257],
                                         rhs=s["BC"][0:8, qq * 512:(qq + 1) * 512],
                                         start=True, stop=True))
            ctx.on("tensor", fmm_s)

            def fsin(v, qq=qq):
                ctx.w(v, "tt")
                v.tensor_copy(out=s["TA"][0:1, qq * 512:(qq + 1) * 512],
                              in_=psin[0:1, :])
                v.drain()
                ctx.tick("tv", v.engine_nop())
            ctx.on("vector", fsin)

    with nc.psum_tensor(f"ps2_{ctx.uid()}", [128, 128], F32) as ps2, \
         nc.psum_tensor(f"pc2_{ctx.uid()}", [1, 128], F32) as pc2:
        for h in range(H):
            def fini(v, h=h):
                ctx.w(v, "tv")
                v.tensor_copy(out=s["SM"][0:1, 64:192],
                              in_=s["TA"][0:1, h * 128:(h + 1) * 128])
                v.drain()
                ctx.tick("tv", v.engine_nop())
            ctx.on("vector", fini)

            for b in range(NT):
                def ffix(v, h=h, b=b):
                    ctx.w(v, "tt")
                    v.tensor_scalar(out=s["WS2"][0:1, 512:640], in0=s["SM"][0:1, 64:192],
                                    scalar1=s["SM"][0:1, 32 + h:33 + h], scalar2=None,
                                    op0=ALU.mult)
                    v.drain()
                    u0 = ctx.BB(b * C + h * 128, 128)[0:1, :]
                    v.tensor_add(out=u0, in0=u0, in1=s["WS2"][0:1, 512:640])
                    v.drain()
                    ctx.tick("tv", v.engine_nop())
                ctx.on("vector", ffix)

                def fmm2(te, h=h, b=b):
                    ctx.w(te, "tv")
                    te.matmul(out=ps2[:],
                              lhsT=s["QG"][:, 2048 + h * 128:2048 + (h + 1) * 128],
                              rhs=ctx.BB(b * C + h * 128, 128), start=True, stop=True)
                    ctx.tick("tt", te.matmul(out=pc2[:], lhsT=s["SC"][:, h:h + 1],
                                             rhs=ctx.BB(b * C + h * 128, 128),
                                             start=True, stop=True))
                ctx.on("tensor", fmm2)

                def fev2(v, h=h, b=b):
                    ctx.w(v, "tt")
                    v.tensor_copy(out=s["SM"][0:1, 64:192], in_=pc2[0:1, :])
                    v.drain()
                    dst = ctx.BB(b * C + h * 128, 128)
                    v.tensor_tensor(out=dst, in0=ps2[:],
                                    in1=ctx.BA(b * C + h * 128, 128), op=ALU.mult)
                    v.drain()
                    ctx.tick("tv", v.engine_nop())
                ctx.on("vector", fev2)

    ctx.barrier()
    if upto == "scan":
        return
    with nc.psum_tensor(f"ptr0_{ctx.uid()}", [128, 128], BF16) as q0, \
         nc.psum_tensor(f"ptr1_{ctx.uid()}", [128, 128], BF16) as q1:
        qp = [q0, q1]
        i = 0
        for cc in range(NC):
            for t in range(NT):
                def ftp(te, cc=cc, t=t, i=i):
                    ctx.w(te, "tv")
                    ctx.tick("tt", te.transpose(
                        out=qp[i % 2][:], in_=ctx.BB(t * C + cc * 128, 128),
                        identity=s["ID"][:]))

                def ftc(v, cc=cc, t=t, i=i):
                    ctx.w(v, "tt")
                    ctx.tick("tv", v.tensor_copy(
                        out=ctx.BA(cc * T_LOC + t * 128, 128), in_=qp[i % 2][:]))
                ctx.on("tensor", ftp)
                ctx.on("vector", ftc)
                i += 1

    def wo_evac(t, oq, pb):
        def fv(v, t=t, oq=oq, pb=pb):
            ctx.w(v, "tt")
            ctx.tick("tv", v.tensor_copy(out=ctx.BB(t * C + oq * 512, 512), in_=pb[:]))
        ctx.on("vector", fv)

    emit_proj(ctx, T_LOC, 0, NT, C,
              lambda cc, o0, ol: d["l0_woT"][128 * cc:128 * (cc + 1), o0:o0 + ol],
              F32, wo_evac,
              lhsT_fn=lambda cc, a, n: ctx.BA(cc * T_LOC + a, n))

    ctx.barrier()
    emit_bc_table(ctx, d["l0_lnw"])
    for t in range(NT):
        def fln(v, t=t):
            ctx.w(v, "dl")
            ctx.w(v, "dg")
            ctx.w(v, "tv")
            y = ctx.BB(t * C, C)
            v.reduce_sum(out=s["ST"][:, 24:25], in_=y, axis=AX)
            v.drain()
            v.tensor_scalar_mul(out=s["ST"][:, 24:25], in0=s["ST"][:, 24:25],
                                scalar1=1.0 / C)
            v.drain()
            ctx.tick("tv", v.engine_nop())
        ctx.on("vector", fln)

        def fln_s(sc, t=t):
            ctx.w(sc, "tv")
            y = ctx.BB(t * C, C)
            for k in range(4):
                sc.activation(out=s["TA"][:, 0:512], in_=y[:, 512 * k:512 * (k + 1)],
                              func=AF.Square, bias=s["ST"][:, 24:25], scale=-1.0,
                              accum_out=s["ST"][:, 40 + k:41 + k])
            ctx.tick("ts", sc.copy(out=s["SM"][:, 191:192], in_=s["SM"][:, 191:192]))
        ctx.on("scalar", fln_s)

        def fln_v(v, t=t):
            ctx.w(v, "ts")
            v.tensor_add(out=s["ST"][:, 40:41], in0=s["ST"][:, 40:41],
                         in1=s["ST"][:, 41:42])
            v.tensor_add(out=s["ST"][:, 42:43], in0=s["ST"][:, 42:43],
                         in1=s["ST"][:, 43:44])
            v.drain()
            v.tensor_add(out=s["ST"][:, 40:41], in0=s["ST"][:, 40:41],
                         in1=s["ST"][:, 42:43])
            v.drain()
            ctx.tick("tv", v.tensor_scalar(out=s["ST"][:, 26:27], in0=s["ST"][:, 40:41],
                                           scalar1=1.0 / C, scalar2=LN_EPS,
                                           op0=ALU.mult, op1=ALU.add))
        ctx.on("vector", fln_v)

        def fln_s2(sc, t=t):
            ctx.w(sc, "tv")
            sc.activation(out=s["ST"][:, 26:27], in_=s["ST"][:, 26:27], func=AF.Ln)
            ctx.tick("ts", sc.activation(out=s["ST"][:, 26:27], in_=s["ST"][:, 26:27],
                                         func=AF.Exp, scale=-0.5))
        ctx.on("scalar", fln_s2)

        def fln2(v, t=t):
            ctx.w(v, "ts")
            y = ctx.BB(t * C, C)
            v.tensor_scalar(out=s["TA"][:], in0=y, scalar1=s["ST"][:, 24:25],
                            scalar2=s["ST"][:, 26:27], op0=ALU.subtract, op1=ALU.mult)
            v.tensor_tensor(out=s["TA"][:], in0=s["TA"][:], in1=s["BC"][:], op=ALU.mult)
            xr = ctx.xr(1 + t)
            ctx.tick("tv", v.tensor_add(out=xr, in0=xr, in1=s["TA"][:]))
        ctx.on("vector", fln2)

    ctx.barrier()
    emit_bc_table(ctx, d["l0_lnb"])
    import os as _os
    if not _os.environ.get("SKIP_LNB"):
        for t in range(NT):
            def flb(v, t=t):
                ctx.w(v, "dl")
                ctx.w(v, "dg")
                xr = ctx.xr(1 + t)
                ctx.tick("tv", v.tensor_add(out=xr, in0=xr, in1=s["BC"][:]))
            ctx.on("vector", flb)
    ctx.barrier()



def emit_l1_attn(ctx, upto="full", tb=0):
    nc, s, d = ctx.nc, ctx.s, ctx.d
    Tt = T_LOC
    ctx.barrier()

    def fld(sync):
        ctx.dtick("dl", sync.dma_start(out=s["ST"][:, 48:64],
                                       in_=d["l1_slope"].broadcast_to([128, H])))
        ctx.dtick("dl", sync.dma_start(out=s["CW"][:, 0:2], in_=d["l1_km"]))
        ctx.dtick("dl", sync.dma_start(out=s["ST"][:, 32:40],
                                       in_=d["cc_pownT"].broadcast_to([128, 8])))
        ctx.dtick("dl", sync.dma_start(out=s["ST"][:, 40:48],
                                       in_=d["cc_selT"].broadcast_to([128, 8])))
    ctx.on("sync", fld)
    PS = lambda h: s["ST"][:, 48 + h:49 + h]

    def silu_evac(dstf):
        def evac(t, oq, pb):
            def fs(sc, t=t, oq=oq, pb=pb):
                ctx.w(sc, "tt")
                ctx.tick("ts", sc.activation(
                    out=dstf(t)[:, oq * 512:(oq + 1) * 512], in_=pb[:], func=AF.Silu))
            ctx.on("scalar", fs)
        return evac

    emit_proj(ctx, Tt, 0, NT, C,
              lambda cc, o0, ol: d["l1_wqkvT"][128 * cc:128 * (cc + 1),
                                               2048 + o0:2048 + o0 + ol],
              F32, silu_evac(lambda t: ctx.BA(t * C, C)))
    emit_proj(ctx, Tt, 0, NT, C,
              lambda cc, o0, ol: d["l1_wqkvT"][128 * cc:128 * (cc + 1),
                                               4096 + o0:4096 + o0 + ol],
              F32, silu_evac(lambda t: ctx.BB(t * C, C)))

    if upto == "proj":
        return
    with nc.psum_tensor(f"pkv1_{ctx.uid()}", [128, 128], F32) as pkv:
        for h in range(H):
            def fkd(sc, h=h):
                if h == 0:
                    ctx.w(sc, "ts")
                    ctx.w(sc, "dl")
                ctx.w(sc, "tv")
                ctx.tick("ts", sc.activation(out=s["SM"][:, 2:4], in_=s["CW"][:, 0:2],
                                             func=AF.Exp, scale=PS(h)))
            ctx.on("scalar", fkd)

            def fbd(v, h=h):
                ctx.w(v, "ts")
                v.tensor_scalar_mul(out=s["SM"][:, 4:5], in0=PS(h), scalar1=-256.0)
                v.drain()
                ctx.tick("tv", v.engine_nop())
            ctx.on("vector", fbd)

            def fbd2(sc, h=h):
                ctx.w(sc, "tv")
                ctx.tick("ts", sc.activation(out=s["SM"][:, 4:5], in_=s["SM"][:, 4:5],
                                             func=AF.Exp))
            ctx.on("scalar", fbd2)

            for b in range(4):
                for jh in range(2):
                    def fks(v, h=h, b=b, jh=jh):
                        ctx.w(v, "ts")
                        ctx.w(v, "tt")
                        ksl = ctx.BA((2 * b + jh) * C + h * 128, 128)
                        v.tensor_scalar(
                            out=s["BC"][:, 128:256], in0=ksl,
                            scalar1=s["SM"][:, 2 + jh:3 + jh], scalar2=None,
                            op0=ALU.mult)
                        v.drain()
                        ctx.tick("tv", v.engine_nop())
                    ctx.on("vector", fks)

                    def fkm(te, h=h, b=b, jh=jh):
                        ctx.w(te, "tv")
                        ctx.tick("tt", te.matmul(
                            out=pkv[:], lhsT=s["BC"][:, 128:256],
                            rhs=ctx.BB((2 * b + jh) * C + h * 128, 128),
                            start=(jh == 0), stop=(jh == 1)))
                    ctx.on("tensor", fkm)

                def fkc(v, h=h, b=b):
                    ctx.w(v, "tt")
                    kvs = s["SM"][:, 64:192]
                    if b == 0:
                        inst = v.tensor_copy(out=kvs, in_=pkv[:])
                    else:
                        inst = v.scalar_tensor_tensor(out=kvs, in0=kvs,
                                                      scalar=s["SM"][:, 4:5], in1=pkv[:],
                                                      op0=ALU.mult, op1=ALU.add)
                    ctx.tick("tv", inst)
                    v.drain()
                ctx.on("vector", fkc)

            def fst(g, h=h):
                ctx.w(g, "tv")
                ctx.dtick("dg", g.dma_start(out=d["cin1"][h * 128:(h + 1) * 128, :],
                                            in_=s["SM"][:, 64:192]))
            ctx.on("gpsimd", fst)

    def fcc(g):
        ctx.w(g, "dg")
        ctx.tick("cc", g.collective_compute(
            "AllGather", ALU.bypass, replica_groups=RG, ins=[d["cin1"]],
            outs=[d["cout1"]]))
    ctx.on("gpsimd", fcc)

    for grp in range(4):
        emit_proj(ctx, Tt, 0, NT, 512,
                  lambda cc, o0, ol, grp=grp: d["l1_wqkvT"][128 * cc:128 * (cc + 1),
                                                            grp * 512 + o0:grp * 512 + o0 + ol],
                  F32, silu_evac(lambda t: s["QG"][:, t * 512:(t + 1) * 512]))

        with nc.psum_tensor(f"pq0_{ctx.uid()}", [128, 256], F32) as pq0, \
             nc.psum_tensor(f"pq1_{ctx.uid()}", [128, 256], F32) as pq1, \
             nc.psum_tensor(f"po0_{ctx.uid()}", [128, 128], F32) as po0, \
             nc.psum_tensor(f"po1_{ctx.uid()}", [128, 128], F32) as po1, \
             nc.psum_tensor(f"pkv2_{ctx.uid()}", [128, 128], F32) as pkv2, \
             nc.psum_tensor(f"ptq0_{ctx.uid()}", [128, 128], BF16) as ptq0, \
             nc.psum_tensor(f"ptq1_{ctx.uid()}", [128, 128], BF16) as ptq1:
            for hh in range(4):
                h = grp * 4 + hh

                def fcmb(g, h=h):
                    if h == 0:
                        ctx.w(g, "cc")
                    ctx.w(g, "tv")
                    ctx.w(g, "ts")
                    for j in range(8):
                        ctx.dtick("dg", g.dma_start(
                            out=s["TA"][:, j * 128:(j + 1) * 128],
                            in_=d["cout1"][j * 2048 + h * 128:j * 2048 + (h + 1) * 128, :]))
                ctx.on("gpsimd", fcmb)

                def fw8(sc, h=h):
                    ctx.w(sc, "dg")
                    ctx.tick("ts", sc.activation(out=s["SM"][:, 8:16],
                                                 in_=s["ST"][:, 32:40],
                                                 func=AF.Exp, scale=PS(h)))
                ctx.on("scalar", fw8)

                def fw8b(v, h=h):
                    ctx.w(v, "ts")
                    v.tensor_tensor(out=s["SM"][:, 8:16], in0=s["SM"][:, 8:16],
                                    in1=s["ST"][:, 40:48], op=ALU.mult)
                    kvs = s["SM"][:, 64:192]
                    v.drain()
                    v.tensor_scalar(out=kvs, in0=s["TA"][:, 0:128],
                                    scalar1=s["SM"][:, 8:9], scalar2=None, op0=ALU.mult)
                    for j in range(1, 8):
                        v.drain()
                        v.scalar_tensor_tensor(out=kvs,
                                               in0=s["TA"][:, j * 128:(j + 1) * 128],
                                               scalar=s["SM"][:, 8 + j:9 + j], in1=kvs,
                                               op0=ALU.mult, op1=ALU.add)
                    ctx.tick("tv", v.engine_nop())
                ctx.on("vector", fw8b)

                def ftl(sync, h=h):
                    ctx.w(sync, "tv")
                    ctx.dtick("dl", sync.dma_start(out=s["TA"][:, 0:256], in_=d["l1_i0"]))
                    ctx.dtick("dl", sync.dma_start(out=s["TA"][:, 256:512], in_=d["l1_i1"]))
                    ctx.dtick("dl", sync.dma_start(out=s["TA"][:, 512:768], in_=d["l1_msk0"]))
                    ctx.dtick("dl", sync.dma_start(out=s["TA"][:, 768:1024], in_=d["l1_msk1"]))
                    ctx.dtick("dl", sync.dma_start(out=s["TA"][:, 1024:2048],
                                                   in_=d["l1_qm"].broadcast_to([128, T_LOC])))
                ctx.on("sync", ftl)

                def fdec(sc, h=h):
                    ctx.w(sc, "dl")
                    sc.activation(out=s["TA"][:, 0:256], in_=s["TA"][:, 0:256],
                                  func=AF.Exp, scale=PS(h))
                    sc.activation(out=s["TA"][:, 256:512], in_=s["TA"][:, 256:512],
                                  func=AF.Exp, scale=PS(h))
                    ctx.tick("ts", sc.activation(out=s["TA"][:, 1024:2048],
                                                 in_=s["TA"][:, 1024:2048],
                                                 func=AF.Exp, scale=PS(h)))
                ctx.on("scalar", fdec)

                def fdecm(v, h=h):
                    ctx.w(v, "ts")
                    v.tensor_tensor(out=s["SC"][:, 0:256], in0=s["TA"][:, 0:256],
                                    in1=s["TA"][:, 512:768], op=ALU.mult)
                    ctx.tick("tv", v.tensor_tensor(out=s["SC"][:, 256:512],
                                                   in0=s["TA"][:, 256:512],
                                                   in1=s["TA"][:, 768:1024], op=ALU.mult))
                ctx.on("vector", fdecm)

                for t in range(NT):
                    def ftr(te, h=h, hh=hh, t=t):
                        ctx.w(te, "tv")
                        te.transpose(out=ptq0[:], in_=ctx.BA(t * C + h * 128, 128),
                                     identity=s["ID"][:])
                        ctx.tick("tt", te.transpose(
                            out=ptq1[:],
                            in_=s["QG"][:, t * 512 + hh * 128:t * 512 + (hh + 1) * 128],
                            identity=s["ID"][:]))
                    ctx.on("tensor", ftr)

                    def ftrc(v, t=t):
                        ctx.w(v, "tt")
                        v.tensor_copy(out=s["WSa"][:, 1024 + t * 128:1024 + (t + 1) * 128],
                                      in_=ptq0[:])
                        ctx.tick("tv", v.tensor_copy(
                            out=s["WSa"][:, t * 128:(t + 1) * 128], in_=ptq1[:]))
                    ctx.on("vector", ftrc)

                def fqd(v, h=h):
                    ctx.w(v, "tv")
                    ctx.tick("tv", v.tensor_tensor(out=s["WSb"][:, 0:1024],
                                                   in0=s["WSa"][:, 0:1024],
                                                   in1=s["TA"][:, 1024:2048],
                                                   op=ALU.mult))
                ctx.on("vector", fqd)

                def fkd2(sc, h=h):
                    ctx.w(sc, "tv")
                    ctx.tick("ts", sc.activation(out=s["SM"][:, 2:4], in_=s["CW"][:, 0:2],
                                                 func=AF.Exp, scale=PS(h)))
                ctx.on("scalar", fkd2)

                def fbd_2(v, h=h):
                    ctx.w(v, "ts")
                    v.tensor_scalar_mul(out=s["SM"][:, 4:5], in0=PS(h), scalar1=-256.0)
                    v.drain()
                    ctx.tick("tv", v.engine_nop())
                ctx.on("vector", fbd_2)

                def fbd2_2(sc, h=h):
                    ctx.w(sc, "tv")
                    ctx.tick("ts", sc.activation(out=s["SM"][:, 4:5], in_=s["SM"][:, 4:5],
                                                 func=AF.Exp))
                ctx.on("scalar", fbd2_2)

                for b in range(4):
                    def fkb(v, h=h, b=b):
                        ctx.w(v, "ts")
                        ctx.w(v, "tt")
                        v.tensor_copy(out=s["BC"][:, 0:128], in_=s["SM"][:, 64:192])
                        v.tensor_scalar(out=s["BC"][:, 128:256],
                                        in0=ctx.BA((2 * b) * C + h * 128, 128),
                                        scalar1=s["SM"][:, 2:3], scalar2=None,
                                        op0=ALU.mult)
                        v.tensor_scalar(
                            out=s["BC"][:, 256:384],
                            in0=ctx.BA((2 * b + 1) * C + h * 128, 128),
                            scalar1=s["SM"][:, 3:4], scalar2=None, op0=ALU.mult)
                        v.drain()
                        ctx.tick("tv", v.engine_nop())
                    ctx.on("vector", fkb)

                    def fqk(te, h=h, b=b):
                        ctx.w(te, "tv")
                        te.matmul(out=pq0[:],
                                  lhsT=s["WSa"][:, 1024 + (2 * b) * 128:1024 + (2 * b + 1) * 128],
                                  rhs=s["WSa"][:, b * 256:(b + 1) * 256],
                                  start=True, stop=True)
                        ctx.tick("tt", te.matmul(
                            out=pq1[:],
                            lhsT=s["WSa"][:, 1024 + (2 * b + 1) * 128:1024 + (2 * b + 2) * 128],
                            rhs=s["WSa"][:, b * 256:(b + 1) * 256],
                            start=True, stop=True))
                    ctx.on("tensor", fqk)

                    def fqs(v, b=b):
                        ctx.w(v, "tt")
                        v.tensor_tensor(out=s["WSb"][:, 1024:1280], in0=pq0[:],
                                        in1=s["SC"][:, 0:256], op=ALU.mult)
                        v.tensor_tensor(out=s["WSb"][:, 1280:1536], in0=pq1[:],
                                        in1=s["SC"][:, 256:512], op=ALU.mult)
                        v.drain()
                        ctx.tick("tv", v.engine_nop())
                    ctx.on("vector", fqs)

                    def fo(te, h=h, b=b):
                        ctx.w(te, "tv")
                        for mh, po in ((0, po0), (1, po1)):
                            te.matmul(out=po[:],
                                      lhsT=s["WSb"][:, 1024 + mh * 128:1152 + mh * 128],
                                      rhs=ctx.BB((2 * b) * C + h * 128, 128),
                                      start=True, stop=False)
                            te.matmul(out=po[:],
                                      lhsT=s["WSb"][:, 1280 + mh * 128:1408 + mh * 128],
                                      rhs=ctx.BB((2 * b + 1) * C + h * 128, 128),
                                      start=False, stop=False)
                            te.matmul(out=po[:],
                                      lhsT=s["WSb"][:, b * 256 + mh * 128:b * 256 + (mh + 1) * 128],
                                      rhs=s["BC"][:, 0:128],
                                      start=False, stop=True)
                        te.matmul(out=pkv2[:], lhsT=s["BC"][:, 128:256],
                                  rhs=ctx.BB((2 * b) * C + h * 128, 128),
                                  start=True, stop=False)
                        ctx.tick("tt", te.matmul(
                            out=pkv2[:], lhsT=s["BC"][:, 256:384],
                            rhs=ctx.BB((2 * b + 1) * C + h * 128, 128),
                            start=False, stop=True))
                    ctx.on("tensor", fo)

                    def fev(v, h=h, b=b):
                        ctx.w(v, "tt")
                        v.tensor_copy(out=ctx.BA((2 * b) * C + h * 128, 128), in_=po0[:])
                        v.tensor_copy(out=ctx.BA((2 * b + 1) * C + h * 128, 128),
                                      in_=po1[:])
                        kvs = s["SM"][:, 64:192]
                        ctx.tick("tv", v.scalar_tensor_tensor(
                            out=kvs, in0=kvs, scalar=s["SM"][:, 4:5], in1=pkv2[:],
                            op0=ALU.mult, op1=ALU.add))
                        v.drain()
                    ctx.on("vector", fev)

    ctx.barrier()
    if upto == "attn":
        return
    emit_bc_table(ctx, d["l1_normw"])

    for t in range(NT):
        def frs_s(sc, t=t):
            if t == 0:
                ctx.w(sc, "dl")
                ctx.w(sc, "dg")
                ctx.w(sc, "tv")
            ao = ctx.BA(t * C, C)
            for k in range(4):
                sc.activation(out=s["TA"][:, 0:512], in_=ao[:, 512 * k:512 * (k + 1)],
                              func=AF.Square, accum_out=s["ST"][:, 40 + k:41 + k])
            ctx.tick("ts", sc.copy(out=s["SM"][:, 191:192], in_=s["SM"][:, 191:192]))
        ctx.on("scalar", frs_s)

        def frs_v(v, t=t):
            ctx.w(v, "ts")
            rcol = s["SM"][:, 24 + t:25 + t]
            v.tensor_add(out=s["ST"][:, 40:41], in0=s["ST"][:, 40:41],
                         in1=s["ST"][:, 41:42])
            v.tensor_add(out=s["ST"][:, 42:43], in0=s["ST"][:, 42:43],
                         in1=s["ST"][:, 43:44])
            v.drain()
            v.tensor_add(out=s["ST"][:, 40:41], in0=s["ST"][:, 40:41],
                         in1=s["ST"][:, 42:43])
            v.drain()
            v.tensor_scalar(out=rcol, in0=s["ST"][:, 40:41],
                            scalar1=1.0 / C, scalar2=L1_RMS_EPS,
                            op0=ALU.mult, op1=ALU.add)
            v.drain()
            ctx.tick("tv", v.engine_nop())
        ctx.on("vector", frs_v)

        def frs_s2(sc, t=t):
            ctx.w(sc, "tv")
            rcol = s["SM"][:, 24 + t:25 + t]
            sc.activation(out=rcol, in_=rcol, func=AF.Ln)
            ctx.tick("ts", sc.activation(out=rcol, in_=rcol, func=AF.Exp, scale=-0.5))
        ctx.on("scalar", frs_s2)

    def gt_evac(t, oq, pb):
        def fg1(sc, t=t, oq=oq, pb=pb):
            ctx.w(sc, "tt")
            ctx.w(sc, "tv")
            ctx.tick("ts", sc.activation(out=s["TA"][:, 0:512], in_=pb[:],
                                         func=AF.Sigmoid))
        ctx.on("scalar", fg1)

        def fg2(v, t=t, oq=oq):
            ctx.w(v, "ts")
            ctx.w(v, "dg")
            v.tensor_tensor(out=s["TA"][:, 0:512], in0=s["TA"][:, 0:512],
                            in1=ctx.BA(t * C + oq * 512, 512), op=ALU.mult)
            v.tensor_scalar(out=s["TA"][:, 0:512], in0=s["TA"][:, 0:512],
                            scalar1=s["SM"][:, 24 + t:25 + t], scalar2=None,
                            op0=ALU.mult)
            ctx.tick("tv", v.tensor_tensor(
                out=ctx.BB(t * C + oq * 512, 512), in0=s["TA"][:, 0:512],
                in1=s["BC"][:, oq * 512:(oq + 1) * 512], op=ALU.mult))
        ctx.on("vector", fg2)

    emit_proj(ctx, Tt, 0, NT, C,
              lambda cc, o0, ol: d["l1_wgT"][128 * cc:128 * (cc + 1), o0:o0 + ol],
              F32, gt_evac)

    with nc.psum_tensor(f"ptg0_{ctx.uid()}", [128, 128], BF16) as g0, \
         nc.psum_tensor(f"ptg1_{ctx.uid()}", [128, 128], BF16) as g1:
        gp = [g0, g1]
        i = 0
        for cc in range(NC):
            for t in range(NT):
                def ftp(te, cc=cc, t=t, i=i):
                    ctx.w(te, "tv")
                    ctx.tick("tt", te.transpose(
                        out=gp[i % 2][:], in_=ctx.BB(t * C + cc * 128, 128),
                        identity=s["ID"][:]))

                def ftc(v, cc=cc, t=t, i=i):
                    ctx.w(v, "tt")
                    ctx.tick("tv", v.tensor_copy(
                        out=ctx.BA(cc * T_LOC + t * 128, 128), in_=gp[i % 2][:]))
                ctx.on("tensor", ftp)
                ctx.on("vector", ftc)
                i += 1

    def out_evac(t, oq, pb):
        def fv(v, t=t, oq=oq, pb=pb):
            ctx.w(v, "tt")
            xr = ctx.xr(tb + t)
            ctx.tick("tv", v.tensor_add(out=xr[:, oq * 512:(oq + 1) * 512],
                                        in0=xr[:, oq * 512:(oq + 1) * 512], in1=pb[:]))
        ctx.on("vector", fv)

    emit_proj(ctx, T_LOC, 0, NT, C,
              lambda cc, o0, ol: d["l1_woT"][128 * cc:128 * (cc + 1), o0:o0 + ol],
              F32, out_evac,
              lhsT_fn=lambda cc, a, n: ctx.BA(cc * T_LOC + a, n))
    ctx.barrier()



def declare_dram(ctx, nc, fixed, pc, part):
    d = ctx.d
    need = {
        1: ["l0_pgT", "l0_puT", "l0_sg", "l0_su", "l0_pdT", "l0_sd", "shfcol",
            "identity"],
        2: ["l0_wqT", "l0_wkT", "l0_wvT", "l0_woT", "l0_wgT", "l0_convw", "l0_convb",
            "l0_gateb", "l0_lnw", "l0_lnb", "l0_gamma", "l0_gamma_rep", "identity",
            "iota_mj", "mask_mj", "grev", "shfcol", "cc_sel", "cc_pow"],
        4: ["l1_qgT", "l1_quT", "l1_qdT", "l1_sgc", "l1_suc", "l1_sd", "shfcol",
            "identity"],
        5: ["l1_slope", "l1_wqkvT", "l1_wgT", "l1_woT", "l1_normw", "l1_i0", "l1_i1",
            "l1_msk0", "l1_msk1", "l1_qm", "l1_km", "shfcol", "identity",
            "cc_selT", "cc_pownT"],
    }
    names = need[2 if part == 2 else (5 if part == 5 else (1 if part in (1, 3) else 4))]
    srcs = dict(fixed)
    srcs.update(pc)
    dtmap = {np.dtype(np.float32): F32, np.dtype(np.int32): I32,
             np.dtype(np.float16): F16}
    Tin = T_EXT if part <= 2 else T_LOC
    Tout = T_EXT if part == 1 else T_LOC
    d["xin"] = nc.dram_tensor("xin", [Tin, C], F32, kind="ExternalInput").ap()
    d["xout"] = nc.dram_tensor("xout", [Tout, C], F32, kind="ExternalOutput").ap()
    feed = {"xin": srcs["xin"][:Tin]}
    for n in names:
        a = srcs[n]
        d[n] = nc.dram_tensor(n, list(a.shape), dtmap[a.dtype], kind="ExternalInput").ap()
        feed[n] = a
    if part in (1, 3):
        d["cin_w"] = nc.dram_tensor("cin_w", [35 * 128, 1024], BF16, kind="Internal").ap()
        d["cout_w"] = nc.dram_tensor("cout_w", [280 * 128, 1024], BF16, kind="Internal", addr_space="Shared").ap()
        for n in ("myshard", "myscale"):
            a = pc[n]
            d[n] = nc.dram_tensor(n, list(a.shape), dtmap[a.dtype], kind="ExternalInput").ap()
    if part == 2:
        d["xc_d"] = nc.dram_tensor("xc_d", [NC, 128, T_LOC], BF16, kind="Internal").ap()
        d["cin0"] = nc.dram_tensor("cin0", [1, C], F32, kind="Internal").ap()
        d["cout0"] = nc.dram_tensor("cout0", [8, C], F32, kind="Internal").ap()
    if part == 5:
        d["cin1"] = nc.dram_tensor("cin1", [C, 128], F32, kind="Internal").ap()
        d["cout1"] = nc.dram_tensor("cout1", [8 * C, 128], F32, kind="Internal").ap()
    return feed


def build_part(part, fixed, per_core):
    nc = bass.Bass("TRN2", target_bir_lowering=False, debug=False)
    es = ExitStack()
    with es:
        ctx = Ctx(nc, es)
        declare_dram(ctx, nc, fixed, per_core[0], part)
        alloc_common(ctx, nc, es)
        block = es.enter_context(nc.Block())
        ctx.block = block
        emit_preamble(ctx)
        if part == 1:
            emit_load_x(ctx, NTE)
            emit_l0_dequant(ctx)
            emit_ln(ctx, 0, NTE, False)
            emit_ffn(ctx, "l0", 0, NTE, NTE * 128)
            emit_store_x(ctx, 0, NTE)
        elif part == 2:
            emit_load_x(ctx, NTE)
            emit_ln(ctx, 0, NTE, False)
            emit_l0_attn(ctx)
            emit_store_x(ctx, 1, NT)
        elif part == 3:
            emit_load_x(ctx, NT)
            emit_l0_dequant(ctx)
            emit_ln(ctx, 0, NT, False)
            emit_ffn(ctx, "l0", 0, NT, T_LOC)
            emit_store_x(ctx, 0, NT)
        elif part in (4, 6):
            emit_load_x(ctx, NT)
            emit_ln(ctx, 0, NT, False)
            emit_ffn(ctx, "l1", 0, NT, T_LOC)
            emit_store_x(ctx, 0, NT)
        elif part == 5:
            emit_load_x(ctx, NT)
            emit_ln(ctx, 0, NT, False)
            emit_l1_attn(ctx)
            emit_store_x(ctx, 0, NT)
    return nc


def feeds_for(part, fixed, per_core):
    need = {
        1: ["l0_pgT", "l0_puT", "l0_sg", "l0_su", "l0_pdT", "l0_sd", "shfcol",
            "identity"],
        2: ["l0_wqT", "l0_wkT", "l0_wvT", "l0_woT", "l0_wgT", "l0_convw", "l0_convb",
            "l0_gateb", "l0_lnw", "l0_lnb", "l0_gamma", "l0_gamma_rep", "identity",
            "iota_mj", "mask_mj", "grev", "shfcol", "cc_sel", "cc_pow"],
        4: ["l1_qgT", "l1_quT", "l1_qdT", "l1_sgc", "l1_suc", "l1_sd", "shfcol",
            "identity"],
        5: ["l1_slope", "l1_wqkvT", "l1_wgT", "l1_woT", "l1_normw", "l1_i0", "l1_i1",
            "l1_msk0", "l1_msk1", "l1_qm", "l1_km", "shfcol", "identity",
            "cc_selT", "cc_pownT"],
    }
    names = need[2 if part == 2 else (5 if part == 5 else (1 if part in (1, 3) else 4))]
    Tin = T_EXT if part <= 2 else T_LOC
    feeds = []
    for pc in per_core:
        f = {"xin": np.ascontiguousarray(pc["xin"][:Tin])}
        if part in (1, 3):
            f["myshard"] = pc["myshard"]
            f["myscale"] = pc["myscale"]
        for n in names:
            f[n] = pc[n] if n in pc else fixed[n]
        feeds.append(f)
    return feeds


PARTS = [1, 2, 3, 4, 5, 6]


ALL_NAMES = sorted(set(
    ["l0_pgT", "l0_puT", "l0_sg", "l0_su", "l0_pdT", "l0_sd", "shfcol", "identity",
     "l0_wqT", "l0_wkT", "l0_wvT", "l0_woT", "l0_wgT", "l0_convw", "l0_convb",
     "l0_gateb", "l0_lnw", "l0_lnb", "l0_gamma", "l0_gamma_rep",
     "iota_mj", "mask_mj", "grev", "cc_sel", "cc_pow",
     "l1_qgT", "l1_quT", "l1_qdT", "l1_sgc", "l1_suc", "l1_sd",
     "l1_slope", "l1_wqkvT", "l1_wgT", "l1_woT", "l1_normw", "l1_i0", "l1_i1",
     "l1_msk0", "l1_msk1", "l1_qm", "l1_km", "cc_selT", "cc_pownT"]))


def declare_dram_fused(ctx, nc, fixed, pc):
    d = ctx.d
    srcs = dict(fixed)
    srcs.update(pc)
    dtmap = {np.dtype(np.float32): F32, np.dtype(np.int32): I32,
             np.dtype(np.float16): F16}
    d["xin"] = nc.dram_tensor("xin", [T_EXT, C], F32, kind="ExternalInput").ap()
    d["xout"] = nc.dram_tensor("xout", [T_LOC, C], F32, kind="ExternalOutput").ap()
    for n in ALL_NAMES:
        a = srcs[n]
        d[n] = nc.dram_tensor(n, list(a.shape), dtmap[a.dtype], kind="ExternalInput").ap()
    d["cin_w"] = nc.dram_tensor("cin_w", [35 * 128, 1024], BF16, kind="Internal").ap()
    d["cout_w"] = nc.dram_tensor("cout_w", [280 * 128, 1024], BF16, kind="Internal", addr_space="Shared").ap()
    for n in ("myshard", "myscale"):
        a = pc[n]
        d[n] = nc.dram_tensor(n, list(a.shape), dtmap[a.dtype], kind="ExternalInput").ap()
    d["xc_d"] = nc.dram_tensor("xc_d", [NC, 128, T_LOC], BF16, kind="Internal").ap()
    d["cin0"] = nc.dram_tensor("cin0", [1, C], F32, kind="Internal").ap()
    d["cout0"] = nc.dram_tensor("cout0", [8, C], F32, kind="Internal").ap()
    d["cin1"] = nc.dram_tensor("cin1", [C, 128], F32, kind="Internal").ap()
    d["cout1"] = nc.dram_tensor("cout1", [8 * C, 128], F32, kind="Internal").ap()


def build_fused(fixed, per_core):
    nc = bass.Bass("TRN2", target_bir_lowering=False, debug=False)
    es = ExitStack()
    with es:
        ctx = Ctx(nc, es)
        declare_dram_fused(ctx, nc, fixed, per_core[0])
        alloc_common(ctx, nc, es)
        block = es.enter_context(nc.Block())
        ctx.block = block
        import os as _os
        nsub = int(_os.environ.get("N_SUB", "6"))
        emit_preamble(ctx)
        emit_load_x(ctx, NTE)
        emit_l0_dequant(ctx)
        emit_ln(ctx, 0, NTE, False)
        emit_ffn(ctx, "l0", 0, NTE, NTE * 128)
        if nsub >= 2:
            emit_ln(ctx, 0, NTE, False)
            emit_l0_attn(ctx)
        if nsub >= 3:
            emit_ln(ctx, 1, NT, False)
            emit_ffn(ctx, "l0", 1, NT, T_LOC)
        if nsub >= 4:
            emit_ln(ctx, 1, NT, False)
            emit_ffn(ctx, "l1", 1, NT, T_LOC)
        if nsub >= 5:
            emit_ln(ctx, 1, NT, False)
            emit_l1_attn(ctx, tb=1)
        if nsub >= 6:
            emit_ln(ctx, 1, NT, False)
            emit_ffn(ctx, "l1", 1, NT, T_LOC)
        if _os.environ.get("DBG_BIG"):
            dbg = nc.dram_tensor("dbg", [128, 2 * NT * C], BF16,
                                 kind="ExternalOutput").ap()

            def fdbg(g):
                ctx.wall(g)
                ctx.dtick("dg", g.dma_start(out=dbg, in_=ctx.s["BIG"][:, :]))
                ctx.w(g, "dg")
            ctx.on("gpsimd", fdbg)
        emit_store_x(ctx, 1, NT)
    return nc


def feeds_fused(fixed, per_core):
    feeds = []
    for pc in per_core:
        f = {"xin": np.ascontiguousarray(pc["xin"]),
             "myshard": pc["myshard"], "myscale": pc["myscale"]}
        for n in ALL_NAMES:
            f[n] = pc[n] if n in pc else fixed[n]
        feeds.append(f)
    return feeds


FUSED = True


def kernel(**inputs):
    fixed, per_core = prep(inputs)
    if FUSED:
        nc = build_fused(fixed, per_core)
        feeds = feeds_fused(fixed, per_core)
        res = run_bass_kernel_spmd(nc, feeds, core_ids=list(range(NCORES)))
        xs = [np.asarray(r["xout"]) for r in res.results]
        return np.concatenate(xs, 0).reshape(1, 8192, C).astype(np.float32)
    xs = [pc["xin"] for pc in per_core]
    for part in PARTS:
        for i in range(NCORES):
            per_core[i]["xin"] = xs[i]
        nc = build_part(part, fixed, per_core)
        feeds = feeds_for(part, fixed, per_core)
        res = run_bass_kernel_spmd(nc, feeds, core_ids=list(range(NCORES)))
        xs = [np.asarray(r["xout"]) for r in res.results]
    return np.concatenate(xs, 0).reshape(1, 8192, C).astype(np.float32)


# version 06
# speedup vs baseline: 1.6214x; 1.1040x over previous
import numpy as np
from contextlib import ExitStack

import concourse.bass as bass
import concourse.mybir as mybir
from concourse.bass_utils import run_bass_kernel_spmd

F32 = mybir.dt.float32
BF16 = mybir.dt.bfloat16
F16 = mybir.dt.float16
I32 = mybir.dt.int32
AF = mybir.ActivationFunctionType
ALU = mybir.AluOpType
AX = mybir.AxisListType.X

C = 2048
NC = 16
IF = 5632
NI = 44
H = 16
T_LOC = 1024
HALO = 128
T_EXT = T_LOC + HALO
NT = 8
NTE = 9
NCORES = 8
LN_EPS = 1e-5
L0_NORM_EPS = 1e-12
L0_FFN_EPS = 1e-6
L1_RMS_EPS = 1e-6
BLK1 = 256
RG = [[0, 1, 2, 3, 4, 5, 6, 7]]


def perm128(n):
    base = np.arange(n) // 128 * 128
    r = np.arange(n) % 128
    return base + 8 * (r % 16) + (r // 16)



def prep(inputs):
    f = {}
    x = np.asarray(inputs["L0_attn_x"], np.float32).reshape(8192, C)
    xp = np.concatenate([np.zeros((HALO, C), np.float32), x], 0)
    per_core = []
    for i in range(NCORES):
        st = i * T_LOC
        pc = {"xin": np.ascontiguousarray(xp[st:st + T_EXT])}
        sel = np.zeros((8, 1), np.float32)
        pw = np.zeros((8, 1), np.float32)
        for j in range(i):
            sel[j] = 1.0
            pw[j] = 1024.0 * float(i - 1 - j)
        pc["cc_sel"] = sel
        pc["cc_pow"] = pw
        pc["cc_selT"] = np.ascontiguousarray(sel.T)
        pc["cc_pownT"] = np.ascontiguousarray((-pw).T)
        per_core.append(pc)

    T = lambda a: np.ascontiguousarray(np.asarray(a).T)
    f["l0_wqT"] = T(inputs["L0_attn_Wq"])
    f["l0_wkT"] = T(inputs["L0_attn_Wk"])
    f["l0_wvT"] = T(inputs["L0_attn_Wv"])
    f["l0_woT"] = T(inputs["L0_attn_Wo"])
    f["l0_wgT"] = T(inputs["L0_attn_gate_w"])
    cw = np.asarray(inputs["L0_attn_conv_w"], np.float32).reshape(C, 4)
    f["l0_convw"] = np.ascontiguousarray(
        cw.reshape(NC, 128, 4).transpose(1, 0, 2).reshape(128, NC * 4))
    f["l0_convb"] = np.ascontiguousarray(
        np.asarray(inputs["L0_attn_conv_b"], np.float32).reshape(NC, 128).T)
    f["l0_gateb"] = np.asarray(inputs["L0_attn_gate_b"], np.float32).reshape(1, C)
    f["l0_lnw"] = np.asarray(inputs["L0_attn_ln_w"], np.float32).reshape(1, C)
    f["l0_lnb"] = np.asarray(inputs["L0_attn_ln_b"], np.float32).reshape(1, C)
    gam = np.asarray(inputs["L0_attn_gamma"], np.float32).reshape(H)
    f["l0_gamma"] = gam.reshape(1, H)
    f["l0_gamma_rep"] = np.ascontiguousarray(np.repeat(gam, 128).reshape(1, C))

    gup = np.asarray(inputs["L0_ffn_gate_up_packed"], np.int32)
    gsc = np.asarray(inputs["L0_ffn_gate_up_scales"])
    f["l0_pgT"] = T(gup[:IF])
    f["l0_puT"] = T(gup[IF:])
    f["l0_sg"] = np.ascontiguousarray(gsc[:, :IF])
    f["l0_su"] = np.ascontiguousarray(gsc[:, IF:])
    f["l0_pdT"] = T(np.asarray(inputs["L0_ffn_down_packed"], np.int32))
    f["l0_sd"] = np.ascontiguousarray(np.asarray(inputs["L0_ffn_down_scales"]))

    f["l1_slope"] = np.asarray(inputs["L1_attn_slope_rate"], np.float32).reshape(1, H)
    f["l1_wqkvT"] = T(inputs["L1_attn_w_qkv"])
    f["l1_wgT"] = T(inputs["L1_attn_w_gate"])
    f["l1_woT"] = T(inputs["L1_attn_w_out"])
    f["l1_normw"] = np.asarray(inputs["L1_attn_norm_w"], np.float32).reshape(1, C)
    f["l1_qgT"] = T(np.asarray(inputs["L1_ffn_wq_gate"], np.int32))
    f["l1_quT"] = T(np.asarray(inputs["L1_ffn_wq_up"], np.int32))
    f["l1_qdT"] = T(np.asarray(inputs["L1_ffn_wq_down"], np.int32))
    f["l1_sgc"] = np.ascontiguousarray(
        np.asarray(inputs["L1_ffn_s_gate"], np.float32).reshape(NI, 128).T)
    f["l1_suc"] = np.ascontiguousarray(
        np.asarray(inputs["L1_ffn_s_up"], np.float32).reshape(NI, 128).T)
    f["l1_sd"] = np.asarray(inputs["L1_ffn_s_down"], np.float32).reshape(1, C)

    f["identity"] = np.eye(128, dtype=np.float32)
    ar = np.arange(128, dtype=np.float32)
    f["iota_mj"] = np.ascontiguousarray(ar[None, :] - ar[:, None])
    f["mask_mj"] = np.ascontiguousarray((ar[None, :] >= ar[:, None]).astype(np.float32))
    f["grev"] = np.ascontiguousarray((127.0 - ar).reshape(128, 1))
    jm0 = np.arange(0, 128)[:, None] - np.arange(BLK1)[None, :]
    jm1 = np.arange(128, 256)[:, None] - np.arange(BLK1)[None, :]
    f["l1_i0"] = np.ascontiguousarray(jm0.astype(np.float32))
    f["l1_i1"] = np.ascontiguousarray(jm1.astype(np.float32))
    f["l1_msk0"] = np.ascontiguousarray((jm0 <= 0).astype(np.float32))
    f["l1_msk1"] = np.ascontiguousarray((jm1 <= 0).astype(np.float32))
    f["l1_qm"] = (-((np.arange(T_LOC) % BLK1) + 1.0)).astype(np.float32).reshape(1, T_LOC)
    f["l1_km"] = np.ascontiguousarray(np.stack(
        [-(255.0 - np.arange(0, 128)), -(255.0 - np.arange(128, 256))], 1).astype(np.float32))
    f["shfcol"] = (4 * (np.arange(128) // 16)).astype(np.int32).reshape(128, 1)

    pieces = []
    for cc in range(NC):
        for sl in range(6):
            pieces.append(("g", cc, sl))
    for cc in range(NC):
        for sl in range(6):
            pieces.append(("u", cc, sl))
    for kk in range(NI):
        for sl in range(2):
            pieces.append(("d", kk, sl))
    for j in range(NCORES):
        sh = np.zeros((35, 16, 1024), np.int32)
        sc = np.zeros((35, 1024), np.float16)
        for i, (kind, a, sl) in enumerate(pieces[35 * j:35 * (j + 1)]):
            if kind in ("g", "u"):
                w0 = sl * 1024
                wd = min(1024, IF - w0)
                pk = f["l0_pgT"] if kind == "g" else f["l0_puT"]
                ss = f["l0_sg"] if kind == "g" else f["l0_su"]
                sh[i, :, :wd] = pk[16 * a:16 * (a + 1), w0:w0 + wd]
                sc[i, :wd] = ss[a, w0:w0 + wd]
            else:
                sh[i] = f["l0_pdT"][16 * a:16 * (a + 1), sl * 1024:(sl + 1) * 1024]
                sc[i] = f["l0_sd"][a, sl * 1024:(sl + 1) * 1024]
        per_core[j]["myshard"] = sh
        per_core[j]["myscale"] = sc
    return f, per_core



class Ctx:
    def __init__(self, nc, es):
        self.nc = nc
        self.es = es
        self.block = None
        self.cnt = {}
        self.sem = {}
        self.d = {}
        self.s = {}
        self._uid = 0
        for name in ["dl", "dg", "tv", "ts", "tt", "cc"]:
            self.sem[name] = es.enter_context(nc.semaphore(name))
            self.cnt[name] = 0

    def uid(self):
        self._uid += 1
        return self._uid

    def tick(self, name, inst, n=1):
        inst.then_inc(self.sem[name], n)
        self.cnt[name] += n
        return self.cnt[name]

    def dtick(self, name, inst):
        return self.tick(name, inst, 16)

    def w(self, eng, name, val=None):
        eng.wait_ge(self.sem[name], self.cnt[name] if val is None else val)

    def on(self, engname, fn):
        getattr(self.block, engname)(fn)

    def wall(self, eng):
        for n in ["dl", "dg", "tv", "ts", "tt"]:
            self.w(eng, n)

    def barrier(self):
        def fv(v):
            self.tick("tv", v.engine_nop())
        self.on("vector", fv)

        def fg(g):
            self.tick("dg", g.engine_nop(), 16)
        self.on("gpsimd", fg)
        for e in ["sync", "vector", "scalar", "tensor", "gpsimd"]:
            def fw(eng, _e=e):
                self.wall(eng)
            self.on(e, fw)

    def xr(self, t):
        return self.s["XR"][:, t * C:(t + 1) * C]

    def xnc(self, cc, Tt):
        return self.s["XN"][:, cc * Tt:(cc + 1) * Tt]

    def BA(self, i, n):
        return self.s["BIG"][:, i:i + n]

    def BB(self, i, n):
        return self.s["BIG"][:, NT * C + i:NT * C + i + n]


def alloc_common(ctx, nc, es):
    S = lambda name, shape, dt: es.enter_context(nc.sbuf_tensor(name, shape, dt))
    s = ctx.s
    s["XR"] = S("XR", [128, NTE * C], F32)
    s["XN"] = S("XN", [128, NC * T_EXT], BF16)
    s["TA"] = S("TA", [128, C], F32)
    s["BIG"] = S("BIG", [128, 2 * NT * C], BF16)
    s["WSa"] = S("WSa", [128, 2048], BF16)
    s["WSb"] = S("WSb", [128, 2048], BF16)
    s["WS2"] = S("WS2", [128, 1024], F32)
    s["BC"] = S("BC", [128, C], BF16)
    s["SM"] = S("SM", [128, 192], F32)
    s["SC"] = S("SC", [128, 512], BF16)
    s["ID"] = S("ID", [128, 128], BF16)
    s["ST"] = S("ST", [128, 64], F32)
    s["CW"] = S("CW", [128, 80], F32)
    s["QG"] = S("QG", [128, 4096], BF16)
    s["WSi"] = s["QG"].bitcast(I32)


def emit_bc_table(ctx, row_ap):
    s = ctx.s

    def fd(sync):
        ctx.w(sync, "tv")
        ctx.dtick("dl", sync.dma_start(out=s["TA"][:, :],
                                       in_=row_ap.broadcast_to([128, C])))
    ctx.on("sync", fd)

    def fc(v):
        ctx.w(v, "dl")
        ctx.tick("tv", v.tensor_copy(out=s["BC"][:], in_=s["TA"][:, :]))
    ctx.on("vector", fc)


def emit_preamble(ctx):
    s, d = ctx.s, ctx.d

    def f(sync):
        ctx.dtick("dl", sync.dma_start(out=s["SM"][:, 128:129].bitcast(I32), in_=d["shfcol"]))
        ctx.dtick("dl", sync.dma_start(out=s["WS2"][:, 0:128], in_=d["identity"]))
    ctx.on("sync", f)

    def fv(v):
        ctx.w(v, "dl")
        ctx.tick("tv", v.tensor_copy(out=s["ID"][:], in_=s["WS2"][:, 0:128]))
    ctx.on("vector", fv)


def emit_load_x(ctx, ntiles):
    def f(sync):
        for t in range(ntiles):
            ctx.dtick("dl", sync.dma_start(
                out=ctx.xr(t), in_=ctx.d["xin"][t * 128:(t + 1) * 128, :]))
    ctx.on("sync", f)


def emit_store_x(ctx, t0, ntiles):
    def f(g):
        ctx.wall(g)
        for t in range(ntiles):
            ctx.dtick("dg", g.dma_start(
                out=ctx.d["xout"][t * 128:(t + 1) * 128, :], in_=ctx.xr(t0 + t)))
        ctx.w(g, "dg")
    ctx.on("gpsimd", f)



def emit_ln(ctx, t0, ntiles, permuted):
    nc, s = ctx.nc, ctx.s
    Tt = ntiles * 128
    ctx.barrier()
    with nc.psum_tensor(f"ptl0_{ctx.uid()}", [128, 128], BF16) as p0, \
         nc.psum_tensor(f"ptl1_{ctx.uid()}", [128, 128], BF16) as p1, \
         nc.psum_tensor(f"ptlf0_{ctx.uid()}", [128, 128], F32) as f0, \
         nc.psum_tensor(f"ptlf1_{ctx.uid()}", [128, 128], F32) as f1:
        pt = [p0, p1]
        pf = [f0, f1]
        xtmp = ctx.BA(0, C)
        for t in range(ntiles):
            xr = ctx.xr(t0 + t)

            def fv(v, t=t, xr=xr):
                ctx.w(v, "tv")
                if t > 0:
                    ctx.w(v, "tt")
                v.reduce_sum(out=s["ST"][:, 24:25], in_=xr, axis=AX)
                v.drain()
                v.tensor_scalar_mul(out=s["ST"][:, 24:25], in0=s["ST"][:, 24:25],
                                    scalar1=1.0 / C)
                v.drain()
                ctx.tick("tv", v.engine_nop())
            ctx.on("vector", fv)

            def fsr(sc, t=t, xr=xr):
                ctx.w(sc, "tv")
                for k in range(4):
                    sc.activation(out=s["SM"][:, 128:160] if False else s["TA"][:, 0:512],
                                  in_=xr[:, 512 * k:512 * (k + 1)], func=AF.Square,
                                  bias=s["ST"][:, 24:25], scale=-1.0,
                                  accum_out=s["ST"][:, 40 + k:41 + k])
                ctx.tick("ts", sc.copy(out=s["SM"][:, 191:192], in_=s["SM"][:, 191:192]))
            ctx.on("scalar", fsr)

            def fv15(v, t=t):
                ctx.w(v, "ts")
                v.tensor_add(out=s["ST"][:, 40:41], in0=s["ST"][:, 40:41],
                             in1=s["ST"][:, 41:42])
                v.tensor_add(out=s["ST"][:, 42:43], in0=s["ST"][:, 42:43],
                             in1=s["ST"][:, 43:44])
                v.drain()
                v.tensor_add(out=s["ST"][:, 40:41], in0=s["ST"][:, 40:41],
                             in1=s["ST"][:, 42:43])
                v.drain()
                ctx.tick("tv", v.tensor_scalar(out=s["ST"][:, 26:27],
                                               in0=s["ST"][:, 40:41],
                                               scalar1=1.0 / C, scalar2=LN_EPS,
                                               op0=ALU.mult, op1=ALU.add))
            ctx.on("vector", fv15)

            def fsr2(sc, t=t):
                ctx.w(sc, "tv")
                sc.activation(out=s["ST"][:, 26:27], in_=s["ST"][:, 26:27], func=AF.Ln)
                ctx.tick("ts", sc.activation(out=s["ST"][:, 26:27],
                                             in_=s["ST"][:, 26:27],
                                             func=AF.Exp, scale=-0.5))
            ctx.on("scalar", fsr2)

            def fnm(v, t=t, xr=xr):
                ctx.w(v, "ts")
                ctx.tick("tv", v.tensor_scalar(
                    out=xtmp, in0=xr, scalar1=s["ST"][:, 24:25],
                    scalar2=s["ST"][:, 26:27], op0=ALU.subtract, op1=ALU.mult))
            ctx.on("vector", fnm)

            for cc in range(NC):
                def ft(te, t=t, cc=cc):
                    ctx.w(te, "tv")
                    src = xtmp[:, cc * 128:(cc + 1) * 128]
                    ctx.tick("tt", te.transpose(out=pt[cc % 2][:], in_=src,
                                                identity=s["ID"][:]))

                def fv2(v, t=t, cc=cc):
                    ctx.w(v, "tt")
                    ctx.tick("tv", v.tensor_copy(
                        out=ctx.xnc(cc, Tt)[:, t * 128:(t + 1) * 128],
                        in_=pt[cc % 2][:]))
                ctx.on("tensor", ft)
                ctx.on("vector", fv2)
    ctx.barrier()



def emit_proj(ctx, Tt, a0, ntiles, O, wsrc, wdt, evac, lhsT_fn=None):
    nc, s = ctx.nc, ctx.s
    if lhsT_fn is None:
        def lhsT_fn(cc, a, n):
            return ctx.xnc(cc, Tt)[:, a:a + n]
    nq = O // 512
    ctx.barrier()
    with ExitStack() as ps:
        pb = [ps.enter_context(nc.psum_tensor(f"pj{i}_{ctx.uid()}", [128, 512], F32))
              for i in range(ntiles)]
        for oq in range(nq):
            for cc in range(NC):
                buf = s["WSa"] if cc % 2 == 0 else s["WSb"]
                wtile = buf[:, 0:512]
                if wdt == BF16:
                    def fdma(sync, oq=oq, cc=cc, wtile=wtile):
                        ctx.w(sync, "tt")
                        ctx.dtick("dl", sync.dma_start(out=wtile,
                                                       in_=wsrc(cc, oq * 512, 512)))
                    ctx.on("sync", fdma)
                else:
                    def fdma(sync, oq=oq, cc=cc):
                        ctx.w(sync, "tv")
                        if wdt == F32:
                            stage = s["WS2"][:, 0:512] if cc % 2 == 0 else s["WS2"][:, 512:1024]
                        else:
                            stage = s["WSi"][:, 0:512] if cc % 2 == 0 else s["WSi"][:, 512:1024]
                        ctx.dtick("dl", sync.dma_start(out=stage,
                                                       in_=wsrc(cc, oq * 512, 512)))
                    ctx.on("sync", fdma)

                    def fcvt(v, oq=oq, cc=cc, wtile=wtile):
                        ctx.w(v, "dl")
                        ctx.w(v, "tt")
                        if wdt == F32:
                            stage = s["WS2"][:, 0:512] if cc % 2 == 0 else s["WS2"][:, 512:1024]
                            ctx.tick("tv", v.tensor_copy(out=wtile, in_=stage))
                        else:
                            stage = s["WSi"][:, 0:512] if cc % 2 == 0 else s["WSi"][:, 512:1024]
                            ctx.tick("tv", v.tensor_scalar(
                                out=wtile, in0=stage, scalar1=-128.0, scalar2=None,
                                op0=ALU.add))
                    ctx.on("vector", fcvt)

                def fmm(te, oq=oq, cc=cc, wtile=wtile):
                    ctx.w(te, "dl" if wdt == BF16 else "tv")
                    if cc == 0:
                        ctx.w(te, "ts")
                        ctx.w(te, "tv")
                    for t in range(ntiles):
                        inst = te.matmul(out=pb[t][:], lhsT=lhsT_fn(cc, a0 + t * 128, 128),
                                         rhs=wtile, start=(cc == 0), stop=(cc == NC - 1))
                        if t == ntiles - 1:
                            ctx.tick("tt", inst)
                ctx.on("tensor", fmm)

            for t in range(ntiles):
                evac(t, oq, pb[t])
    ctx.barrier()



def emit_l0_dequant(ctx):
    s = ctx.s
    shf = s["SM"][:, 128:129].bitcast(I32)
    ctx.barrier()

    def fv0(v):
        ctx.w(v, "dl")
        v.memset(s["ST"][:, 32:33], -2.0)
        v.memset(s["ST"][:, 33:34], -6.0)
        v.drain()
        ctx.tick("tv", v.engine_nop())
    ctx.on("vector", fv0)

    def emit_piece(words_src, width, scale_src, out_dst):
        def fdma(sync):
            ctx.w(sync, "tv")
            for j in range(8):
                ctx.dtick("dl", sync.dma_start(
                    out=s["WSi"][16 * j:16 * (j + 1), 0:width], in_=words_src))
            ctx.dtick("dl", sync.dma_start(
                out=s["WSa"].bitcast(F16)[:, 0:width],
                in_=scale_src.broadcast_to([128, width])))
        ctx.on("sync", fdma)

        def fv1(v):
            ctx.w(v, "dl")
            ctx.w(v, "ts")
            v.tensor_scalar(
                out=s["WSi"][:, 1024:1024 + width], in0=s["WSi"][:, 0:width],
                scalar1=shf, scalar2=7,
                op0=ALU.logical_shift_right, op1=ALU.bitwise_and)
            ctx.tick("tv", v.tensor_copy(out=s["WS2"][:, 0:width],
                                         in_=s["WSi"][:, 1024:1024 + width]))
        ctx.on("vector", fv1)

        def fs1(sc):
            ctx.w(sc, "tv")
            sc.activation(out=s["TA"][:, 0:width], in_=s["WS2"][:, 0:width],
                          func=AF.Relu, bias=s["ST"][:, 32:33], scale=0.5)
            ctx.tick("ts", sc.activation(
                out=s["TA"][:, 1024:1024 + width], in_=s["WS2"][:, 0:width],
                func=AF.Relu, bias=s["ST"][:, 33:34], scale=1.0))
        ctx.on("scalar", fs1)

        def fv2(v):
            ctx.w(v, "ts")
            ctx.w(v, "dg")
            m_f = s["WS2"][:, 0:width]
            v.scalar_tensor_tensor(out=m_f, in0=m_f, scalar=0.5,
                                   in1=s["TA"][:, 0:width], op0=ALU.mult, op1=ALU.add)
            v.tensor_add(out=m_f, in0=m_f, in1=s["TA"][:, 1024:1024 + width])
            v.tensor_scalar(out=s["WSi"][:, 1024:1024 + width], in0=s["WSi"][:, 0:width],
                            scalar1=shf, scalar2=8,
                            op0=ALU.logical_shift_right, op1=ALU.bitwise_and)
            v.tensor_scalar(out=s["TA"][:, 0:width], in0=s["WSi"][:, 1024:1024 + width],
                            scalar1=-0.25, scalar2=1.0, op0=ALU.mult, op1=ALU.add)
            v.tensor_mul(out=m_f, in0=m_f, in1=s["TA"][:, 0:width])
            ctx.tick("tv", v.tensor_tensor(
                out=s["WSb"][:, 0:width], in0=m_f,
                in1=s["WSa"].bitcast(F16)[:, 0:width], op=ALU.mult))
        ctx.on("vector", fv2)

        def fst(g):
            ctx.w(g, "tv")
            rview = out_dst.rearrange("(m j) w -> m j w", m=16, j=8)
            for j in range(8):
                ctx.dtick("dg", g.dma_start(
                    out=rview[:, j, :],
                    in_=s["WSb"][16 * j:16 * (j + 1), 0:width]))
        ctx.on("gpsimd", fst)

    for i in range(35):
        emit_piece(ctx.d["myshard"][i], 1024, ctx.d["myscale"][i:i + 1, :],
                   ctx.d["cin_w"][i * 128:(i + 1) * 128, :])

    def fcc(g):
        ctx.w(g, "dg")
        ctx.tick("cc", g.collective_compute(
            "AllGather", ALU.bypass, replica_groups=RG, ins=[ctx.d["cin_w"]],
            outs=[ctx.d["cout_w"]]))
    ctx.on("gpsimd", fcc)

    def fwait(sync):
        ctx.w(sync, "cc")
    ctx.on("sync", fwait)
    ctx.barrier()


def wcache(ctx):
    base = ctx.d["cout_w"].rearrange("(s p) w -> s p w", p=128)
    return base



def emit_ffn(ctx, layer, t0, ntiles, Tt):
    nc, s, d = ctx.nc, ctx.s, ctx.d
    chunks = []
    t = 0
    while t < ntiles:
        n = min(4, ntiles - t)
        chunks.append((t, n))
        t += n

    ctx.barrier()
    if layer == "l1":
        def fld(sync):
            ctx.dtick("dl", sync.dma_start(out=s["SM"][:, 0:44], in_=d["l1_sgc"]))
            ctx.dtick("dl", sync.dma_start(out=s["SM"][:, 44:88], in_=d["l1_suc"]))
        ctx.on("sync", fld)

    for (tc, ntc) in chunks:
        Tch = ntc * 128
        with nc.psum_tensor(f"pgA_{ctx.uid()}", [128, 512], F32) as pg, \
             nc.psum_tensor(f"puA_{ctx.uid()}", [128, 512], F32) as pu:
            for k in range(NI):
                if layer == "l0":
                    def fdg(sync, k=k):
                        ctx.w(sync, "tt")
                        base = wcache(ctx)
                        sl, col = k // 8, (k % 8) * 128
                        gsrc = base[sl:96:6, :, col:col + 128].transpose([1, 0, 2])
                        usrc = base[96 + sl:192:6, :, col:col + 128].transpose([1, 0, 2])
                        ctx.dtick("dl", sync.dma_start(
                            out=s["WSa"].rearrange("p (cc i) -> p cc i", cc=16),
                            in_=gsrc))
                        ctx.dtick("dl", sync.dma_start(
                            out=s["WSb"].rearrange("p (cc i) -> p cc i", cc=16),
                            in_=usrc))
                    ctx.on("sync", fdg)
                else:
                    def fdg1(sync, k=k):
                        ctx.w(sync, "tv")
                        ctx.dtick("dl", sync.dma_start(
                            out=s["WSi"].rearrange("p (cc i) -> p cc i", cc=16),
                            in_=d["l1_qgT"].rearrange("(cc p) i -> p cc i", p=128)[:, :, 128 * k:128 * (k + 1)]))
                    ctx.on("sync", fdg1)

                    def fcv1(v, k=k):
                        ctx.w(v, "dl")
                        ctx.w(v, "tt")
                        ctx.tick("tv", v.tensor_scalar(
                            out=s["WSa"][:], in0=s["WSi"][:], scalar1=-128.0,
                            scalar2=None, op0=ALU.add))
                    ctx.on("vector", fcv1)

                    def fdg2(sync, k=k):
                        ctx.w(sync, "tv")
                        ctx.dtick("dl", sync.dma_start(
                            out=s["WSi"].rearrange("p (cc i) -> p cc i", cc=16),
                            in_=d["l1_quT"].rearrange("(cc p) i -> p cc i", p=128)[:, :, 128 * k:128 * (k + 1)]))
                    ctx.on("sync", fdg2)

                    def fcv2(v, k=k):
                        ctx.w(v, "dl")
                        ctx.w(v, "tt")
                        ctx.tick("tv", v.tensor_scalar(
                            out=s["WSb"][:], in0=s["WSi"][:], scalar1=-128.0,
                            scalar2=None, op0=ALU.add))
                    ctx.on("vector", fcv2)

                def fmmA(te, k=k, tc=tc, Tch=Tch):
                    ctx.w(te, "dl" if layer == "l0" else "tv")
                    ctx.w(te, "ts")
                    if layer == "l0":
                        ctx.w(te, "tv")
                    for cc in range(NC):
                        te.matmul(out=pg[:, 0:Tch],
                                  lhsT=s["WSa"][:, cc * 128:(cc + 1) * 128],
                                  rhs=ctx.xnc(cc, Tt)[:, tc * 128:tc * 128 + Tch],
                                  start=(cc == 0), stop=(cc == NC - 1))
                    for cc in range(NC):
                        inst = te.matmul(out=pu[:, 0:Tch],
                                         lhsT=s["WSb"][:, cc * 128:(cc + 1) * 128],
                                         rhs=ctx.xnc(cc, Tt)[:, tc * 128:tc * 128 + Tch],
                                         start=(cc == 0), stop=(cc == NC - 1))
                        if cc == NC - 1:
                            ctx.tick("tt", inst)
                ctx.on("tensor", fmmA)

                def fevA_s(sc, k=k, Tch=Tch):
                    ctx.w(sc, "tt")
                    ctx.w(sc, "tv")
                    if layer == "l1":
                        ctx.w(sc, "dl")
                        inst = sc.activation(out=s["WS2"][:, 0:Tch], in_=pg[:, 0:Tch],
                                             func=AF.Silu, scale=s["SM"][:, k:k + 1])
                    else:
                        inst = sc.activation(out=s["WS2"][:, 0:Tch], in_=pg[:, 0:Tch],
                                             func=AF.Silu)
                    ctx.tick("ts", inst)
                ctx.on("scalar", fevA_s)

                def fevA_v(v, k=k, Tch=Tch):
                    ctx.w(v, "ts")
                    hdst = ctx.s["BIG"][:, k * 512:k * 512 + Tch]
                    if layer == "l0":
                        inst = v.tensor_tensor(out=hdst, in0=s["WS2"][:, 0:Tch],
                                               in1=pu[:, 0:Tch], op=ALU.mult)
                    else:
                        inst = v.scalar_tensor_tensor(out=hdst, in0=pu[:, 0:Tch],
                                                      scalar=s["SM"][:, 44 + k:45 + k],
                                                      in1=s["WS2"][:, 0:Tch],
                                                      op0=ALU.mult, op1=ALU.mult)
                    ctx.tick("tv", inst)
                ctx.on("vector", fevA_v)

        ctx.barrier()
        if layer == "l1":
            emit_bc_table(ctx, d["l1_sd"])

            def fbc2(v):
                ctx.tick("tv", v.tensor_scalar_mul(out=s["BC"][:], in0=s["BC"][:],
                                                   scalar1=0.5))
            ctx.on("vector", fbc2)

        with ExitStack() as ps:
            pyb = [ps.enter_context(nc.psum_tensor(f"pyB{i}_{ctx.uid()}", [128, 512], F32))
                   for i in range(ntc)]
            for oq in range(4):
                for kp in range(NI // 2):
                    buf = s["WSa"] if kp % 2 == 0 else s["WSb"]
                    wt = buf[:, 0:1024]
                    k0 = 2 * kp
                    if layer == "l0":
                        def fdB(sync, oq=oq, k0=k0, wt=wt):
                            ctx.w(sync, "tt")
                            base = wcache(ctx)
                            slot = 192 + 2 * k0 + (oq // 2)
                            src2 = base[slot:slot + 3:2, :, (oq % 2) * 512:(oq % 2) * 512 + 512]
                            ctx.dtick("dl", sync.dma_start(
                                out=wt.rearrange("p (kk o) -> p kk o", kk=2),
                                in_=src2.transpose([1, 0, 2])))
                        ctx.on("sync", fdB)
                    else:
                        def fdB(sync, oq=oq, k0=k0, kp=kp):
                            ctx.w(sync, "tv")
                            stage = s["WSi"][:, 0:1024] if kp % 2 == 0 else s["WSi"][:, 1024:2048]
                            src2 = d["l1_qdT"][128 * k0:128 * (k0 + 2),
                                               512 * oq:512 * (oq + 1)].rearrange(
                                "(kk p) o -> p kk o", p=128)
                            ctx.dtick("dl", sync.dma_start(
                                out=stage.rearrange("p (kk o) -> p kk o", kk=2),
                                in_=src2))
                        ctx.on("sync", fdB)

                        def fcB(v, oq=oq, kp=kp, wt=wt):
                            ctx.w(v, "dl")
                            ctx.w(v, "tt")
                            stage = s["WSi"][:, 0:1024] if kp % 2 == 0 else s["WSi"][:, 1024:2048]
                            ctx.tick("tv", v.tensor_scalar(
                                out=wt, in0=stage, scalar1=-128.0, scalar2=None,
                                op0=ALU.add))
                        ctx.on("vector", fcB)

                    def fmB(te, oq=oq, k0=k0, wt=wt, ntc=ntc):
                        ctx.w(te, "dl" if layer == "l0" else "tv")
                        if k0 == 0:
                            ctx.w(te, "tv")
                            ctx.w(te, "ts")
                        for kk in range(2):
                            k = k0 + kk
                            for i in range(ntc):
                                inst = te.matmul(
                                    out=pyb[i][:],
                                    lhsT=ctx.s["BIG"][:, k * 512 + i * 128:k * 512 + (i + 1) * 128],
                                    rhs=wt[:, kk * 512:(kk + 1) * 512],
                                    start=(k == 0), stop=(k == NI - 1))
                                if kk == 1 and i == ntc - 1:
                                    ctx.tick("tt", inst)
                        ctx.on
                    ctx.on("tensor", fmB)

                for i in range(ntc):
                    if layer == "l0":
                        def fqv(v, i=i, oq=oq, tc=tc):
                            ctx.w(v, "tt")
                            ctx.tick("tv", v.tensor_copy(
                                out=ctx.s["BIG"][:, NI * 512 + i * C + oq * 512:NI * 512 + i * C + (oq + 1) * 512],
                                in_=pyb[i][:]))
                        ctx.on("vector", fqv)
                    else:
                        def fqv(v, i=i, oq=oq, tc=tc):
                            ctx.w(v, "tt")
                            v.tensor_tensor(out=s["TA"][:, 0:512], in0=pyb[i][:],
                                            in1=s["BC"][:, oq * 512:(oq + 1) * 512],
                                            op=ALU.mult)
                            xr = ctx.xr(t0 + tc + i)
                            ctx.tick("tv", v.tensor_add(
                                out=xr[:, oq * 512:(oq + 1) * 512],
                                in0=xr[:, oq * 512:(oq + 1) * 512],
                                in1=s["TA"][:, 0:512]))
                        ctx.on("vector", fqv)

            if layer == "l0":
                for i in range(ntc):
                    def frms_s(sc, i=i, tc=tc):
                        ctx.w(sc, "tv")
                        y = ctx.s["BIG"][:, NI * 512 + i * C:NI * 512 + (i + 1) * C]
                        for k4 in range(4):
                            sc.activation(out=s["TA"][:, 0:512],
                                          in_=y[:, 512 * k4:512 * (k4 + 1)],
                                          func=AF.Square,
                                          accum_out=s["ST"][:, 40 + k4:41 + k4])
                        ctx.tick("ts", sc.copy(out=s["SM"][:, 191:192], in_=s["SM"][:, 191:192]))
                    ctx.on("scalar", frms_s)

                    def frms_v(v, i=i, tc=tc):
                        ctx.w(v, "ts")
                        v.tensor_add(out=s["ST"][:, 40:41], in0=s["ST"][:, 40:41],
                                     in1=s["ST"][:, 41:42])
                        v.tensor_add(out=s["ST"][:, 42:43], in0=s["ST"][:, 42:43],
                                     in1=s["ST"][:, 43:44])
                        v.drain()
                        v.tensor_add(out=s["ST"][:, 40:41], in0=s["ST"][:, 40:41],
                                     in1=s["ST"][:, 42:43])
                        v.drain()
                        v.tensor_scalar(out=s["ST"][:, 26:27], in0=s["ST"][:, 40:41],
                                        scalar1=1.0 / C, scalar2=L0_FFN_EPS,
                                        op0=ALU.mult, op1=ALU.add)
                        v.drain()
                        ctx.tick("tv", v.engine_nop())
                    ctx.on("vector", frms_v)

                    def frms_s2(sc, i=i, tc=tc):
                        ctx.w(sc, "tv")
                        sc.activation(out=s["ST"][:, 26:27], in_=s["ST"][:, 26:27],
                                      func=AF.Ln)
                        ctx.tick("ts", sc.activation(out=s["ST"][:, 26:27],
                                                     in_=s["ST"][:, 26:27],
                                                     func=AF.Exp, scale=-0.5))
                    ctx.on("scalar", frms_s2)

                    def frms2(v, i=i, tc=tc):
                        ctx.w(v, "ts")
                        y = ctx.s["BIG"][:, NI * 512 + i * C:NI * 512 + (i + 1) * C]
                        v.tensor_scalar_mul(out=s["ST"][:, 26:27], in0=s["ST"][:, 26:27],
                                            scalar1=0.5)
                        v.drain()
                        v.tensor_scalar(out=s["TA"][:], in0=y, scalar1=s["ST"][:, 26:27],
                                        scalar2=None, op0=ALU.mult)
                        xr = ctx.xr(t0 + tc + i)
                        ctx.tick("tv", v.tensor_add(out=xr, in0=xr, in1=s["TA"][:]))
                    ctx.on("vector", frms2)
        ctx.barrier()



def emit_l0_attn(ctx, upto="full"):
    nc, s, d = ctx.nc, ctx.s, ctx.d
    Tt = T_EXT

    ctx.barrier()

    def fld(sync):
        ctx.dtick("dl", sync.dma_start(out=s["SM"][0:1, 32:48], in_=d["l0_gamma"]))
        ctx.dtick("dl", sync.dma_start(out=s["ST"][:, 48:64],
                                       in_=d["l0_gamma"].broadcast_to([128, H])))
        ctx.dtick("dl", sync.dma_start(out=s["CW"][:, 0:64], in_=d["l0_convw"]))
        ctx.dtick("dl", sync.dma_start(out=s["CW"][:, 64:80], in_=d["l0_convb"]))
    ctx.on("sync", fld)

    def fs0(sc):
        ctx.w(sc, "dl")
        ctx.tick("ts", sc.activation(out=s["ST"][:, 48:64], in_=s["ST"][:, 48:64],
                                     func=AF.Ln))
    ctx.on("scalar", fs0)

    for cc in range(NC):
        def fcv(v, cc=cc):
            if cc == 0:
                ctx.w(v, "tv")
                ctx.w(v, "dl")
            else:
                ctx.w(v, "ts")
            xn = ctx.xnc(cc, Tt)
            v.tensor_scalar(out=s["TA"][:, 0:1024], in0=xn[:, 125:125 + 1024],
                            scalar1=s["CW"][:, 4 * cc:4 * cc + 1], scalar2=None,
                            op0=ALU.mult)
            for k in (1, 2, 3):
                v.scalar_tensor_tensor(out=s["TA"][:, 0:1024],
                                       in0=xn[:, 125 + k:125 + k + 1024],
                                       scalar=s["CW"][:, 4 * cc + k:4 * cc + k + 1],
                                       in1=s["TA"][:, 0:1024],
                                       op0=ALU.mult, op1=ALU.add)
            v.drain()
            ctx.tick("tv", v.engine_nop())
        ctx.on("vector", fcv)

        def fcs(sc, cc=cc):
            ctx.w(sc, "tv")
            ctx.w(sc, "dg")
            ctx.tick("ts", sc.activation(out=s["WSb"][:, 0:1024], in_=s["TA"][:, 0:1024],
                                         func=AF.Silu, bias=s["CW"][:, 64 + cc:65 + cc],
                                         scale=1.0))
        ctx.on("scalar", fcs)

        def fcd(g, cc=cc):
            ctx.w(g, "ts")
            ctx.dtick("dg", g.dma_start(out=d["xc_d"][cc], in_=s["WSb"][:, 0:1024]))
        ctx.on("gpsimd", fcd)

    def l2norm_evac(dstf):
        def evac(t, oq, pb):
            def fsq(sc, t=t, oq=oq, pb=pb):
                ctx.w(sc, "tt")
                for hh in range(4):
                    sc.activation(out=s["TA"][:, hh * 128:(hh + 1) * 128],
                                  in_=pb[:, hh * 128:(hh + 1) * 128],
                                  func=AF.Square,
                                  accum_out=s["ST"][:, 40 + hh:41 + hh])
                sc.activation(out=s["ST"][:, 44:48], in_=s["ST"][:, 40:44], func=AF.Ln)
                ctx.tick("ts", sc.activation(out=s["ST"][:, 44:48],
                                             in_=s["ST"][:, 44:48],
                                             func=AF.Exp, scale=-0.5))
            ctx.on("scalar", fsq)

            def fvn(v, t=t, oq=oq, pb=pb):
                ctx.w(v, "ts")
                v.tensor_scalar(out=s["ST"][:, 44:48], in0=s["ST"][:, 44:48],
                                scalar1=1.0 / L0_NORM_EPS, scalar2=None, op0=ALU.min)
                v.drain()
                for hh in range(4):
                    inst = v.tensor_scalar(
                        out=dstf(t)[:, oq * 512 + hh * 128:oq * 512 + (hh + 1) * 128],
                        in0=pb[:, hh * 128:(hh + 1) * 128],
                        scalar1=s["ST"][:, 44 + hh:45 + hh], scalar2=None, op0=ALU.mult)
                ctx.tick("tv", inst)
            ctx.on("vector", fvn)
        return evac

    emit_proj(ctx, Tt, HALO, NT, C,
              lambda cc, o0, ol: d["l0_wkT"][128 * cc:128 * (cc + 1), o0:o0 + ol],
              F32, l2norm_evac(lambda t: ctx.BA(t * C, C)))

    def v_evac(t, oq, pb):
        def fv(v, t=t, oq=oq, pb=pb):
            ctx.w(v, "tt")
            ctx.tick("tv", v.tensor_tensor(
                out=ctx.BB(t * C + oq * 512, 512), in0=pb[:],
                in1=ctx.BA(t * C + oq * 512, 512), op=ALU.mult))
        ctx.on("vector", fv)
    emit_proj(ctx, Tt, HALO, NT, C,
              lambda cc, o0, ol: d["l0_wvT"][128 * cc:128 * (cc + 1), o0:o0 + ol],
              F32, v_evac)

    emit_proj(ctx, Tt, HALO, NT, C,
              lambda cc, o0, ol: d["l0_wqT"][128 * cc:128 * (cc + 1), o0:o0 + ol],
              F32, l2norm_evac(lambda t: ctx.BA(t * C, C)))

    def fxc(sync):
        ctx.wall(sync)
        for cc in range(NC):
            ctx.dtick("dl", sync.dma_start(out=ctx.xnc(cc, T_LOC), in_=d["xc_d"][cc]))
    ctx.on("sync", fxc)

    emit_bc_table(ctx, d["l0_gateb"])

    def g_evac(t, oq, pb):
        def fga(v, t=t, oq=oq, pb=pb):
            ctx.w(v, "tt")
            ctx.w(v, "dg")
            ctx.w(v, "ts")
            ctx.tick("tv", v.tensor_tensor(out=s["TA"][:, 0:512], in0=pb[:],
                                           in1=s["BC"][:, oq * 512:(oq + 1) * 512],
                                           op=ALU.add))
        ctx.on("vector", fga)

        def fgs(sc, t=t, oq=oq):
            ctx.w(sc, "tv")
            ctx.tick("ts", sc.activation(out=s["TA"][:, 512:1024],
                                         in_=s["TA"][:, 0:512], func=AF.Sigmoid))
        ctx.on("scalar", fgs)

        def fgm(v, t=t, oq=oq):
            ctx.w(v, "ts")
            dst = ctx.BA(t * C + oq * 512, 512)
            ctx.tick("tv", v.tensor_tensor(out=dst, in0=s["TA"][:, 512:1024],
                                           in1=dst, op=ALU.mult))
        ctx.on("vector", fgm)

    emit_proj(ctx, T_LOC, 0, NT, C,
              lambda cc, o0, ol: d["l0_wgT"][128 * cc:128 * (cc + 1), o0:o0 + ol],
              F32, g_evac)
    if upto == "proj":
        return

    def fsc_ld(sync):
        ctx.wall(sync)
        ctx.dtick("dl", sync.dma_start(out=s["WS2"][:, 0:128], in_=d["iota_mj"]))
        ctx.dtick("dl", sync.dma_start(out=s["WS2"][:, 128:256], in_=d["mask_mj"]))
        ctx.dtick("dl", sync.dma_start(out=s["WS2"][:, 256:257], in_=d["grev"]))
    ctx.on("sync", fsc_ld)

    for h in range(H):
        def fdt_s(sc, h=h):
            ctx.w(sc, "dl")
            ctx.w(sc, "tv")
            sc.activation(out=s["TA"][:, 0:128], in_=s["WS2"][:, 0:128],
                          func=AF.Exp, scale=s["ST"][:, 48 + h:49 + h])
            ctx.tick("ts", sc.activation(out=s["TA"][:, 128:129], in_=s["WS2"][:, 256:257],
                                         func=AF.Exp, scale=s["ST"][:, 48 + h:49 + h]))
        ctx.on("scalar", fdt_s)

        def fdt_v(v, h=h):
            ctx.w(v, "ts")
            v.tensor_tensor(out=s["QG"][:, 2048 + h * 128:2048 + (h + 1) * 128],
                            in0=s["TA"][:, 0:128], in1=s["WS2"][:, 128:256], op=ALU.mult)
            v.tensor_copy(out=s["SC"][:, h:h + 1], in_=s["TA"][:, 128:129])
            v.drain()
            ctx.tick("tv", v.engine_nop())
        ctx.on("vector", fdt_v)

    with nc.psum_tensor(f"pc1_{ctx.uid()}", [1, 128], F32) as pc:
        for h in range(H):
            def fg128(v, h=h):
                ctx.w(v, "ts")
                v.tensor_scalar_mul(out=s["ST"][0:1, 40:41],
                                    in0=s["ST"][0:1, 48 + h:49 + h], scalar1=128.0)
                v.drain()
                ctx.tick("tv", v.engine_nop())
            ctx.on("vector", fg128)

            def fg128b(sc, h=h):
                ctx.w(sc, "tv")
                ctx.tick("ts", sc.activation(out=s["ST"][0:1, 40:41],
                                             in_=s["ST"][0:1, 40:41], func=AF.Exp))
            ctx.on("scalar", fg128b)

            for b in range(NT):
                def fmm(te, h=h, b=b):
                    ctx.w(te, "tv")
                    ctx.tick("tt", te.matmul(
                        out=pc[:], lhsT=s["SC"][:, h:h + 1],
                        rhs=ctx.BB(b * C + h * 128, 128), start=True, stop=True))
                ctx.on("tensor", fmm)

                def fcar(v, h=h, b=b):
                    ctx.w(v, "tt")
                    ctx.w(v, "ts")
                    scar = s["SM"][0:1, 64:192]
                    if b == 0:
                        inst = v.tensor_copy(out=scar, in_=pc[0:1, :])
                    else:
                        inst = v.scalar_tensor_tensor(out=scar, in0=scar,
                                                      scalar=s["ST"][0:1, 40:41],
                                                      in1=pc[0:1, :],
                                                      op0=ALU.mult, op1=ALU.add)
                    ctx.tick("tv", inst)
                    v.drain()
                ctx.on("vector", fcar)

            def fA(v, h=h):
                v.tensor_copy(out=s["TA"][0:1, h * 128:(h + 1) * 128],
                              in_=s["SM"][0:1, 64:192])
                v.drain()
                ctx.tick("tv", v.engine_nop())
            ctx.on("vector", fA)

    def fcc(g):
        ctx.w(g, "tv")
        ctx.dtick("dg", g.dma_start(out=d["cin0"], in_=s["TA"][0:1, :]))
        ctx.w(g, "dg")
        ctx.tick("cc", g.collective_compute(
            "AllGather", ALU.bypass, replica_groups=RG, ins=[d["cin0"]],
            outs=[d["cout0"]]))
        ctx.w(g, "cc")
        ctx.dtick("dg", g.dma_start(out=s["SM"][0:8, 0:1], in_=d["cc_sel"]))
        ctx.dtick("dg", g.dma_start(out=s["SM"][0:8, 1:2], in_=d["cc_pow"]))
    ctx.on("gpsimd", fcc)

    for half in range(2):
        def fch_g(g, half=half):
            ctx.w(g, "tv")
            ctx.dtick("dg", g.dma_start(out=s["TA"][0:8, 0:1024],
                                        in_=d["cout0"][:, half * 1024:(half + 1) * 1024]))
            ctx.dtick("dg", g.dma_start(
                out=s["TA"][0:8, 1024:2048],
                in_=d["l0_gamma_rep"][0:1, half * 1024:(half + 1) * 1024].broadcast_to([8, 1024])))
        ctx.on("gpsimd", fch_g)

        def fch_s(sc, half=half):
            ctx.w(sc, "dg")
            sc.activation(out=s["TA"][0:8, 1024:2048], in_=s["TA"][0:8, 1024:2048],
                          func=AF.Ln)
            ctx.tick("ts", sc.activation(out=s["TA"][0:8, 1024:2048],
                                         in_=s["TA"][0:8, 1024:2048],
                                         func=AF.Exp, scale=s["SM"][0:8, 1:2]))
        ctx.on("scalar", fch_s)

        def fch_v(v, half=half):
            ctx.w(v, "ts")
            v.tensor_scalar(out=s["TA"][0:8, 1024:2048], in0=s["TA"][0:8, 1024:2048],
                            scalar1=s["SM"][0:8, 0:1], scalar2=None, op0=ALU.mult)
            v.tensor_tensor(
                out=s["BC"][0:8, half * 1024:(half + 1) * 1024],
                in0=s["TA"][0:8, 1024:2048], in1=s["TA"][0:8, 0:1024], op=ALU.mult)
            v.drain()
            ctx.tick("tv", v.engine_nop())
        ctx.on("vector", fch_v)

    def fones(v):
        v.memset(s["SC"][0:8, 256:257], 1.0)
        v.drain()
        ctx.tick("tv", v.engine_nop())
    ctx.on("vector", fones)

    with nc.psum_tensor(f"psin_{ctx.uid()}", [1, 512], F32) as psin:
        for qq in range(4):
            def fmm_s(te, qq=qq):
                ctx.w(te, "tv")
                ctx.tick("tt", te.matmul(out=psin[:], lhsT=s["SC"][0:8, 256:257],
                                         rhs=s["BC"][0:8, qq * 512:(qq + 1) * 512],
                                         start=True, stop=True))
            ctx.on("tensor", fmm_s)

            def fsin(v, qq=qq):
                ctx.w(v, "tt")
                v.tensor_copy(out=s["TA"][0:1, qq * 512:(qq + 1) * 512],
                              in_=psin[0:1, :])
                v.drain()
                ctx.tick("tv", v.engine_nop())
            ctx.on("vector", fsin)

    with nc.psum_tensor(f"ps2_{ctx.uid()}", [128, 128], F32) as ps2, \
         nc.psum_tensor(f"pc2_{ctx.uid()}", [1, 128], F32) as pc2:
        for h in range(H):
            def fini(v, h=h):
                ctx.w(v, "tv")
                v.tensor_copy(out=s["SM"][0:1, 64:192],
                              in_=s["TA"][0:1, h * 128:(h + 1) * 128])
                v.drain()
                ctx.tick("tv", v.engine_nop())
            ctx.on("vector", fini)

            for b in range(NT):
                def ffix(v, h=h, b=b):
                    ctx.w(v, "tt")
                    v.tensor_scalar(out=s["WS2"][0:1, 512:640], in0=s["SM"][0:1, 64:192],
                                    scalar1=s["SM"][0:1, 32 + h:33 + h], scalar2=None,
                                    op0=ALU.mult)
                    v.drain()
                    u0 = ctx.BB(b * C + h * 128, 128)[0:1, :]
                    v.tensor_add(out=u0, in0=u0, in1=s["WS2"][0:1, 512:640])
                    v.drain()
                    ctx.tick("tv", v.engine_nop())
                ctx.on("vector", ffix)

                def fmm2(te, h=h, b=b):
                    ctx.w(te, "tv")
                    te.matmul(out=ps2[:],
                              lhsT=s["QG"][:, 2048 + h * 128:2048 + (h + 1) * 128],
                              rhs=ctx.BB(b * C + h * 128, 128), start=True, stop=True)
                    ctx.tick("tt", te.matmul(out=pc2[:], lhsT=s["SC"][:, h:h + 1],
                                             rhs=ctx.BB(b * C + h * 128, 128),
                                             start=True, stop=True))
                ctx.on("tensor", fmm2)

                def fev2(v, h=h, b=b):
                    ctx.w(v, "tt")
                    v.tensor_copy(out=s["SM"][0:1, 64:192], in_=pc2[0:1, :])
                    v.drain()
                    dst = ctx.BB(b * C + h * 128, 128)
                    v.tensor_tensor(out=dst, in0=ps2[:],
                                    in1=ctx.BA(b * C + h * 128, 128), op=ALU.mult)
                    v.drain()
                    ctx.tick("tv", v.engine_nop())
                ctx.on("vector", fev2)

    ctx.barrier()
    if upto == "scan":
        return
    with nc.psum_tensor(f"ptr0_{ctx.uid()}", [128, 128], BF16) as q0, \
         nc.psum_tensor(f"ptr1_{ctx.uid()}", [128, 128], BF16) as q1:
        qp = [q0, q1]
        i = 0
        for cc in range(NC):
            for t in range(NT):
                def ftp(te, cc=cc, t=t, i=i):
                    ctx.w(te, "tv")
                    ctx.tick("tt", te.transpose(
                        out=qp[i % 2][:], in_=ctx.BB(t * C + cc * 128, 128),
                        identity=s["ID"][:]))

                def ftc(v, cc=cc, t=t, i=i):
                    ctx.w(v, "tt")
                    ctx.tick("tv", v.tensor_copy(
                        out=ctx.BA(cc * T_LOC + t * 128, 128), in_=qp[i % 2][:]))
                ctx.on("tensor", ftp)
                ctx.on("vector", ftc)
                i += 1

    def wo_evac(t, oq, pb):
        def fv(v, t=t, oq=oq, pb=pb):
            ctx.w(v, "tt")
            ctx.tick("tv", v.tensor_copy(out=ctx.BB(t * C + oq * 512, 512), in_=pb[:]))
        ctx.on("vector", fv)

    emit_proj(ctx, T_LOC, 0, NT, C,
              lambda cc, o0, ol: d["l0_woT"][128 * cc:128 * (cc + 1), o0:o0 + ol],
              F32, wo_evac,
              lhsT_fn=lambda cc, a, n: ctx.BA(cc * T_LOC + a, n))

    ctx.barrier()
    emit_bc_table(ctx, d["l0_lnw"])
    for t in range(NT):
        def fln(v, t=t):
            ctx.w(v, "dl")
            ctx.w(v, "dg")
            ctx.w(v, "tv")
            y = ctx.BB(t * C, C)
            v.reduce_sum(out=s["ST"][:, 24:25], in_=y, axis=AX)
            v.drain()
            v.tensor_scalar_mul(out=s["ST"][:, 24:25], in0=s["ST"][:, 24:25],
                                scalar1=1.0 / C)
            v.drain()
            ctx.tick("tv", v.engine_nop())
        ctx.on("vector", fln)

        def fln_s(sc, t=t):
            ctx.w(sc, "tv")
            y = ctx.BB(t * C, C)
            for k in range(4):
                sc.activation(out=s["TA"][:, 0:512], in_=y[:, 512 * k:512 * (k + 1)],
                              func=AF.Square, bias=s["ST"][:, 24:25], scale=-1.0,
                              accum_out=s["ST"][:, 40 + k:41 + k])
            ctx.tick("ts", sc.copy(out=s["SM"][:, 191:192], in_=s["SM"][:, 191:192]))
        ctx.on("scalar", fln_s)

        def fln_v(v, t=t):
            ctx.w(v, "ts")
            v.tensor_add(out=s["ST"][:, 40:41], in0=s["ST"][:, 40:41],
                         in1=s["ST"][:, 41:42])
            v.tensor_add(out=s["ST"][:, 42:43], in0=s["ST"][:, 42:43],
                         in1=s["ST"][:, 43:44])
            v.drain()
            v.tensor_add(out=s["ST"][:, 40:41], in0=s["ST"][:, 40:41],
                         in1=s["ST"][:, 42:43])
            v.drain()
            ctx.tick("tv", v.tensor_scalar(out=s["ST"][:, 26:27], in0=s["ST"][:, 40:41],
                                           scalar1=1.0 / C, scalar2=LN_EPS,
                                           op0=ALU.mult, op1=ALU.add))
        ctx.on("vector", fln_v)

        def fln_s2(sc, t=t):
            ctx.w(sc, "tv")
            sc.activation(out=s["ST"][:, 26:27], in_=s["ST"][:, 26:27], func=AF.Ln)
            ctx.tick("ts", sc.activation(out=s["ST"][:, 26:27], in_=s["ST"][:, 26:27],
                                         func=AF.Exp, scale=-0.5))
        ctx.on("scalar", fln_s2)

        def fln2(v, t=t):
            ctx.w(v, "ts")
            y = ctx.BB(t * C, C)
            v.tensor_scalar(out=s["TA"][:], in0=y, scalar1=s["ST"][:, 24:25],
                            scalar2=s["ST"][:, 26:27], op0=ALU.subtract, op1=ALU.mult)
            v.tensor_tensor(out=s["TA"][:], in0=s["TA"][:], in1=s["BC"][:], op=ALU.mult)
            xr = ctx.xr(1 + t)
            ctx.tick("tv", v.tensor_add(out=xr, in0=xr, in1=s["TA"][:]))
        ctx.on("vector", fln2)

    ctx.barrier()
    emit_bc_table(ctx, d["l0_lnb"])
    import os as _os
    if not _os.environ.get("SKIP_LNB"):
        for t in range(NT):
            def flb(v, t=t):
                ctx.w(v, "dl")
                ctx.w(v, "dg")
                xr = ctx.xr(1 + t)
                ctx.tick("tv", v.tensor_add(out=xr, in0=xr, in1=s["BC"][:]))
            ctx.on("vector", flb)
    ctx.barrier()



def emit_l1_attn(ctx, upto="full", tb=0):
    nc, s, d = ctx.nc, ctx.s, ctx.d
    Tt = T_LOC
    ctx.barrier()

    def fld(sync):
        ctx.dtick("dl", sync.dma_start(out=s["ST"][:, 48:64],
                                       in_=d["l1_slope"].broadcast_to([128, H])))
        ctx.dtick("dl", sync.dma_start(out=s["CW"][:, 0:2], in_=d["l1_km"]))
        ctx.dtick("dl", sync.dma_start(out=s["ST"][:, 32:40],
                                       in_=d["cc_pownT"].broadcast_to([128, 8])))
        ctx.dtick("dl", sync.dma_start(out=s["ST"][:, 40:48],
                                       in_=d["cc_selT"].broadcast_to([128, 8])))
    ctx.on("sync", fld)
    PS = lambda h: s["ST"][:, 48 + h:49 + h]

    def silu_evac(dstf):
        def evac(t, oq, pb):
            def fs(sc, t=t, oq=oq, pb=pb):
                ctx.w(sc, "tt")
                ctx.tick("ts", sc.activation(
                    out=dstf(t)[:, oq * 512:(oq + 1) * 512], in_=pb[:], func=AF.Silu))
            ctx.on("scalar", fs)
        return evac

    emit_proj(ctx, Tt, 0, NT, C,
              lambda cc, o0, ol: d["l1_wqkvT"][128 * cc:128 * (cc + 1),
                                               2048 + o0:2048 + o0 + ol],
              F32, silu_evac(lambda t: ctx.BA(t * C, C)))
    emit_proj(ctx, Tt, 0, NT, C,
              lambda cc, o0, ol: d["l1_wqkvT"][128 * cc:128 * (cc + 1),
                                               4096 + o0:4096 + o0 + ol],
              F32, silu_evac(lambda t: ctx.BB(t * C, C)))

    if upto == "proj":
        return
    with nc.psum_tensor(f"pkv1_{ctx.uid()}", [128, 128], F32) as pkv:
        for h in range(H):
            def fkd(sc, h=h):
                if h == 0:
                    ctx.w(sc, "ts")
                    ctx.w(sc, "dl")
                ctx.w(sc, "tv")
                ctx.tick("ts", sc.activation(out=s["SM"][:, 2:4], in_=s["CW"][:, 0:2],
                                             func=AF.Exp, scale=PS(h)))
            ctx.on("scalar", fkd)

            def fbd(v, h=h):
                ctx.w(v, "ts")
                v.tensor_scalar_mul(out=s["SM"][:, 4:5], in0=PS(h), scalar1=-256.0)
                v.drain()
                ctx.tick("tv", v.engine_nop())
            ctx.on("vector", fbd)

            def fbd2(sc, h=h):
                ctx.w(sc, "tv")
                ctx.tick("ts", sc.activation(out=s["SM"][:, 4:5], in_=s["SM"][:, 4:5],
                                             func=AF.Exp))
            ctx.on("scalar", fbd2)

            for b in range(4):
                for jh in range(2):
                    def fks(v, h=h, b=b, jh=jh):
                        ctx.w(v, "ts")
                        ctx.w(v, "tt")
                        ksl = ctx.BA((2 * b + jh) * C + h * 128, 128)
                        v.tensor_scalar(
                            out=s["BC"][:, 128:256], in0=ksl,
                            scalar1=s["SM"][:, 2 + jh:3 + jh], scalar2=None,
                            op0=ALU.mult)
                        v.drain()
                        ctx.tick("tv", v.engine_nop())
                    ctx.on("vector", fks)

                    def fkm(te, h=h, b=b, jh=jh):
                        ctx.w(te, "tv")
                        ctx.tick("tt", te.matmul(
                            out=pkv[:], lhsT=s["BC"][:, 128:256],
                            rhs=ctx.BB((2 * b + jh) * C + h * 128, 128),
                            start=(jh == 0), stop=(jh == 1)))
                    ctx.on("tensor", fkm)

                def fkc(v, h=h, b=b):
                    ctx.w(v, "tt")
                    kvs = s["SM"][:, 64:192]
                    if b == 0:
                        inst = v.tensor_copy(out=kvs, in_=pkv[:])
                    else:
                        inst = v.scalar_tensor_tensor(out=kvs, in0=kvs,
                                                      scalar=s["SM"][:, 4:5], in1=pkv[:],
                                                      op0=ALU.mult, op1=ALU.add)
                    ctx.tick("tv", inst)
                    v.drain()
                ctx.on("vector", fkc)

            def fst(g, h=h):
                ctx.w(g, "tv")
                ctx.dtick("dg", g.dma_start(out=d["cin1"][h * 128:(h + 1) * 128, :],
                                            in_=s["SM"][:, 64:192]))
            ctx.on("gpsimd", fst)

    def fcc(g):
        ctx.w(g, "dg")
        ctx.tick("cc", g.collective_compute(
            "AllGather", ALU.bypass, replica_groups=RG, ins=[d["cin1"]],
            outs=[d["cout1"]]))
    ctx.on("gpsimd", fcc)

    for grp in range(4):
        emit_proj(ctx, Tt, 0, NT, 512,
                  lambda cc, o0, ol, grp=grp: d["l1_wqkvT"][128 * cc:128 * (cc + 1),
                                                            grp * 512 + o0:grp * 512 + o0 + ol],
                  F32, silu_evac(lambda t: s["QG"][:, t * 512:(t + 1) * 512]))

        with nc.psum_tensor(f"pq0_{ctx.uid()}", [128, 256], F32) as pq0, \
             nc.psum_tensor(f"pq1_{ctx.uid()}", [128, 256], F32) as pq1, \
             nc.psum_tensor(f"po0_{ctx.uid()}", [128, 128], F32) as po0, \
             nc.psum_tensor(f"po1_{ctx.uid()}", [128, 128], F32) as po1, \
             nc.psum_tensor(f"pkv2_{ctx.uid()}", [128, 128], F32) as pkv2, \
             nc.psum_tensor(f"ptq0_{ctx.uid()}", [128, 128], BF16) as ptq0, \
             nc.psum_tensor(f"ptq1_{ctx.uid()}", [128, 128], BF16) as ptq1:
            for hh in range(4):
                h = grp * 4 + hh

                def fcmb(g, h=h):
                    if h == 0:
                        ctx.w(g, "cc")
                    ctx.w(g, "tv")
                    ctx.w(g, "ts")
                    for j in range(8):
                        ctx.dtick("dg", g.dma_start(
                            out=s["TA"][:, j * 128:(j + 1) * 128],
                            in_=d["cout1"][j * 2048 + h * 128:j * 2048 + (h + 1) * 128, :]))
                ctx.on("gpsimd", fcmb)

                def fw8(sc, h=h):
                    ctx.w(sc, "dg")
                    ctx.tick("ts", sc.activation(out=s["SM"][:, 8:16],
                                                 in_=s["ST"][:, 32:40],
                                                 func=AF.Exp, scale=PS(h)))
                ctx.on("scalar", fw8)

                def fw8b(v, h=h):
                    ctx.w(v, "ts")
                    v.tensor_tensor(out=s["SM"][:, 8:16], in0=s["SM"][:, 8:16],
                                    in1=s["ST"][:, 40:48], op=ALU.mult)
                    kvs = s["SM"][:, 64:192]
                    v.drain()
                    v.tensor_scalar(out=kvs, in0=s["TA"][:, 0:128],
                                    scalar1=s["SM"][:, 8:9], scalar2=None, op0=ALU.mult)
                    for j in range(1, 8):
                        v.drain()
                        v.scalar_tensor_tensor(out=kvs,
                                               in0=s["TA"][:, j * 128:(j + 1) * 128],
                                               scalar=s["SM"][:, 8 + j:9 + j], in1=kvs,
                                               op0=ALU.mult, op1=ALU.add)
                    ctx.tick("tv", v.engine_nop())
                ctx.on("vector", fw8b)

                def ftl(sync, h=h):
                    ctx.w(sync, "tv")
                    ctx.dtick("dl", sync.dma_start(out=s["TA"][:, 0:256], in_=d["l1_i0"]))
                    ctx.dtick("dl", sync.dma_start(out=s["TA"][:, 256:512], in_=d["l1_i1"]))
                    ctx.dtick("dl", sync.dma_start(out=s["TA"][:, 512:768], in_=d["l1_msk0"]))
                    ctx.dtick("dl", sync.dma_start(out=s["TA"][:, 768:1024], in_=d["l1_msk1"]))
                    ctx.dtick("dl", sync.dma_start(out=s["TA"][:, 1024:2048],
                                                   in_=d["l1_qm"].broadcast_to([128, T_LOC])))
                ctx.on("sync", ftl)

                def fdec(sc, h=h):
                    ctx.w(sc, "dl")
                    sc.activation(out=s["TA"][:, 0:256], in_=s["TA"][:, 0:256],
                                  func=AF.Exp, scale=PS(h))
                    sc.activation(out=s["TA"][:, 256:512], in_=s["TA"][:, 256:512],
                                  func=AF.Exp, scale=PS(h))
                    ctx.tick("ts", sc.activation(out=s["TA"][:, 1024:2048],
                                                 in_=s["TA"][:, 1024:2048],
                                                 func=AF.Exp, scale=PS(h)))
                ctx.on("scalar", fdec)

                def fdecm(v, h=h):
                    ctx.w(v, "ts")
                    v.tensor_tensor(out=s["SC"][:, 0:256], in0=s["TA"][:, 0:256],
                                    in1=s["TA"][:, 512:768], op=ALU.mult)
                    ctx.tick("tv", v.tensor_tensor(out=s["SC"][:, 256:512],
                                                   in0=s["TA"][:, 256:512],
                                                   in1=s["TA"][:, 768:1024], op=ALU.mult))
                ctx.on("vector", fdecm)

                for t in range(NT):
                    def ftr(te, h=h, hh=hh, t=t):
                        ctx.w(te, "tv")
                        te.transpose(out=ptq0[:], in_=ctx.BA(t * C + h * 128, 128),
                                     identity=s["ID"][:])
                        ctx.tick("tt", te.transpose(
                            out=ptq1[:],
                            in_=s["QG"][:, t * 512 + hh * 128:t * 512 + (hh + 1) * 128],
                            identity=s["ID"][:]))
                    ctx.on("tensor", ftr)

                    def ftrc(v, t=t):
                        ctx.w(v, "tt")
                        v.tensor_copy(out=s["WSa"][:, 1024 + t * 128:1024 + (t + 1) * 128],
                                      in_=ptq0[:])
                        ctx.tick("tv", v.tensor_copy(
                            out=s["WSa"][:, t * 128:(t + 1) * 128], in_=ptq1[:]))
                    ctx.on("vector", ftrc)

                def fqd(v, h=h):
                    ctx.w(v, "tv")
                    ctx.tick("tv", v.tensor_tensor(out=s["WSb"][:, 0:1024],
                                                   in0=s["WSa"][:, 0:1024],
                                                   in1=s["TA"][:, 1024:2048],
                                                   op=ALU.mult))
                ctx.on("vector", fqd)

                def fkd2(sc, h=h):
                    ctx.w(sc, "tv")
                    ctx.tick("ts", sc.activation(out=s["SM"][:, 2:4], in_=s["CW"][:, 0:2],
                                                 func=AF.Exp, scale=PS(h)))
                ctx.on("scalar", fkd2)

                def fbd_2(v, h=h):
                    ctx.w(v, "ts")
                    v.tensor_scalar_mul(out=s["SM"][:, 4:5], in0=PS(h), scalar1=-256.0)
                    v.drain()
                    ctx.tick("tv", v.engine_nop())
                ctx.on("vector", fbd_2)

                def fbd2_2(sc, h=h):
                    ctx.w(sc, "tv")
                    ctx.tick("ts", sc.activation(out=s["SM"][:, 4:5], in_=s["SM"][:, 4:5],
                                                 func=AF.Exp))
                ctx.on("scalar", fbd2_2)

                for b in range(4):
                    def fkb(v, h=h, b=b):
                        ctx.w(v, "ts")
                        ctx.w(v, "tt")
                        v.tensor_copy(out=s["BC"][:, 0:128], in_=s["SM"][:, 64:192])
                        v.tensor_scalar(out=s["BC"][:, 128:256],
                                        in0=ctx.BA((2 * b) * C + h * 128, 128),
                                        scalar1=s["SM"][:, 2:3], scalar2=None,
                                        op0=ALU.mult)
                        v.tensor_scalar(
                            out=s["BC"][:, 256:384],
                            in0=ctx.BA((2 * b + 1) * C + h * 128, 128),
                            scalar1=s["SM"][:, 3:4], scalar2=None, op0=ALU.mult)
                        v.drain()
                        ctx.tick("tv", v.engine_nop())
                    ctx.on("vector", fkb)

                    def fqk(te, h=h, b=b):
                        ctx.w(te, "tv")
                        te.matmul(out=pq0[:],
                                  lhsT=s["WSa"][:, 1024 + (2 * b) * 128:1024 + (2 * b + 1) * 128],
                                  rhs=s["WSa"][:, b * 256:(b + 1) * 256],
                                  start=True, stop=True)
                        ctx.tick("tt", te.matmul(
                            out=pq1[:],
                            lhsT=s["WSa"][:, 1024 + (2 * b + 1) * 128:1024 + (2 * b + 2) * 128],
                            rhs=s["WSa"][:, b * 256:(b + 1) * 256],
                            start=True, stop=True))
                    ctx.on("tensor", fqk)

                    def fqs(v, b=b):
                        ctx.w(v, "tt")
                        v.tensor_tensor(out=s["WSb"][:, 1024:1280], in0=pq0[:],
                                        in1=s["SC"][:, 0:256], op=ALU.mult)
                        v.tensor_tensor(out=s["WSb"][:, 1280:1536], in0=pq1[:],
                                        in1=s["SC"][:, 256:512], op=ALU.mult)
                        v.drain()
                        ctx.tick("tv", v.engine_nop())
                    ctx.on("vector", fqs)

                    def fo(te, h=h, b=b):
                        ctx.w(te, "tv")
                        for mh, po in ((0, po0), (1, po1)):
                            te.matmul(out=po[:],
                                      lhsT=s["WSb"][:, 1024 + mh * 128:1152 + mh * 128],
                                      rhs=ctx.BB((2 * b) * C + h * 128, 128),
                                      start=True, stop=False)
                            te.matmul(out=po[:],
                                      lhsT=s["WSb"][:, 1280 + mh * 128:1408 + mh * 128],
                                      rhs=ctx.BB((2 * b + 1) * C + h * 128, 128),
                                      start=False, stop=False)
                            te.matmul(out=po[:],
                                      lhsT=s["WSb"][:, b * 256 + mh * 128:b * 256 + (mh + 1) * 128],
                                      rhs=s["BC"][:, 0:128],
                                      start=False, stop=True)
                        te.matmul(out=pkv2[:], lhsT=s["BC"][:, 128:256],
                                  rhs=ctx.BB((2 * b) * C + h * 128, 128),
                                  start=True, stop=False)
                        ctx.tick("tt", te.matmul(
                            out=pkv2[:], lhsT=s["BC"][:, 256:384],
                            rhs=ctx.BB((2 * b + 1) * C + h * 128, 128),
                            start=False, stop=True))
                    ctx.on("tensor", fo)

                    def fev(v, h=h, b=b):
                        ctx.w(v, "tt")
                        v.tensor_copy(out=ctx.BA((2 * b) * C + h * 128, 128), in_=po0[:])
                        v.tensor_copy(out=ctx.BA((2 * b + 1) * C + h * 128, 128),
                                      in_=po1[:])
                        kvs = s["SM"][:, 64:192]
                        ctx.tick("tv", v.scalar_tensor_tensor(
                            out=kvs, in0=kvs, scalar=s["SM"][:, 4:5], in1=pkv2[:],
                            op0=ALU.mult, op1=ALU.add))
                        v.drain()
                    ctx.on("vector", fev)

    ctx.barrier()
    if upto == "attn":
        return
    emit_bc_table(ctx, d["l1_normw"])

    for t in range(NT):
        def frs_s(sc, t=t):
            if t == 0:
                ctx.w(sc, "dl")
                ctx.w(sc, "dg")
                ctx.w(sc, "tv")
            ao = ctx.BA(t * C, C)
            for k in range(4):
                sc.activation(out=s["TA"][:, 0:512], in_=ao[:, 512 * k:512 * (k + 1)],
                              func=AF.Square, accum_out=s["ST"][:, 40 + k:41 + k])
            ctx.tick("ts", sc.copy(out=s["SM"][:, 191:192], in_=s["SM"][:, 191:192]))
        ctx.on("scalar", frs_s)

        def frs_v(v, t=t):
            ctx.w(v, "ts")
            rcol = s["SM"][:, 24 + t:25 + t]
            v.tensor_add(out=s["ST"][:, 40:41], in0=s["ST"][:, 40:41],
                         in1=s["ST"][:, 41:42])
            v.tensor_add(out=s["ST"][:, 42:43], in0=s["ST"][:, 42:43],
                         in1=s["ST"][:, 43:44])
            v.drain()
            v.tensor_add(out=s["ST"][:, 40:41], in0=s["ST"][:, 40:41],
                         in1=s["ST"][:, 42:43])
            v.drain()
            v.tensor_scalar(out=rcol, in0=s["ST"][:, 40:41],
                            scalar1=1.0 / C, scalar2=L1_RMS_EPS,
                            op0=ALU.mult, op1=ALU.add)
            v.drain()
            ctx.tick("tv", v.engine_nop())
        ctx.on("vector", frs_v)

        def frs_s2(sc, t=t):
            ctx.w(sc, "tv")
            rcol = s["SM"][:, 24 + t:25 + t]
            sc.activation(out=rcol, in_=rcol, func=AF.Ln)
            ctx.tick("ts", sc.activation(out=rcol, in_=rcol, func=AF.Exp, scale=-0.5))
        ctx.on("scalar", frs_s2)

    def gt_evac(t, oq, pb):
        def fg1(sc, t=t, oq=oq, pb=pb):
            ctx.w(sc, "tt")
            ctx.w(sc, "tv")
            ctx.tick("ts", sc.activation(out=s["TA"][:, 0:512], in_=pb[:],
                                         func=AF.Sigmoid))
        ctx.on("scalar", fg1)

        def fg2(v, t=t, oq=oq):
            ctx.w(v, "ts")
            ctx.w(v, "dg")
            v.tensor_tensor(out=s["TA"][:, 0:512], in0=s["TA"][:, 0:512],
                            in1=ctx.BA(t * C + oq * 512, 512), op=ALU.mult)
            v.tensor_scalar(out=s["TA"][:, 0:512], in0=s["TA"][:, 0:512],
                            scalar1=s["SM"][:, 24 + t:25 + t], scalar2=None,
                            op0=ALU.mult)
            ctx.tick("tv", v.tensor_tensor(
                out=ctx.BB(t * C + oq * 512, 512), in0=s["TA"][:, 0:512],
                in1=s["BC"][:, oq * 512:(oq + 1) * 512], op=ALU.mult))
        ctx.on("vector", fg2)

    emit_proj(ctx, Tt, 0, NT, C,
              lambda cc, o0, ol: d["l1_wgT"][128 * cc:128 * (cc + 1), o0:o0 + ol],
              F32, gt_evac)

    with nc.psum_tensor(f"ptg0_{ctx.uid()}", [128, 128], BF16) as g0, \
         nc.psum_tensor(f"ptg1_{ctx.uid()}", [128, 128], BF16) as g1:
        gp = [g0, g1]
        i = 0
        for cc in range(NC):
            for t in range(NT):
                def ftp(te, cc=cc, t=t, i=i):
                    ctx.w(te, "tv")
                    ctx.tick("tt", te.transpose(
                        out=gp[i % 2][:], in_=ctx.BB(t * C + cc * 128, 128),
                        identity=s["ID"][:]))

                def ftc(v, cc=cc, t=t, i=i):
                    ctx.w(v, "tt")
                    ctx.tick("tv", v.tensor_copy(
                        out=ctx.BA(cc * T_LOC + t * 128, 128), in_=gp[i % 2][:]))
                ctx.on("tensor", ftp)
                ctx.on("vector", ftc)
                i += 1

    def out_evac(t, oq, pb):
        def fv(v, t=t, oq=oq, pb=pb):
            ctx.w(v, "tt")
            xr = ctx.xr(tb + t)
            ctx.tick("tv", v.tensor_add(out=xr[:, oq * 512:(oq + 1) * 512],
                                        in0=xr[:, oq * 512:(oq + 1) * 512], in1=pb[:]))
        ctx.on("vector", fv)

    emit_proj(ctx, T_LOC, 0, NT, C,
              lambda cc, o0, ol: d["l1_woT"][128 * cc:128 * (cc + 1), o0:o0 + ol],
              F32, out_evac,
              lhsT_fn=lambda cc, a, n: ctx.BA(cc * T_LOC + a, n))
    ctx.barrier()



def declare_dram(ctx, nc, fixed, pc, part):
    d = ctx.d
    need = {
        1: ["l0_pgT", "l0_puT", "l0_sg", "l0_su", "l0_pdT", "l0_sd", "shfcol",
            "identity"],
        2: ["l0_wqT", "l0_wkT", "l0_wvT", "l0_woT", "l0_wgT", "l0_convw", "l0_convb",
            "l0_gateb", "l0_lnw", "l0_lnb", "l0_gamma", "l0_gamma_rep", "identity",
            "iota_mj", "mask_mj", "grev", "shfcol", "cc_sel", "cc_pow"],
        4: ["l1_qgT", "l1_quT", "l1_qdT", "l1_sgc", "l1_suc", "l1_sd", "shfcol",
            "identity"],
        5: ["l1_slope", "l1_wqkvT", "l1_wgT", "l1_woT", "l1_normw", "l1_i0", "l1_i1",
            "l1_msk0", "l1_msk1", "l1_qm", "l1_km", "shfcol", "identity",
            "cc_selT", "cc_pownT"],
    }
    names = need[2 if part == 2 else (5 if part == 5 else (1 if part in (1, 3) else 4))]
    srcs = dict(fixed)
    srcs.update(pc)
    dtmap = {np.dtype(np.float32): F32, np.dtype(np.int32): I32,
             np.dtype(np.float16): F16}
    Tin = T_EXT if part <= 2 else T_LOC
    Tout = T_EXT if part == 1 else T_LOC
    d["xin"] = nc.dram_tensor("xin", [Tin, C], F32, kind="ExternalInput").ap()
    d["xout"] = nc.dram_tensor("xout", [Tout, C], F32, kind="ExternalOutput").ap()
    feed = {"xin": srcs["xin"][:Tin]}
    for n in names:
        a = srcs[n]
        d[n] = nc.dram_tensor(n, list(a.shape), dtmap[a.dtype], kind="ExternalInput").ap()
        feed[n] = a
    if part in (1, 3):
        d["cin_w"] = nc.dram_tensor("cin_w", [35 * 128, 1024], BF16, kind="Internal").ap()
        d["cout_w"] = nc.dram_tensor("cout_w", [280 * 128, 1024], BF16, kind="Internal", addr_space="Shared").ap()
        for n in ("myshard", "myscale"):
            a = pc[n]
            d[n] = nc.dram_tensor(n, list(a.shape), dtmap[a.dtype], kind="ExternalInput").ap()
    if part == 2:
        d["xc_d"] = nc.dram_tensor("xc_d", [NC, 128, T_LOC], BF16, kind="Internal").ap()
        d["cin0"] = nc.dram_tensor("cin0", [1, C], F32, kind="Internal").ap()
        d["cout0"] = nc.dram_tensor("cout0", [8, C], F32, kind="Internal").ap()
    if part == 5:
        d["cin1"] = nc.dram_tensor("cin1", [C, 128], F32, kind="Internal").ap()
        d["cout1"] = nc.dram_tensor("cout1", [8 * C, 128], F32, kind="Internal").ap()
    return feed


def build_part(part, fixed, per_core):
    nc = bass.Bass("TRN2", target_bir_lowering=False, debug=False)
    es = ExitStack()
    with es:
        ctx = Ctx(nc, es)
        declare_dram(ctx, nc, fixed, per_core[0], part)
        alloc_common(ctx, nc, es)
        block = es.enter_context(nc.Block())
        ctx.block = block
        emit_preamble(ctx)
        if part == 1:
            emit_load_x(ctx, NTE)
            emit_l0_dequant(ctx)
            emit_ln(ctx, 0, NTE, False)
            emit_ffn(ctx, "l0", 0, NTE, NTE * 128)
            emit_store_x(ctx, 0, NTE)
        elif part == 2:
            emit_load_x(ctx, NTE)
            emit_ln(ctx, 0, NTE, False)
            emit_l0_attn(ctx)
            emit_store_x(ctx, 1, NT)
        elif part == 3:
            emit_load_x(ctx, NT)
            emit_l0_dequant(ctx)
            emit_ln(ctx, 0, NT, False)
            emit_ffn(ctx, "l0", 0, NT, T_LOC)
            emit_store_x(ctx, 0, NT)
        elif part in (4, 6):
            emit_load_x(ctx, NT)
            emit_ln(ctx, 0, NT, False)
            emit_ffn(ctx, "l1", 0, NT, T_LOC)
            emit_store_x(ctx, 0, NT)
        elif part == 5:
            emit_load_x(ctx, NT)
            emit_ln(ctx, 0, NT, False)
            emit_l1_attn(ctx)
            emit_store_x(ctx, 0, NT)
    return nc


def feeds_for(part, fixed, per_core):
    need = {
        1: ["l0_pgT", "l0_puT", "l0_sg", "l0_su", "l0_pdT", "l0_sd", "shfcol",
            "identity"],
        2: ["l0_wqT", "l0_wkT", "l0_wvT", "l0_woT", "l0_wgT", "l0_convw", "l0_convb",
            "l0_gateb", "l0_lnw", "l0_lnb", "l0_gamma", "l0_gamma_rep", "identity",
            "iota_mj", "mask_mj", "grev", "shfcol", "cc_sel", "cc_pow"],
        4: ["l1_qgT", "l1_quT", "l1_qdT", "l1_sgc", "l1_suc", "l1_sd", "shfcol",
            "identity"],
        5: ["l1_slope", "l1_wqkvT", "l1_wgT", "l1_woT", "l1_normw", "l1_i0", "l1_i1",
            "l1_msk0", "l1_msk1", "l1_qm", "l1_km", "shfcol", "identity",
            "cc_selT", "cc_pownT"],
    }
    names = need[2 if part == 2 else (5 if part == 5 else (1 if part in (1, 3) else 4))]
    Tin = T_EXT if part <= 2 else T_LOC
    feeds = []
    for pc in per_core:
        f = {"xin": np.ascontiguousarray(pc["xin"][:Tin])}
        if part in (1, 3):
            f["myshard"] = pc["myshard"]
            f["myscale"] = pc["myscale"]
        for n in names:
            f[n] = pc[n] if n in pc else fixed[n]
        feeds.append(f)
    return feeds


PARTS = [1, 2, 3, 4, 5, 6]


ALL_NAMES = sorted(set(
    ["l0_pgT", "l0_puT", "l0_sg", "l0_su", "l0_pdT", "l0_sd", "shfcol", "identity",
     "l0_wqT", "l0_wkT", "l0_wvT", "l0_woT", "l0_wgT", "l0_convw", "l0_convb",
     "l0_gateb", "l0_lnw", "l0_lnb", "l0_gamma", "l0_gamma_rep",
     "iota_mj", "mask_mj", "grev", "cc_sel", "cc_pow",
     "l1_qgT", "l1_quT", "l1_qdT", "l1_sgc", "l1_suc", "l1_sd",
     "l1_slope", "l1_wqkvT", "l1_wgT", "l1_woT", "l1_normw", "l1_i0", "l1_i1",
     "l1_msk0", "l1_msk1", "l1_qm", "l1_km", "cc_selT", "cc_pownT"]))


def declare_dram_fused(ctx, nc, fixed, pc):
    d = ctx.d
    srcs = dict(fixed)
    srcs.update(pc)
    dtmap = {np.dtype(np.float32): F32, np.dtype(np.int32): I32,
             np.dtype(np.float16): F16}
    d["xin"] = nc.dram_tensor("xin", [T_EXT, C], F32, kind="ExternalInput").ap()
    d["xout"] = nc.dram_tensor("xout", [T_LOC, C], F32, kind="ExternalOutput").ap()
    for n in ALL_NAMES:
        a = srcs[n]
        d[n] = nc.dram_tensor(n, list(a.shape), dtmap[a.dtype], kind="ExternalInput").ap()
    d["cin_w"] = nc.dram_tensor("cin_w", [35 * 128, 1024], BF16, kind="Internal").ap()
    d["cout_w"] = nc.dram_tensor("cout_w", [280 * 128, 1024], BF16, kind="Internal", addr_space="Shared").ap()
    for n in ("myshard", "myscale"):
        a = pc[n]
        d[n] = nc.dram_tensor(n, list(a.shape), dtmap[a.dtype], kind="ExternalInput").ap()
    d["xc_d"] = nc.dram_tensor("xc_d", [NC, 128, T_LOC], BF16, kind="Internal").ap()
    d["cin0"] = nc.dram_tensor("cin0", [1, C], F32, kind="Internal").ap()
    d["cout0"] = nc.dram_tensor("cout0", [8, C], F32, kind="Internal").ap()
    d["cin1"] = nc.dram_tensor("cin1", [C, 128], F32, kind="Internal").ap()
    d["cout1"] = nc.dram_tensor("cout1", [8 * C, 128], F32, kind="Internal").ap()


def build_fused(fixed, per_core):
    nc = bass.Bass("TRN2", target_bir_lowering=False, debug=False)
    es = ExitStack()
    with es:
        ctx = Ctx(nc, es)
        declare_dram_fused(ctx, nc, fixed, per_core[0])
        alloc_common(ctx, nc, es)
        block = es.enter_context(nc.Block())
        ctx.block = block
        import os as _os
        nsub = int(_os.environ.get("N_SUB", "6"))
        emit_preamble(ctx)
        emit_load_x(ctx, NTE)
        emit_l0_dequant(ctx)
        emit_ln(ctx, 0, NTE, False)
        emit_ffn(ctx, "l0", 0, NTE, NTE * 128)
        if nsub >= 2:
            emit_ln(ctx, 0, NTE, False)
            emit_l0_attn(ctx)
        if nsub >= 3:
            emit_ln(ctx, 1, NT, False)
            emit_ffn(ctx, "l0", 1, NT, T_LOC)
        if nsub >= 4:
            emit_ln(ctx, 1, NT, False)
            emit_ffn(ctx, "l1", 1, NT, T_LOC)
        if nsub >= 5:
            emit_ln(ctx, 1, NT, False)
            emit_l1_attn(ctx, tb=1)
        if nsub >= 6:
            emit_ln(ctx, 1, NT, False)
            emit_ffn(ctx, "l1", 1, NT, T_LOC)
        if _os.environ.get("DBG_BIG"):
            dbg = nc.dram_tensor("dbg", [128, 2 * NT * C], BF16,
                                 kind="ExternalOutput").ap()

            def fdbg(g):
                ctx.wall(g)
                ctx.dtick("dg", g.dma_start(out=dbg, in_=ctx.s["BIG"][:, :]))
                ctx.w(g, "dg")
            ctx.on("gpsimd", fdbg)
        emit_store_x(ctx, 1, NT)
    return nc


def feeds_fused(fixed, per_core):
    feeds = []
    for pc in per_core:
        f = {"xin": np.ascontiguousarray(pc["xin"]),
             "myshard": pc["myshard"], "myscale": pc["myscale"]}
        for n in ALL_NAMES:
            f[n] = pc[n] if n in pc else fixed[n]
        feeds.append(f)
    return feeds


FUSED = True


def kernel(**inputs):
    fixed, per_core = prep(inputs)
    if FUSED:
        nc = build_fused(fixed, per_core)
        feeds = feeds_fused(fixed, per_core)
        res = run_bass_kernel_spmd(nc, feeds, core_ids=list(range(NCORES)))
        xs = [np.asarray(r["xout"]) for r in res.results]
        return np.concatenate(xs, 0).reshape(1, 8192, C).astype(np.float32)
    xs = [pc["xin"] for pc in per_core]
    for part in PARTS:
        for i in range(NCORES):
            per_core[i]["xin"] = xs[i]
        nc = build_part(part, fixed, per_core)
        feeds = feeds_for(part, fixed, per_core)
        res = run_bass_kernel_spmd(nc, feeds, core_ids=list(range(NCORES)))
        xs = [np.asarray(r["xout"]) for r in res.results]
    return np.concatenate(xs, 0).reshape(1, 8192, C).astype(np.float32)


# version 07 (final)
# speedup vs baseline: 1.6740x; 1.0324x over previous
import numpy as np
from contextlib import ExitStack

import concourse.bass as bass
import concourse.mybir as mybir
from concourse.bass_utils import run_bass_kernel_spmd

F32 = mybir.dt.float32
BF16 = mybir.dt.bfloat16
F16 = mybir.dt.float16
I32 = mybir.dt.int32
AF = mybir.ActivationFunctionType
ALU = mybir.AluOpType
AX = mybir.AxisListType.X

C = 2048
NC = 16
IF = 5632
NI = 44
H = 16
T_LOC = 1024
HALO = 128
T_EXT = T_LOC + HALO
NT = 8
NTE = 9
NCORES = 8
LN_EPS = 1e-5
L0_NORM_EPS = 1e-12
L0_FFN_EPS = 1e-6
L1_RMS_EPS = 1e-6
BLK1 = 256
RG = [[0, 1, 2, 3, 4, 5, 6, 7]]


def perm128(n):
    base = np.arange(n) // 128 * 128
    r = np.arange(n) % 128
    return base + 8 * (r % 16) + (r // 16)



def prep(inputs):
    f = {}
    x = np.asarray(inputs["L0_attn_x"], np.float32).reshape(8192, C)
    xp = np.concatenate([np.zeros((HALO, C), np.float32), x], 0)
    per_core = []
    for i in range(NCORES):
        st = i * T_LOC
        pc = {"xin": np.ascontiguousarray(xp[st:st + T_EXT])}
        sel = np.zeros((8, 1), np.float32)
        pw = np.zeros((8, 1), np.float32)
        for j in range(i):
            sel[j] = 1.0
            pw[j] = 1024.0 * float(i - 1 - j)
        pc["cc_sel"] = sel
        pc["cc_pow"] = pw
        pc["cc_selT"] = np.ascontiguousarray(sel.T)
        pc["cc_pownT"] = np.ascontiguousarray((-pw).T)
        per_core.append(pc)

    T = lambda a: np.ascontiguousarray(np.asarray(a).T)
    f["l0_wqT"] = T(inputs["L0_attn_Wq"])
    f["l0_wkT"] = T(inputs["L0_attn_Wk"])
    f["l0_wvT"] = T(inputs["L0_attn_Wv"])
    f["l0_woT"] = T(inputs["L0_attn_Wo"])
    f["l0_wgT"] = T(inputs["L0_attn_gate_w"])
    cw = np.asarray(inputs["L0_attn_conv_w"], np.float32).reshape(C, 4)
    f["l0_convw"] = np.ascontiguousarray(
        cw.reshape(NC, 128, 4).transpose(1, 0, 2).reshape(128, NC * 4))
    f["l0_convb"] = np.ascontiguousarray(
        np.asarray(inputs["L0_attn_conv_b"], np.float32).reshape(NC, 128).T)
    f["l0_gateb"] = np.asarray(inputs["L0_attn_gate_b"], np.float32).reshape(1, C)
    f["l0_lnw"] = np.asarray(inputs["L0_attn_ln_w"], np.float32).reshape(1, C)
    f["l0_lnb"] = np.asarray(inputs["L0_attn_ln_b"], np.float32).reshape(1, C)
    gam = np.asarray(inputs["L0_attn_gamma"], np.float32).reshape(H)
    f["l0_gamma"] = gam.reshape(1, H)
    f["l0_gamma_rep"] = np.ascontiguousarray(np.repeat(gam, 128).reshape(1, C))

    gup = np.asarray(inputs["L0_ffn_gate_up_packed"], np.int32)
    gsc = np.asarray(inputs["L0_ffn_gate_up_scales"])
    f["l0_pgT"] = T(gup[:IF])
    f["l0_puT"] = T(gup[IF:])
    f["l0_sg"] = np.ascontiguousarray(gsc[:, :IF])
    f["l0_su"] = np.ascontiguousarray(gsc[:, IF:])
    f["l0_pdT"] = T(np.asarray(inputs["L0_ffn_down_packed"], np.int32))
    f["l0_sd"] = np.ascontiguousarray(np.asarray(inputs["L0_ffn_down_scales"]))

    f["l1_slope"] = np.asarray(inputs["L1_attn_slope_rate"], np.float32).reshape(1, H)
    f["l1_wqkvT"] = T(inputs["L1_attn_w_qkv"])
    f["l1_wgT"] = T(inputs["L1_attn_w_gate"])
    f["l1_woT"] = T(inputs["L1_attn_w_out"])
    f["l1_normw"] = np.asarray(inputs["L1_attn_norm_w"], np.float32).reshape(1, C)
    f["l1_qgT"] = T(np.asarray(inputs["L1_ffn_wq_gate"], np.int32))
    f["l1_quT"] = T(np.asarray(inputs["L1_ffn_wq_up"], np.int32))
    f["l1_qdT"] = T(np.asarray(inputs["L1_ffn_wq_down"], np.int32))
    f["l1_sgc"] = np.ascontiguousarray(
        np.asarray(inputs["L1_ffn_s_gate"], np.float32).reshape(NI, 128).T)
    f["l1_suc"] = np.ascontiguousarray(
        np.asarray(inputs["L1_ffn_s_up"], np.float32).reshape(NI, 128).T)
    f["l1_sd"] = np.asarray(inputs["L1_ffn_s_down"], np.float32).reshape(1, C)

    f["identity"] = np.eye(128, dtype=np.float32)
    ar = np.arange(128, dtype=np.float32)
    f["iota_mj"] = np.ascontiguousarray(ar[None, :] - ar[:, None])
    f["mask_mj"] = np.ascontiguousarray((ar[None, :] >= ar[:, None]).astype(np.float32))
    f["grev"] = np.ascontiguousarray((127.0 - ar).reshape(128, 1))
    jm0 = np.arange(0, 128)[:, None] - np.arange(BLK1)[None, :]
    jm1 = np.arange(128, 256)[:, None] - np.arange(BLK1)[None, :]
    f["l1_i0"] = np.ascontiguousarray(jm0.astype(np.float32))
    f["l1_i1"] = np.ascontiguousarray(jm1.astype(np.float32))
    f["l1_msk0"] = np.ascontiguousarray((jm0 <= 0).astype(np.float32))
    f["l1_msk1"] = np.ascontiguousarray((jm1 <= 0).astype(np.float32))
    f["l1_qm"] = (-((np.arange(T_LOC) % BLK1) + 1.0)).astype(np.float32).reshape(1, T_LOC)
    f["l1_km"] = np.ascontiguousarray(np.stack(
        [-(255.0 - np.arange(0, 128)), -(255.0 - np.arange(128, 256))], 1).astype(np.float32))
    f["shfcol"] = (4 * (np.arange(128) // 16)).astype(np.int32).reshape(128, 1)

    pieces = []
    for cc in range(NC):
        for sl in range(6):
            pieces.append(("g", cc, sl))
    for cc in range(NC):
        for sl in range(6):
            pieces.append(("u", cc, sl))
    for kk in range(NI):
        for sl in range(2):
            pieces.append(("d", kk, sl))
    for j in range(NCORES):
        sh = np.zeros((35, 16, 1024), np.int32)
        sc = np.zeros((35, 1024), np.float16)
        for i, (kind, a, sl) in enumerate(pieces[35 * j:35 * (j + 1)]):
            if kind in ("g", "u"):
                w0 = sl * 1024
                wd = min(1024, IF - w0)
                pk = f["l0_pgT"] if kind == "g" else f["l0_puT"]
                ss = f["l0_sg"] if kind == "g" else f["l0_su"]
                sh[i, :, :wd] = pk[16 * a:16 * (a + 1), w0:w0 + wd]
                sc[i, :wd] = ss[a, w0:w0 + wd]
            else:
                sh[i] = f["l0_pdT"][16 * a:16 * (a + 1), sl * 1024:(sl + 1) * 1024]
                sc[i] = f["l0_sd"][a, sl * 1024:(sl + 1) * 1024]
        per_core[j]["myshard"] = sh
        per_core[j]["myscale"] = sc
    return f, per_core



class Ctx:
    def __init__(self, nc, es):
        self.nc = nc
        self.es = es
        self.block = None
        self.cnt = {}
        self.sem = {}
        self.d = {}
        self.s = {}
        self._uid = 0
        for name in ["dl", "dg", "tv", "ts", "tt", "cc"]:
            self.sem[name] = es.enter_context(nc.semaphore(name))
            self.cnt[name] = 0

    def uid(self):
        self._uid += 1
        return self._uid

    def tick(self, name, inst, n=1):
        inst.then_inc(self.sem[name], n)
        self.cnt[name] += n
        return self.cnt[name]

    def dtick(self, name, inst):
        return self.tick(name, inst, 16)

    def w(self, eng, name, val=None):
        eng.wait_ge(self.sem[name], self.cnt[name] if val is None else val)

    def on(self, engname, fn):
        getattr(self.block, engname)(fn)

    def wall(self, eng):
        for n in ["dl", "dg", "tv", "ts", "tt"]:
            self.w(eng, n)

    def barrier(self):
        def fv(v):
            self.tick("tv", v.engine_nop())
        self.on("vector", fv)

        def fg(g):
            self.tick("dg", g.engine_nop(), 16)
        self.on("gpsimd", fg)
        for e in ["sync", "vector", "scalar", "tensor", "gpsimd"]:
            def fw(eng, _e=e):
                self.wall(eng)
            self.on(e, fw)

    def xr(self, t):
        return self.s["XR"][:, t * C:(t + 1) * C]

    def xnc(self, cc, Tt):
        return self.s["XN"][:, cc * Tt:(cc + 1) * Tt]

    def BA(self, i, n):
        return self.s["BIG"][:, i:i + n]

    def BB(self, i, n):
        return self.s["BIG"][:, NT * C + i:NT * C + i + n]


def alloc_common(ctx, nc, es):
    S = lambda name, shape, dt: es.enter_context(nc.sbuf_tensor(name, shape, dt))
    s = ctx.s
    s["XR"] = S("XR", [128, NTE * C], F32)
    s["XN"] = S("XN", [128, NC * T_EXT], BF16)
    s["TA"] = S("TA", [128, C], F32)
    s["BIG"] = S("BIG", [128, 2 * NT * C], BF16)
    s["WSa"] = S("WSa", [128, 2048], BF16)
    s["WSb"] = S("WSb", [128, 2048], BF16)
    s["WS2"] = S("WS2", [128, 1024], F32)
    s["BC"] = S("BC", [128, C], BF16)
    s["SM"] = S("SM", [128, 192], F32)
    s["SC"] = S("SC", [128, 512], BF16)
    s["ID"] = S("ID", [128, 128], BF16)
    s["ST"] = S("ST", [128, 64], F32)
    s["CW"] = S("CW", [128, 80], F32)
    s["QG"] = S("QG", [128, 4096], BF16)
    s["WSi"] = s["QG"].bitcast(I32)


def emit_bc_table(ctx, row_ap):
    s = ctx.s

    def fd(sync):
        ctx.w(sync, "tv")
        ctx.dtick("dl", sync.dma_start(out=s["TA"][:, :],
                                       in_=row_ap.broadcast_to([128, C])))
    ctx.on("sync", fd)

    def fc(v):
        ctx.w(v, "dl")
        ctx.tick("tv", v.tensor_copy(out=s["BC"][:], in_=s["TA"][:, :]))
    ctx.on("vector", fc)


def emit_preamble(ctx):
    s, d = ctx.s, ctx.d

    def f(sync):
        ctx.dtick("dl", sync.dma_start(out=s["SM"][:, 128:129].bitcast(I32), in_=d["shfcol"]))
        ctx.dtick("dl", sync.dma_start(out=s["WS2"][:, 0:128], in_=d["identity"]))
    ctx.on("sync", f)

    def fv(v):
        ctx.w(v, "dl")
        ctx.tick("tv", v.tensor_copy(out=s["ID"][:], in_=s["WS2"][:, 0:128]))
    ctx.on("vector", fv)


def emit_load_x(ctx, ntiles):
    def f(sync):
        for t in range(ntiles):
            ctx.dtick("dl", sync.dma_start(
                out=ctx.xr(t), in_=ctx.d["xin"][t * 128:(t + 1) * 128, :]))
    ctx.on("sync", f)


def emit_store_x(ctx, t0, ntiles):
    def f(g):
        ctx.wall(g)
        for t in range(ntiles):
            ctx.dtick("dg", g.dma_start(
                out=ctx.d["xout"][t * 128:(t + 1) * 128, :], in_=ctx.xr(t0 + t)))
        ctx.w(g, "dg")
    ctx.on("gpsimd", f)



def emit_ln(ctx, t0, ntiles, permuted):
    nc, s = ctx.nc, ctx.s
    Tt = ntiles * 128
    ctx.barrier()
    with nc.psum_tensor(f"ptl0_{ctx.uid()}", [128, 128], BF16) as p0, \
         nc.psum_tensor(f"ptl1_{ctx.uid()}", [128, 128], BF16) as p1, \
         nc.psum_tensor(f"ptlf0_{ctx.uid()}", [128, 128], F32) as f0, \
         nc.psum_tensor(f"ptlf1_{ctx.uid()}", [128, 128], F32) as f1:
        pt = [p0, p1]
        pf = [f0, f1]
        xtmp = ctx.BA(0, C)
        for t in range(ntiles):
            xr = ctx.xr(t0 + t)

            def fv(v, t=t, xr=xr):
                ctx.w(v, "tv")
                if t > 0:
                    ctx.w(v, "tt")
                v.reduce_sum(out=s["ST"][:, 24:25], in_=xr, axis=AX)
                v.drain()
                v.tensor_scalar_mul(out=s["ST"][:, 24:25], in0=s["ST"][:, 24:25],
                                    scalar1=1.0 / C)
                v.drain()
                ctx.tick("tv", v.engine_nop())
            ctx.on("vector", fv)

            def fsr(sc, t=t, xr=xr):
                ctx.w(sc, "tv")
                for k in range(4):
                    sc.activation(out=s["SM"][:, 128:160] if False else s["TA"][:, 0:512],
                                  in_=xr[:, 512 * k:512 * (k + 1)], func=AF.Square,
                                  bias=s["ST"][:, 24:25], scale=-1.0,
                                  accum_out=s["ST"][:, 40 + k:41 + k])
                ctx.tick("ts", sc.copy(out=s["SM"][:, 191:192], in_=s["SM"][:, 191:192]))
            ctx.on("scalar", fsr)

            def fv15(v, t=t):
                ctx.w(v, "ts")
                v.tensor_add(out=s["ST"][:, 40:41], in0=s["ST"][:, 40:41],
                             in1=s["ST"][:, 41:42])
                v.tensor_add(out=s["ST"][:, 42:43], in0=s["ST"][:, 42:43],
                             in1=s["ST"][:, 43:44])
                v.drain()
                v.tensor_add(out=s["ST"][:, 40:41], in0=s["ST"][:, 40:41],
                             in1=s["ST"][:, 42:43])
                v.drain()
                ctx.tick("tv", v.tensor_scalar(out=s["ST"][:, 26:27],
                                               in0=s["ST"][:, 40:41],
                                               scalar1=1.0 / C, scalar2=LN_EPS,
                                               op0=ALU.mult, op1=ALU.add))
            ctx.on("vector", fv15)

            def fsr2(sc, t=t):
                ctx.w(sc, "tv")
                sc.activation(out=s["ST"][:, 26:27], in_=s["ST"][:, 26:27], func=AF.Ln)
                ctx.tick("ts", sc.activation(out=s["ST"][:, 26:27],
                                             in_=s["ST"][:, 26:27],
                                             func=AF.Exp, scale=-0.5))
            ctx.on("scalar", fsr2)

            def fnm(v, t=t, xr=xr):
                ctx.w(v, "ts")
                ctx.tick("tv", v.tensor_scalar(
                    out=xtmp, in0=xr, scalar1=s["ST"][:, 24:25],
                    scalar2=s["ST"][:, 26:27], op0=ALU.subtract, op1=ALU.mult))
            ctx.on("vector", fnm)

            for cc in range(NC):
                def ft(te, t=t, cc=cc):
                    ctx.w(te, "tv")
                    src = xtmp[:, cc * 128:(cc + 1) * 128]
                    ctx.tick("tt", te.transpose(out=pt[cc % 2][:], in_=src,
                                                identity=s["ID"][:]))

                def fv2(v, t=t, cc=cc):
                    ctx.w(v, "tt")
                    ctx.tick("tv", v.tensor_copy(
                        out=ctx.xnc(cc, Tt)[:, t * 128:(t + 1) * 128],
                        in_=pt[cc % 2][:]))
                ctx.on("tensor", ft)
                ctx.on("vector", fv2)
    ctx.barrier()



def emit_proj(ctx, Tt, a0, ntiles, O, wsrc, wdt, evac, lhsT_fn=None):
    nc, s = ctx.nc, ctx.s
    if lhsT_fn is None:
        def lhsT_fn(cc, a, n):
            return ctx.xnc(cc, Tt)[:, a:a + n]
    nq = O // 512
    assert wdt == F32
    ctx.barrier()
    with ExitStack() as ps:
        pb = [ps.enter_context(nc.psum_tensor(f"pj{i}_{ctx.uid()}", [128, 512], F32))
              for i in range(ntiles)]
        for oq in range(nq):
            for ccp in range(NC // 2):
                cc0 = 2 * ccp
                buf = s["WSa"] if ccp % 2 == 0 else s["WSb"]
                wpair = buf[:, 0:1024]

                def fdma(sync, oq=oq, cc0=cc0):
                    ctx.w(sync, "tv")
                    ctx.dtick("dl", sync.dma_start(out=s["WS2"][:, 0:512],
                                                   in_=wsrc(cc0, oq * 512, 512)))
                    ctx.dtick("dl", sync.dma_start(out=s["WS2"][:, 512:1024],
                                                   in_=wsrc(cc0 + 1, oq * 512, 512)))
                ctx.on("sync", fdma)

                def fcvt(v, oq=oq, cc0=cc0, wpair=wpair):
                    ctx.w(v, "dl")
                    ctx.w(v, "tt")
                    ctx.tick("tv", v.tensor_copy(out=wpair, in_=s["WS2"][:, 0:1024]))
                ctx.on("vector", fcvt)

                def fmm(te, oq=oq, cc0=cc0, wpair=wpair):
                    ctx.w(te, "tv")
                    if cc0 == 0:
                        ctx.w(te, "ts")
                    for kk in range(2):
                        cc = cc0 + kk
                        for t in range(ntiles):
                            inst = te.matmul(out=pb[t][:],
                                             lhsT=lhsT_fn(cc, a0 + t * 128, 128),
                                             rhs=wpair[:, kk * 512:(kk + 1) * 512],
                                             start=(cc == 0), stop=(cc == NC - 1))
                            if kk == 1 and t == ntiles - 1:
                                ctx.tick("tt", inst)
                ctx.on("tensor", fmm)

            for t in range(ntiles):
                evac(t, oq, pb[t])
    ctx.barrier()



def emit_l0_dequant(ctx):
    s = ctx.s
    shf = s["SM"][:, 128:129].bitcast(I32)
    ctx.barrier()

    def fv0(v):
        ctx.w(v, "dl")
        v.memset(s["ST"][:, 32:33], -2.0)
        v.memset(s["ST"][:, 33:34], -6.0)
        v.drain()
        ctx.tick("tv", v.engine_nop())
    ctx.on("vector", fv0)

    def emit_piece(words_src, width, scale_src, out_dst):
        def fdma(sync):
            ctx.w(sync, "tv")
            for j in range(8):
                ctx.dtick("dl", sync.dma_start(
                    out=s["WSi"][16 * j:16 * (j + 1), 0:width], in_=words_src))
            ctx.dtick("dl", sync.dma_start(
                out=s["WSa"].bitcast(F16)[:, 0:width],
                in_=scale_src.broadcast_to([128, width])))
        ctx.on("sync", fdma)

        def fv1(v):
            ctx.w(v, "dl")
            ctx.w(v, "ts")
            v.tensor_scalar(
                out=s["WSi"][:, 1024:1024 + width], in0=s["WSi"][:, 0:width],
                scalar1=shf, scalar2=7,
                op0=ALU.logical_shift_right, op1=ALU.bitwise_and)
            ctx.tick("tv", v.tensor_copy(out=s["WS2"][:, 0:width],
                                         in_=s["WSi"][:, 1024:1024 + width]))
        ctx.on("vector", fv1)

        def fs1(sc):
            ctx.w(sc, "tv")
            sc.activation(out=s["TA"][:, 0:width], in_=s["WS2"][:, 0:width],
                          func=AF.Relu, bias=s["ST"][:, 32:33], scale=0.5)
            ctx.tick("ts", sc.activation(
                out=s["TA"][:, 1024:1024 + width], in_=s["WS2"][:, 0:width],
                func=AF.Relu, bias=s["ST"][:, 33:34], scale=1.0))
        ctx.on("scalar", fs1)

        def fv2(v):
            ctx.w(v, "ts")
            ctx.w(v, "dg")
            m_f = s["WS2"][:, 0:width]
            v.scalar_tensor_tensor(out=m_f, in0=m_f, scalar=0.5,
                                   in1=s["TA"][:, 0:width], op0=ALU.mult, op1=ALU.add)
            v.tensor_add(out=m_f, in0=m_f, in1=s["TA"][:, 1024:1024 + width])
            v.tensor_scalar(out=s["WSi"][:, 1024:1024 + width], in0=s["WSi"][:, 0:width],
                            scalar1=shf, scalar2=8,
                            op0=ALU.logical_shift_right, op1=ALU.bitwise_and)
            v.tensor_scalar(out=s["TA"][:, 0:width], in0=s["WSi"][:, 1024:1024 + width],
                            scalar1=-0.25, scalar2=1.0, op0=ALU.mult, op1=ALU.add)
            v.tensor_mul(out=m_f, in0=m_f, in1=s["TA"][:, 0:width])
            ctx.tick("tv", v.tensor_tensor(
                out=s["WSb"][:, 0:width], in0=m_f,
                in1=s["WSa"].bitcast(F16)[:, 0:width], op=ALU.mult))
        ctx.on("vector", fv2)

        def fst(g):
            ctx.w(g, "tv")
            rview = out_dst.rearrange("(m j) w -> m j w", m=16, j=8)
            for j in range(8):
                ctx.dtick("dg", g.dma_start(
                    out=rview[:, j, :],
                    in_=s["WSb"][16 * j:16 * (j + 1), 0:width]))
        ctx.on("gpsimd", fst)

    for i in range(35):
        emit_piece(ctx.d["myshard"][i], 1024, ctx.d["myscale"][i:i + 1, :],
                   ctx.d["cin_w"][i * 128:(i + 1) * 128, :])

    def fcc(g):
        ctx.w(g, "dg")
        ctx.tick("cc", g.collective_compute(
            "AllGather", ALU.bypass, replica_groups=RG, ins=[ctx.d["cin_w"]],
            outs=[ctx.d["cout_w"]]))
    ctx.on("gpsimd", fcc)

    def fwait(sync):
        ctx.w(sync, "cc")
    ctx.on("sync", fwait)
    ctx.barrier()


def wcache(ctx):
    base = ctx.d["cout_w"].rearrange("(s p) w -> s p w", p=128)
    return base



def emit_ffn(ctx, layer, t0, ntiles, Tt):
    nc, s, d = ctx.nc, ctx.s, ctx.d
    chunks = []
    t = 0
    while t < ntiles:
        n = min(4, ntiles - t)
        chunks.append((t, n))
        t += n

    ctx.barrier()
    if layer == "l1":
        def fld(sync):
            ctx.dtick("dl", sync.dma_start(out=s["SM"][:, 0:44], in_=d["l1_sgc"]))
            ctx.dtick("dl", sync.dma_start(out=s["SM"][:, 44:88], in_=d["l1_suc"]))
        ctx.on("sync", fld)

    for (tc, ntc) in chunks:
        Tch = ntc * 128
        with nc.psum_tensor(f"pgA_{ctx.uid()}", [128, 512], F32) as pg, \
             nc.psum_tensor(f"puA_{ctx.uid()}", [128, 512], F32) as pu:
            for k in range(NI):
                if layer == "l0":
                    def fdg(sync, k=k):
                        ctx.w(sync, "tt")
                        base = wcache(ctx)
                        sl, col = k // 8, (k % 8) * 128
                        gsrc = base[sl:96:6, :, col:col + 128].transpose([1, 0, 2])
                        usrc = base[96 + sl:192:6, :, col:col + 128].transpose([1, 0, 2])
                        ctx.dtick("dl", sync.dma_start(
                            out=s["WSa"].rearrange("p (cc i) -> p cc i", cc=16),
                            in_=gsrc))
                        ctx.dtick("dl", sync.dma_start(
                            out=s["WSb"].rearrange("p (cc i) -> p cc i", cc=16),
                            in_=usrc))
                    ctx.on("sync", fdg)
                else:
                    def fdg1(sync, k=k):
                        ctx.w(sync, "tv")
                        ctx.dtick("dl", sync.dma_start(
                            out=s["WSi"].rearrange("p (cc i) -> p cc i", cc=16),
                            in_=d["l1_qgT"].rearrange("(cc p) i -> p cc i", p=128)[:, :, 128 * k:128 * (k + 1)]))
                    ctx.on("sync", fdg1)

                    def fcv1(v, k=k):
                        ctx.w(v, "dl")
                        ctx.w(v, "tt")
                        ctx.tick("tv", v.tensor_scalar(
                            out=s["WSa"][:], in0=s["WSi"][:], scalar1=-128.0,
                            scalar2=None, op0=ALU.add))
                    ctx.on("vector", fcv1)

                    def fdg2(sync, k=k):
                        ctx.w(sync, "tv")
                        ctx.dtick("dl", sync.dma_start(
                            out=s["WSi"].rearrange("p (cc i) -> p cc i", cc=16),
                            in_=d["l1_quT"].rearrange("(cc p) i -> p cc i", p=128)[:, :, 128 * k:128 * (k + 1)]))
                    ctx.on("sync", fdg2)

                    def fcv2(v, k=k):
                        ctx.w(v, "dl")
                        ctx.w(v, "tt")
                        ctx.tick("tv", v.tensor_scalar(
                            out=s["WSb"][:], in0=s["WSi"][:], scalar1=-128.0,
                            scalar2=None, op0=ALU.add))
                    ctx.on("vector", fcv2)

                def fmmA(te, k=k, tc=tc, Tch=Tch):
                    ctx.w(te, "dl" if layer == "l0" else "tv")
                    ctx.w(te, "ts")
                    if layer == "l0":
                        ctx.w(te, "tv")
                    for cc in range(NC):
                        te.matmul(out=pg[:, 0:Tch],
                                  lhsT=s["WSa"][:, cc * 128:(cc + 1) * 128],
                                  rhs=ctx.xnc(cc, Tt)[:, tc * 128:tc * 128 + Tch],
                                  start=(cc == 0), stop=(cc == NC - 1))
                    for cc in range(NC):
                        inst = te.matmul(out=pu[:, 0:Tch],
                                         lhsT=s["WSb"][:, cc * 128:(cc + 1) * 128],
                                         rhs=ctx.xnc(cc, Tt)[:, tc * 128:tc * 128 + Tch],
                                         start=(cc == 0), stop=(cc == NC - 1))
                        if cc == NC - 1:
                            ctx.tick("tt", inst)
                ctx.on("tensor", fmmA)

                def fevA_s(sc, k=k, Tch=Tch):
                    ctx.w(sc, "tt")
                    ctx.w(sc, "tv")
                    if layer == "l1":
                        ctx.w(sc, "dl")
                        inst = sc.activation(out=s["WS2"][:, 0:Tch], in_=pg[:, 0:Tch],
                                             func=AF.Silu, scale=s["SM"][:, k:k + 1])
                    else:
                        inst = sc.activation(out=s["WS2"][:, 0:Tch], in_=pg[:, 0:Tch],
                                             func=AF.Silu)
                    ctx.tick("ts", inst)
                ctx.on("scalar", fevA_s)

                def fevA_v(v, k=k, Tch=Tch):
                    ctx.w(v, "ts")
                    hdst = ctx.s["BIG"][:, k * 512:k * 512 + Tch]
                    if layer == "l0":
                        inst = v.tensor_tensor(out=hdst, in0=s["WS2"][:, 0:Tch],
                                               in1=pu[:, 0:Tch], op=ALU.mult)
                    else:
                        inst = v.scalar_tensor_tensor(out=hdst, in0=pu[:, 0:Tch],
                                                      scalar=s["SM"][:, 44 + k:45 + k],
                                                      in1=s["WS2"][:, 0:Tch],
                                                      op0=ALU.mult, op1=ALU.mult)
                    ctx.tick("tv", inst)
                ctx.on("vector", fevA_v)

        ctx.barrier()
        if layer == "l1":
            emit_bc_table(ctx, d["l1_sd"])

            def fbc2(v):
                ctx.tick("tv", v.tensor_scalar_mul(out=s["BC"][:], in0=s["BC"][:],
                                                   scalar1=0.5))
            ctx.on("vector", fbc2)

        with ExitStack() as ps:
            pyb = [ps.enter_context(nc.psum_tensor(f"pyB{i}_{ctx.uid()}", [128, 512], F32))
                   for i in range(ntc)]
            for oq in range(4):
                for kp in range(NI // 2):
                    buf = s["WSa"] if kp % 2 == 0 else s["WSb"]
                    wt = buf[:, 0:1024]
                    k0 = 2 * kp
                    if layer == "l0":
                        def fdB(sync, oq=oq, k0=k0, wt=wt):
                            ctx.w(sync, "tt")
                            base = wcache(ctx)
                            slot = 192 + 2 * k0 + (oq // 2)
                            src2 = base[slot:slot + 3:2, :, (oq % 2) * 512:(oq % 2) * 512 + 512]
                            ctx.dtick("dl", sync.dma_start(
                                out=wt.rearrange("p (kk o) -> p kk o", kk=2),
                                in_=src2.transpose([1, 0, 2])))
                        ctx.on("sync", fdB)
                    else:
                        def fdB(sync, oq=oq, k0=k0, kp=kp):
                            ctx.w(sync, "tv")
                            stage = s["WSi"][:, 0:1024] if kp % 2 == 0 else s["WSi"][:, 1024:2048]
                            src2 = d["l1_qdT"][128 * k0:128 * (k0 + 2),
                                               512 * oq:512 * (oq + 1)].rearrange(
                                "(kk p) o -> p kk o", p=128)
                            ctx.dtick("dl", sync.dma_start(
                                out=stage.rearrange("p (kk o) -> p kk o", kk=2),
                                in_=src2))
                        ctx.on("sync", fdB)

                        def fcB(v, oq=oq, kp=kp, wt=wt):
                            ctx.w(v, "dl")
                            ctx.w(v, "tt")
                            stage = s["WSi"][:, 0:1024] if kp % 2 == 0 else s["WSi"][:, 1024:2048]
                            ctx.tick("tv", v.tensor_scalar(
                                out=wt, in0=stage, scalar1=-128.0, scalar2=None,
                                op0=ALU.add))
                        ctx.on("vector", fcB)

                    def fmB(te, oq=oq, k0=k0, wt=wt, ntc=ntc):
                        ctx.w(te, "dl" if layer == "l0" else "tv")
                        if k0 == 0:
                            ctx.w(te, "tv")
                            ctx.w(te, "ts")
                        for kk in range(2):
                            k = k0 + kk
                            for i in range(ntc):
                                inst = te.matmul(
                                    out=pyb[i][:],
                                    lhsT=ctx.s["BIG"][:, k * 512 + i * 128:k * 512 + (i + 1) * 128],
                                    rhs=wt[:, kk * 512:(kk + 1) * 512],
                                    start=(k == 0), stop=(k == NI - 1))
                                if kk == 1 and i == ntc - 1:
                                    ctx.tick("tt", inst)
                        ctx.on
                    ctx.on("tensor", fmB)

                for i in range(ntc):
                    if layer == "l0":
                        def fqv(v, i=i, oq=oq, tc=tc):
                            ctx.w(v, "tt")
                            ctx.tick("tv", v.tensor_copy(
                                out=ctx.s["BIG"][:, NI * 512 + i * C + oq * 512:NI * 512 + i * C + (oq + 1) * 512],
                                in_=pyb[i][:]))
                        ctx.on("vector", fqv)
                    else:
                        def fqv(v, i=i, oq=oq, tc=tc):
                            ctx.w(v, "tt")
                            v.tensor_tensor(out=s["TA"][:, 0:512], in0=pyb[i][:],
                                            in1=s["BC"][:, oq * 512:(oq + 1) * 512],
                                            op=ALU.mult)
                            xr = ctx.xr(t0 + tc + i)
                            ctx.tick("tv", v.tensor_add(
                                out=xr[:, oq * 512:(oq + 1) * 512],
                                in0=xr[:, oq * 512:(oq + 1) * 512],
                                in1=s["TA"][:, 0:512]))
                        ctx.on("vector", fqv)

            if layer == "l0":
                for i in range(ntc):
                    def frms_s(sc, i=i, tc=tc):
                        ctx.w(sc, "tv")
                        y = ctx.s["BIG"][:, NI * 512 + i * C:NI * 512 + (i + 1) * C]
                        for k4 in range(4):
                            sc.activation(out=s["TA"][:, 0:512],
                                          in_=y[:, 512 * k4:512 * (k4 + 1)],
                                          func=AF.Square,
                                          accum_out=s["ST"][:, 40 + k4:41 + k4])
                        ctx.tick("ts", sc.copy(out=s["SM"][:, 191:192], in_=s["SM"][:, 191:192]))
                    ctx.on("scalar", frms_s)

                    def frms_v(v, i=i, tc=tc):
                        ctx.w(v, "ts")
                        v.tensor_add(out=s["ST"][:, 40:41], in0=s["ST"][:, 40:41],
                                     in1=s["ST"][:, 41:42])
                        v.tensor_add(out=s["ST"][:, 42:43], in0=s["ST"][:, 42:43],
                                     in1=s["ST"][:, 43:44])
                        v.drain()
                        v.tensor_add(out=s["ST"][:, 40:41], in0=s["ST"][:, 40:41],
                                     in1=s["ST"][:, 42:43])
                        v.drain()
                        v.tensor_scalar(out=s["ST"][:, 26:27], in0=s["ST"][:, 40:41],
                                        scalar1=1.0 / C, scalar2=L0_FFN_EPS,
                                        op0=ALU.mult, op1=ALU.add)
                        v.drain()
                        ctx.tick("tv", v.engine_nop())
                    ctx.on("vector", frms_v)

                    def frms_s2(sc, i=i, tc=tc):
                        ctx.w(sc, "tv")
                        sc.activation(out=s["ST"][:, 26:27], in_=s["ST"][:, 26:27],
                                      func=AF.Ln)
                        ctx.tick("ts", sc.activation(out=s["ST"][:, 26:27],
                                                     in_=s["ST"][:, 26:27],
                                                     func=AF.Exp, scale=-0.5))
                    ctx.on("scalar", frms_s2)

                    def frms2(v, i=i, tc=tc):
                        ctx.w(v, "ts")
                        y = ctx.s["BIG"][:, NI * 512 + i * C:NI * 512 + (i + 1) * C]
                        v.tensor_scalar_mul(out=s["ST"][:, 26:27], in0=s["ST"][:, 26:27],
                                            scalar1=0.5)
                        v.drain()
                        v.tensor_scalar(out=s["TA"][:], in0=y, scalar1=s["ST"][:, 26:27],
                                        scalar2=None, op0=ALU.mult)
                        xr = ctx.xr(t0 + tc + i)
                        ctx.tick("tv", v.tensor_add(out=xr, in0=xr, in1=s["TA"][:]))
                    ctx.on("vector", frms2)
        ctx.barrier()



def emit_l0_attn(ctx, upto="full"):
    nc, s, d = ctx.nc, ctx.s, ctx.d
    Tt = T_EXT

    ctx.barrier()

    def fld(sync):
        ctx.dtick("dl", sync.dma_start(out=s["SM"][0:1, 32:48], in_=d["l0_gamma"]))
        ctx.dtick("dl", sync.dma_start(out=s["ST"][:, 48:64],
                                       in_=d["l0_gamma"].broadcast_to([128, H])))
        ctx.dtick("dl", sync.dma_start(out=s["CW"][:, 0:64], in_=d["l0_convw"]))
        ctx.dtick("dl", sync.dma_start(out=s["CW"][:, 64:80], in_=d["l0_convb"]))
    ctx.on("sync", fld)

    def fs0(sc):
        ctx.w(sc, "dl")
        ctx.tick("ts", sc.activation(out=s["ST"][:, 48:64], in_=s["ST"][:, 48:64],
                                     func=AF.Ln))
    ctx.on("scalar", fs0)

    for cc in range(NC):
        def fcv(v, cc=cc):
            if cc == 0:
                ctx.w(v, "tv")
                ctx.w(v, "dl")
            else:
                ctx.w(v, "ts")
            xn = ctx.xnc(cc, Tt)
            v.tensor_scalar(out=s["TA"][:, 0:1024], in0=xn[:, 125:125 + 1024],
                            scalar1=s["CW"][:, 4 * cc:4 * cc + 1], scalar2=None,
                            op0=ALU.mult)
            for k in (1, 2, 3):
                v.scalar_tensor_tensor(out=s["TA"][:, 0:1024],
                                       in0=xn[:, 125 + k:125 + k + 1024],
                                       scalar=s["CW"][:, 4 * cc + k:4 * cc + k + 1],
                                       in1=s["TA"][:, 0:1024],
                                       op0=ALU.mult, op1=ALU.add)
            v.drain()
            ctx.tick("tv", v.engine_nop())
        ctx.on("vector", fcv)

        def fcs(sc, cc=cc):
            ctx.w(sc, "tv")
            ctx.w(sc, "dg")
            ctx.tick("ts", sc.activation(out=s["WSb"][:, 0:1024], in_=s["TA"][:, 0:1024],
                                         func=AF.Silu, bias=s["CW"][:, 64 + cc:65 + cc],
                                         scale=1.0))
        ctx.on("scalar", fcs)

        def fcd(g, cc=cc):
            ctx.w(g, "ts")
            ctx.dtick("dg", g.dma_start(out=d["xc_d"][cc], in_=s["WSb"][:, 0:1024]))
        ctx.on("gpsimd", fcd)

    def l2norm_evac(dstf):
        def evac(t, oq, pb):
            def fsq(sc, t=t, oq=oq, pb=pb):
                ctx.w(sc, "tt")
                for hh in range(4):
                    sc.activation(out=s["TA"][:, hh * 128:(hh + 1) * 128],
                                  in_=pb[:, hh * 128:(hh + 1) * 128],
                                  func=AF.Square,
                                  accum_out=s["ST"][:, 40 + hh:41 + hh])
                sc.activation(out=s["ST"][:, 44:48], in_=s["ST"][:, 40:44], func=AF.Ln)
                ctx.tick("ts", sc.activation(out=s["ST"][:, 44:48],
                                             in_=s["ST"][:, 44:48],
                                             func=AF.Exp, scale=-0.5))
            ctx.on("scalar", fsq)

            def fvn(v, t=t, oq=oq, pb=pb):
                ctx.w(v, "ts")
                v.tensor_scalar(out=s["ST"][:, 44:48], in0=s["ST"][:, 44:48],
                                scalar1=1.0 / L0_NORM_EPS, scalar2=None, op0=ALU.min)
                v.drain()
                for hh in range(4):
                    inst = v.tensor_scalar(
                        out=dstf(t)[:, oq * 512 + hh * 128:oq * 512 + (hh + 1) * 128],
                        in0=pb[:, hh * 128:(hh + 1) * 128],
                        scalar1=s["ST"][:, 44 + hh:45 + hh], scalar2=None, op0=ALU.mult)
                ctx.tick("tv", inst)
            ctx.on("vector", fvn)
        return evac

    emit_proj(ctx, Tt, HALO, NT, C,
              lambda cc, o0, ol: d["l0_wkT"][128 * cc:128 * (cc + 1), o0:o0 + ol],
              F32, l2norm_evac(lambda t: ctx.BA(t * C, C)))

    def v_evac(t, oq, pb):
        def fv(v, t=t, oq=oq, pb=pb):
            ctx.w(v, "tt")
            ctx.tick("tv", v.tensor_tensor(
                out=ctx.BB(t * C + oq * 512, 512), in0=pb[:],
                in1=ctx.BA(t * C + oq * 512, 512), op=ALU.mult))
        ctx.on("vector", fv)
    emit_proj(ctx, Tt, HALO, NT, C,
              lambda cc, o0, ol: d["l0_wvT"][128 * cc:128 * (cc + 1), o0:o0 + ol],
              F32, v_evac)

    emit_proj(ctx, Tt, HALO, NT, C,
              lambda cc, o0, ol: d["l0_wqT"][128 * cc:128 * (cc + 1), o0:o0 + ol],
              F32, l2norm_evac(lambda t: ctx.BA(t * C, C)))

    def fxc(sync):
        ctx.wall(sync)
        for cc in range(NC):
            ctx.dtick("dl", sync.dma_start(out=ctx.xnc(cc, T_LOC), in_=d["xc_d"][cc]))
    ctx.on("sync", fxc)

    emit_bc_table(ctx, d["l0_gateb"])

    def g_evac(t, oq, pb):
        def fga(v, t=t, oq=oq, pb=pb):
            ctx.w(v, "tt")
            ctx.w(v, "dg")
            ctx.w(v, "ts")
            ctx.tick("tv", v.tensor_tensor(out=s["TA"][:, 0:512], in0=pb[:],
                                           in1=s["BC"][:, oq * 512:(oq + 1) * 512],
                                           op=ALU.add))
        ctx.on("vector", fga)

        def fgs(sc, t=t, oq=oq):
            ctx.w(sc, "tv")
            ctx.tick("ts", sc.activation(out=s["TA"][:, 512:1024],
                                         in_=s["TA"][:, 0:512], func=AF.Sigmoid))
        ctx.on("scalar", fgs)

        def fgm(v, t=t, oq=oq):
            ctx.w(v, "ts")
            dst = ctx.BA(t * C + oq * 512, 512)
            ctx.tick("tv", v.tensor_tensor(out=dst, in0=s["TA"][:, 512:1024],
                                           in1=dst, op=ALU.mult))
        ctx.on("vector", fgm)

    emit_proj(ctx, T_LOC, 0, NT, C,
              lambda cc, o0, ol: d["l0_wgT"][128 * cc:128 * (cc + 1), o0:o0 + ol],
              F32, g_evac)
    if upto == "proj":
        return

    def fsc_ld(sync):
        ctx.wall(sync)
        ctx.dtick("dl", sync.dma_start(out=s["WS2"][:, 0:128], in_=d["iota_mj"]))
        ctx.dtick("dl", sync.dma_start(out=s["WS2"][:, 128:256], in_=d["mask_mj"]))
        ctx.dtick("dl", sync.dma_start(out=s["WS2"][:, 256:257], in_=d["grev"]))
    ctx.on("sync", fsc_ld)

    for h in range(H):
        def fdt_s(sc, h=h):
            ctx.w(sc, "dl")
            ctx.w(sc, "tv")
            sc.activation(out=s["TA"][:, 0:128], in_=s["WS2"][:, 0:128],
                          func=AF.Exp, scale=s["ST"][:, 48 + h:49 + h])
            ctx.tick("ts", sc.activation(out=s["TA"][:, 128:129], in_=s["WS2"][:, 256:257],
                                         func=AF.Exp, scale=s["ST"][:, 48 + h:49 + h]))
        ctx.on("scalar", fdt_s)

        def fdt_v(v, h=h):
            ctx.w(v, "ts")
            v.tensor_tensor(out=s["QG"][:, 2048 + h * 128:2048 + (h + 1) * 128],
                            in0=s["TA"][:, 0:128], in1=s["WS2"][:, 128:256], op=ALU.mult)
            v.tensor_copy(out=s["SC"][:, h:h + 1], in_=s["TA"][:, 128:129])
            v.drain()
            ctx.tick("tv", v.engine_nop())
        ctx.on("vector", fdt_v)

    with nc.psum_tensor(f"pc1_{ctx.uid()}", [1, 128], F32) as pc:
        for h in range(H):
            def fg128(v, h=h):
                ctx.w(v, "ts")
                v.tensor_scalar_mul(out=s["ST"][0:1, 40:41],
                                    in0=s["ST"][0:1, 48 + h:49 + h], scalar1=128.0)
                v.drain()
                ctx.tick("tv", v.engine_nop())
            ctx.on("vector", fg128)

            def fg128b(sc, h=h):
                ctx.w(sc, "tv")
                ctx.tick("ts", sc.activation(out=s["ST"][0:1, 40:41],
                                             in_=s["ST"][0:1, 40:41], func=AF.Exp))
            ctx.on("scalar", fg128b)

            for b in range(NT):
                def fmm(te, h=h, b=b):
                    ctx.w(te, "tv")
                    ctx.tick("tt", te.matmul(
                        out=pc[:], lhsT=s["SC"][:, h:h + 1],
                        rhs=ctx.BB(b * C + h * 128, 128), start=True, stop=True))
                ctx.on("tensor", fmm)

                def fcar(v, h=h, b=b):
                    ctx.w(v, "tt")
                    ctx.w(v, "ts")
                    scar = s["SM"][0:1, 64:192]
                    if b == 0:
                        inst = v.tensor_copy(out=scar, in_=pc[0:1, :])
                    else:
                        inst = v.scalar_tensor_tensor(out=scar, in0=scar,
                                                      scalar=s["ST"][0:1, 40:41],
                                                      in1=pc[0:1, :],
                                                      op0=ALU.mult, op1=ALU.add)
                    ctx.tick("tv", inst)
                    v.drain()
                ctx.on("vector", fcar)

            def fA(v, h=h):
                v.tensor_copy(out=s["TA"][0:1, h * 128:(h + 1) * 128],
                              in_=s["SM"][0:1, 64:192])
                v.drain()
                ctx.tick("tv", v.engine_nop())
            ctx.on("vector", fA)

    def fcc(g):
        ctx.w(g, "tv")
        ctx.dtick("dg", g.dma_start(out=d["cin0"], in_=s["TA"][0:1, :]))
        ctx.w(g, "dg")
        ctx.tick("cc", g.collective_compute(
            "AllGather", ALU.bypass, replica_groups=RG, ins=[d["cin0"]],
            outs=[d["cout0"]]))
        ctx.w(g, "cc")
        ctx.dtick("dg", g.dma_start(out=s["SM"][0:8, 0:1], in_=d["cc_sel"]))
        ctx.dtick("dg", g.dma_start(out=s["SM"][0:8, 1:2], in_=d["cc_pow"]))
    ctx.on("gpsimd", fcc)

    for half in range(2):
        def fch_g(g, half=half):
            ctx.w(g, "tv")
            ctx.dtick("dg", g.dma_start(out=s["TA"][0:8, 0:1024],
                                        in_=d["cout0"][:, half * 1024:(half + 1) * 1024]))
            ctx.dtick("dg", g.dma_start(
                out=s["TA"][0:8, 1024:2048],
                in_=d["l0_gamma_rep"][0:1, half * 1024:(half + 1) * 1024].broadcast_to([8, 1024])))
        ctx.on("gpsimd", fch_g)

        def fch_s(sc, half=half):
            ctx.w(sc, "dg")
            sc.activation(out=s["TA"][0:8, 1024:2048], in_=s["TA"][0:8, 1024:2048],
                          func=AF.Ln)
            ctx.tick("ts", sc.activation(out=s["TA"][0:8, 1024:2048],
                                         in_=s["TA"][0:8, 1024:2048],
                                         func=AF.Exp, scale=s["SM"][0:8, 1:2]))
        ctx.on("scalar", fch_s)

        def fch_v(v, half=half):
            ctx.w(v, "ts")
            v.tensor_scalar(out=s["TA"][0:8, 1024:2048], in0=s["TA"][0:8, 1024:2048],
                            scalar1=s["SM"][0:8, 0:1], scalar2=None, op0=ALU.mult)
            v.tensor_tensor(
                out=s["BC"][0:8, half * 1024:(half + 1) * 1024],
                in0=s["TA"][0:8, 1024:2048], in1=s["TA"][0:8, 0:1024], op=ALU.mult)
            v.drain()
            ctx.tick("tv", v.engine_nop())
        ctx.on("vector", fch_v)

    def fones(v):
        v.memset(s["SC"][0:8, 256:257], 1.0)
        v.drain()
        ctx.tick("tv", v.engine_nop())
    ctx.on("vector", fones)

    with nc.psum_tensor(f"psin_{ctx.uid()}", [1, 512], F32) as psin:
        for qq in range(4):
            def fmm_s(te, qq=qq):
                ctx.w(te, "tv")
                ctx.tick("tt", te.matmul(out=psin[:], lhsT=s["SC"][0:8, 256:257],
                                         rhs=s["BC"][0:8, qq * 512:(qq + 1) * 512],
                                         start=True, stop=True))
            ctx.on("tensor", fmm_s)

            def fsin(v, qq=qq):
                ctx.w(v, "tt")
                v.tensor_copy(out=s["TA"][0:1, qq * 512:(qq + 1) * 512],
                              in_=psin[0:1, :])
                v.drain()
                ctx.tick("tv", v.engine_nop())
            ctx.on("vector", fsin)

    with nc.psum_tensor(f"ps2_{ctx.uid()}", [128, 128], F32) as ps2, \
         nc.psum_tensor(f"pc2_{ctx.uid()}", [1, 128], F32) as pc2:
        for h in range(H):
            def fini(v, h=h):
                ctx.w(v, "tv")
                v.tensor_copy(out=s["SM"][0:1, 64:192],
                              in_=s["TA"][0:1, h * 128:(h + 1) * 128])
                v.drain()
                ctx.tick("tv", v.engine_nop())
            ctx.on("vector", fini)

            for b in range(NT):
                def ffix(v, h=h, b=b):
                    ctx.w(v, "tt")
                    v.tensor_scalar(out=s["WS2"][0:1, 512:640], in0=s["SM"][0:1, 64:192],
                                    scalar1=s["SM"][0:1, 32 + h:33 + h], scalar2=None,
                                    op0=ALU.mult)
                    v.drain()
                    u0 = ctx.BB(b * C + h * 128, 128)[0:1, :]
                    v.tensor_add(out=u0, in0=u0, in1=s["WS2"][0:1, 512:640])
                    v.drain()
                    ctx.tick("tv", v.engine_nop())
                ctx.on("vector", ffix)

                def fmm2(te, h=h, b=b):
                    ctx.w(te, "tv")
                    te.matmul(out=ps2[:],
                              lhsT=s["QG"][:, 2048 + h * 128:2048 + (h + 1) * 128],
                              rhs=ctx.BB(b * C + h * 128, 128), start=True, stop=True)
                    ctx.tick("tt", te.matmul(out=pc2[:], lhsT=s["SC"][:, h:h + 1],
                                             rhs=ctx.BB(b * C + h * 128, 128),
                                             start=True, stop=True))
                ctx.on("tensor", fmm2)

                def fev2(v, h=h, b=b):
                    ctx.w(v, "tt")
                    v.tensor_copy(out=s["SM"][0:1, 64:192], in_=pc2[0:1, :])
                    v.drain()
                    dst = ctx.BB(b * C + h * 128, 128)
                    v.tensor_tensor(out=dst, in0=ps2[:],
                                    in1=ctx.BA(b * C + h * 128, 128), op=ALU.mult)
                    v.drain()
                    ctx.tick("tv", v.engine_nop())
                ctx.on("vector", fev2)

    ctx.barrier()
    if upto == "scan":
        return
    with nc.psum_tensor(f"ptr0_{ctx.uid()}", [128, 128], BF16) as q0, \
         nc.psum_tensor(f"ptr1_{ctx.uid()}", [128, 128], BF16) as q1:
        qp = [q0, q1]
        i = 0
        for cc in range(NC):
            for t in range(NT):
                def ftp(te, cc=cc, t=t, i=i):
                    ctx.w(te, "tv")
                    ctx.tick("tt", te.transpose(
                        out=qp[i % 2][:], in_=ctx.BB(t * C + cc * 128, 128),
                        identity=s["ID"][:]))

                def ftc(v, cc=cc, t=t, i=i):
                    ctx.w(v, "tt")
                    ctx.tick("tv", v.tensor_copy(
                        out=ctx.BA(cc * T_LOC + t * 128, 128), in_=qp[i % 2][:]))
                ctx.on("tensor", ftp)
                ctx.on("vector", ftc)
                i += 1

    def wo_evac(t, oq, pb):
        def fv(v, t=t, oq=oq, pb=pb):
            ctx.w(v, "tt")
            ctx.tick("tv", v.tensor_copy(out=ctx.BB(t * C + oq * 512, 512), in_=pb[:]))
        ctx.on("vector", fv)

    emit_proj(ctx, T_LOC, 0, NT, C,
              lambda cc, o0, ol: d["l0_woT"][128 * cc:128 * (cc + 1), o0:o0 + ol],
              F32, wo_evac,
              lhsT_fn=lambda cc, a, n: ctx.BA(cc * T_LOC + a, n))

    ctx.barrier()
    emit_bc_table(ctx, d["l0_lnw"])
    for t in range(NT):
        def fln(v, t=t):
            ctx.w(v, "dl")
            ctx.w(v, "dg")
            ctx.w(v, "tv")
            y = ctx.BB(t * C, C)
            v.reduce_sum(out=s["ST"][:, 24:25], in_=y, axis=AX)
            v.drain()
            v.tensor_scalar_mul(out=s["ST"][:, 24:25], in0=s["ST"][:, 24:25],
                                scalar1=1.0 / C)
            v.drain()
            ctx.tick("tv", v.engine_nop())
        ctx.on("vector", fln)

        def fln_s(sc, t=t):
            ctx.w(sc, "tv")
            y = ctx.BB(t * C, C)
            for k in range(4):
                sc.activation(out=s["TA"][:, 0:512], in_=y[:, 512 * k:512 * (k + 1)],
                              func=AF.Square, bias=s["ST"][:, 24:25], scale=-1.0,
                              accum_out=s["ST"][:, 40 + k:41 + k])
            ctx.tick("ts", sc.copy(out=s["SM"][:, 191:192], in_=s["SM"][:, 191:192]))
        ctx.on("scalar", fln_s)

        def fln_v(v, t=t):
            ctx.w(v, "ts")
            v.tensor_add(out=s["ST"][:, 40:41], in0=s["ST"][:, 40:41],
                         in1=s["ST"][:, 41:42])
            v.tensor_add(out=s["ST"][:, 42:43], in0=s["ST"][:, 42:43],
                         in1=s["ST"][:, 43:44])
            v.drain()
            v.tensor_add(out=s["ST"][:, 40:41], in0=s["ST"][:, 40:41],
                         in1=s["ST"][:, 42:43])
            v.drain()
            ctx.tick("tv", v.tensor_scalar(out=s["ST"][:, 26:27], in0=s["ST"][:, 40:41],
                                           scalar1=1.0 / C, scalar2=LN_EPS,
                                           op0=ALU.mult, op1=ALU.add))
        ctx.on("vector", fln_v)

        def fln_s2(sc, t=t):
            ctx.w(sc, "tv")
            sc.activation(out=s["ST"][:, 26:27], in_=s["ST"][:, 26:27], func=AF.Ln)
            ctx.tick("ts", sc.activation(out=s["ST"][:, 26:27], in_=s["ST"][:, 26:27],
                                         func=AF.Exp, scale=-0.5))
        ctx.on("scalar", fln_s2)

        def fln2(v, t=t):
            ctx.w(v, "ts")
            y = ctx.BB(t * C, C)
            v.tensor_scalar(out=s["TA"][:], in0=y, scalar1=s["ST"][:, 24:25],
                            scalar2=s["ST"][:, 26:27], op0=ALU.subtract, op1=ALU.mult)
            v.tensor_tensor(out=s["TA"][:], in0=s["TA"][:], in1=s["BC"][:], op=ALU.mult)
            xr = ctx.xr(1 + t)
            ctx.tick("tv", v.tensor_add(out=xr, in0=xr, in1=s["TA"][:]))
        ctx.on("vector", fln2)

    ctx.barrier()
    emit_bc_table(ctx, d["l0_lnb"])
    import os as _os
    if not _os.environ.get("SKIP_LNB"):
        for t in range(NT):
            def flb(v, t=t):
                ctx.w(v, "dl")
                ctx.w(v, "dg")
                xr = ctx.xr(1 + t)
                ctx.tick("tv", v.tensor_add(out=xr, in0=xr, in1=s["BC"][:]))
            ctx.on("vector", flb)
    ctx.barrier()



def emit_l1_attn(ctx, upto="full", tb=0):
    nc, s, d = ctx.nc, ctx.s, ctx.d
    Tt = T_LOC
    ctx.barrier()

    def fld(sync):
        ctx.dtick("dl", sync.dma_start(out=s["ST"][:, 48:64],
                                       in_=d["l1_slope"].broadcast_to([128, H])))
        ctx.dtick("dl", sync.dma_start(out=s["CW"][:, 0:2], in_=d["l1_km"]))
        ctx.dtick("dl", sync.dma_start(out=s["ST"][:, 32:40],
                                       in_=d["cc_pownT"].broadcast_to([128, 8])))
        ctx.dtick("dl", sync.dma_start(out=s["ST"][:, 40:48],
                                       in_=d["cc_selT"].broadcast_to([128, 8])))
    ctx.on("sync", fld)
    PS = lambda h: s["ST"][:, 48 + h:49 + h]

    def silu_evac(dstf):
        def evac(t, oq, pb):
            def fs(sc, t=t, oq=oq, pb=pb):
                ctx.w(sc, "tt")
                ctx.tick("ts", sc.activation(
                    out=dstf(t)[:, oq * 512:(oq + 1) * 512], in_=pb[:], func=AF.Silu))
            ctx.on("scalar", fs)
        return evac

    emit_proj(ctx, Tt, 0, NT, C,
              lambda cc, o0, ol: d["l1_wqkvT"][128 * cc:128 * (cc + 1),
                                               2048 + o0:2048 + o0 + ol],
              F32, silu_evac(lambda t: ctx.BA(t * C, C)))
    emit_proj(ctx, Tt, 0, NT, C,
              lambda cc, o0, ol: d["l1_wqkvT"][128 * cc:128 * (cc + 1),
                                               4096 + o0:4096 + o0 + ol],
              F32, silu_evac(lambda t: ctx.BB(t * C, C)))

    if upto == "proj":
        return
    with nc.psum_tensor(f"pkv1_{ctx.uid()}", [128, 128], F32) as pkv:
        for h in range(H):
            def fkd(sc, h=h):
                if h == 0:
                    ctx.w(sc, "ts")
                    ctx.w(sc, "dl")
                ctx.w(sc, "tv")
                ctx.tick("ts", sc.activation(out=s["SM"][:, 2:4], in_=s["CW"][:, 0:2],
                                             func=AF.Exp, scale=PS(h)))
            ctx.on("scalar", fkd)

            def fbd(v, h=h):
                ctx.w(v, "ts")
                v.tensor_scalar_mul(out=s["SM"][:, 4:5], in0=PS(h), scalar1=-256.0)
                v.drain()
                ctx.tick("tv", v.engine_nop())
            ctx.on("vector", fbd)

            def fbd2(sc, h=h):
                ctx.w(sc, "tv")
                ctx.tick("ts", sc.activation(out=s["SM"][:, 4:5], in_=s["SM"][:, 4:5],
                                             func=AF.Exp))
            ctx.on("scalar", fbd2)

            for b in range(4):
                for jh in range(2):
                    def fks(v, h=h, b=b, jh=jh):
                        ctx.w(v, "ts")
                        ctx.w(v, "tt")
                        ksl = ctx.BA((2 * b + jh) * C + h * 128, 128)
                        v.tensor_scalar(
                            out=s["BC"][:, 128:256], in0=ksl,
                            scalar1=s["SM"][:, 2 + jh:3 + jh], scalar2=None,
                            op0=ALU.mult)
                        v.drain()
                        ctx.tick("tv", v.engine_nop())
                    ctx.on("vector", fks)

                    def fkm(te, h=h, b=b, jh=jh):
                        ctx.w(te, "tv")
                        ctx.tick("tt", te.matmul(
                            out=pkv[:], lhsT=s["BC"][:, 128:256],
                            rhs=ctx.BB((2 * b + jh) * C + h * 128, 128),
                            start=(jh == 0), stop=(jh == 1)))
                    ctx.on("tensor", fkm)

                def fkc(v, h=h, b=b):
                    ctx.w(v, "tt")
                    kvs = s["SM"][:, 64:192]
                    if b == 0:
                        inst = v.tensor_copy(out=kvs, in_=pkv[:])
                    else:
                        inst = v.scalar_tensor_tensor(out=kvs, in0=kvs,
                                                      scalar=s["SM"][:, 4:5], in1=pkv[:],
                                                      op0=ALU.mult, op1=ALU.add)
                    ctx.tick("tv", inst)
                    v.drain()
                ctx.on("vector", fkc)

            def fst(g, h=h):
                ctx.w(g, "tv")
                ctx.dtick("dg", g.dma_start(out=d["cin1"][h * 128:(h + 1) * 128, :],
                                            in_=s["SM"][:, 64:192]))
            ctx.on("gpsimd", fst)

    def fcc(g):
        ctx.w(g, "dg")
        ctx.tick("cc", g.collective_compute(
            "AllGather", ALU.bypass, replica_groups=RG, ins=[d["cin1"]],
            outs=[d["cout1"]]))
    ctx.on("gpsimd", fcc)

    for grp in range(4):
        emit_proj(ctx, Tt, 0, NT, 512,
                  lambda cc, o0, ol, grp=grp: d["l1_wqkvT"][128 * cc:128 * (cc + 1),
                                                            grp * 512 + o0:grp * 512 + o0 + ol],
                  F32, silu_evac(lambda t: s["QG"][:, t * 512:(t + 1) * 512]))

        with nc.psum_tensor(f"pq0_{ctx.uid()}", [128, 256], F32) as pq0, \
             nc.psum_tensor(f"pq1_{ctx.uid()}", [128, 256], F32) as pq1, \
             nc.psum_tensor(f"po0_{ctx.uid()}", [128, 128], F32) as po0, \
             nc.psum_tensor(f"po1_{ctx.uid()}", [128, 128], F32) as po1, \
             nc.psum_tensor(f"pkv2_{ctx.uid()}", [128, 128], F32) as pkv2, \
             nc.psum_tensor(f"ptq0_{ctx.uid()}", [128, 128], BF16) as ptq0, \
             nc.psum_tensor(f"ptq1_{ctx.uid()}", [128, 128], BF16) as ptq1:
            for hh in range(4):
                h = grp * 4 + hh

                def fcmb(g, h=h):
                    if h == 0:
                        ctx.w(g, "cc")
                    ctx.w(g, "tv")
                    ctx.w(g, "ts")
                    for j in range(8):
                        ctx.dtick("dg", g.dma_start(
                            out=s["TA"][:, j * 128:(j + 1) * 128],
                            in_=d["cout1"][j * 2048 + h * 128:j * 2048 + (h + 1) * 128, :]))
                ctx.on("gpsimd", fcmb)

                def fw8(sc, h=h):
                    ctx.w(sc, "dg")
                    ctx.tick("ts", sc.activation(out=s["SM"][:, 8:16],
                                                 in_=s["ST"][:, 32:40],
                                                 func=AF.Exp, scale=PS(h)))
                ctx.on("scalar", fw8)

                def fw8b(v, h=h):
                    ctx.w(v, "ts")
                    v.tensor_tensor(out=s["SM"][:, 8:16], in0=s["SM"][:, 8:16],
                                    in1=s["ST"][:, 40:48], op=ALU.mult)
                    kvs = s["SM"][:, 64:192]
                    v.drain()
                    v.tensor_scalar(out=kvs, in0=s["TA"][:, 0:128],
                                    scalar1=s["SM"][:, 8:9], scalar2=None, op0=ALU.mult)
                    for j in range(1, 8):
                        v.drain()
                        v.scalar_tensor_tensor(out=kvs,
                                               in0=s["TA"][:, j * 128:(j + 1) * 128],
                                               scalar=s["SM"][:, 8 + j:9 + j], in1=kvs,
                                               op0=ALU.mult, op1=ALU.add)
                    ctx.tick("tv", v.engine_nop())
                ctx.on("vector", fw8b)

                def ftl(sync, h=h):
                    ctx.w(sync, "tv")
                    ctx.dtick("dl", sync.dma_start(out=s["TA"][:, 0:256], in_=d["l1_i0"]))
                    ctx.dtick("dl", sync.dma_start(out=s["TA"][:, 256:512], in_=d["l1_i1"]))
                    ctx.dtick("dl", sync.dma_start(out=s["TA"][:, 512:768], in_=d["l1_msk0"]))
                    ctx.dtick("dl", sync.dma_start(out=s["TA"][:, 768:1024], in_=d["l1_msk1"]))
                    ctx.dtick("dl", sync.dma_start(out=s["TA"][:, 1024:2048],
                                                   in_=d["l1_qm"].broadcast_to([128, T_LOC])))
                ctx.on("sync", ftl)

                def fdec(sc, h=h):
                    ctx.w(sc, "dl")
                    sc.activation(out=s["TA"][:, 0:256], in_=s["TA"][:, 0:256],
                                  func=AF.Exp, scale=PS(h))
                    sc.activation(out=s["TA"][:, 256:512], in_=s["TA"][:, 256:512],
                                  func=AF.Exp, scale=PS(h))
                    ctx.tick("ts", sc.activation(out=s["TA"][:, 1024:2048],
                                                 in_=s["TA"][:, 1024:2048],
                                                 func=AF.Exp, scale=PS(h)))
                ctx.on("scalar", fdec)

                def fdecm(v, h=h):
                    ctx.w(v, "ts")
                    v.tensor_tensor(out=s["SC"][:, 0:256], in0=s["TA"][:, 0:256],
                                    in1=s["TA"][:, 512:768], op=ALU.mult)
                    ctx.tick("tv", v.tensor_tensor(out=s["SC"][:, 256:512],
                                                   in0=s["TA"][:, 256:512],
                                                   in1=s["TA"][:, 768:1024], op=ALU.mult))
                ctx.on("vector", fdecm)

                for t in range(NT):
                    def ftr(te, h=h, hh=hh, t=t):
                        ctx.w(te, "tv")
                        te.transpose(out=ptq0[:], in_=ctx.BA(t * C + h * 128, 128),
                                     identity=s["ID"][:])
                        ctx.tick("tt", te.transpose(
                            out=ptq1[:],
                            in_=s["QG"][:, t * 512 + hh * 128:t * 512 + (hh + 1) * 128],
                            identity=s["ID"][:]))
                    ctx.on("tensor", ftr)

                    def ftrc(v, t=t):
                        ctx.w(v, "tt")
                        v.tensor_copy(out=s["WSa"][:, 1024 + t * 128:1024 + (t + 1) * 128],
                                      in_=ptq0[:])
                        ctx.tick("tv", v.tensor_copy(
                            out=s["WSa"][:, t * 128:(t + 1) * 128], in_=ptq1[:]))
                    ctx.on("vector", ftrc)

                def fqd(v, h=h):
                    ctx.w(v, "tv")
                    ctx.tick("tv", v.tensor_tensor(out=s["WSb"][:, 0:1024],
                                                   in0=s["WSa"][:, 0:1024],
                                                   in1=s["TA"][:, 1024:2048],
                                                   op=ALU.mult))
                ctx.on("vector", fqd)

                def fkd2(sc, h=h):
                    ctx.w(sc, "tv")
                    ctx.tick("ts", sc.activation(out=s["SM"][:, 2:4], in_=s["CW"][:, 0:2],
                                                 func=AF.Exp, scale=PS(h)))
                ctx.on("scalar", fkd2)

                def fbd_2(v, h=h):
                    ctx.w(v, "ts")
                    v.tensor_scalar_mul(out=s["SM"][:, 4:5], in0=PS(h), scalar1=-256.0)
                    v.drain()
                    ctx.tick("tv", v.engine_nop())
                ctx.on("vector", fbd_2)

                def fbd2_2(sc, h=h):
                    ctx.w(sc, "tv")
                    ctx.tick("ts", sc.activation(out=s["SM"][:, 4:5], in_=s["SM"][:, 4:5],
                                                 func=AF.Exp))
                ctx.on("scalar", fbd2_2)

                for b in range(4):
                    def fkb(v, h=h, b=b):
                        ctx.w(v, "ts")
                        ctx.w(v, "tt")
                        v.tensor_copy(out=s["BC"][:, 0:128], in_=s["SM"][:, 64:192])
                        v.tensor_scalar(out=s["BC"][:, 128:256],
                                        in0=ctx.BA((2 * b) * C + h * 128, 128),
                                        scalar1=s["SM"][:, 2:3], scalar2=None,
                                        op0=ALU.mult)
                        v.tensor_scalar(
                            out=s["BC"][:, 256:384],
                            in0=ctx.BA((2 * b + 1) * C + h * 128, 128),
                            scalar1=s["SM"][:, 3:4], scalar2=None, op0=ALU.mult)
                        v.drain()
                        ctx.tick("tv", v.engine_nop())
                    ctx.on("vector", fkb)

                    def fqk(te, h=h, b=b):
                        ctx.w(te, "tv")
                        te.matmul(out=pq0[:],
                                  lhsT=s["WSa"][:, 1024 + (2 * b) * 128:1024 + (2 * b + 1) * 128],
                                  rhs=s["WSa"][:, b * 256:(b + 1) * 256],
                                  start=True, stop=True)
                        ctx.tick("tt", te.matmul(
                            out=pq1[:],
                            lhsT=s["WSa"][:, 1024 + (2 * b + 1) * 128:1024 + (2 * b + 2) * 128],
                            rhs=s["WSa"][:, b * 256:(b + 1) * 256],
                            start=True, stop=True))
                    ctx.on("tensor", fqk)

                    def fqs(v, b=b):
                        ctx.w(v, "tt")
                        v.tensor_tensor(out=s["WSb"][:, 1024:1280], in0=pq0[:],
                                        in1=s["SC"][:, 0:256], op=ALU.mult)
                        v.tensor_tensor(out=s["WSb"][:, 1280:1536], in0=pq1[:],
                                        in1=s["SC"][:, 256:512], op=ALU.mult)
                        v.drain()
                        ctx.tick("tv", v.engine_nop())
                    ctx.on("vector", fqs)

                    def fo(te, h=h, b=b):
                        ctx.w(te, "tv")
                        for mh, po in ((0, po0), (1, po1)):
                            te.matmul(out=po[:],
                                      lhsT=s["WSb"][:, 1024 + mh * 128:1152 + mh * 128],
                                      rhs=ctx.BB((2 * b) * C + h * 128, 128),
                                      start=True, stop=False)
                            te.matmul(out=po[:],
                                      lhsT=s["WSb"][:, 1280 + mh * 128:1408 + mh * 128],
                                      rhs=ctx.BB((2 * b + 1) * C + h * 128, 128),
                                      start=False, stop=False)
                            te.matmul(out=po[:],
                                      lhsT=s["WSb"][:, b * 256 + mh * 128:b * 256 + (mh + 1) * 128],
                                      rhs=s["BC"][:, 0:128],
                                      start=False, stop=True)
                        te.matmul(out=pkv2[:], lhsT=s["BC"][:, 128:256],
                                  rhs=ctx.BB((2 * b) * C + h * 128, 128),
                                  start=True, stop=False)
                        ctx.tick("tt", te.matmul(
                            out=pkv2[:], lhsT=s["BC"][:, 256:384],
                            rhs=ctx.BB((2 * b + 1) * C + h * 128, 128),
                            start=False, stop=True))
                    ctx.on("tensor", fo)

                    def fev(v, h=h, b=b):
                        ctx.w(v, "tt")
                        v.tensor_copy(out=ctx.BA((2 * b) * C + h * 128, 128), in_=po0[:])
                        v.tensor_copy(out=ctx.BA((2 * b + 1) * C + h * 128, 128),
                                      in_=po1[:])
                        kvs = s["SM"][:, 64:192]
                        ctx.tick("tv", v.scalar_tensor_tensor(
                            out=kvs, in0=kvs, scalar=s["SM"][:, 4:5], in1=pkv2[:],
                            op0=ALU.mult, op1=ALU.add))
                        v.drain()
                    ctx.on("vector", fev)

    ctx.barrier()
    if upto == "attn":
        return
    emit_bc_table(ctx, d["l1_normw"])

    for t in range(NT):
        def frs_s(sc, t=t):
            if t == 0:
                ctx.w(sc, "dl")
                ctx.w(sc, "dg")
                ctx.w(sc, "tv")
            ao = ctx.BA(t * C, C)
            for k in range(4):
                sc.activation(out=s["TA"][:, 0:512], in_=ao[:, 512 * k:512 * (k + 1)],
                              func=AF.Square, accum_out=s["ST"][:, 40 + k:41 + k])
            ctx.tick("ts", sc.copy(out=s["SM"][:, 191:192], in_=s["SM"][:, 191:192]))
        ctx.on("scalar", frs_s)

        def frs_v(v, t=t):
            ctx.w(v, "ts")
            rcol = s["SM"][:, 24 + t:25 + t]
            v.tensor_add(out=s["ST"][:, 40:41], in0=s["ST"][:, 40:41],
                         in1=s["ST"][:, 41:42])
            v.tensor_add(out=s["ST"][:, 42:43], in0=s["ST"][:, 42:43],
                         in1=s["ST"][:, 43:44])
            v.drain()
            v.tensor_add(out=s["ST"][:, 40:41], in0=s["ST"][:, 40:41],
                         in1=s["ST"][:, 42:43])
            v.drain()
            v.tensor_scalar(out=rcol, in0=s["ST"][:, 40:41],
                            scalar1=1.0 / C, scalar2=L1_RMS_EPS,
                            op0=ALU.mult, op1=ALU.add)
            v.drain()
            ctx.tick("tv", v.engine_nop())
        ctx.on("vector", frs_v)

        def frs_s2(sc, t=t):
            ctx.w(sc, "tv")
            rcol = s["SM"][:, 24 + t:25 + t]
            sc.activation(out=rcol, in_=rcol, func=AF.Ln)
            ctx.tick("ts", sc.activation(out=rcol, in_=rcol, func=AF.Exp, scale=-0.5))
        ctx.on("scalar", frs_s2)

    def gt_evac(t, oq, pb):
        def fg1(sc, t=t, oq=oq, pb=pb):
            ctx.w(sc, "tt")
            ctx.w(sc, "tv")
            ctx.tick("ts", sc.activation(out=s["TA"][:, 0:512], in_=pb[:],
                                         func=AF.Sigmoid))
        ctx.on("scalar", fg1)

        def fg2(v, t=t, oq=oq):
            ctx.w(v, "ts")
            ctx.w(v, "dg")
            v.tensor_tensor(out=s["TA"][:, 0:512], in0=s["TA"][:, 0:512],
                            in1=ctx.BA(t * C + oq * 512, 512), op=ALU.mult)
            v.tensor_scalar(out=s["TA"][:, 0:512], in0=s["TA"][:, 0:512],
                            scalar1=s["SM"][:, 24 + t:25 + t], scalar2=None,
                            op0=ALU.mult)
            ctx.tick("tv", v.tensor_tensor(
                out=ctx.BB(t * C + oq * 512, 512), in0=s["TA"][:, 0:512],
                in1=s["BC"][:, oq * 512:(oq + 1) * 512], op=ALU.mult))
        ctx.on("vector", fg2)

    emit_proj(ctx, Tt, 0, NT, C,
              lambda cc, o0, ol: d["l1_wgT"][128 * cc:128 * (cc + 1), o0:o0 + ol],
              F32, gt_evac)

    with nc.psum_tensor(f"ptg0_{ctx.uid()}", [128, 128], BF16) as g0, \
         nc.psum_tensor(f"ptg1_{ctx.uid()}", [128, 128], BF16) as g1:
        gp = [g0, g1]
        i = 0
        for cc in range(NC):
            for t in range(NT):
                def ftp(te, cc=cc, t=t, i=i):
                    ctx.w(te, "tv")
                    ctx.tick("tt", te.transpose(
                        out=gp[i % 2][:], in_=ctx.BB(t * C + cc * 128, 128),
                        identity=s["ID"][:]))

                def ftc(v, cc=cc, t=t, i=i):
                    ctx.w(v, "tt")
                    ctx.tick("tv", v.tensor_copy(
                        out=ctx.BA(cc * T_LOC + t * 128, 128), in_=gp[i % 2][:]))
                ctx.on("tensor", ftp)
                ctx.on("vector", ftc)
                i += 1

    def out_evac(t, oq, pb):
        def fv(v, t=t, oq=oq, pb=pb):
            ctx.w(v, "tt")
            xr = ctx.xr(tb + t)
            ctx.tick("tv", v.tensor_add(out=xr[:, oq * 512:(oq + 1) * 512],
                                        in0=xr[:, oq * 512:(oq + 1) * 512], in1=pb[:]))
        ctx.on("vector", fv)

    emit_proj(ctx, T_LOC, 0, NT, C,
              lambda cc, o0, ol: d["l1_woT"][128 * cc:128 * (cc + 1), o0:o0 + ol],
              F32, out_evac,
              lhsT_fn=lambda cc, a, n: ctx.BA(cc * T_LOC + a, n))
    ctx.barrier()



def declare_dram(ctx, nc, fixed, pc, part):
    d = ctx.d
    need = {
        1: ["l0_pgT", "l0_puT", "l0_sg", "l0_su", "l0_pdT", "l0_sd", "shfcol",
            "identity"],
        2: ["l0_wqT", "l0_wkT", "l0_wvT", "l0_woT", "l0_wgT", "l0_convw", "l0_convb",
            "l0_gateb", "l0_lnw", "l0_lnb", "l0_gamma", "l0_gamma_rep", "identity",
            "iota_mj", "mask_mj", "grev", "shfcol", "cc_sel", "cc_pow"],
        4: ["l1_qgT", "l1_quT", "l1_qdT", "l1_sgc", "l1_suc", "l1_sd", "shfcol",
            "identity"],
        5: ["l1_slope", "l1_wqkvT", "l1_wgT", "l1_woT", "l1_normw", "l1_i0", "l1_i1",
            "l1_msk0", "l1_msk1", "l1_qm", "l1_km", "shfcol", "identity",
            "cc_selT", "cc_pownT"],
    }
    names = need[2 if part == 2 else (5 if part == 5 else (1 if part in (1, 3) else 4))]
    srcs = dict(fixed)
    srcs.update(pc)
    dtmap = {np.dtype(np.float32): F32, np.dtype(np.int32): I32,
             np.dtype(np.float16): F16}
    Tin = T_EXT if part <= 2 else T_LOC
    Tout = T_EXT if part == 1 else T_LOC
    d["xin"] = nc.dram_tensor("xin", [Tin, C], F32, kind="ExternalInput").ap()
    d["xout"] = nc.dram_tensor("xout", [Tout, C], F32, kind="ExternalOutput").ap()
    feed = {"xin": srcs["xin"][:Tin]}
    for n in names:
        a = srcs[n]
        d[n] = nc.dram_tensor(n, list(a.shape), dtmap[a.dtype], kind="ExternalInput").ap()
        feed[n] = a
    if part in (1, 3):
        d["cin_w"] = nc.dram_tensor("cin_w", [35 * 128, 1024], BF16, kind="Internal").ap()
        d["cout_w"] = nc.dram_tensor("cout_w", [280 * 128, 1024], BF16, kind="Internal", addr_space="Shared").ap()
        for n in ("myshard", "myscale"):
            a = pc[n]
            d[n] = nc.dram_tensor(n, list(a.shape), dtmap[a.dtype], kind="ExternalInput").ap()
    if part == 2:
        d["xc_d"] = nc.dram_tensor("xc_d", [NC, 128, T_LOC], BF16, kind="Internal").ap()
        d["cin0"] = nc.dram_tensor("cin0", [1, C], F32, kind="Internal").ap()
        d["cout0"] = nc.dram_tensor("cout0", [8, C], F32, kind="Internal").ap()
    if part == 5:
        d["cin1"] = nc.dram_tensor("cin1", [C, 128], F32, kind="Internal").ap()
        d["cout1"] = nc.dram_tensor("cout1", [8 * C, 128], F32, kind="Internal").ap()
    return feed


def build_part(part, fixed, per_core):
    nc = bass.Bass("TRN2", target_bir_lowering=False, debug=False)
    es = ExitStack()
    with es:
        ctx = Ctx(nc, es)
        declare_dram(ctx, nc, fixed, per_core[0], part)
        alloc_common(ctx, nc, es)
        block = es.enter_context(nc.Block())
        ctx.block = block
        emit_preamble(ctx)
        if part == 1:
            emit_load_x(ctx, NTE)
            emit_l0_dequant(ctx)
            emit_ln(ctx, 0, NTE, False)
            emit_ffn(ctx, "l0", 0, NTE, NTE * 128)
            emit_store_x(ctx, 0, NTE)
        elif part == 2:
            emit_load_x(ctx, NTE)
            emit_ln(ctx, 0, NTE, False)
            emit_l0_attn(ctx)
            emit_store_x(ctx, 1, NT)
        elif part == 3:
            emit_load_x(ctx, NT)
            emit_l0_dequant(ctx)
            emit_ln(ctx, 0, NT, False)
            emit_ffn(ctx, "l0", 0, NT, T_LOC)
            emit_store_x(ctx, 0, NT)
        elif part in (4, 6):
            emit_load_x(ctx, NT)
            emit_ln(ctx, 0, NT, False)
            emit_ffn(ctx, "l1", 0, NT, T_LOC)
            emit_store_x(ctx, 0, NT)
        elif part == 5:
            emit_load_x(ctx, NT)
            emit_ln(ctx, 0, NT, False)
            emit_l1_attn(ctx)
            emit_store_x(ctx, 0, NT)
    return nc


def feeds_for(part, fixed, per_core):
    need = {
        1: ["l0_pgT", "l0_puT", "l0_sg", "l0_su", "l0_pdT", "l0_sd", "shfcol",
            "identity"],
        2: ["l0_wqT", "l0_wkT", "l0_wvT", "l0_woT", "l0_wgT", "l0_convw", "l0_convb",
            "l0_gateb", "l0_lnw", "l0_lnb", "l0_gamma", "l0_gamma_rep", "identity",
            "iota_mj", "mask_mj", "grev", "shfcol", "cc_sel", "cc_pow"],
        4: ["l1_qgT", "l1_quT", "l1_qdT", "l1_sgc", "l1_suc", "l1_sd", "shfcol",
            "identity"],
        5: ["l1_slope", "l1_wqkvT", "l1_wgT", "l1_woT", "l1_normw", "l1_i0", "l1_i1",
            "l1_msk0", "l1_msk1", "l1_qm", "l1_km", "shfcol", "identity",
            "cc_selT", "cc_pownT"],
    }
    names = need[2 if part == 2 else (5 if part == 5 else (1 if part in (1, 3) else 4))]
    Tin = T_EXT if part <= 2 else T_LOC
    feeds = []
    for pc in per_core:
        f = {"xin": np.ascontiguousarray(pc["xin"][:Tin])}
        if part in (1, 3):
            f["myshard"] = pc["myshard"]
            f["myscale"] = pc["myscale"]
        for n in names:
            f[n] = pc[n] if n in pc else fixed[n]
        feeds.append(f)
    return feeds


PARTS = [1, 2, 3, 4, 5, 6]


ALL_NAMES = sorted(set(
    ["l0_pgT", "l0_puT", "l0_sg", "l0_su", "l0_pdT", "l0_sd", "shfcol", "identity",
     "l0_wqT", "l0_wkT", "l0_wvT", "l0_woT", "l0_wgT", "l0_convw", "l0_convb",
     "l0_gateb", "l0_lnw", "l0_lnb", "l0_gamma", "l0_gamma_rep",
     "iota_mj", "mask_mj", "grev", "cc_sel", "cc_pow",
     "l1_qgT", "l1_quT", "l1_qdT", "l1_sgc", "l1_suc", "l1_sd",
     "l1_slope", "l1_wqkvT", "l1_wgT", "l1_woT", "l1_normw", "l1_i0", "l1_i1",
     "l1_msk0", "l1_msk1", "l1_qm", "l1_km", "cc_selT", "cc_pownT"]))


def declare_dram_fused(ctx, nc, fixed, pc):
    d = ctx.d
    srcs = dict(fixed)
    srcs.update(pc)
    dtmap = {np.dtype(np.float32): F32, np.dtype(np.int32): I32,
             np.dtype(np.float16): F16}
    d["xin"] = nc.dram_tensor("xin", [T_EXT, C], F32, kind="ExternalInput").ap()
    d["xout"] = nc.dram_tensor("xout", [T_LOC, C], F32, kind="ExternalOutput").ap()
    for n in ALL_NAMES:
        a = srcs[n]
        d[n] = nc.dram_tensor(n, list(a.shape), dtmap[a.dtype], kind="ExternalInput").ap()
    d["cin_w"] = nc.dram_tensor("cin_w", [35 * 128, 1024], BF16, kind="Internal").ap()
    d["cout_w"] = nc.dram_tensor("cout_w", [280 * 128, 1024], BF16, kind="Internal", addr_space="Shared").ap()
    for n in ("myshard", "myscale"):
        a = pc[n]
        d[n] = nc.dram_tensor(n, list(a.shape), dtmap[a.dtype], kind="ExternalInput").ap()
    d["xc_d"] = nc.dram_tensor("xc_d", [NC, 128, T_LOC], BF16, kind="Internal").ap()
    d["cin0"] = nc.dram_tensor("cin0", [1, C], F32, kind="Internal").ap()
    d["cout0"] = nc.dram_tensor("cout0", [8, C], F32, kind="Internal").ap()
    d["cin1"] = nc.dram_tensor("cin1", [C, 128], F32, kind="Internal").ap()
    d["cout1"] = nc.dram_tensor("cout1", [8 * C, 128], F32, kind="Internal").ap()


def build_fused(fixed, per_core):
    nc = bass.Bass("TRN2", target_bir_lowering=False, debug=False)
    es = ExitStack()
    with es:
        ctx = Ctx(nc, es)
        declare_dram_fused(ctx, nc, fixed, per_core[0])
        alloc_common(ctx, nc, es)
        block = es.enter_context(nc.Block())
        ctx.block = block
        import os as _os
        nsub = int(_os.environ.get("N_SUB", "6"))
        emit_preamble(ctx)
        emit_load_x(ctx, NTE)
        emit_l0_dequant(ctx)
        emit_ln(ctx, 0, NTE, False)
        emit_ffn(ctx, "l0", 0, NTE, NTE * 128)
        if nsub >= 2:
            emit_ln(ctx, 0, NTE, False)
            emit_l0_attn(ctx)
        if nsub >= 3:
            emit_ln(ctx, 1, NT, False)
            emit_ffn(ctx, "l0", 1, NT, T_LOC)
        if nsub >= 4:
            emit_ln(ctx, 1, NT, False)
            emit_ffn(ctx, "l1", 1, NT, T_LOC)
        if nsub >= 5:
            emit_ln(ctx, 1, NT, False)
            emit_l1_attn(ctx, tb=1)
        if nsub >= 6:
            emit_ln(ctx, 1, NT, False)
            emit_ffn(ctx, "l1", 1, NT, T_LOC)
        if _os.environ.get("DBG_BIG"):
            dbg = nc.dram_tensor("dbg", [128, 2 * NT * C], BF16,
                                 kind="ExternalOutput").ap()

            def fdbg(g):
                ctx.wall(g)
                ctx.dtick("dg", g.dma_start(out=dbg, in_=ctx.s["BIG"][:, :]))
                ctx.w(g, "dg")
            ctx.on("gpsimd", fdbg)
        emit_store_x(ctx, 1, NT)
    return nc


def feeds_fused(fixed, per_core):
    feeds = []
    for pc in per_core:
        f = {"xin": np.ascontiguousarray(pc["xin"]),
             "myshard": pc["myshard"], "myscale": pc["myscale"]}
        for n in ALL_NAMES:
            f[n] = pc[n] if n in pc else fixed[n]
        feeds.append(f)
    return feeds


FUSED = True


def kernel(**inputs):
    fixed, per_core = prep(inputs)
    if FUSED:
        nc = build_fused(fixed, per_core)
        feeds = feeds_fused(fixed, per_core)
        res = run_bass_kernel_spmd(nc, feeds, core_ids=list(range(NCORES)))
        xs = [np.asarray(r["xout"]) for r in res.results]
        return np.concatenate(xs, 0).reshape(1, 8192, C).astype(np.float32)
    xs = [pc["xin"] for pc in per_core]
    for part in PARTS:
        for i in range(NCORES):
            per_core[i]["xin"] = xs[i]
        nc = build_part(part, fixed, per_core)
        feeds = feeds_for(part, fixed, per_core)
        res = run_bass_kernel_spmd(nc, feeds, core_ids=list(range(NCORES)))
        xs = [np.asarray(r["xout"]) for r in res.results]
    return np.concatenate(xs, 0).reshape(1, 8192, C).astype(np.float32)
